# version 02
# baseline (speedup 1.0000x reference)
import os
import math
import numpy as np
import ml_dtypes
import concourse.bass as bass
import concourse.mybir as mybir
from concourse.bass_utils import run_bass_kernel_spmd

F32 = mybir.dt.float32
BF16 = mybir.dt.bfloat16
I32 = mybir.dt.int32
AF = mybir.ActivationFunctionType
ALU = mybir.AluOpType
BF = ml_dtypes.bfloat16

T = 2048
D = 1024
H = 4
DH = 256
NCORES = 8
EPS = 1e-5
NT = T // 128
ND = D // 128
ENGINES = ("sync", "tensor", "vector", "scalar", "gpsimd")


class MK:
    def __init__(self, debug_taps=(), stop_after=None):
        self.nc = bass.Bass("TRN2", target_bir_lowering=False)
        self.debug_taps = set(debug_taps)
        self.stop_after = stop_after
        self.q = {e: [] for e in ENGINES}
        self.sems = {}
        self.semcnt = {}
        self.resolved = {}
        self.dbg_list = []
        self.consts = {}
        self.regions = {}
        self.nview = 0

    KEEP = {"edst0", "edst1", "emix", "ld", "cast", "tr", "dsl", "dwsc",
            "dwsl", "espl", "elsm", "eztr", "eql", "eqdq",
            "mm", "evp", "hist", "silu", "convg", "diag_b", "beta_mm",
            "beta_sig", "mdone", "duev", "dbce", "dsqe", "dgte", "dat", "dgt",
            "dws", "dout", "doutv", "dsr", "dsu", "dgram", "dbc", "dsq",
            "duw", "dcr", "datr", "btr", "beta_tr", "st_out", "eie",
            "datr_ev", "btr_ev", "etrpt", "etrpe"}

    def _resolve(self, name, eng=None, dma=False):
        if name is None or name in self.KEEP:
            return name
        g = None
        if eng is not None:
            g = f"__d_{eng}" if dma else f"__e_{eng}"
        if name in self.resolved:
            assert g is None or g == self.resolved[name], \
                f"sem {name} incremented from {g} and {self.resolved[name]}"
            return self.resolved[name]
        if g is None:
            raise KeyError(f"sem {name} waited before any inc")
        self.resolved[name] = g
        return g

    def _sem(self, name):
        if name not in self.sems:
            self.sems[name] = self.nc.alloc_semaphore(name)
            self.semcnt[name] = 0
        return self.sems[name]

    def inc(self, name, by=1, eng=None, dma=False):
        name = self._resolve(name, eng, dma)
        h = self._sem(name)
        self.semcnt[name] += by
        return h

    def cnt(self, name):
        if name not in self.KEEP and name in self.resolved:
            name = self.resolved[name]
        elif name not in self.KEEP and name not in self.sems:
            return 0
        self._sem(name)
        return self.semcnt[name]

    def wait(self, eng, name, val=None):
        nm = name
        if nm not in self.KEEP and nm in self.resolved:
            nm = self.resolved[nm]
        elif nm not in self.KEEP and nm not in self.sems:
            if val is None or val == 0:
                return
            raise KeyError(f"wait on unknown sem {name}")
        h = self._sem(nm)
        if val is None:
            val = self.semcnt[nm]
        if val > 0:
            self.q[eng].append(lambda e, h=h, val=val: e.wait_ge(h, val))

    REGION_BASE = 229344 - 12 * 16384

    def place(self, reg, shape, dtype, off=0):
        if isinstance(reg, str):
            reg = self.RN.index(reg)
        self.nview += 1
        return self.nc.alloc_sbuf_tensor_at(
            f"t{self.nview}", shape, dtype, offset=self.REGION_BASE + reg * 16384 + off)

    def sb(self, name, shape, dtype):
        return self.nc.alloc_sbuf_tensor(name, shape, dtype)

    def addr(self, t):
        return self.nc.lookup_mloc(t).addr

    def over(self, t, shape, dtype, off=0):
        self.nview += 1
        return self.nc.alloc_sbuf_tensor_at(
            f"t{self.nview}", shape, dtype, offset=self.addr(t) + off)

    def ps(self, name, shape, dtype=F32):
        return self.nc.alloc_psum_tensor(name, shape, dtype)

    def const(self, name, arr):
        if name not in self.consts:
            self.consts[name] = self.nc.inline_tensor(np.ascontiguousarray(arr), name).ap()
        return self.consts[name]

    def emit(self, eng, fn):
        self.q[eng].append(fn)

    def op(self, eng, fn, sem=None, by=1):
        h = self.inc(sem, by, eng=eng) if sem else None
        def g(e, fn=fn, h=h, by=by):
            ins = fn(e)
            if h is not None:
                ins.then_inc(h, by)
        self.q[eng].append(g)

    def dma(self, eng, dst, src, sem="ld"):
        h = self.inc(sem, 16, eng=eng, dma=True)
        self.q[eng].append(lambda e, h=h, dst=dst, src=src: e.dma_start(dst, src).then_inc(h, 16))

    def dmaT(self, dst, src, sem="tr"):
        h = self.inc(sem, 16, eng="sync", dma=True)
        self.q["sync"].append(lambda e, h=h, dst=dst, src=src: e.dma_start_transpose(dst, src).then_inc(h, 16))

    def ldbg(self, tap, srcs, shape, dtype):
        if tap not in self.debug_taps:
            return
        d = self.nc.dram_tensor("dbg_" + tap, shape, dtype, kind="ExternalOutput").ap()
        if not isinstance(srcs, (list, tuple)):
            srcs = [srcs]
        nrow = shape[0] // len(srcs)
        for i, src in enumerate(srcs):
            self.dma("sync", d[i * nrow:(i + 1) * nrow], src, sem="st_out")

    def dbg(self, tap, src_aps, shape, dtype, ready_sem):
        if tap not in self.debug_taps:
            return
        d = self.nc.dram_tensor("dbg_" + tap, shape, dtype, kind="ExternalOutput").ap()
        self.dbg_list.append((tap, d, src_aps, ready_sem, self.semcnt.get(ready_sem, 0)))

    def finalize(self, out_dram, out_src, ready_sems=()):
        for tap, d, srcs, rsem, rval in self.dbg_list:
            if rsem is not None and rval > 0:
                h = self.sems[rsem]
                self.q["sync"].append(lambda e, h=h, v=rval: e.wait_ge(h, v))
            if not isinstance(srcs, (list, tuple)):
                srcs = [srcs]
            nrow = d.shape[0] // len(srcs)
            for i, src in enumerate(srcs):
                h2 = self.inc("st_out", 16)
                self.q["sync"].append(lambda e, d=d, i=i, nrow=nrow, src=src, h2=h2:
                                      e.dma_start(d[i * nrow:(i + 1) * nrow], src).then_inc(h2, 16))
        if out_src is not None:
            for s in ready_sems:
                self.wait("sync", s)
            h2 = self.inc("st_out", 16)
            self.q["sync"].append(lambda e, h2=h2, out_dram=out_dram, out_src=out_src:
                                  e.dma_start(out_dram, out_src).then_inc(h2, 16))
        v = self.cnt("st_out")
        if v > 0:
            h = self.sems["st_out"]
            self.q["sync"].append(lambda e, h=h, v=v: e.wait_ge(h, v))

    def build(self):
        nc = self.nc
        assert nc.sbuf_base <= self.REGION_BASE, \
            f"normal sbuf allocs ({nc.sbuf_base}) overlap regions ({self.REGION_BASE})"
        with nc.Block() as blk:
            @blk.sync
            def _(e):
                for f in self.q["sync"]:
                    f(e)
            @blk.tensor
            def _(e):
                for f in self.q["tensor"]:
                    f(e)
            @blk.vector
            def _(e):
                for f in self.q["vector"]:
                    f(e)
            @blk.scalar
            def _(e):
                for f in self.q["scalar"]:
                    f(e)
            @blk.gpsimd
            def _(e):
                for f in self.q["gpsimd"]:
                    f(e)
        return nc


class FeatT:
    def __init__(self, mk, r0, r1):
        self.h = [mk.place(r0, [128, 4, T], BF16), mk.place(r1, [128, 4, T], BF16)]

    def ks(self, ks, sl=slice(None)):
        return self.h[ks // 4][:, ks % 4, sl]


class TokT:
    def __init__(self, mk, r0, r1, dtype=BF16, C=D):
        n = 16384 // (mybir.dt.size(dtype) * C)
        self.n = n
        self.h = [mk.place(r0, [128, n, C], dtype), mk.place(r1, [128, n, C], dtype)]

    def s(self, s, sl=slice(None)):
        return self.h[s // self.n][:, s % self.n, sl]


def dmaT_tok_from_feat(mk, tok, feat):
    for ds in range(ND):
        for half in range(2):
            dst = tok.h[half][:, :, 128 * ds:128 * ds + 128]
            mk.dmaT(dst, feat.ks(ds, slice(1024 * half, 1024 * half + 1024)))


def dmaT_feat_from_tok(mk, feat, tok):
    for s in range(NT):
        for half in range(2):
            dst = feat.h[half][:, :, 128 * s:128 * s + 128]
            mk.dmaT(dst, tok.s(s, slice(512 * half, 512 * half + 512)))


def build_program(debug_taps=(), stop_after=None, delta_macros=NT, delta_stage='all'):
    mk = MK(debug_taps, stop_after)
    nc = mk.nc

    x_in = nc.dram_tensor("x", [T, D], F32, kind="ExternalInput").ap()
    wq_in = nc.dram_tensor("wq", [D, D], F32, kind="ExternalInput").ap()
    wk_in = nc.dram_tensor("wk", [D, D], F32, kind="ExternalInput").ap()
    wv_in = nc.dram_tensor("wv", [D, D], F32, kind="ExternalInput").ap()
    wb_in = nc.dram_tensor("wb", [H, D], F32, kind="ExternalInput").ap()
    cq_in = nc.dram_tensor("cq", [D, 4], F32, kind="ExternalInput").ap()
    ck_in = nc.dram_tensor("ck", [D, 4], F32, kind="ExternalInput").ap()
    cv_in = nc.dram_tensor("cv", [D, 4], F32, kind="ExternalInput").ap()
    out_dram = nc.dram_tensor("out", [T, D], F32, kind="ExternalOutput").ap()
    fw1_in = nc.dram_tensor("fw1", [D, D + 16], F32, kind="ExternalInput").ap()
    fb1_in = nc.dram_tensor("fb1", [D], F32, kind="ExternalInput").ap()
    fw2_in = nc.dram_tensor("fw2", [H, D], F32, kind="ExternalInput").ap()
    fb2_in = nc.dram_tensor("fb2", [H], F32, kind="ExternalInput").ap()
    ltemp_in = nc.dram_tensor("ltemp", [1], F32, kind="ExternalInput").ap()
    rss_in = nc.dram_tensor("rss", [1], F32, kind="ExternalInput").ap()
    rsl_in = nc.dram_tensor("rsl", [1], F32, kind="ExternalInput").ap()
    onw_in = nc.dram_tensor("onw", [DH], F32, kind="ExternalInput").ap()
    wo_in = nc.dram_tensor("wo", [D, D], F32, kind="ExternalInput").ap()
    firs_in = nc.dram_tensor("firs", [H, DH, 5], F32, kind="ExternalInput").ap()
    firl_in = nc.dram_tensor("firl", [H, DH, 64], F32, kind="ExternalInput").ap()
    qw_in = nc.dram_tensor("qw", [6, D, D], I32, kind="ExternalInput").ap()
    qs_in = nc.dram_tensor("qs", [6, D, 64], F32, kind="ExternalInput").ap()
    qb_in = nc.dram_tensor("qb", [6, D], F32, kind="ExternalInput").ap()
    la_in = nc.dram_tensor("la", [6, 32, D], F32, kind="ExternalInput").ap()
    lb_in = nc.dram_tensor("lb", [6, D, 32], F32, kind="ExternalInput").ap()
    lng_in = nc.dram_tensor("lng", [3, D], F32, kind="ExternalInput").ap()
    lnb_in = nc.dram_tensor("lnb", [3, D], F32, kind="ExternalInput").ap()
    dscr = nc.dram_tensor("dscr", [2, 128, 8, D], BF16, kind="Internal").ap()

    mk.RN = ["RA0", "RA1", "RB0", "RB1", "RC0", "RC1", "RD0", "RD1", "RE0", "RE1", "RF0", "RF1"]
    ident = mk.sb("ident", [128, 128], BF16)
    cw = mk.sb("cw", [128, ND, 3, 4], F32)
    bst = mk.sb("bst", [H, D], BF16)
    wbT = mk.sb("wbT", [128, ND, H], BF16)
    beta_tok = mk.sb("beta_tok", [128, NT, H], F32)
    rqrk = mk.sb("rqrk", [128, NT, 2 * H], F32)
    sqacc = mk.sb("sqacc", [128, NT, H], F32)
    sqscr = mk.sb("sqscr", [128, 256], BF16)
    pre = mk.sb("pre", [128, 2, 4 + 512 + 4], BF16)
    diag = mk.sb("diag", [128, 2, 4, 128], BF16)

    idc = mk.const("identc", np.eye(128, dtype=np.float32).astype(BF))

    x_tok = TokT(mk, "RA0", "RA1")
    x_feat = FeatT(mk, "RB0", "RB1")
    featst = FeatT(mk, "RC0", "RC1")
    v_tok = TokT(mk, "RD0", "RD1")
    qn_tok = TokT(mk, "RA0", "RA1")
    kn_tok = TokT(mk, "RF0", "RF1")
    wT = [mk.place("RE0", [128, ND, D], BF16), mk.place("RE1", [128, ND, D], BF16)]
    stg32 = mk.place("RD0", [128, 4, 1024], F32)
    bst32 = mk.place("RD1", [H, 1024], F32)
    wstg = mk.place("RF0", [128, ND, D], BF16)
    beta_f = mk.place("RC0", [H, T], F32)
    brow16 = mk.place("RC0", [H, T], BF16, off=8192)

    pp = [mk.ps(f"pp{i}", [128, 512], F32) for i in range(2)]
    pc = [mk.ps(f"pc{i}", [128, 512], F32) for i in range(2)]
    pm = mk.ps("pm", [128, 512], F32)
    pt_bf = mk.ps("pt_bf", [128, 512], BF16)

    mk.dma("sync", cw[:, :, 0, :], cq_in.rearrange("(s p) j -> p s j", p=128))
    mk.dma("sync", cw[:, :, 1, :], ck_in.rearrange("(s p) j -> p s j", p=128))
    mk.dma("sync", cw[:, :, 2, :], cv_in.rearrange("(s p) j -> p s j", p=128))
    mk.dma("sync", ident[:], idc)
    LD_ALL = mk.cnt("ld")

    for qt in range(4):
        if qt > 0:
            mk.wait("sync", "cast", qt)
        mk.dma("sync", stg32[:], x_in[512 * qt:512 * qt + 512].rearrange(
            "(s p) d -> p s d", p=128), sem="lds")
        mk.wait("vector", "lds", mk.cnt("lds"))
        mk.op("vector", lambda e, qt=qt: e.tensor_copy(
            x_tok.h[qt // 2][:, 4 * (qt % 2):4 * (qt % 2) + 4, :], stg32[:]), "cast")

    def load_weight(win, slot, gate=None):
        if gate is not None:
            mk.wait("sync", *gate)
        for hf in range(2):
            mk.wait("sync", "cast")
            mk.dma("sync", stg32[:], win.rearrange("(s p) d -> p s d", p=128)[:, 4 * hf:4 * hf + 4, :],
                   sem="lds")
            mk.wait("vector", "lds", mk.cnt("lds"))
            if hf == 0:
                mk.wait("vector", "tr")
            mk.op("vector", lambda e, hf=hf: e.tensor_copy(
                wstg[:, 4 * hf:4 * hf + 4, :], stg32[:]), "cast")
        mk.wait("sync", "cast")
        for os_ in range(ND):
            mk.dmaT(wT[slot][:, :, 128 * os_:128 * os_ + 128], wstg[:, os_, :])
        return mk.cnt("tr")

    mk.wait("sync", "cast", 4)
    dmaT_feat_from_tok(mk, x_feat, x_tok)
    TR_X = mk.cnt("tr")

    TR_WQ = load_weight(wq_in, 0)
    TR_WK = load_weight(wk_in, 1)
    mk.wait("sync", "cast")
    mk.dma("sync", bst32[:], wb_in, sem="lds")
    mk.wait("vector", "lds", mk.cnt("lds"))
    mk.op("vector", lambda e: e.tensor_copy(bst[:], bst32[:]), "cast")
    CAST_BST = mk.cnt("cast")

    mk.wait("tensor", "ld", LD_ALL)
    mk.wait("tensor", "cast", CAST_BST)
    mk.wait("tensor", "tr", TR_X)
    for ds in range(ND):
        sem = "btr" if ds == ND - 1 else None
        mk.op("tensor", lambda e, ds=ds: e.transpose(
            pt_bf[:, 4 * ds:4 * ds + 4], bst[:, 128 * ds:128 * ds + 128], ident[0:H, 0:H]), sem)
    mk.wait("vector", "btr")
    mk.op("vector", lambda e: e.tensor_copy(wbT[:].rearrange("p s h -> p (s h)"), pt_bf[:, 0:32]), "btr_ev")
    mk.wait("tensor", "btr_ev")
    for tt in range(4):
        if tt > 0:
            mk.wait("tensor", "beta_sig", tt)
        hb = mk.inc("beta_mm")
        for ks in range(ND):
            def f(e, tt=tt, ks=ks, hb=hb, last=(ks == ND - 1)):
                ins = e.matmul(pm[0:4, :], wbT[:, ks, :], x_feat.ks(ks, slice(512 * tt, 512 * tt + 512)),
                               start=(ks == 0), stop=(ks == ND - 1))
                if last:
                    ins.then_inc(hb, 1)
            mk.emit("tensor", f)
        mk.wait("scalar", "beta_mm", tt + 1)
        mk.op("scalar", lambda e, tt=tt: e.activation(
            beta_f[:, 512 * tt:512 * tt + 512], pm[0:4, :], AF.Sigmoid), "beta_sig")
    mk.wait("vector", "beta_sig", 4)
    mk.op("vector", lambda e: e.tensor_copy(brow16[:], beta_f[:]), "beta_cast")
    mk.wait("tensor", "beta_cast")
    for ts_ in range(NT):
        sem = "beta_tr" if ts_ == NT - 1 else None
        mk.op("tensor", lambda e, ts_=ts_: e.transpose(
            pt_bf[:, 4 * ts_:4 * ts_ + 4], brow16[:, 128 * ts_:128 * ts_ + 128], ident[0:H, 0:H]), sem)
    mk.wait("vector", "beta_tr")
    mk.op("vector", lambda e: e.tensor_copy(
        beta_tok[:].rearrange("p s h -> p (s h)"), pt_bf[:, 0:64]), "beta_tok_rdy")

    mk.wait("vector", "ld", LD_ALL)

    G = [0]
    DG = [0]

    def emit_proj(slot, cwi, tr_needed, featst_free=None):
        mk.wait("tensor", "tr", tr_needed)
        if featst_free is not None:
            mk.wait("scalar", "tr", featst_free)
        for os_ in range(ND):
            d = DG[0]
            if d >= 2:
                mk.wait("vector", "convg", 4 * (d - 1))
            for j in range(4):
                sem = "diag_b" if j == 3 else None
                def f(e, par=d % 2, j=j, os_=os_, cwi=cwi):
                    return e.tensor_scalar(out=diag[:, par, j, :], in0=ident[:],
                                           scalar1=cw[:, os_, cwi, 3 - j:4 - j], scalar2=None,
                                           op0=ALU.mult)
                mk.op("vector", f, sem)
            dtarget = mk.cnt("diag_b")
            DG[0] += 1
            for tt in range(4):
                g = G[0]
                bank = pp[g % 2]
                cbank = pc[g % 2]
                par = g % 2
                if g >= 2:
                    mk.wait("tensor", "evp", g - 1)
                hmm_ = mk.inc("mm")
                for ks in range(ND):
                    def f(e, bank=bank, ks=ks, os_=os_, tt=tt, slot=slot, hmm_=hmm_, last=(ks == ND - 1)):
                        ins = e.matmul(bank[:], wT[slot][:, ks, 128 * os_:128 * os_ + 128],
                                       x_feat.ks(ks, slice(512 * tt, 512 * tt + 512)),
                                       start=(ks == 0), stop=(ks == ND - 1))
                        if last:
                            ins.then_inc(hmm_, 1)
                    mk.emit("tensor", f)
                mk.wait("scalar", "mm", g + 1)
                if g >= 1:
                    mk.wait("scalar", "hist", g)
                mk.op("scalar", lambda e, bank=bank, par=par: e.activation(
                    pre[:, par, 4:516], bank[:], AF.Copy), "evp")
                if tt == 0:
                    if g >= 2:
                        mk.wait("vector", "convg", g - 1)
                    mk.op("vector", lambda e, par=par: e.memset(pre[:, par, 0:4], 0.0), "hist")
                else:
                    mk.wait("vector", "evp", g)
                    mk.op("vector", lambda e, par=par: e.tensor_copy(
                        pre[:, par, 0:4], pre[:, 1 - par, 512:516]), "hist")
                mk.wait("tensor", "evp", g + 1)
                mk.wait("tensor", "hist", g + 1)
                if g >= 2:
                    mk.wait("tensor", "silu", g - 1)
                mk.wait("tensor", "diag_b", dtarget)
                hcv = mk.inc("convg")
                for j in range(4):
                    def f(e, cbank=cbank, par2=d % 2, j=j, par=par, hcv=hcv, last=(j == 3)):
                        ins = e.matmul(cbank[:], diag[:, par2, j, :],
                                       pre[:, par, 4 - j:516 - j],
                                       start=(j == 0), stop=(j == 3))
                        if last:
                            ins.then_inc(hcv, 1)
                    mk.emit("tensor", f)
                mk.wait("scalar", "convg", g + 1)
                mk.op("scalar", lambda e, cbank=cbank, os_=os_, tt=tt: e.activation(
                    featst.ks(os_, slice(512 * tt, 512 * tt + 512)), cbank[:], AF.Silu), "silu")
                G[0] += 1

    def emit_tok(nm, tok_dst, norm_col):
        mk.wait("sync", "silu")
        dmaT_tok_from_feat(mk, tok_dst, featst)
        trv = mk.cnt("tr")
        if norm_col is None:
            mk.wait("vector", "tr", trv)
            mk.op("vector", lambda e: e.memset(sqscr[0:1, 0:1], 0.0), nm + "_tok_rdy")
            return
        mk.wait("scalar", "tr", trv)
        for s in range(NT):
            for hh_ in range(H):
                sem = nm + "_sq" if (s == NT - 1 and hh_ == H - 1) else None
                def f(e, s=s, hh_=hh_, tok_dst=tok_dst):
                    return e.activation(sqscr[:], tok_dst.s(s, slice(DH * hh_, DH * hh_ + DH)), AF.Square,
                                        accum_out=sqacc[:, s, hh_:hh_ + 1])
                mk.op("scalar", f, sem)
        mk.wait("scalar", nm + "_sq")
        mk.emit("scalar", lambda e: e.drain())
        mk.op("scalar", lambda e: e.activation(sqacc[:], sqacc[:], AF.Sqrt), nm + "_sqrt")
        mk.wait("vector", nm + "_sqrt")
        mk.op("vector", lambda e, norm_col=norm_col: e.reciprocal(
            rqrk[:, :, norm_col * H:(norm_col + 1) * H], sqacc[:]), nm + "_rs")
        mk.wait("vector", nm + "_rs")
        for s in range(NT):
            for hh_ in range(H):
                sem = nm + "_tok_rdy" if (s == NT - 1 and hh_ == H - 1) else None
                def f(e, s=s, hh_=hh_, tok_dst=tok_dst, norm_col=norm_col):
                    return e.tensor_scalar(out=tok_dst.s(s, slice(DH * hh_, DH * hh_ + DH)),
                                           in0=tok_dst.s(s, slice(DH * hh_, DH * hh_ + DH)),
                                           scalar1=rqrk[:, s, norm_col * H + hh_:norm_col * H + hh_ + 1],
                                           scalar2=None, op0=ALU.mult)
                mk.op("vector", f, sem)

    emit_proj(0, 0, TR_WQ)
    emit_tok("q", qn_tok, 0)
    TR_QTOK = mk.cnt("tr")
    emit_proj(1, 1, TR_WK, featst_free=TR_QTOK)
    TR_WV = load_weight(wv_in, 0, gate=("silu", 32))
    emit_tok("k", kn_tok, 1)
    TR_KTOK = mk.cnt("tr")
    emit_proj(0, 2, TR_WV, featst_free=TR_KTOK)
    emit_tok("v", v_tok, None)

    delta_tok = TokT(mk, "RC0", "RC1")
    pb6 = mk.ps("pb6", [128, 512], F32)
    pb7 = mk.ps("pb7", [128, 512], F32)
    pt2 = pt_bf
    S_sb = mk.sb("S_sb", [128, H, 2, DH], BF16)

    qn_sl = mk.place("RE0", [128, 8, 128], BF16, off=0)
    kn_sl = mk.place("RE0", [128, 8, 128], BF16, off=2048)
    vb_sl = mk.place("RE0", [128, D], BF16, off=4096)
    kbt_sl = mk.place("RE0", [128, D], BF16, off=6144)
    w_sc = mk.place("RE0", [128, H, DH], BF16, off=8192)
    w_sl = mk.place("RE0", [128, H, 2, 128], BF16, off=10240)
    uadj = mk.place("RE0", [128, D], BF16, off=12288)
    GTw = mk.place("RE0", [128, H, 128], BF16, off=14336)
    GTq = mk.place("RE0", [128, H, 128], BF16, off=15360)
    invm = {}
    for i, nm_ in enumerate(["S1", "S1t", "P2", "P2t", "P4", "P4t", "P8", "P8t",
                             "P16", "P16t", "Brl0", "Brl1", "IB", "tmpA"]):
        invm[nm_] = mk.place("RE1", [128, H, 128], BF16, off=1024 * i)

    _blk = np.zeros((128, 128), np.float32)
    for j in range(4):
        _blk[32 * j:32 * j + 32, 32 * j:32 * j + 32] = np.tril(np.ones((32, 32)), -1)
    maskA_c = mk.const("maskA_c", (-_blk).astype(BF))
    _gt = np.zeros((128, 128), np.float32)
    for l in range(4):
        for j in range(4):
            if l < j:
                _gt[32 * l:32 * l + 32, 32 * j:32 * j + 32] = 1.0
            elif l == j:
                _gt[32 * l:32 * l + 32, 32 * j:32 * j + 32] = np.triu(np.ones((32, 32)))
    maskGT_c = mk.const("maskGT_c", _gt.astype(BF))
    maskA = mk.sb("maskA", [128, 128], BF16)
    maskGT = mk.sb("maskGT", [128, 128], BF16)
    mk.dma("gpsimd", maskA[:], maskA_c, sem="ldm")
    mk.dma("gpsimd", maskGT[:], maskGT_c, sem="ldm")

    UA = [pp[0], pp[1]]
    OUT = [pc[0], pc[1]]
    ROT = [pm, pb6]
    DS_BANKS = [pm, pb6, pb7, pp[0]]

    STAGES = ["nil", "dsl", "ts", "mm", "stt", "gram", "inv", "uw", "gtw", "gtq", "gt", "chain", "all"]
    def stage_ge(a):
        return STAGES.index(delta_stage) >= STAGES.index(a)

    def emit_delta():
        mk.op("vector", lambda e: e.memset(S_sb[:].rearrange("p h k d -> p (h k d)"), 0.0), "s_init")
        mk.wait("vector", "ldm")
        for eng in ("tensor", "sync", "vector", "scalar"):
            mk.wait(eng, "q_tok_rdy")
            mk.wait(eng, "k_tok_rdy")
            mk.wait(eng, "v_tok_rdy")
            mk.wait(eng, "beta_tok_rdy")
        mk.wait("tensor", "s_init")
        mk.wait("tensor", "ldm")

        for m in range(delta_macros):
            if m > 0:
                mk.wait("sync", "mdone", m)
            if not stage_ge("dsl"):
                mk.op("vector", lambda e: e.memset(sqscr[0:1, 0:1], 0.0), "mdone")
                continue
            mk.dmaT(qn_sl[:], qn_tok.s(m), sem="dsl")
            mk.dmaT(kn_sl[:], kn_tok.s(m), sem="dsl")
            DSL = mk.cnt("dsl")
            if not stage_ge("ts"):
                mk.wait("vector", "dsl", DSL)
                mk.op("vector", lambda e: e.memset(sqscr[0:1, 0:1], 0.0), "mdone")
                continue
            if m > 0:
                mk.wait("vector", "mdone", m)
            for hh_ in range(H):
                mk.op("vector", lambda e, hh_=hh_, m=m: e.tensor_scalar(
                    out=vb_sl[:, DH * hh_:DH * hh_ + DH], in0=v_tok.s(m, slice(DH * hh_, DH * hh_ + DH)),
                    scalar1=beta_tok[:, m, hh_:hh_ + 1], scalar2=None, op0=ALU.mult),
                    "dts" if hh_ == H - 1 else None)
            for hh_ in range(H):
                mk.op("vector", lambda e, hh_=hh_, m=m: e.tensor_scalar(
                    out=kbt_sl[:, DH * hh_:DH * hh_ + DH], in0=kn_tok.s(m, slice(DH * hh_, DH * hh_ + DH)),
                    scalar1=beta_tok[:, m, hh_:hh_ + 1], scalar2=None, op0=ALU.mult),
                    "dts2" if hh_ == H - 1 else None)
            if not stage_ge("mm"):
                mk.wait("vector", "dts")
                mk.wait("vector", "dts2")
                mk.wait("vector", "dsl", DSL)
                mk.op("vector", lambda e: e.memset(sqscr[0:1, 0:1], 0.0), "mdone")
                continue
            mk.wait("tensor", "dsl", DSL)
            hgr = mk.inc("dgram")
            for hh_ in range(H):
                for ks in range(2):
                    def f(e, hh_=hh_, ks=ks, hgr=hgr, last=(hh_ == H - 1 and ks == 1)):
                        ins = e.matmul(ROT[0][:, 128 * hh_:128 * hh_ + 128],
                                       kn_sl[:, 2 * hh_ + ks, :], kn_sl[:, 2 * hh_ + ks, :],
                                       start=(ks == 0), stop=(ks == 1))
                        if last:
                            ins.then_inc(hgr, 1)
                    mk.emit("tensor", f)
            if not stage_ge("stt"):
                mk.wait("vector", "dgram")
                mk.op("vector", lambda e: e.memset(sqscr[0:1, 0:1], 0.0), "mdone")
                continue
            mk.wait("vector", "dgram")
            for hh_ in range(H):
                mk.op("vector", lambda e, hh_=hh_, m=m: e.scalar_tensor_tensor(
                    out=invm["S1"][:, hh_, :], in0=ROT[0][:, 128 * hh_:128 * hh_ + 128],
                    scalar=beta_tok[:, m, hh_:hh_ + 1], in1=maskA[:],
                    op0=ALU.mult, op1=ALU.mult), None)
            for hh_ in range(H):
                mk.op("vector", lambda e, hh_=hh_, m=m: e.tensor_scalar(
                    out=invm["tmpA"][:, hh_, :], in0=ROT[0][:, 128 * hh_:128 * hh_ + 128],
                    scalar1=beta_tok[:, m, hh_:hh_ + 1], scalar2=None, op0=ALU.mult),
                    "da0")
            mk.wait("tensor", "da0")
            htr = mk.inc("datr")
            for hh_ in range(H):
                def f(e, hh_=hh_, htr=htr, last=(hh_ == H - 1)):
                    ins = e.transpose(pt2[:, 128 * hh_:128 * hh_ + 128], invm["S1"][:, hh_, :], ident[:])
                    if last:
                        ins.then_inc(htr, 1)
                mk.emit("tensor", f)
            mk.wait("vector", "datr")
            mk.op("vector", lambda e: e.tensor_copy(
                invm["S1t"][:].rearrange("p h n -> p (h n)"), pt2[:]), "datr_ev")
            if not stage_ge("inv"):
                mk.wait("vector", "datr_ev")
                mk.op("vector", lambda e: e.memset(sqscr[0:1, 0:1], 0.0), "mdone")
                continue
            sq_plan = [("P2", "S1t", "S1"), ("P2t", "S1", "S1t"),
                       ("P4", "P2t", "P2"), ("P4t", "P2", "P2t"),
                       ("P8", "P4t", "P4"), ("P8t", "P4", "P4t"),
                       ("P16", "P8t", "P8"), ("P16t", "P8", "P8t")]
            mk.wait("tensor", "datr_ev")
            for si, (dstn, ln, rn) in enumerate(sq_plan):
                rb = ROT[si % 2]
                if si >= 2:
                    mk.wait("tensor", "dsqe", mk.cnt("dsqe") - 1)
                hsq = mk.inc("dsq")
                for hh_ in range(H):
                    def f(e, hh_=hh_, rb=rb, ln=ln, rn=rn, hsq=hsq, last=(hh_ == H - 1)):
                        ins = e.matmul(rb[:, 128 * hh_:128 * hh_ + 128],
                                       invm[ln][:, hh_, :], invm[rn][:, hh_, :],
                                       start=True, stop=True)
                        if last:
                            ins.then_inc(hsq, 1)
                    mk.emit("tensor", f)
                mk.wait("vector", "dsq")
                mk.op("vector", lambda e, rb=rb, dstn=dstn: e.tensor_copy(
                    invm[dstn][:].rearrange("p h n -> p (h n)"), rb[:]), "dsqe")
            bcur = "S1t"
            for r, (pn, ptn) in enumerate([("P2", "P2t"), ("P4", "P4t"), ("P8", "P8t"), ("P16", "P16t")]):
                rb = ROT[r % 2]
                mk.wait("tensor", "dsqe")
                if r >= 1:
                    mk.wait("tensor", "dbce", mk.cnt("dbce"))
                hbc = mk.inc("dbc")
                for hh_ in range(H):
                    def f(e, hh_=hh_, rb=rb, pn=pn, ptn=ptn, bcur=bcur, hbc=hbc, last=(hh_ == H - 1)):
                        e.matmul(rb[:, 128 * hh_:128 * hh_ + 128], invm[pn][:, hh_, :],
                                 invm[bcur][:, hh_, :], start=True, stop=False)
                        e.matmul(rb[:, 128 * hh_:128 * hh_ + 128], ident[:],
                                 invm[bcur][:, hh_, :], start=False, stop=False)
                        ins = e.matmul(rb[:, 128 * hh_:128 * hh_ + 128], ident[:],
                                       invm[ptn][:, hh_, :], start=False, stop=True)
                        if last:
                            ins.then_inc(hbc, 1)
                    mk.emit("tensor", f)
                nxt = "Brl%d" % (r % 2) if r < 3 else "IB"
                mk.wait("vector", "dbc")
                if r < 3:
                    mk.op("vector", lambda e, rb=rb, nxt=nxt: e.tensor_copy(
                        invm[nxt][:].rearrange("p h n -> p (h n)"), rb[:]), "dbce")
                else:
                    for hh_ in range(H):
                        mk.op("vector", lambda e, rb=rb, hh_=hh_: e.tensor_tensor(
                            invm["IB"][:, hh_, :], rb[:, 128 * hh_:128 * hh_ + 128],
                            ident[:], ALU.add), "dbce" if hh_ == H - 1 else None)
                bcur = nxt
            if not stage_ge("uw"):
                mk.wait("vector", "dbce")
                mk.op("vector", lambda e: e.memset(sqscr[0:1, 0:1], 0.0), "mdone")
                continue
            mk.wait("tensor", "dbce")
            mk.wait("tensor", "dts")
            mk.wait("tensor", "dts2")
            huw = mk.inc("duw")
            for hh_ in range(H):
                def f(e, hh_=hh_):
                    e.matmul(UA[hh_ // 2][:, 256 * (hh_ % 2):256 * (hh_ % 2) + 256],
                             invm["IB"][:, hh_, :], vb_sl[:, DH * hh_:DH * hh_ + DH],
                             start=(hh_ % 2 == 0), stop=False)
                mk.emit("tensor", f)
            for hh_ in range(H):
                def f(e, hh_=hh_, huw=huw, last=(hh_ == H - 1)):
                    ins = e.matmul(ROT[hh_ // 2][:, 256 * (hh_ % 2):256 * (hh_ % 2) + 256],
                                   invm["IB"][:, hh_, :], kbt_sl[:, DH * hh_:DH * hh_ + DH],
                                   start=(hh_ % 2 == 0), stop=(hh_ % 2 == 1))
                    if last:
                        ins.then_inc(huw, 1)
                mk.emit("tensor", f)
            mk.wait("vector", "duw")
            for half in range(2):
                mk.op("vector", lambda e, half=half: e.tensor_scalar(
                    out=w_sc[:, 2 * half:2 * half + 2, :].rearrange("p a b -> p (a b)"),
                    in0=ROT[half][:], scalar1=-1.0, scalar2=None, op0=ALU.mult),
                    "dwsc" if half == 1 else None)
            WSC = mk.cnt("dwsc")
            mk.wait("sync", "dwsc", WSC)
            for hh_ in range(H):
                mk.dmaT(w_sl[:, hh_, :, :], w_sc[:, hh_, :], sem="dwsl")
            DWSL = mk.cnt("dwsl")
            if not stage_ge("gtw"):
                mk.wait("vector", "dwsc", WSC)
                mk.op("vector", lambda e: e.memset(sqscr[0:1, 0:1], 0.0), "mdone")
                continue
            mk.wait("tensor", "dwsc", WSC)
            if m > 0:
                mk.wait("tensor", "dgte", mk.cnt("dgte"))
            hgt = mk.inc("dgt")
            if stage_ge("gtw"):
                for hh_ in range(H):
                    def f(e, hh_=hh_):
                        e.matmul(pb7[:, 128 * hh_:128 * hh_ + 128], invm["tmpA"][:, hh_, :],
                                 invm["IB"][:, hh_, :], start=True, stop=True)
                    mk.emit("tensor", f)
            if stage_ge("gtq"):
                for hh_ in range(H):
                    for ks in range(2):
                        def f(e, hh_=hh_, ks=ks, hgt=hgt, last=(hh_ == H - 1 and ks == 1)):
                            ins = e.matmul(ROT[0][:, 128 * hh_:128 * hh_ + 128],
                                           kn_sl[:, 2 * hh_ + ks, :], qn_sl[:, 2 * hh_ + ks, :],
                                           start=(ks == 0), stop=(ks == 1))
                            if last:
                                ins.then_inc(hgt, 1)
                        mk.emit("tensor", f)
            else:
                def f(e, hgt=hgt):
                    e.matmul(ROT[0][:, 0:128], invm["tmpA"][:, 0, :],
                             invm["IB"][:, 0, :], start=True, stop=True).then_inc(hgt, 1)
                mk.emit("tensor", f)
            mk.wait("vector", "dgt")
            if stage_ge("gtw"):
                mk.op("vector", lambda e: e.tensor_scalar(
                    out=GTw[:].rearrange("p h n -> p (h n)"), in0=pb7[:],
                    scalar1=-1.0, scalar2=None, op0=ALU.mult), None)
            if stage_ge("gtq"):
                for hh_ in range(H):
                    mk.op("vector", lambda e, hh_=hh_: e.tensor_tensor(
                        GTq[:, hh_, :], ROT[0][:, 128 * hh_:128 * hh_ + 128], maskGT[:], ALU.mult),
                        "dgte" if hh_ == H - 1 else None)
            else:
                mk.op("vector", lambda e: e.memset(GTq[:, 0, 0:1], 0.0), "dgte")
            if not stage_ge("chain"):
                mk.wait("vector", "dgte")
                mk.op("vector", lambda e: e.memset(sqscr[0:1, 0:1], 0.0), "mdone")
                continue
            mk.wait("tensor", "dwsl", DWSL)
            mk.wait("tensor", "dgte")
            if m > 0:
                mk.wait("tensor", "dout")
                mk.wait("tensor", "doutv")
            hws = mk.inc("dws")
            for hh_ in range(H):
                for ks in range(2):
                    def f(e, hh_=hh_, ks=ks, hws=hws, last=(hh_ == H - 1 and ks == 1)):
                        ins = e.matmul(UA[hh_ // 2][:, 256 * (hh_ % 2):256 * (hh_ % 2) + 256],
                                       w_sl[:, hh_, ks, :], S_sb[:, hh_, ks, :],
                                       start=False, stop=(ks == 1))
                        if last:
                            ins.then_inc(hws, 1)
                    mk.emit("tensor", f)
            for hh_ in range(H):
                for ks in range(2):
                    def f(e, hh_=hh_, ks=ks):
                        e.matmul(OUT[hh_ // 2][:, 256 * (hh_ % 2):256 * (hh_ % 2) + 256],
                                 qn_sl[:, 2 * hh_ + ks, :], S_sb[:, hh_, ks, :],
                                 start=(ks == 0 and hh_ % 2 == 0), stop=(ks == 1))
                    mk.emit("tensor", f)
            for j in range(4):
                if j > 0:
                    mk.wait("tensor", "duev", mk.cnt("duev"))
                    hcr = mk.inc("dcr")
                    for hh_ in range(H):
                        def f(e, j=j, hh_=hh_, hcr=hcr, last=(hh_ == H - 1)):
                            ins = e.matmul(
                                UA[hh_ // 2][32 * j:32 * j + 32, 256 * (hh_ % 2):256 * (hh_ % 2) + 256],
                                GTw[0:32 * j, hh_, 32 * j:32 * j + 32],
                                uadj[0:32 * j, DH * hh_:DH * hh_ + DH],
                                start=False, stop=True, tile_position=(0, 32 * j))
                            if last:
                                ins.then_inc(hcr, 1)
                        mk.emit("tensor", f)
                    for hh_ in range(H):
                        def f(e, j=j, hh_=hh_):
                            e.matmul(
                                OUT[hh_ // 2][32 * j:32 * j + 32, 256 * (hh_ % 2):256 * (hh_ % 2) + 256],
                                GTq[0:32 * j, hh_, 32 * j:32 * j + 32],
                                uadj[0:32 * j, DH * hh_:DH * hh_ + DH],
                                start=False, stop=True, tile_position=(0, 32 * j))
                        mk.emit("tensor", f)
                    mk.wait("vector", "dcr")
                    mk.wait("scalar", "dcr")
                else:
                    mk.wait("vector", "dws")
                    mk.wait("scalar", "dws")
                mk.op("vector", lambda e, j=j: e.tensor_copy(
                    uadj[32 * j:32 * j + 32, 0:512], UA[0][32 * j:32 * j + 32, :]), "duev")
                mk.op("scalar", lambda e, j=j: e.activation(
                    uadj[32 * j:32 * j + 32, 512:1024], UA[1][32 * j:32 * j + 32, :], AF.Copy), "duev")
                mk.wait("tensor", "duev", mk.cnt("duev"))
                hat = mk.inc("dat")
                for hh_ in range(H):
                    def f(e, j=j, hh_=hh_, hat=hat, last=(hh_ == H - 1)):
                        ins = e.matmul(
                            OUT[hh_ // 2][32 * j:32 * j + 32, 256 * (hh_ % 2):256 * (hh_ % 2) + 256],
                            GTq[32 * j:32 * j + 32, hh_, 32 * j:32 * j + 32],
                            uadj[32 * j:32 * j + 32, DH * hh_:DH * hh_ + DH],
                            start=False, stop=True, tile_position=(32 * j, 32 * j))
                        if last:
                            ins.then_inc(hat, 1)
                    mk.emit("tensor", f)
            hds = mk.inc("dsu")
            for hh_ in range(H):
                for ks in range(2):
                    def f(e, hh_=hh_, ks=ks, m=m, hds=hds, last=(hh_ == H - 1 and ks == 1)):
                        ins = e.matmul(DS_BANKS[hh_][:, 256 * ks:256 * ks + 256],
                                       kn_tok.s(m, slice(DH * hh_ + 128 * ks, DH * hh_ + 128 * ks + 128)),
                                       uadj[:, DH * hh_:DH * hh_ + DH],
                                       start=(ks == 0), stop=(ks == 1))
                        if last:
                            ins.then_inc(hds, 1)
                    mk.emit("tensor", f)
            mk.wait("vector", "dat")
            mk.wait("scalar", "dat")
            mk.op("vector", lambda e, m=m: e.tensor_copy(delta_tok.s(m, slice(0, 512)), OUT[0][:]), "doutv")
            mk.op("scalar", lambda e, m=m: e.activation(delta_tok.s(m, slice(512, 1024)), OUT[1][:], AF.Copy), "dout")
            mk.wait("vector", "dsu")
            for hh_ in range(H):
                mk.op("vector", lambda e, hh_=hh_: e.tensor_tensor(
                    S_sb[:, hh_, :, :].rearrange("p k d -> p (k d)"),
                    S_sb[:, hh_, :, :].rearrange("p k d -> p (k d)"),
                    DS_BANKS[hh_][:], ALU.add), "dsr" if hh_ == H - 1 else None)
            mk.wait("vector", "dout")
            mk.wait("vector", "dsr")
            mk.op("vector", lambda e: e.memset(sqscr[0:1, 0:1], 0.0), "mdone")
        mk.wait("sync", "mdone", delta_macros)

    emit_delta()

    FL, FF = 192, 97

    def dftc():
        l = np.arange(FL)[:, None]
        f = np.arange(FF)[None, :]
        ang = 2.0 * np.pi * l * f / FL
        Fre, Fim = np.cos(ang), -np.sin(ang)
        lo = np.arange(64, FL)[None, :]
        angi = 2.0 * np.pi * f.T * lo / FL
        c = np.full((FF, 1), 2.0); c[0, 0] = 1.0; c[96, 0] = 1.0
        iRe = (c / FL) * np.cos(angi)
        iIm = -(c / FL) * np.sin(angi)
        return Fre, Fim, iRe, iIm

    Fre_np, Fim_np, iRe_np, iIm_np = dftc()
    FB_np = np.stack([Fre_np[64:192], Fim_np[64:192]], 1)
    FA_np = np.zeros((128, 2, FF), np.float64)
    FA_np[64:128, 0, :] = Fre_np[0:64]
    FA_np[64:128, 1, :] = Fim_np[0:64]
    iF_np = np.stack([iRe_np, iIm_np], 1)
    def fwmat(K):
        k = np.arange(K)[:, None]
        f = np.arange(FF)[None, :]
        ang = 2.0 * np.pi * f * (K - 1 - k) / FL
        return np.stack([np.cos(ang), -np.sin(ang)], 1)
    FAc = mk.sb("FAc", [128, 2, FF], BF16)
    FBc = mk.sb("FBc", [128, 2, FF], BF16)
    iFc = mk.sb("iFc", [FF, 2, 128], BF16)
    Fw64 = mk.sb("Fw64", [64, 2, FF], BF16)
    Fw5 = mk.sb("Fw5", [5, 2, FF], BF16)
    mk.dma("sync", FAc[:], mk.const("FAc_c", FA_np.astype(BF)), sem="lde")
    mk.dma("sync", FBc[:], mk.const("FBc_c", FB_np.astype(BF)), sem="lde")
    mk.dma("sync", iFc[:], mk.const("iFc_c", iF_np.astype(BF)), sem="lde")
    mk.dma("sync", Fw64[:], mk.const("Fw64_c", fwmat(64).astype(BF)), sem="lde")
    mk.dma("sync", Fw5[:], mk.const("Fw5_c", fwmat(5).astype(BF)), sem="lde")

    W2T = mk.sb("W2T", [128, ND, H], BF16)
    b1f = mk.sb("b1f", [128, ND], F32)
    b2row = mk.sb("b2row", [1, H], BF16)
    b2rep = mk.sb("b2rep", [128, H], F32)
    onwrow = mk.sb("onwrow", [1, DH], BF16)
    assert mk.addr(sqacc) + 256 == mk.addr(wbT) + 1088
    onwrep = mk.over(wbT, [128, DH], BF16, 0)
    rtemp = mk.over(wbT, [128, 1], F32, 512)
    rss128 = mk.over(wbT, [128, 1], F32, 544)
    rsl128 = mk.over(wbT, [128, 1], F32, 576)
    rmsacc = mk.over(wbT, [128, NT, H], F32, 608)
    scal3 = mk.over(cw, [1, 8], F32, 0)
    onesrow = mk.sb("onesrow", [1, 128], BF16)
    mk.dma("sync", onesrow[:], mk.const("ones_c", np.ones((1, 128), np.float32).astype(BF)), sem="lde")
    ones512_c = mk.const("ones512_c", np.ones((1, 512), np.float32).astype(BF))
    assert mk.addr(maskA) == mk.addr(S_sb) + 4096
    assert mk.addr(maskGT) == mk.addr(maskA) + 256
    mv4 = mk.over(S_sb, [128, NT, H, 4, 2], BF16, 0)
    bn6 = mk.over(S_sb, [128, NT, H, 6], F32, 1024)
    aacc = mk.over(S_sb, [128, NT, H, 4], F32, 2560)
    ones512 = mk.over(S_sb, [1, 512], BF16, 3584)
    stats_tok = mk.place("RC1", [128, NT, H, 16], BF16, off=6144)
    logits_tok = mk.over(diag, [128, NT, H, H], F32, 0)
    fw_tok = mk.over(diag, [128, NT, H, H], F32, 1024)
    smax = mk.over(pre, [128, NT, H, 1], F32, 1696)

    W1xT = mk.place("RE0", [128, ND, D], BF16)
    WoT = mk.place("RE1", [128, ND, D], BF16)
    stg32b = mk.place("RC0", [128, 4, 1024], F32)
    wstgb = mk.place("RC1", [128, ND, D], BF16)
    Xsb = mk.place("RC0", [FF, 2, D], BF16, off=0)
    t_ab = mk.place("RC0", [FF, 2, D], BF16, off=4096)
    Ysb = mk.place("RC0", [FF, 2, D], BF16, off=8192)
    Whats = mk.place("RC1", [FF, 4, D], BF16, off=0)
    firT64 = mk.place("RC1", [64, ND, 128], BF16, off=8192)
    firT5 = mk.place("RC1", [5, ND, 128], BF16, off=10240)
    W1sT = mk.place("RC1", [16, D], BF16, off=12288)
    ll_tok = TokT(mk, "RA0", "RA1")
    ls_tok = TokT(mk, "RF0", "RF1")
    o_tok = ls_tok
    G0t = mk.place("RC0", [128, ND, 512], BF16, off=0)
    hdnt = mk.place("RC0", [128, ND, 512], BF16, off=8192)
    statsKt = mk.place("RC1", [16, H, 4, 128], BF16, off=2048)
    dstage = mk.place("RC0", [128, 2, D], BF16, off=0)
    mixA = mk.place("RC0", [128, D], BF16, off=4096)
    mixB = mk.place("RC0", [128, D], BF16, off=6144)
    zstage = mk.place("RC0", [128, ND, 512], BF16, off=0)
    ztokt = mk.place("RC0", [128, 4, D], BF16, off=8192)
    o_feat = FeatT(mk, "RA0", "RA1")
    x1_feat = o_feat
    hfeat = FeatT(mk, "RC0", "RC1")
    qwstg = mk.place("RB0", [128, 4, 1024], I32)
    qprim = mk.place("RB1", [128, 4, 1024], BF16)
    qbrow = mk.place("RB1", [1, D], BF16, off=8192)
    lngrow = mk.place("RB1", [1, D], BF16, off=10240)
    lnbrow = mk.place("RB1", [1, D], BF16, off=12288)
    qsF = mk.place("RB1", [128, ND, 64], F32, off=14336)
    WdqT = mk.place("RE0", [128, ND, D], BF16)
    ystage = mk.place("RD0", [128, ND, 512], BF16, off=0)
    ytokt = mk.place("RD0", [128, 4, D], BF16, off=8192)
    Zlora = mk.place("RD1", [32, T], BF16, off=0)
    grep = mk.place("RD1", [128, D], BF16, off=4096)
    brep = mk.place("RD1", [128, D], BF16, off=6144)
    laT = mk.place("RD1", [128, ND, 32], BF16, off=8192)
    lbT = mk.place("RD1", [32, ND, 128], BF16, off=8704)
    larow = mk.place("RD1", [32, D], BF16, off=10752)
    lbstg = mk.place("RD1", [128, ND, 32], BF16, off=12800)
    outstage = mk.place("RD0", [128, 4, D], F32, off=0)
    ln2tok = mk.place("RD1", [128, 4, D], BF16, off=8192)
    z_feat = FeatT(mk, "RF0", "RF1")
    w1s_stg = mk.over(pre, [128, ND, 16], BF16, 1120)
    lnmv = mk.over(pre, [128, ND, 2], F32, 1376)
    sder = mk.over(pre, [128, NT, H], F32, 1440)

    ROT3 = [pm, pb6, pb7]

    def eng_gate(name):
        for eng in ("tensor", "vector", "scalar", "sync", "gpsimd"):
            mk.wait(eng, name)

    def stats_pass(X, j, gate_sem):
        mk.wait("vector", gate_sem)
        mk.wait("scalar", gate_sem)
        for s in range(NT):
            for hh_ in range(H):
                sem = "ebn" if (s == NT - 1 and hh_ == H - 1) else None
                mk.op("vector", lambda e, s=s, hh_=hh_, X=X: e.bn_stats(
                    bn6[:, s, hh_, :], X.s(s, slice(DH * hh_, DH * hh_ + DH))), sem)
        mk.emit("vector", lambda e: e.drain())
        for s in range(NT):
            for hh_ in range(H):
                sem = "ebn2" if (s == NT - 1 and hh_ == H - 1) else None
                mk.op("vector", lambda e, s=s, hh_=hh_, j=j: e.bn_aggr(
                    mv4[:, s, hh_, j, :], bn6[:, s, hh_, :]), sem)
        for s in range(NT):
            for hh_ in range(H):
                sem = "eab" if (s == NT - 1 and hh_ == H - 1) else None
                mk.op("scalar", lambda e, s=s, hh_=hh_, j=j, X=X: e.activation(
                    sqscr[:], X.s(s, slice(DH * hh_, DH * hh_ + DH)), AF.Abs,
                    accum_out=aacc[:, s, hh_, j:j + 1]), sem)

    def peg(mmfns, sem):
        h = mk.inc(sem, eng="tensor")
        n = len(mmfns)
        for i, fn in enumerate(mmfns):
            def g(e, fn=fn, h=h, last=(i == n - 1)):
                ins = fn(e)
                if last:
                    ins.then_inc(h, 1)
            mk.emit("tensor", g)

    def emit_epilogue():
        stats_pass(delta_tok, 2, "mdone")
        stats_pass(v_tok, 3, "v_tok_rdy")
        mk.wait("sync", "mdone")
        mk.ldbg("Ldelta", [delta_tok.h[0][:].rearrange("p s n -> p (s n)"),
                           delta_tok.h[1][:].rearrange("p s n -> p (s n)")], [256, 8 * D], BF16)
        for half in range(2):
            mk.dma("sync", dscr[half], delta_tok.h[half][:], sem="espl")
        SPL = mk.cnt("espl")
        mk.wait("sync", "espl", SPL)
        if "Ldsc3" in mk.debug_taps:
            mk.dma("sync", wstgb[:, 0, :], dscr[0, :, 3, :], sem="edbg")
            mk.wait("sync", "edbg", mk.cnt("edbg"))
            mk.ldbg("Ldsc3", wstgb[:, 0, :], [128, D], BF16)
        mk.wait("sync", "ebn")
        mk.wait("sync", "eab")
        mk.dma("sync", ones512[:], ones512_c, sem="elds")

        def load_weightE(win, dstT):
            for hf in range(2):
                mk.wait("sync", "ecast")
                mk.dma("sync", stg32b[:], win.rearrange("(s p) d -> p s d", p=128)[:, 4 * hf:4 * hf + 4, :],
                       sem="elds")
                mk.wait("vector", "elds", mk.cnt("elds"))
                if hf == 0:
                    mk.wait("vector", "etr")
                mk.op("vector", lambda e, hf=hf: e.tensor_copy(
                    wstgb[:, 4 * hf:4 * hf + 4, :], stg32b[:]), "ecast")
            mk.wait("sync", "ecast")
            for os_ in range(ND):
                mk.dmaT(dstT[:, :, 128 * os_:128 * os_ + 128], wstgb[:, os_, :], sem="etr")
            return mk.cnt("etr")

        TR_W1X = load_weightE(fw1_in[:, 0:D], W1xT)
        TR_WO = load_weightE(wo_in, WoT)
        firstg = mk.over(pre, [128, ND, 64], BF16, 0)
        firstg5 = mk.over(pre, [128, ND, 5], BF16, 1024)
        mk.wait("gpsimd", "mdone")
        mk.dma("gpsimd", w1s_stg[:], fw1_in[:, D:D + 16].rearrange("(s p) j -> p s j", p=128), sem="elsm")
        mk.dma("gpsimd", bst[:], fw2_in, sem="elsm")
        mk.dma("gpsimd", b2row[:], fb2_in[None, :], sem="elsm")
        mk.dma("gpsimd", onwrow[:], onw_in[None, :], sem="elsm")
        mk.dma("gpsimd", scal3[0:1, 0:1], ltemp_in[None, :], sem="elsm")
        mk.dma("gpsimd", scal3[0:1, 1:2], rss_in[None, :], sem="elsm")
        mk.dma("gpsimd", scal3[0:1, 2:3], rsl_in[None, :], sem="elsm")
        mk.dma("gpsimd", firstg[:], firl_in.rearrange("h (s p) k -> p (h s) k", p=128), sem="elsm")
        mk.dma("gpsimd", firstg5[:], firs_in.rearrange("h (s p) k -> p (h s) k", p=128), sem="elsm")
        h_ = mk.inc("elsm", 16, eng="gpsimd", dma=True)
        mk.emit("gpsimd", lambda e, h_=h_: e.dma_start(
            b1f[:], fb1_in.rearrange("(s p) -> p s", p=128),
            allow_slow_non_contiguous=True).then_inc(h_, 16))
        ELSM = mk.cnt("elsm")
        mk.wait("tensor", "elsm", ELSM)
        mk.wait("tensor", "lde")

        mk.wait("vector", "etr")

        def tpair(tin_fn, nin, nout, out_ap):
            mk.wait("tensor", "etrpe")
            mk.op("tensor", lambda e, tin_fn=tin_fn, nin=nin, nout=nout: e.transpose(
                pt_bf[0:nout, 0:nin], tin_fn(), ident[0:nin, 0:nin]), "etrpt")
            mk.wait("vector", "etrpt")
            mk.op("vector", lambda e, out_ap=out_ap, nin=nin, nout=nout: e.tensor_copy(
                out_ap, pt_bf[0:nout, 0:nin]), "etrpe")

        for ds in range(ND):
            tpair(lambda ds=ds: firstg[:, ds, :], 128, 64, firT64[:, ds, :])
        for ds in range(ND):
            tpair(lambda ds=ds: firstg5[:, ds, :], 128, 5, firT5[:, ds, :])
        for ds in range(ND):
            tpair(lambda ds=ds: w1s_stg[:, ds, :], 128, 16, W1sT[:, 128 * ds:128 * ds + 128])
        for ds in range(ND):
            tpair(lambda ds=ds: bst[:, 128 * ds:128 * ds + 128], 4, 128, W2T[:, ds, :])
        mk.dbg("firT64", firT64[:].rearrange("p s n -> p (s n)"), [64, ND * 128], BF16, "etrpe")
        mk.dbg("firstg", firstg[:].rearrange("p s n -> p (s n)"), [128, ND * 64], BF16, "etrpe")

        mk.wait("scalar", "elsm", ELSM)
        mk.op("scalar", lambda e: e.activation(scal3[0:1, 3:4], scal3[0:1, 0:1], AF.Abs), None)
        mk.emit("scalar", lambda e: e.drain())
        mk.op("scalar", lambda e: e.activation(scal3[0:1, 4:5], scal3[0:1, 3:4], AF.Exp, scale=-1.0), None)
        mk.emit("scalar", lambda e: e.drain())
        mk.op("scalar", lambda e: e.activation(scal3[0:1, 5:6], scal3[0:1, 4:5], AF.Ln, bias=1.0), None)
        mk.emit("scalar", lambda e: e.drain())
        mk.op("scalar", lambda e: e.activation(scal3[0:1, 6:7], scal3[0:1, 0:1], AF.Relu), "esp")
        mk.wait("vector", "esp")
        mk.op("vector", lambda e: e.tensor_add(scal3[0:1, 7:8], scal3[0:1, 5:6], scal3[0:1, 6:7]), None)
        mk.emit("vector", lambda e: e.drain())
        mk.op("vector", lambda e: e.tensor_scalar(out=scal3[0:1, 7:8], in0=scal3[0:1, 7:8],
                                                  scalar1=1e-4, scalar2=None, op0=ALU.add), None)
        mk.emit("vector", lambda e: e.drain())
        mk.op("vector", lambda e: e.reciprocal(scal3[0:1, 3:4], scal3[0:1, 7:8]), "esp2")
        mk.wait("tensor", "esp2")
        mk.wait("tensor", "etrpe")
        sc16 = mk.over(cw, [1, 8], BF16, 32)
        mk.wait("vector", "esp2")
        mk.op("vector", lambda e: e.tensor_copy(sc16[:], scal3[0:1, 0:8]), "esc16")
        mk.wait("tensor", "esc16")
        peg([lambda e: e.matmul(ROT3[0][0:128, 0:1], onesrow[0:1, 0:128], sc16[0:1, 3:4], start=True, stop=True),
             lambda e: e.matmul(ROT3[0][0:128, 1:2], onesrow[0:1, 0:128], sc16[0:1, 1:2], start=False, stop=True),
             lambda e: e.matmul(ROT3[0][0:128, 2:3], onesrow[0:1, 0:128], sc16[0:1, 2:3], start=False, stop=True),
             lambda e: e.matmul(ROT3[0][0:128, 4:8], onesrow[0:1, 0:128], b2row[0:1, :], start=False, stop=True),
             lambda e: e.matmul(ROT3[0][0:128, 8:264], onesrow[0:1, 0:128], onwrow[0:1, :], start=False, stop=True)],
            "ebc")
        mk.wait("vector", "ebc")
        mk.op("vector", lambda e: e.tensor_copy(rtemp[:], ROT3[0][:, 0:1]), None)
        mk.op("vector", lambda e: e.tensor_copy(rss128[:], ROT3[0][:, 1:2]), None)
        mk.op("vector", lambda e: e.tensor_copy(rsl128[:], ROT3[0][:, 2:3]), None)
        mk.op("vector", lambda e: e.tensor_copy(b2rep[:], ROT3[0][:, 4:8]), None)
        mk.op("vector", lambda e: e.tensor_copy(onwrep[:], ROT3[0][:, 8:264]), "ebce")

        mk.wait("tensor", "ebce")
        for fi, (Fw, firT, wslot) in enumerate([(Fw64, firT64, 0), (Fw5, firT5, 2)]):
            for ri in range(2):
                for nh in range(2):
                    peg([lambda e, Fw=Fw, firT=firT, ri=ri, nh=nh: e.matmul(
                        ROT3[nh][0:FF, 0:512], Fw[:, ri, :],
                        firT[:].rearrange("k s n -> k (s n)")[:, 512 * nh:512 * nh + 512],
                        start=True, stop=True)], "ewh")
                mk.wait("vector", "ewh")
                mk.op("vector", lambda e, wslot=wslot, ri=ri: e.tensor_copy(
                    Whats[:, wslot + ri, 0:512], ROT3[0][0:FF, 0:512]), None)
                mk.op("vector", lambda e, wslot=wslot, ri=ri: e.tensor_copy(
                    Whats[:, wslot + ri, 512:1024], ROT3[1][0:FF, 0:512]), "ewhe")
                mk.wait("tensor", "ewhe", mk.cnt("ewhe"))

        XB = [pp[0], pp[1], pc[0], pc[1]]
        for b in range(NT):
            mms = []
            for ri in range(2):
                for nh in range(2):
                    def f(e, b=b, ri=ri, nh=nh):
                        return e.matmul(XB[2 * ri + nh][0:FF, :], FBc[:, ri, :],
                                        v_tok.s(b, slice(512 * nh, 512 * nh + 512)),
                                        start=True, stop=(b == 0))
                    mms.append(f)
                    if b > 0:
                        def f2(e, b=b, ri=ri, nh=nh):
                            return e.matmul(XB[2 * ri + nh][0:FF, :], FAc[64:128, ri, :],
                                            v_tok.s(b - 1, slice(512 * nh, 512 * nh + 512))[64:128, :],
                                            start=False, stop=True)
                        mms.append(f2)
            if b > 0:
                mk.wait("tensor", "exe", mk.cnt("exe"))
            peg(mms, "efw")
            mk.wait("scalar", "efw")
            for k4 in range(4):
                sem = "exe" if k4 == 3 else None
                mk.op("scalar", lambda e, k4=k4: e.activation(
                    Xsb[:, k4 // 2, 512 * (k4 % 2):512 * (k4 % 2) + 512],
                    XB[k4][0:FF, :], AF.Copy), sem)
            mk.wait("vector", "exe")
            for fi, wslot in ((0, 0), (1, 2)):
                mk.op("vector", lambda e, wslot=wslot: e.tensor_mul(
                    t_ab[:, 0, :], Xsb[:, 0, :], Whats[:, wslot + 0, :]), None)
                mk.op("vector", lambda e, wslot=wslot: e.tensor_mul(
                    t_ab[:, 1, :], Xsb[:, 1, :], Whats[:, wslot + 1, :]), None)
                mk.emit("vector", lambda e: e.drain())
                mk.op("vector", lambda e: e.tensor_sub(Ysb[:, 0, :], t_ab[:, 0, :], t_ab[:, 1, :]), None)
                mk.op("vector", lambda e, wslot=wslot: e.tensor_mul(
                    t_ab[:, 0, :], Xsb[:, 0, :], Whats[:, wslot + 1, :]), None)
                mk.op("vector", lambda e, wslot=wslot: e.tensor_mul(
                    t_ab[:, 1, :], Xsb[:, 1, :], Whats[:, wslot + 0, :]), None)
                mk.emit("vector", lambda e: e.drain())
                mk.op("vector", lambda e: e.tensor_add(Ysb[:, 1, :], t_ab[:, 0, :], t_ab[:, 1, :]), "epw")
                mk.wait("tensor", "epw")
                if b > 0 or fi > 0:
                    mk.wait("tensor", "eie", mk.cnt("eie"))
                mms = []
                for nh in range(2):
                    for ri in range(2):
                        def f(e, nh=nh, ri=ri):
                            return e.matmul(ROT3[nh][:, :], iFc[:, ri, :],
                                            Ysb[:, ri, 512 * nh:512 * nh + 512],
                                            start=(ri == 0), stop=(ri == 1))
                        mms.append(f)
                peg(mms, "einv")
                mk.wait("vector", "einv")
                mk.wait("scalar", "einv")
                dstT = ll_tok if fi == 0 else ls_tok
                mk.op("vector", lambda e, dstT=dstT, b=b: e.tensor_copy(
                    dstT.s(b, slice(0, 512)), ROT3[0][:, :]), "eie")
                mk.op("scalar", lambda e, dstT=dstT, b=b: e.activation(
                    dstT.s(b, slice(512, 1024)), ROT3[1][:, :], AF.Copy), "eie")
        mk.dbg("Xsb", Xsb[:].rearrange("p a n -> p (a n)"), [FF, 2 * D], BF16, "eie")
        if "Lll" in mk.debug_taps or "Lls" in mk.debug_taps:
            mk.wait("sync", "eie", mk.cnt("eie"))
            mk.ldbg("Lll", [ll_tok.h[0][:].rearrange("p s n -> p (s n)"),
                            ll_tok.h[1][:].rearrange("p s n -> p (s n)")], [256, 8 * D], BF16)
            mk.ldbg("Lls", [ls_tok.h[0][:].rearrange("p s n -> p (s n)"),
                            ls_tok.h[1][:].rearrange("p s n -> p (s n)")], [256, 8 * D], BF16)
        mk.op("vector", lambda e: e.memset(sqscr[0:1, 0:1], 0.0), "efir")

        stats_pass(ll_tok, 1, "efir")
        stats_pass(ls_tok, 0, "efir")
        mk.wait("vector", "ebn2")
        mk.wait("scalar", "eab")
        mk.wait("vector", "eab")
        for j in range(4):
            mk.op("vector", lambda e, j=j: e.tensor_copy(
                stats_tok[:, :, :, 4 * j + 0], mv4[:, :, :, j, 0]), None)
            mk.op("vector", lambda e, j=j: e.tensor_copy(
                stats_tok[:, :, :, 4 * j + 1], mv4[:, :, :, j, 1]), None)
            mk.op("vector", lambda e, j=j: e.tensor_scalar(
                out=stats_tok[:, :, :, 4 * j + 2], in0=aacc[:, :, :, j],
                scalar1=1.0 / DH, scalar2=None, op0=ALU.mult), None)
            mk.op("vector", lambda e, j=j: e.tensor_mul(
                sder[:, :, :], mv4[:, :, :, j, 0], mv4[:, :, :, j, 0]), None)
            mk.emit("vector", lambda e: e.drain())
            mk.op("vector", lambda e, j=j: e.tensor_add(
                sder[:, :, :], sder[:, :, :], mv4[:, :, :, j, 1]), "esd")
            mk.wait("scalar", "esd")
            mk.op("scalar", lambda e, j=j: e.activation(
                stats_tok[:, :, :, 4 * j + 3], sder[:, :, :], AF.Sqrt, scale=float(DH)), "esd2")
        mk.wait("vector", "esd2")
        mk.op("vector", lambda e: e.memset(sqscr[0:1, 0:1], 0.0), "estats")
        if "Lstats" in mk.debug_taps:
            mk.wait("sync", "esd2", mk.cnt("esd2"))
            mk.ldbg("Lstats", stats_tok[:].rearrange("p s h c -> p (s h c)"), [128, NT * H * 16], BF16)

        mk.wait("tensor", "etr", TR_WO)
        mk.wait("tensor", "estats")
        GR = [pp[0], pp[1]]
        HR = [pc[0], pc[1]]
        for tt in range(4):
            for hh_ in range(H):
                for sr in range(4):
                    mk.wait("tensor", "egkt", mk.cnt("egkt"))
                    mk.op("tensor", lambda e, hh_=hh_, sr=sr, tt=tt: e.transpose(
                        pt_bf[0:16, 128 * sr:128 * sr + 128], stats_tok[:, 4 * tt + sr, hh_, :],
                        ident[:]), "egk")
                mk.wait("vector", "egk")
                mk.op("vector", lambda e, hh_=hh_: e.tensor_copy(
                    statsKt[:, hh_, :, :].rearrange("p a n -> p (a n)"), pt_bf[0:16, :]), "egkt")
            for os_ in range(ND):
                gb = GR[os_ % 2]
                if os_ >= 2:
                    mk.wait("tensor", "eg0e", mk.cnt("eg0e") - 1)
                mms = []
                for ks in range(ND):
                    def f(e, gb=gb, ks=ks, os_=os_, tt=tt):
                        return e.matmul(gb[:], W1xT[:, ks, 128 * os_:128 * os_ + 128],
                                        x_feat.ks(ks, slice(512 * tt, 512 * tt + 512)),
                                        start=(ks == 0), stop=(ks == ND - 1))
                    mms.append(f)
                peg(mms, "eg0")
                mk.wait("scalar", "eg0")
                mk.op("scalar", lambda e, gb=gb, os_=os_: e.activation(
                    G0t[:, os_, :], gb[:], AF.Copy), "eg0e")
            for hh_ in range(H):
                mk.wait("tensor", "eg0e", mk.cnt("eg0e"))
                mk.wait("tensor", "egkt", mk.cnt("egkt"))
                for os_ in range(ND):
                    hb = HR[os_ % 2]
                    if os_ >= 2 or hh_ > 0:
                        mk.wait("tensor", "ehde", mk.cnt("ehde") - 1)
                    peg([lambda e, hb=hb, os_=os_, hh_=hh_: e.matmul(
                            hb[:], W1sT[:, 128 * os_:128 * os_ + 128], statsKt[:, hh_, :, :].rearrange("p a n -> p (a n)"),
                            start=True, stop=False),
                         lambda e, hb=hb, os_=os_: e.matmul(
                            hb[:], ident[:], G0t[:, os_, :], start=False, stop=True)], "ehd")
                    mk.wait("scalar", "ehd")
                    mk.op("scalar", lambda e, hb=hb, os_=os_: e.activation(
                        hdnt[:, os_, :], hb[:], AF.Gelu, bias=b1f[:, os_:os_ + 1]), "ehde")
                mk.wait("tensor", "ehde", mk.cnt("ehde"))
                mms = []
                for c4 in range(4):
                    for os_ in range(ND):
                        def f(e, c4=c4, os_=os_):
                            return e.matmul(pm[:, 4 * c4:4 * c4 + 4],
                                            hdnt[:, os_, 128 * c4:128 * c4 + 128], W2T[:, os_, :],
                                            start=(os_ == 0), stop=(os_ == ND - 1))
                        mms.append(f)
                peg(mms, "elg")
                mk.wait("vector", "elg")
                mk.op("vector", lambda e, tt=tt, hh_=hh_: e.tensor_copy(
                    logits_tok[:, 4 * tt:4 * tt + 4, hh_, :], pm[:, 0:16].rearrange("p (a b) -> p a b", a=4)), "elge")
                mk.wait("tensor", "elge", mk.cnt("elge"))
        mk.wait("vector", "elge")
        mk.op("vector", lambda e: e.tensor_add(
            logits_tok[:], logits_tok[:],
            b2rep[:, None, None, :].broadcast_to([128, NT, H, H])), None)
        mk.emit("vector", lambda e: e.drain())
        mk.op("vector", lambda e: e.reduce_max(
            smax[:], logits_tok[:], mybir.AxisListType.X), None)
        mk.emit("vector", lambda e: e.drain())
        mk.op("vector", lambda e: e.tensor_sub(
            logits_tok[:], logits_tok[:], smax[:].broadcast_to([128, NT, H, H])), "esm1")
        mk.wait("scalar", "esm1")
        mk.op("scalar", lambda e: e.activation(
            fw_tok[:], logits_tok[:], AF.Exp, scale=rtemp[:, 0:1]), "esm2")
        mk.wait("vector", "esm2")
        mk.op("vector", lambda e: e.reduce_sum(
            smax[:], fw_tok[:], mybir.AxisListType.X), None)
        mk.emit("vector", lambda e: e.drain())
        mk.op("vector", lambda e: e.reciprocal(smax[:], smax[:]), None)
        mk.emit("vector", lambda e: e.drain())
        mk.op("vector", lambda e: e.tensor_mul(
            fw_tok[:], fw_tok[:], smax[:].broadcast_to([128, NT, H, H])), None)
        mk.emit("vector", lambda e: e.drain())
        mk.op("vector", lambda e: e.tensor_scalar(
            out=fw_tok[:, :, :, 0], in0=fw_tok[:, :, :, 0],
            scalar1=rss128[:, 0:1], scalar2=None, op0=ALU.add), None)
        mk.emit("vector", lambda e: e.drain())
        mk.op("vector", lambda e: e.tensor_scalar(
            out=fw_tok[:, :, :, 1], in0=fw_tok[:, :, :, 1],
            scalar1=rsl128[:, 0:1], scalar2=None, op0=ALU.add), "efwr")

        mk.wait("sync", "efwr")
        mk.ldbg("Lfw", fw_tok[:].rearrange("p s h c -> p (s h c)"), [128, NT * H * H], F32)
        mk.ldbg("Lvm", [v_tok.h[0][:].rearrange("p s n -> p (s n)"),
                        v_tok.h[1][:].rearrange("p s n -> p (s n)")], [256, 8 * D], BF16)
        mk.ldbg("Lllm", [ll_tok.h[0][:].rearrange("p s n -> p (s n)"),
                         ll_tok.h[1][:].rearrange("p s n -> p (s n)")], [256, 8 * D], BF16)
        mk.ldbg("Llsm", [ls_tok.h[0][:].rearrange("p s n -> p (s n)"),
                         ls_tok.h[1][:].rearrange("p s n -> p (s n)")], [256, 8 * D], BF16)
        for s in range(NT):
            if s >= 2:
                mk.wait("sync", "emix", s - 1)
            mk.dma("sync", dstage[:, s % 2, :], dscr[s // 8, :, s % 8, :], sem=f"edst{s % 2}")
            mk.wait("vector", f"edst{s % 2}", mk.cnt(f"edst{s % 2}"))
            mk.wait("vector", "efwr")

            def bc(j, s=s):
                return fw_tok[:, s, :, j:j + 1].broadcast_to([128, H, DH])
            mk.op("vector", lambda e, a=dstage[:, s % 2, :].rearrange("p (h d) -> p h d", h=H), b=bc(2):
                  e.tensor_tensor(mixA[:].rearrange("p (h d) -> p h d", h=H), a, b, ALU.mult), None)
            mk.op("vector", lambda e, a=ll_tok.s(s).rearrange("p (h d) -> p h d", h=H), b=bc(1):
                  e.tensor_tensor(mixB[:].rearrange("p (h d) -> p h d", h=H), a, b, ALU.mult), None)
            mk.emit("vector", lambda e: e.drain())
            mk.op("vector", lambda e: e.tensor_add(mixA[:], mixA[:], mixB[:]), None)
            mk.op("vector", lambda e, a=ls_tok.s(s).rearrange("p (h d) -> p h d", h=H), b=bc(0):
                  e.tensor_tensor(mixB[:].rearrange("p (h d) -> p h d", h=H), a, b, ALU.mult), None)
            mk.emit("vector", lambda e: e.drain())
            mk.op("vector", lambda e: e.tensor_add(mixA[:], mixA[:], mixB[:]), None)
            mk.op("vector", lambda e, a=v_tok.s(s).rearrange("p (h d) -> p h d", h=H), b=bc(3):
                  e.tensor_tensor(mixB[:].rearrange("p (h d) -> p h d", h=H), a, b, ALU.mult), None)
            mk.emit("vector", lambda e: e.drain())
            mk.op("vector", lambda e, a=o_tok.s(s): e.tensor_add(a, mixA[:], mixB[:]), "emix")
            mk.wait("scalar", "emix", mk.cnt("emix"))
            for hh_ in range(H):
                sem = "erms" if hh_ == H - 1 else None
                mk.op("scalar", lambda e, s=s, hh_=hh_: e.activation(
                    sqscr[:], o_tok.s(s, slice(DH * hh_, DH * hh_ + DH)), AF.Square,
                    accum_out=rmsacc[:, s, hh_:hh_ + 1]), sem)
        if "Lo2" in mk.debug_taps:
            mk.wait("sync", "emix", mk.cnt("emix"))
            mk.ldbg("Lo2", [o_tok.h[0][:].rearrange("p s n -> p (s n)"),
                            o_tok.h[1][:].rearrange("p s n -> p (s n)")], [256, 8 * D], BF16)
            mk.wait("vector", "st_out", mk.cnt("st_out"))
        mk.wait("vector", "erms")
        mk.op("vector", lambda e: e.tensor_scalar(
            out=rmsacc[:], in0=rmsacc[:], scalar1=1.0 / DH, scalar2=EPS,
            op0=ALU.mult, op1=ALU.add), "erst")
        mk.wait("scalar", "erst")
        mk.op("scalar", lambda e: e.activation(rmsacc[:], rmsacc[:], AF.Sqrt), "erst2")
        mk.wait("vector", "erst2")
        mk.op("vector", lambda e: e.reciprocal(rmsacc[:], rmsacc[:]), "erst3")
        mk.wait("vector", "erst3")
        for s in range(NT):
            for hh_ in range(H):
                sem = "eon" if (s == NT - 1 and hh_ == H - 1) else None
                mk.op("vector", lambda e, s=s, hh_=hh_: e.scalar_tensor_tensor(
                    out=o_tok.s(s, slice(DH * hh_, DH * hh_ + DH)),
                    in0=o_tok.s(s, slice(DH * hh_, DH * hh_ + DH)),
                    scalar=rmsacc[:, s, hh_:hh_ + 1], in1=onwrep[:],
                    op0=ALU.mult, op1=ALU.mult), sem)
        mk.wait("sync", "eon")
        mk.ldbg("Lo", [o_tok.h[0][:].rearrange("p s n -> p (s n)"),
                       o_tok.h[1][:].rearrange("p s n -> p (s n)")], [256, 8 * D], BF16)
        dmaT_feat_from_tok(mk, o_feat, o_tok)
        TR_O = mk.cnt("tr")

        if "Lof" in mk.debug_taps:
            mk.wait("sync", "tr", TR_O)
            mk.ldbg("Lof", [o_feat.h[0][:].rearrange("p s n -> p (s n)"),
                            o_feat.h[1][:].rearrange("p s n -> p (s n)")], [256, 4 * T], BF16)
        if "Lonw" in mk.debug_taps:
            mk.wait("sync", "ebce", mk.cnt("ebce"))
            mk.ldbg("Lonw", onwrep[:], [128, DH], BF16)
        mk.wait("tensor", "tr", TR_O)
        for tt in range(4):
            for ds2 in range(ND):
                zb = GR[ds2 % 2]
                if ds2 >= 2 or tt > 0:
                    mk.wait("tensor", "eze", mk.cnt("eze") - 1)
                mms = []
                for cs in range(ND):
                    def f(e, zb=zb, cs=cs, ds2=ds2, tt=tt):
                        return e.matmul(zb[:], WoT[:, cs, 128 * ds2:128 * ds2 + 128],
                                        o_feat.ks(cs, slice(512 * tt, 512 * tt + 512)),
                                        start=(cs == 0), stop=False)
                    mms.append(f)
                mms.append(lambda e, zb=zb, ds2=ds2, tt=tt: e.matmul(
                    zb[:], ident[:], x_feat.ks(ds2, slice(512 * tt, 512 * tt + 512)),
                    start=False, stop=True))
                peg(mms, "ez")
                mk.wait("scalar", "ez")
                mk.op("scalar", lambda e, zb=zb, ds2=ds2: e.activation(
                    zstage[:, ds2, :], zb[:], AF.Copy), "eze")
            mk.wait("sync", "eze", mk.cnt("eze"))
            if tt > 0:
                mk.wait("sync", "ex1t", mk.cnt("ex1t"))
            if tt == 0:
                mk.ldbg("Lz0", zstage[:].rearrange("p s n -> p (s n)"), [128, ND * 512], BF16)
            for ds in range(ND):
                mk.dmaT(ztokt[:, :, 128 * ds:128 * ds + 128], zstage[:, ds, :], sem="eztr")
            ZTR = mk.cnt("eztr")
            if tt == 0 and "Lzt0" in mk.debug_taps:
                mk.wait("sync", "eztr", ZTR)
                mk.ldbg("Lzt0", ztokt[:].rearrange("p s n -> p (s n)"), [128, 4 * D], BF16)
            mk.wait("vector", "eztr", ZTR)
            for sr in range(4):
                for hf2 in range(2):
                    sem = "ebnz" if (sr == 3 and hf2 == 1) else None
                    mk.op("vector", lambda e, sr=sr, hf2=hf2: e.bn_stats(
                        bn6[:, 2 * sr + hf2, 0, :], ztokt[:, sr, 512 * hf2:512 * hf2 + 512]), sem)
            mk.emit("vector", lambda e: e.drain())
            for sr in range(4):
                sem = "ebnz2" if sr == 3 else None
                mk.op("vector", lambda e, sr=sr: e.bn_aggr(
                    lnmv[:, sr, :], bn6[:, 2 * sr:2 * sr + 2, 0, :]), sem)
            mk.emit("vector", lambda e: e.drain())
            mk.op("vector", lambda e: e.tensor_scalar(
                out=lnmv[:, 0:4, 1], in0=lnmv[:, 0:4, 1], scalar1=EPS, scalar2=None, op0=ALU.add), "elnv")
            mk.wait("scalar", "elnv", mk.cnt("elnv"))
            mk.op("scalar", lambda e: e.activation(lnmv[:, 0:4, 1], lnmv[:, 0:4, 1], AF.Sqrt), "elns")
            mk.wait("vector", "elns", mk.cnt("elns"))
            mk.op("vector", lambda e: e.reciprocal(lnmv[:, 0:4, 1], lnmv[:, 0:4, 1]), "elnr")
            mk.emit("vector", lambda e: e.drain())
            for sr in range(4):
                sem = "ex1" if sr == 3 else None
                mk.op("vector", lambda e, sr=sr: e.tensor_scalar(
                    out=ztokt[:, sr, :], in0=ztokt[:, sr, :],
                    scalar1=lnmv[:, sr, 0:1], scalar2=lnmv[:, sr, 1:2],
                    op0=ALU.subtract, op1=ALU.mult), sem)
            mk.wait("sync", "ex1", mk.cnt("ex1"))
            if tt == 0:
                mk.ldbg("Lzn0", ztokt[:].rearrange("p s n -> p (s n)"), [128, 4 * D], BF16)
                mk.ldbg("Llnmv", lnmv[:].rearrange("p s n -> p (s n)"), [128, 2 * ND], F32)
            for sr in range(4):
                s_ = 4 * tt + sr
                for half in range(2):
                    mk.dmaT(x1_feat.h[half][:, :, 128 * s_:128 * s_ + 128],
                            ztokt[:, sr, 512 * half:512 * half + 512], sem="ex1t")
        mk.wait("vector", "ex1t", mk.cnt("ex1t"))
        for half in range(2):
            sem = "ehcp" if half == 1 else None
            mk.op("vector", lambda e, half=half: e.tensor_copy(
                hfeat.h[half][:], x1_feat.h[half][:]), sem)

        if "Lx1" in mk.debug_taps:
            mk.wait("sync", "ex1t", mk.cnt("ex1t"))
            mk.ldbg("Lx1", [x1_feat.h[0][:].rearrange("p s n -> p (s n)"),
                            x1_feat.h[1][:].rearrange("p s n -> p (s n)")], [256, 4 * T], BF16)

        YR = [pp[0], pp[1]]

        def qlora(i, IN, OUT_IS_H, want_ln_gelu):
            b = i // 2
            mk.wait("gpsimd", "eqmm")
            mk.dma("gpsimd", larow[0:32, :], la_in[i], sem="eql")
            mk.dma("gpsimd", lbstg[:], lb_in[i].rearrange("(s p) r -> p s r", p=128), sem="eql")
            mk.dma("gpsimd", qbrow[:], qb_in[i][None, :], sem="eql")
            mk.dma("gpsimd", qsF[:], qs_in[i].rearrange("(s p) g -> p s g", p=128), sem="eql")
            if want_ln_gelu:
                mk.dma("gpsimd", lngrow[:], lng_in[b][None, :], sem="eql")
                mk.dma("gpsimd", lnbrow[:], lnb_in[b][None, :], sem="eql")
            QL = mk.cnt("eql")
            mk.wait("tensor", "eql", QL)
            for ds in range(ND):
                tpair(lambda ds=ds: larow[0:32, 128 * ds:128 * ds + 128], 32, 128, laT[:, ds, :])
            for ds in range(ND):
                tpair(lambda ds=ds: lbstg[:, ds, :], 128, 32, lbT[:, ds, :])
            if want_ln_gelu:
                mk.wait("tensor", "egbe", mk.cnt("egbe"))
                peg([lambda e: e.matmul(ROT3[2][:, 0:512], onesrow[0:1, 0:128], lngrow[0:1, 0:512],
                                        start=True, stop=True),
                     lambda e: e.matmul(ROT3[1][:, 0:512],
                                        onesrow[0:1, 0:128], lngrow[0:1, 512:1024], start=True, stop=True)],
                    "egb")
                mk.wait("vector", "egb")
                mk.op("vector", lambda e: e.tensor_copy(grep[:, 0:512], ROT3[2][:, 0:512]), None)
                mk.op("vector", lambda e: e.tensor_copy(grep[:, 512:1024], ROT3[1][:, 0:512]), "egbe")
                mk.wait("tensor", "egbe", mk.cnt("egbe"))
                peg([lambda e: e.matmul(ROT3[2][:, 0:512], onesrow[0:1, 0:128], lnbrow[0:1, 0:512],
                                        start=True, stop=True),
                     lambda e: e.matmul(ROT3[1][:, 0:512], onesrow[0:1, 0:128], lnbrow[0:1, 512:1024],
                                        start=True, stop=True)], "egb")
                mk.wait("vector", "egb")
                mk.op("vector", lambda e: e.tensor_copy(brep[:, 0:512], ROT3[2][:, 0:512]), None)
                mk.op("vector", lambda e: e.tensor_copy(brep[:, 512:1024], ROT3[1][:, 0:512]), "egbe")
            for hf in range(2):
                mk.wait("sync", "eqdq")
                if i > 0 and hf == 0:
                    mk.wait("sync", "eqmm")
                mk.dma("sync", qwstg[:], qw_in[i].rearrange("(s p) d -> p s d", p=128)[:, 4 * hf:4 * hf + 4, :],
                       sem="eqw")
                mk.wait("scalar", "eqw", mk.cnt("eqw"))
                mk.wait("scalar", "eql", QL)
                mk.op("scalar", lambda e: e.activation(qprim[:], qwstg[:], AF.Copy,
                                                       scale=1.0 / 7.5, bias=-1.0), "eqa")
                mk.wait("vector", "eqa", mk.cnt("eqa"))
                for osr in range(4):
                    sem = "eqd" if osr == 3 else None
                    mk.op("vector", lambda e, osr=osr, hf=hf: e.tensor_tensor(
                        qprim[:, osr, :].rearrange("p (g r) -> p g r", g=64),
                        qprim[:, osr, :].rearrange("p (g r) -> p g r", g=64),
                        qsF[:, 4 * hf + osr, :, None].broadcast_to([128, 64, 16]), ALU.mult), sem)
                mk.wait("sync", "eqd", mk.cnt("eqd"))
                for osr in range(4):
                    mk.dmaT(WdqT[:, :, 128 * (4 * hf + osr):128 * (4 * hf + osr) + 128],
                            qprim[:, osr, :], sem="eqdq")
            DQ = mk.cnt("eqdq")
            mk.wait("tensor", "eqdq", DQ)
            mk.wait("tensor", "etrpe", mk.cnt("etrpe"))
            mk.wait("tensor", "ehcp")
            mk.wait("tensor", "ezf", mk.cnt("ezf"))
            mk.wait("tensor", "eye", mk.cnt("eye"))
            for tt in range(4):
                mms = []
                for ks in range(ND):
                    def f(e, ks=ks, tt=tt, IN=IN):
                        return e.matmul(pc[0][0:32, :], laT[:, ks, :],
                                        IN.ks(ks, slice(512 * tt, 512 * tt + 512)),
                                        start=(ks == 0), stop=(ks == ND - 1))
                    mms.append(f)
                peg(mms, "ezl")
                mk.wait("vector", "ezl")
                mk.op("vector", lambda e, tt=tt: e.tensor_copy(
                    Zlora[0:32, 512 * tt:512 * tt + 512], pc[0][0:32, :]), "ezle")
                mk.wait("tensor", "ezle", mk.cnt("ezle"))
            for tt in range(4):
                for os_ in range(ND):
                    yb = YR[os_ % 2]
                    if os_ >= 2 or tt > 0:
                        mk.wait("tensor", "eye", mk.cnt("eye") - 1)
                    mms = []
                    for ks in range(ND):
                        def f(e, yb=yb, ks=ks, os_=os_, tt=tt, IN=IN):
                            return e.matmul(yb[:], WdqT[:, ks, 128 * os_:128 * os_ + 128],
                                            IN.ks(ks, slice(512 * tt, 512 * tt + 512)),
                                            start=(ks == 0), stop=False)
                        mms.append(f)
                    mms.append(lambda e, yb=yb, os_=os_: e.matmul(
                        yb[:], qbrow[0:1, 128 * os_:128 * os_ + 128], ones512[0:1, :],
                        start=False, stop=False))
                    mms.append(lambda e, yb=yb, os_=os_, tt=tt: e.matmul(
                        yb[:], lbT[0:32, os_, :], Zlora[0:32, 512 * tt:512 * tt + 512],
                        start=False, stop=not OUT_IS_H))
                    if OUT_IS_H:
                        mms.append(lambda e, yb=yb, os_=os_, tt=tt: e.matmul(
                            yb[:], ident[:], hfeat.ks(os_, slice(512 * tt, 512 * tt + 512)),
                            start=False, stop=True))
                    peg(mms, "eym")
                    mk.wait("scalar", "eym")
                    if OUT_IS_H:
                        mk.op("scalar", lambda e, yb=yb, os_=os_, tt=tt: e.activation(
                            hfeat.ks(os_, slice(512 * tt, 512 * tt + 512)), yb[:], AF.Copy), "eye")
                    else:
                        mk.op("scalar", lambda e, yb=yb, os_=os_: e.activation(
                            ystage[:, os_, :], yb[:], AF.Copy), "eye")
                if not OUT_IS_H:
                    mk.wait("sync", "eye", mk.cnt("eye"))
                    mk.wait("sync", "ezf", mk.cnt("ezf"))
                    for ds in range(ND):
                        mk.dmaT(ytokt[:, :, 128 * ds:128 * ds + 128], ystage[:, ds, :], sem="eytr")
                    mk.wait("vector", "eytr", mk.cnt("eytr"))
                    for sr in range(4):
                        for hf2 in range(2):
                            sem = "ebny" if (sr == 3 and hf2 == 1) else None
                            mk.op("vector", lambda e, sr=sr, hf2=hf2: e.bn_stats(
                                bn6[:, 2 * sr + hf2, 0, :], ytokt[:, sr, 512 * hf2:512 * hf2 + 512]), sem)
                    mk.emit("vector", lambda e: e.drain())
                    for sr in range(4):
                        sem = "ebny2" if sr == 3 else None
                        mk.op("vector", lambda e, sr=sr: e.bn_aggr(
                            lnmv[:, sr, :], bn6[:, 2 * sr:2 * sr + 2, 0, :]), sem)
                    mk.emit("vector", lambda e: e.drain())
                    mk.op("vector", lambda e: e.tensor_scalar(
                        out=lnmv[:, 0:4, 1], in0=lnmv[:, 0:4, 1], scalar1=EPS, scalar2=None,
                        op0=ALU.add), "elnv2")
                    mk.wait("scalar", "elnv2", mk.cnt("elnv2"))
                    mk.op("scalar", lambda e: e.activation(lnmv[:, 0:4, 1], lnmv[:, 0:4, 1], AF.Sqrt), "elns2")
                    mk.wait("vector", "elns2", mk.cnt("elns2"))
                    mk.op("vector", lambda e: e.reciprocal(lnmv[:, 0:4, 1], lnmv[:, 0:4, 1]), None)
                    mk.emit("vector", lambda e: e.drain())
                    for sr in range(4):
                        mk.op("vector", lambda e, sr=sr: e.tensor_scalar(
                            out=ytokt[:, sr, :], in0=ytokt[:, sr, :],
                            scalar1=lnmv[:, sr, 0:1], scalar2=lnmv[:, sr, 1:2],
                            op0=ALU.subtract, op1=ALU.mult), None)
                    mk.emit("vector", lambda e: e.drain())
                    mk.op("vector", lambda e: e.tensor_tensor(
                        ytokt[:], ytokt[:], grep[:, None, :].broadcast_to([128, 4, D]), ALU.mult), None)
                    mk.emit("vector", lambda e: e.drain())
                    mk.op("vector", lambda e: e.tensor_tensor(
                        ytokt[:], ytokt[:], brep[:, None, :].broadcast_to([128, 4, D]), ALU.add), "elngb")
                    mk.wait("scalar", "elngb", mk.cnt("elngb"))
                    mk.op("scalar", lambda e: e.activation(ytokt[:], ytokt[:], AF.Gelu), "egel")
                    mk.wait("sync", "egel", mk.cnt("egel"))
                    for sr in range(4):
                        s_ = 4 * tt + sr
                        for half in range(2):
                            mk.dmaT(z_feat.h[half][:, :, 128 * s_:128 * s_ + 128],
                                    ytokt[:, sr, 512 * half:512 * half + 512], sem="ezf")
            mk.wait("gpsimd", "eym")
            hqm = mk.inc("eqmm", eng="gpsimd")
            mk.emit("gpsimd", lambda e, hqm=hqm: e.sem_inc(hqm, 1))

        mk.wait("gpsimd", "ehcp")
        for b in range(3):
            IN1 = hfeat
            qlora(2 * b, IN1, False, True)
            qlora(2 * b + 1, z_feat, True, False)
            if f"Lh{b + 1}" in mk.debug_taps:
                mk.wait("sync", "eye", mk.cnt("eye"))
                mk.ldbg(f"Lh{b + 1}", [hfeat.h[0][:].rearrange("p s n -> p (s n)"),
                                       hfeat.h[1][:].rearrange("p s n -> p (s n)")], [256, 4 * T], BF16)

        mk.wait("tensor", "eye", mk.cnt("eye"))
        for tt in range(4):
            if tt > 0:
                mk.wait("scalar", "st_out", mk.cnt("st_out"))
            for ds2 in range(ND):
                zb = YR[ds2 % 2]
                if ds2 >= 2 or tt > 0:
                    mk.wait("tensor", "efze", mk.cnt("efze") - 1)
                peg([lambda e, zb=zb, ds2=ds2, tt=tt: e.matmul(
                        zb[:], ident[:], x1_feat.ks(ds2, slice(512 * tt, 512 * tt + 512)),
                        start=True, stop=False),
                     lambda e, zb=zb, ds2=ds2, tt=tt: e.matmul(
                        zb[:], ident[:], hfeat.ks(ds2, slice(512 * tt, 512 * tt + 512)),
                        start=False, stop=True)], "efz")
                mk.wait("scalar", "efz")
                mk.op("scalar", lambda e, zb=zb, ds2=ds2: e.activation(
                    ystage[:, ds2, :], zb[:], AF.Copy), "efze")
            mk.wait("sync", "efze", mk.cnt("efze"))
            for ds in range(ND):
                mk.dmaT(ln2tok[:, :, 128 * ds:128 * ds + 128], ystage[:, ds, :], sem="el2t")
            mk.wait("vector", "el2t", mk.cnt("el2t"))
            for sr in range(4):
                for hf2 in range(2):
                    sem = "ebnf" if (sr == 3 and hf2 == 1) else None
                    mk.op("vector", lambda e, sr=sr, hf2=hf2: e.bn_stats(
                        bn6[:, 2 * sr + hf2, 0, :], ln2tok[:, sr, 512 * hf2:512 * hf2 + 512]), sem)
            mk.emit("vector", lambda e: e.drain())
            for sr in range(4):
                sem = "ebnf2" if sr == 3 else None
                mk.op("vector", lambda e, sr=sr: e.bn_aggr(
                    lnmv[:, sr, :], bn6[:, 2 * sr:2 * sr + 2, 0, :]), sem)
            mk.emit("vector", lambda e: e.drain())
            mk.op("vector", lambda e: e.tensor_scalar(
                out=lnmv[:, 0:4, 1], in0=lnmv[:, 0:4, 1], scalar1=EPS, scalar2=None,
                op0=ALU.add), "efnv")
            mk.wait("scalar", "efnv", mk.cnt("efnv"))
            mk.op("scalar", lambda e: e.activation(lnmv[:, 0:4, 1], lnmv[:, 0:4, 1], AF.Sqrt), "efns")
            mk.wait("vector", "efns", mk.cnt("efns"))
            mk.op("vector", lambda e: e.reciprocal(lnmv[:, 0:4, 1], lnmv[:, 0:4, 1]), None)
            mk.emit("vector", lambda e: e.drain())
            for sr in range(4):
                sem = "efout" if sr == 3 else None
                mk.op("vector", lambda e, sr=sr: e.tensor_scalar(
                    out=outstage[:, sr, :], in0=ln2tok[:, sr, :],
                    scalar1=lnmv[:, sr, 0:1], scalar2=lnmv[:, sr, 1:2],
                    op0=ALU.subtract, op1=ALU.mult), sem)
            mk.wait("sync", "efout", mk.cnt("efout"))
            for sr in range(4):
                s_ = 4 * tt + sr
                mk.dma("sync", out_dram[128 * s_:128 * s_ + 128, :], outstage[:, sr, :], sem="st_out")

    emit_epilogue()


    def tok_dump(t):
        return [t.h[0][:].rearrange("p s d -> p (s d)"), t.h[1][:].rearrange("p s d -> p (s d)")]
    mk.dbg("qn_tok", tok_dump(qn_tok), [256, 8 * D], BF16, "q_tok_rdy")
    mk.dbg("delta_tok", tok_dump(delta_tok), [256, 8 * D], BF16, "mdone")
    if delta_stage == 'all' and delta_macros == NT:
        mk.dbg("ll_tok", tok_dump(ll_tok), [256, 8 * D], BF16, "eie")
        mk.dbg("ls_tok", tok_dump(ls_tok), [256, 8 * D], BF16, "eie")
        mk.dbg("stats_tok", stats_tok[:].rearrange("p s h k -> p (s h k)"), [128, NT * H * 16], BF16, "estats")
        mk.dbg("fw_tok", fw_tok[:].rearrange("p s h j -> p (s h j)"), [128, NT * H * H], F32, "efwr")
        mk.dbg("o_tok", tok_dump(o_tok), [256, 8 * D], BF16, "eon")
        mk.dbg("Whats", Whats[:].rearrange("p a n -> p (a n)"), [FF, 4 * D], BF16, "ewhe")
        mk.dbg("mv4", mv4[:].rearrange("p s h j a -> p (s h j a)"), [128, NT * H * 8], BF16, "estats")

    for nm_ in ("S1", "S1t", "IB", "tmpA", "P2"):
        mk.dbg("inv_" + nm_, invm[nm_][:].rearrange("p h n -> p (h n)"), [128, H * 128], BF16, "mdone")
    mk.dbg("uadj", uadj[:], [128, D], BF16, "mdone")
    mk.dbg("GTq", GTq[:].rearrange("p h n -> p (h n)"), [128, H * 128], BF16, "mdone")
    mk.dbg("GTw", GTw[:].rearrange("p h n -> p (h n)"), [128, H * 128], BF16, "mdone")
    mk.dbg("S_sb", S_sb[:].rearrange("p h k d -> p (h k d)"), [128, H * 2 * DH], BF16, "mdone")
    mk.dbg("w_sl", w_sl[:].rearrange("p h k n -> p (h k n)"), [128, H * 2 * 128], BF16, "mdone")
    mk.dbg("kn_tok", tok_dump(kn_tok), [256, 8 * D], BF16, "k_tok_rdy")
    mk.dbg("v_tok", tok_dump(v_tok), [256, 8 * D], BF16, "v_tok_rdy")
    mk.dbg("beta_tok", beta_tok[:].rearrange("p s h -> p (s h)"), [128, NT * H], F32, "beta_tok_rdy")
    mk.dbg("x_feat", [x_feat.h[0][:].rearrange("p s t -> p (s t)"),
                      x_feat.h[1][:].rearrange("p s t -> p (s t)")], [256, 4 * T], BF16, "tr")

    mk.finalize(out_dram, None)
    return mk


def shard_inputs(inputs):
    x = np.ascontiguousarray(inputs["L0_attn_hidden_states"])
    shared = {
        "wq": np.ascontiguousarray(inputs["L0_attn_q_proj_w"]),
        "wk": np.ascontiguousarray(inputs["L0_attn_k_proj_w"]),
        "wv": np.ascontiguousarray(inputs["L0_attn_v_proj_w"]),
        "wb": np.ascontiguousarray(inputs["L0_attn_b_proj_w"]),
        "cq": np.ascontiguousarray(inputs["L0_attn_q_conv_w"]),
        "ck": np.ascontiguousarray(inputs["L0_attn_k_conv_w"]),
        "cv": np.ascontiguousarray(inputs["L0_attn_v_conv_w"]),
        "fw1": np.ascontiguousarray(inputs["L0_attn_fuse_w1"]),
        "fb1": np.ascontiguousarray(inputs["L0_attn_fuse_b1"]),
        "fw2": np.ascontiguousarray(inputs["L0_attn_fuse_w2"]),
        "fb2": np.ascontiguousarray(inputs["L0_attn_fuse_b2"]),
        "ltemp": np.ascontiguousarray(inputs["L0_attn_logit_temp"]),
        "rss": np.ascontiguousarray(inputs["L0_attn_res_scale_short"]),
        "rsl": np.ascontiguousarray(inputs["L0_attn_res_scale_long"]),
        "onw": np.ascontiguousarray(inputs["L0_attn_o_norm_w"]),
        "wo": np.ascontiguousarray(inputs["L0_attn_o_proj_w"]),
        "firs": np.ascontiguousarray(inputs["L0_attn_fir_short_w"]),
        "firl": np.ascontiguousarray(inputs["L0_attn_fir_long_w"]),
        "qw": np.ascontiguousarray(inputs["L0_ffn_qweights"]),
        "qs": np.ascontiguousarray(inputs["L0_ffn_scales"]),
        "qb": np.ascontiguousarray(inputs["L0_ffn_biases"]),
        "la": np.ascontiguousarray(inputs["L0_ffn_lora_a"]),
        "lb": np.ascontiguousarray(inputs["L0_ffn_lora_b"]),
        "lng": np.ascontiguousarray(inputs["L0_ffn_ln_gamma"]),
        "lnb": np.ascontiguousarray(inputs["L0_ffn_ln_beta"]),
    }
    maps = []
    for c in range(NCORES):
        m = dict(shared)
        m["x"] = np.ascontiguousarray(x[c])
        maps.append(m)
    return maps


def kernel(**inputs):
    mk = build_program()
    nc = mk.build()
    maps = shard_inputs(inputs)
    res = run_bass_kernel_spmd(nc, maps, core_ids=list(range(NCORES)))
    return np.stack([r["out"] for r in res.results], axis=0)


# version 03
# speedup vs baseline: 1.0037x; 1.0037x over previous
import os
import math
import numpy as np
import ml_dtypes
import concourse.bass as bass
import concourse.mybir as mybir
from concourse.bass_utils import run_bass_kernel_spmd

F32 = mybir.dt.float32
BF16 = mybir.dt.bfloat16
I32 = mybir.dt.int32
AF = mybir.ActivationFunctionType
ALU = mybir.AluOpType
BF = ml_dtypes.bfloat16

T = 2048
D = 1024
H = 4
DH = 256
NCORES = 8
EPS = 1e-5
NT = T // 128
ND = D // 128
ENGINES = ("sync", "tensor", "vector", "scalar", "gpsimd")


class MK:
    def __init__(self, debug_taps=(), stop_after=None):
        self.nc = bass.Bass("TRN2", target_bir_lowering=False)
        self.debug_taps = set(debug_taps)
        self.stop_after = stop_after
        self.q = {e: [] for e in ENGINES}
        self.sems = {}
        self.semcnt = {}
        self.resolved = {}
        self.dbg_list = []
        self.consts = {}
        self.regions = {}
        self.nview = 0

    KEEP = {"edst0", "edst1", "emix", "ld", "cast", "tr", "dsl", "dwsc",
            "dwsl", "espl", "elsm", "eztr", "eql", "eqdq",
            "mm", "evp", "hist", "silu", "convg", "diag_b", "beta_mm",
            "beta_sig", "mdone", "duev", "dbce", "dsqe", "dgte", "dat", "dgt",
            "dws", "dout", "doutv", "dsr", "dsu", "dgram", "dbc", "dsq",
            "duw", "dcr", "datr", "btr", "beta_tr", "st_out", "eie",
            "datr_ev", "btr_ev", "etrpt", "etrpe"}

    def _resolve(self, name, eng=None, dma=False):
        if name is None or name in self.KEEP:
            return name
        g = None
        if eng is not None:
            g = f"__d_{eng}" if dma else f"__e_{eng}"
        if name in self.resolved:
            assert g is None or g == self.resolved[name], \
                f"sem {name} incremented from {g} and {self.resolved[name]}"
            return self.resolved[name]
        if g is None:
            raise KeyError(f"sem {name} waited before any inc")
        self.resolved[name] = g
        return g

    def _sem(self, name):
        if name not in self.sems:
            self.sems[name] = self.nc.alloc_semaphore(name)
            self.semcnt[name] = 0
        return self.sems[name]

    def inc(self, name, by=1, eng=None, dma=False):
        name = self._resolve(name, eng, dma)
        h = self._sem(name)
        self.semcnt[name] += by
        return h

    def cnt(self, name):
        if name not in self.KEEP and name in self.resolved:
            name = self.resolved[name]
        elif name not in self.KEEP and name not in self.sems:
            return 0
        self._sem(name)
        return self.semcnt[name]

    def wait(self, eng, name, val=None):
        nm = name
        if nm not in self.KEEP and nm in self.resolved:
            nm = self.resolved[nm]
        elif nm not in self.KEEP and nm not in self.sems:
            if val is None or val == 0:
                return
            raise KeyError(f"wait on unknown sem {name}")
        h = self._sem(nm)
        if val is None:
            val = self.semcnt[nm]
        if val > 0:
            self.q[eng].append(lambda e, h=h, val=val: e.wait_ge(h, val))

    REGION_BASE = 229344 - 12 * 16384

    def place(self, reg, shape, dtype, off=0):
        if isinstance(reg, str):
            reg = self.RN.index(reg)
        self.nview += 1
        return self.nc.alloc_sbuf_tensor_at(
            f"t{self.nview}", shape, dtype, offset=self.REGION_BASE + reg * 16384 + off)

    def sb(self, name, shape, dtype):
        return self.nc.alloc_sbuf_tensor(name, shape, dtype)

    def addr(self, t):
        return self.nc.lookup_mloc(t).addr

    def over(self, t, shape, dtype, off=0):
        self.nview += 1
        return self.nc.alloc_sbuf_tensor_at(
            f"t{self.nview}", shape, dtype, offset=self.addr(t) + off)

    def ps(self, name, shape, dtype=F32):
        return self.nc.alloc_psum_tensor(name, shape, dtype)

    def const(self, name, arr):
        if name not in self.consts:
            self.consts[name] = self.nc.inline_tensor(np.ascontiguousarray(arr), name).ap()
        return self.consts[name]

    def emit(self, eng, fn):
        self.q[eng].append(fn)

    def op(self, eng, fn, sem=None, by=1):
        h = self.inc(sem, by, eng=eng) if sem else None
        def g(e, fn=fn, h=h, by=by):
            ins = fn(e)
            if h is not None:
                ins.then_inc(h, by)
        self.q[eng].append(g)

    def dma(self, eng, dst, src, sem="ld"):
        h = self.inc(sem, 16, eng=eng, dma=True)
        self.q[eng].append(lambda e, h=h, dst=dst, src=src: e.dma_start(dst, src).then_inc(h, 16))

    def dmaT(self, dst, src, sem="tr"):
        h = self.inc(sem, 16, eng="sync", dma=True)
        self.q["sync"].append(lambda e, h=h, dst=dst, src=src: e.dma_start_transpose(dst, src).then_inc(h, 16))

    def ldbg(self, tap, srcs, shape, dtype):
        if tap not in self.debug_taps:
            return
        d = self.nc.dram_tensor("dbg_" + tap, shape, dtype, kind="ExternalOutput").ap()
        if not isinstance(srcs, (list, tuple)):
            srcs = [srcs]
        nrow = shape[0] // len(srcs)
        for i, src in enumerate(srcs):
            self.dma("sync", d[i * nrow:(i + 1) * nrow], src, sem="st_out")

    def dbg(self, tap, src_aps, shape, dtype, ready_sem):
        if tap not in self.debug_taps:
            return
        d = self.nc.dram_tensor("dbg_" + tap, shape, dtype, kind="ExternalOutput").ap()
        self.dbg_list.append((tap, d, src_aps, ready_sem, self.semcnt.get(ready_sem, 0)))

    def finalize(self, out_dram, out_src, ready_sems=()):
        for tap, d, srcs, rsem, rval in self.dbg_list:
            if rsem is not None and rval > 0:
                h = self.sems[rsem]
                self.q["sync"].append(lambda e, h=h, v=rval: e.wait_ge(h, v))
            if not isinstance(srcs, (list, tuple)):
                srcs = [srcs]
            nrow = d.shape[0] // len(srcs)
            for i, src in enumerate(srcs):
                h2 = self.inc("st_out", 16)
                self.q["sync"].append(lambda e, d=d, i=i, nrow=nrow, src=src, h2=h2:
                                      e.dma_start(d[i * nrow:(i + 1) * nrow], src).then_inc(h2, 16))
        if out_src is not None:
            for s in ready_sems:
                self.wait("sync", s)
            h2 = self.inc("st_out", 16)
            self.q["sync"].append(lambda e, h2=h2, out_dram=out_dram, out_src=out_src:
                                  e.dma_start(out_dram, out_src).then_inc(h2, 16))
        v = self.cnt("st_out")
        if v > 0:
            h = self.sems["st_out"]
            self.q["sync"].append(lambda e, h=h, v=v: e.wait_ge(h, v))

    def build(self):
        nc = self.nc
        assert nc.sbuf_base <= self.REGION_BASE, \
            f"normal sbuf allocs ({nc.sbuf_base}) overlap regions ({self.REGION_BASE})"
        with nc.Block() as blk:
            @blk.sync
            def _(e):
                for f in self.q["sync"]:
                    f(e)
            @blk.tensor
            def _(e):
                for f in self.q["tensor"]:
                    f(e)
            @blk.vector
            def _(e):
                for f in self.q["vector"]:
                    f(e)
            @blk.scalar
            def _(e):
                for f in self.q["scalar"]:
                    f(e)
            @blk.gpsimd
            def _(e):
                for f in self.q["gpsimd"]:
                    f(e)
        return nc


class FeatT:
    def __init__(self, mk, r0, r1):
        self.h = [mk.place(r0, [128, 4, T], BF16), mk.place(r1, [128, 4, T], BF16)]

    def ks(self, ks, sl=slice(None)):
        return self.h[ks // 4][:, ks % 4, sl]


class TokT:
    def __init__(self, mk, r0, r1, dtype=BF16, C=D):
        n = 16384 // (mybir.dt.size(dtype) * C)
        self.n = n
        self.h = [mk.place(r0, [128, n, C], dtype), mk.place(r1, [128, n, C], dtype)]

    def s(self, s, sl=slice(None)):
        return self.h[s // self.n][:, s % self.n, sl]


def dmaT_tok_from_feat(mk, tok, feat):
    for ds in range(ND):
        for half in range(2):
            dst = tok.h[half][:, :, 128 * ds:128 * ds + 128]
            mk.dmaT(dst, feat.ks(ds, slice(1024 * half, 1024 * half + 1024)))


def dmaT_feat_from_tok(mk, feat, tok):
    for s in range(NT):
        for half in range(2):
            dst = feat.h[half][:, :, 128 * s:128 * s + 128]
            mk.dmaT(dst, tok.s(s, slice(512 * half, 512 * half + 512)))


def build_program(debug_taps=(), stop_after=None, delta_macros=NT, delta_stage='all'):
    mk = MK(debug_taps, stop_after)
    nc = mk.nc

    x_in = nc.dram_tensor("x", [T, D], F32, kind="ExternalInput").ap()
    wq_in = nc.dram_tensor("wq", [D, D], F32, kind="ExternalInput").ap()
    wk_in = nc.dram_tensor("wk", [D, D], F32, kind="ExternalInput").ap()
    wv_in = nc.dram_tensor("wv", [D, D], F32, kind="ExternalInput").ap()
    wb_in = nc.dram_tensor("wb", [H, D], F32, kind="ExternalInput").ap()
    cq_in = nc.dram_tensor("cq", [D, 4], F32, kind="ExternalInput").ap()
    ck_in = nc.dram_tensor("ck", [D, 4], F32, kind="ExternalInput").ap()
    cv_in = nc.dram_tensor("cv", [D, 4], F32, kind="ExternalInput").ap()
    out_dram = nc.dram_tensor("out", [T, D], F32, kind="ExternalOutput").ap()
    fw1_in = nc.dram_tensor("fw1", [D, D + 16], F32, kind="ExternalInput").ap()
    fb1_in = nc.dram_tensor("fb1", [D], F32, kind="ExternalInput").ap()
    fw2_in = nc.dram_tensor("fw2", [H, D], F32, kind="ExternalInput").ap()
    fb2_in = nc.dram_tensor("fb2", [H], F32, kind="ExternalInput").ap()
    ltemp_in = nc.dram_tensor("ltemp", [1], F32, kind="ExternalInput").ap()
    rss_in = nc.dram_tensor("rss", [1], F32, kind="ExternalInput").ap()
    rsl_in = nc.dram_tensor("rsl", [1], F32, kind="ExternalInput").ap()
    onw_in = nc.dram_tensor("onw", [DH], F32, kind="ExternalInput").ap()
    wo_in = nc.dram_tensor("wo", [D, D], F32, kind="ExternalInput").ap()
    firs_in = nc.dram_tensor("firs", [H, DH, 5], F32, kind="ExternalInput").ap()
    firl_in = nc.dram_tensor("firl", [H, DH, 64], F32, kind="ExternalInput").ap()
    qw_in = nc.dram_tensor("qw", [6, D, D], I32, kind="ExternalInput").ap()
    qs_in = nc.dram_tensor("qs", [6, D, 64], F32, kind="ExternalInput").ap()
    qb_in = nc.dram_tensor("qb", [6, D], F32, kind="ExternalInput").ap()
    la_in = nc.dram_tensor("la", [6, 32, D], F32, kind="ExternalInput").ap()
    lb_in = nc.dram_tensor("lb", [6, D, 32], F32, kind="ExternalInput").ap()
    lng_in = nc.dram_tensor("lng", [3, D], F32, kind="ExternalInput").ap()
    lnb_in = nc.dram_tensor("lnb", [3, D], F32, kind="ExternalInput").ap()
    dscr = nc.dram_tensor("dscr", [2, 128, 8, D], BF16, kind="Internal").ap()

    mk.RN = ["RA0", "RA1", "RB0", "RB1", "RC0", "RC1", "RD0", "RD1", "RE0", "RE1", "RF0", "RF1"]
    ident = mk.sb("ident", [128, 128], BF16)
    cw = mk.sb("cw", [128, ND, 3, 4], F32)
    bst = mk.sb("bst", [H, D], BF16)
    wbT = mk.sb("wbT", [128, ND, H], BF16)
    beta_tok = mk.sb("beta_tok", [128, NT, H], F32)
    rqrk = mk.sb("rqrk", [128, NT, 2 * H], F32)
    sqacc = mk.sb("sqacc", [128, NT, H], F32)
    sqscr = mk.sb("sqscr", [128, 256], BF16)
    pre = mk.sb("pre", [128, 2, 4 + 512 + 4], BF16)
    diag = mk.sb("diag", [128, 2, 4, 128], BF16)

    idc = mk.const("identc", np.eye(128, dtype=np.float32).astype(BF))

    x_tok = TokT(mk, "RA0", "RA1")
    x_feat = FeatT(mk, "RB0", "RB1")
    featst = FeatT(mk, "RC0", "RC1")
    v_tok = TokT(mk, "RD0", "RD1")
    qn_tok = TokT(mk, "RA0", "RA1")
    kn_tok = TokT(mk, "RF0", "RF1")
    wT = [mk.place("RE0", [128, ND, D], BF16), mk.place("RE1", [128, ND, D], BF16)]
    stg32 = mk.place("RD0", [128, 4, 1024], F32)
    bst32 = mk.place("RD1", [H, 1024], F32)
    wstg = mk.place("RF0", [128, ND, D], BF16)
    beta_f = mk.place("RC0", [H, T], F32)
    brow16 = mk.place("RC0", [H, T], BF16, off=8192)

    pp = [mk.ps(f"pp{i}", [128, 512], F32) for i in range(2)]
    pc = [mk.ps(f"pc{i}", [128, 512], F32) for i in range(2)]
    pm = mk.ps("pm", [128, 512], F32)
    pt_bf = mk.ps("pt_bf", [128, 512], BF16)

    mk.dma("sync", cw[:, :, 0, :], cq_in.rearrange("(s p) j -> p s j", p=128))
    mk.dma("sync", cw[:, :, 1, :], ck_in.rearrange("(s p) j -> p s j", p=128))
    mk.dma("sync", cw[:, :, 2, :], cv_in.rearrange("(s p) j -> p s j", p=128))
    mk.dma("sync", ident[:], idc)
    LD_ALL = mk.cnt("ld")

    for qt in range(4):
        if qt > 0:
            mk.wait("sync", "cast", qt)
        mk.dma("sync", stg32[:], x_in[512 * qt:512 * qt + 512].rearrange(
            "(s p) d -> p s d", p=128), sem="lds")
        mk.wait("vector", "lds", mk.cnt("lds"))
        mk.op("vector", lambda e, qt=qt: e.tensor_copy(
            x_tok.h[qt // 2][:, 4 * (qt % 2):4 * (qt % 2) + 4, :], stg32[:]), "cast")

    def load_weight(win, slot, gate=None):
        if gate is not None:
            mk.wait("sync", *gate)
        for hf in range(2):
            mk.wait("sync", "cast")
            mk.dma("sync", stg32[:], win.rearrange("(s p) d -> p s d", p=128)[:, 4 * hf:4 * hf + 4, :],
                   sem="lds")
            mk.wait("vector", "lds", mk.cnt("lds"))
            if hf == 0:
                mk.wait("vector", "tr")
            mk.op("vector", lambda e, hf=hf: e.tensor_copy(
                wstg[:, 4 * hf:4 * hf + 4, :], stg32[:]), "cast")
        mk.wait("sync", "cast")
        for os_ in range(ND):
            mk.dmaT(wT[slot][:, :, 128 * os_:128 * os_ + 128], wstg[:, os_, :])
        return mk.cnt("tr")

    mk.wait("sync", "cast", 4)
    dmaT_feat_from_tok(mk, x_feat, x_tok)
    TR_X = mk.cnt("tr")

    TR_WQ = load_weight(wq_in, 0)
    TR_WK = load_weight(wk_in, 1)
    mk.wait("sync", "cast")
    mk.dma("sync", bst32[:], wb_in, sem="lds")
    mk.wait("vector", "lds", mk.cnt("lds"))
    mk.op("vector", lambda e: e.tensor_copy(bst[:], bst32[:]), "cast")
    CAST_BST = mk.cnt("cast")

    mk.wait("tensor", "ld", LD_ALL)
    mk.wait("tensor", "cast", CAST_BST)
    mk.wait("tensor", "tr", TR_X)
    for ds in range(ND):
        sem = "btr" if ds == ND - 1 else None
        mk.op("tensor", lambda e, ds=ds: e.transpose(
            pt_bf[:, 4 * ds:4 * ds + 4], bst[:, 128 * ds:128 * ds + 128], ident[0:H, 0:H]), sem)
    mk.wait("vector", "btr")
    mk.op("vector", lambda e: e.tensor_copy(wbT[:].rearrange("p s h -> p (s h)"), pt_bf[:, 0:32]), "btr_ev")
    mk.wait("tensor", "btr_ev")
    for tt in range(4):
        if tt > 0:
            mk.wait("tensor", "beta_sig", tt)
        hb = mk.inc("beta_mm")
        for ks in range(ND):
            def f(e, tt=tt, ks=ks, hb=hb, last=(ks == ND - 1)):
                ins = e.matmul(pm[0:4, :], wbT[:, ks, :], x_feat.ks(ks, slice(512 * tt, 512 * tt + 512)),
                               start=(ks == 0), stop=(ks == ND - 1))
                if last:
                    ins.then_inc(hb, 1)
            mk.emit("tensor", f)
        mk.wait("scalar", "beta_mm", tt + 1)
        mk.op("scalar", lambda e, tt=tt: e.activation(
            beta_f[:, 512 * tt:512 * tt + 512], pm[0:4, :], AF.Sigmoid), "beta_sig")
    mk.wait("vector", "beta_sig", 4)
    mk.op("vector", lambda e: e.tensor_copy(brow16[:], beta_f[:]), "beta_cast")
    mk.wait("tensor", "beta_cast")
    for ts_ in range(NT):
        sem = "beta_tr" if ts_ == NT - 1 else None
        mk.op("tensor", lambda e, ts_=ts_: e.transpose(
            pt_bf[:, 4 * ts_:4 * ts_ + 4], brow16[:, 128 * ts_:128 * ts_ + 128], ident[0:H, 0:H]), sem)
    mk.wait("vector", "beta_tr")
    mk.op("vector", lambda e: e.tensor_copy(
        beta_tok[:].rearrange("p s h -> p (s h)"), pt_bf[:, 0:64]), "beta_tok_rdy")

    mk.wait("vector", "ld", LD_ALL)

    G = [0]
    DG = [0]

    def emit_proj(slot, cwi, tr_needed, featst_free=None):
        mk.wait("tensor", "tr", tr_needed)
        if featst_free is not None:
            mk.wait("scalar", "tr", featst_free)
        for os_ in range(ND):
            d = DG[0]
            if d >= 2:
                mk.wait("vector", "convg", 4 * (d - 1))
            for j in range(4):
                sem = "diag_b" if j == 3 else None
                def f(e, par=d % 2, j=j, os_=os_, cwi=cwi):
                    return e.tensor_scalar(out=diag[:, par, j, :], in0=ident[:],
                                           scalar1=cw[:, os_, cwi, 3 - j:4 - j], scalar2=None,
                                           op0=ALU.mult)
                mk.op("vector", f, sem)
            dtarget = mk.cnt("diag_b")
            DG[0] += 1
            for tt in range(4):
                g = G[0]
                bank = pp[g % 2]
                cbank = pc[g % 2]
                par = g % 2
                if g >= 2:
                    mk.wait("tensor", "evp", g - 1)
                hmm_ = mk.inc("mm")
                for ks in range(ND):
                    def f(e, bank=bank, ks=ks, os_=os_, tt=tt, slot=slot, hmm_=hmm_, last=(ks == ND - 1)):
                        ins = e.matmul(bank[:], wT[slot][:, ks, 128 * os_:128 * os_ + 128],
                                       x_feat.ks(ks, slice(512 * tt, 512 * tt + 512)),
                                       start=(ks == 0), stop=(ks == ND - 1))
                        if last:
                            ins.then_inc(hmm_, 1)
                    mk.emit("tensor", f)
                mk.wait("scalar", "mm", g + 1)
                if g >= 1:
                    mk.wait("scalar", "hist", g)
                mk.op("scalar", lambda e, bank=bank, par=par: e.activation(
                    pre[:, par, 4:516], bank[:], AF.Copy), "evp")
                if tt == 0:
                    if g >= 2:
                        mk.wait("vector", "convg", g - 1)
                    mk.op("vector", lambda e, par=par: e.memset(pre[:, par, 0:4], 0.0), "hist")
                else:
                    mk.wait("vector", "evp", g)
                    mk.op("vector", lambda e, par=par: e.tensor_copy(
                        pre[:, par, 0:4], pre[:, 1 - par, 512:516]), "hist")
                mk.wait("tensor", "evp", g + 1)
                mk.wait("tensor", "hist", g + 1)
                if g >= 2:
                    mk.wait("tensor", "silu", g - 1)
                mk.wait("tensor", "diag_b", dtarget)
                hcv = mk.inc("convg")
                for j in range(4):
                    def f(e, cbank=cbank, par2=d % 2, j=j, par=par, hcv=hcv, last=(j == 3)):
                        ins = e.matmul(cbank[:], diag[:, par2, j, :],
                                       pre[:, par, 4 - j:516 - j],
                                       start=(j == 0), stop=(j == 3))
                        if last:
                            ins.then_inc(hcv, 1)
                    mk.emit("tensor", f)
                mk.wait("scalar", "convg", g + 1)
                mk.op("scalar", lambda e, cbank=cbank, os_=os_, tt=tt: e.activation(
                    featst.ks(os_, slice(512 * tt, 512 * tt + 512)), cbank[:], AF.Silu), "silu")
                G[0] += 1

    def emit_tok(nm, tok_dst, norm_col):
        mk.wait("sync", "silu")
        dmaT_tok_from_feat(mk, tok_dst, featst)
        trv = mk.cnt("tr")
        if norm_col is None:
            mk.wait("vector", "tr", trv)
            mk.op("vector", lambda e: e.memset(sqscr[0:1, 0:1], 0.0), nm + "_tok_rdy")
            return
        mk.wait("scalar", "tr", trv)
        for s in range(NT):
            for hh_ in range(H):
                sem = nm + "_sq" if (s == NT - 1 and hh_ == H - 1) else None
                def f(e, s=s, hh_=hh_, tok_dst=tok_dst):
                    return e.activation(sqscr[:], tok_dst.s(s, slice(DH * hh_, DH * hh_ + DH)), AF.Square,
                                        accum_out=sqacc[:, s, hh_:hh_ + 1])
                mk.op("scalar", f, sem)
        mk.wait("scalar", nm + "_sq")
        mk.emit("scalar", lambda e: e.drain())
        mk.op("scalar", lambda e: e.activation(sqacc[:], sqacc[:], AF.Sqrt), nm + "_sqrt")
        mk.wait("vector", nm + "_sqrt")
        mk.op("vector", lambda e, norm_col=norm_col: e.reciprocal(
            rqrk[:, :, norm_col * H:(norm_col + 1) * H], sqacc[:]), nm + "_rs")
        mk.wait("vector", nm + "_rs")
        for s in range(NT):
            for hh_ in range(H):
                sem = nm + "_tok_rdy" if (s == NT - 1 and hh_ == H - 1) else None
                def f(e, s=s, hh_=hh_, tok_dst=tok_dst, norm_col=norm_col):
                    return e.tensor_scalar(out=tok_dst.s(s, slice(DH * hh_, DH * hh_ + DH)),
                                           in0=tok_dst.s(s, slice(DH * hh_, DH * hh_ + DH)),
                                           scalar1=rqrk[:, s, norm_col * H + hh_:norm_col * H + hh_ + 1],
                                           scalar2=None, op0=ALU.mult)
                mk.op("vector", f, sem)

    emit_proj(0, 0, TR_WQ)
    emit_tok("q", qn_tok, 0)
    TR_QTOK = mk.cnt("tr")
    emit_proj(1, 1, TR_WK, featst_free=TR_QTOK)
    TR_WV = load_weight(wv_in, 0, gate=("silu", 32))
    emit_tok("k", kn_tok, 1)
    TR_KTOK = mk.cnt("tr")
    emit_proj(0, 2, TR_WV, featst_free=TR_KTOK)
    emit_tok("v", v_tok, None)

    delta_tok = TokT(mk, "RC0", "RC1")
    pb6 = mk.ps("pb6", [128, 512], F32)
    pb7 = mk.ps("pb7", [128, 512], F32)
    pt2 = pt_bf
    S_sb = mk.sb("S_sb", [128, H, 2, DH], BF16)

    qn_sl = mk.place("RE0", [128, 8, 128], BF16, off=0)
    kn_sl = mk.place("RE0", [128, 8, 128], BF16, off=2048)
    vb_sl = mk.place("RE0", [128, D], BF16, off=4096)
    kbt_sl = mk.place("RE0", [128, D], BF16, off=6144)
    w_sc = mk.place("RE0", [128, H, DH], BF16, off=8192)
    w_sl = mk.place("RE0", [128, H, 2, 128], BF16, off=10240)
    uadj = mk.place("RE0", [128, D], BF16, off=12288)
    GTw = mk.place("RE0", [128, H, 128], BF16, off=14336)
    GTq = mk.place("RE0", [128, H, 128], BF16, off=15360)
    invm = {}
    for i, nm_ in enumerate(["S1", "S1t", "P2", "P2t", "P4", "P4t", "P8", "P8t",
                             "P16", "P16t", "Brl0", "Brl1", "IB", "tmpA"]):
        invm[nm_] = mk.place("RE1", [128, H, 128], BF16, off=1024 * i)

    _blk = np.zeros((128, 128), np.float32)
    for j in range(4):
        _blk[32 * j:32 * j + 32, 32 * j:32 * j + 32] = np.tril(np.ones((32, 32)), -1)
    maskA_c = mk.const("maskA_c", (-_blk).astype(BF))
    _gt = np.zeros((128, 128), np.float32)
    for l in range(4):
        for j in range(4):
            if l < j:
                _gt[32 * l:32 * l + 32, 32 * j:32 * j + 32] = 1.0
            elif l == j:
                _gt[32 * l:32 * l + 32, 32 * j:32 * j + 32] = np.triu(np.ones((32, 32)))
    maskGT_c = mk.const("maskGT_c", _gt.astype(BF))
    maskA = mk.sb("maskA", [128, 128], BF16)
    maskGT = mk.sb("maskGT", [128, 128], BF16)
    mk.dma("gpsimd", maskA[:], maskA_c, sem="ldm")
    mk.dma("gpsimd", maskGT[:], maskGT_c, sem="ldm")

    UA = [pp[0], pp[1]]
    OUT = [pc[0], pc[1]]
    ROT = [pm, pb6]
    DS_BANKS = [pm, pb6, pb7, pp[0]]

    STAGES = ["nil", "dsl", "ts", "mm", "stt", "gram", "inv", "uw", "gtw", "gtq", "gt", "chain", "all"]
    def stage_ge(a):
        return STAGES.index(delta_stage) >= STAGES.index(a)

    def emit_delta():
        mk.op("vector", lambda e: e.memset(S_sb[:].rearrange("p h k d -> p (h k d)"), 0.0), "s_init")
        mk.wait("vector", "ldm")
        for eng in ("tensor", "sync", "vector", "scalar"):
            mk.wait(eng, "q_tok_rdy")
            mk.wait(eng, "k_tok_rdy")
            mk.wait(eng, "v_tok_rdy")
            mk.wait(eng, "beta_tok_rdy")
        mk.wait("tensor", "s_init")
        mk.wait("tensor", "ldm")

        for m in range(delta_macros):
            if m > 0:
                mk.wait("sync", "mdone", m)
            if not stage_ge("dsl"):
                mk.op("vector", lambda e: e.memset(sqscr[0:1, 0:1], 0.0), "mdone")
                continue
            mk.dmaT(qn_sl[:], qn_tok.s(m), sem="dsl")
            mk.dmaT(kn_sl[:], kn_tok.s(m), sem="dsl")
            DSL = mk.cnt("dsl")
            if not stage_ge("ts"):
                mk.wait("vector", "dsl", DSL)
                mk.op("vector", lambda e: e.memset(sqscr[0:1, 0:1], 0.0), "mdone")
                continue
            if m > 0:
                mk.wait("vector", "mdone", m)
            for hh_ in range(H):
                mk.op("vector", lambda e, hh_=hh_, m=m: e.tensor_scalar(
                    out=vb_sl[:, DH * hh_:DH * hh_ + DH], in0=v_tok.s(m, slice(DH * hh_, DH * hh_ + DH)),
                    scalar1=beta_tok[:, m, hh_:hh_ + 1], scalar2=None, op0=ALU.mult),
                    "dts" if hh_ == H - 1 else None)
            for hh_ in range(H):
                mk.op("vector", lambda e, hh_=hh_, m=m: e.tensor_scalar(
                    out=kbt_sl[:, DH * hh_:DH * hh_ + DH], in0=kn_tok.s(m, slice(DH * hh_, DH * hh_ + DH)),
                    scalar1=beta_tok[:, m, hh_:hh_ + 1], scalar2=None, op0=ALU.mult),
                    "dts2" if hh_ == H - 1 else None)
            if not stage_ge("mm"):
                mk.wait("vector", "dts")
                mk.wait("vector", "dts2")
                mk.wait("vector", "dsl", DSL)
                mk.op("vector", lambda e: e.memset(sqscr[0:1, 0:1], 0.0), "mdone")
                continue
            mk.wait("tensor", "dsl", DSL)
            hgr = mk.inc("dgram")
            for hh_ in range(H):
                for ks in range(2):
                    def f(e, hh_=hh_, ks=ks, hgr=hgr, last=(hh_ == H - 1 and ks == 1)):
                        ins = e.matmul(ROT[0][:, 128 * hh_:128 * hh_ + 128],
                                       kn_sl[:, 2 * hh_ + ks, :], kn_sl[:, 2 * hh_ + ks, :],
                                       start=(ks == 0), stop=(ks == 1))
                        if last:
                            ins.then_inc(hgr, 1)
                    mk.emit("tensor", f)
            if not stage_ge("stt"):
                mk.wait("vector", "dgram")
                mk.op("vector", lambda e: e.memset(sqscr[0:1, 0:1], 0.0), "mdone")
                continue
            mk.wait("vector", "dgram")
            for hh_ in range(H):
                mk.op("vector", lambda e, hh_=hh_, m=m: e.scalar_tensor_tensor(
                    out=invm["S1"][:, hh_, :], in0=ROT[0][:, 128 * hh_:128 * hh_ + 128],
                    scalar=beta_tok[:, m, hh_:hh_ + 1], in1=maskA[:],
                    op0=ALU.mult, op1=ALU.mult), None)
            for hh_ in range(H):
                mk.op("vector", lambda e, hh_=hh_, m=m: e.tensor_scalar(
                    out=invm["tmpA"][:, hh_, :], in0=ROT[0][:, 128 * hh_:128 * hh_ + 128],
                    scalar1=beta_tok[:, m, hh_:hh_ + 1], scalar2=None, op0=ALU.mult),
                    "da0")
            mk.wait("tensor", "da0")
            htr = mk.inc("datr")
            for hh_ in range(H):
                def f(e, hh_=hh_, htr=htr, last=(hh_ == H - 1)):
                    ins = e.transpose(pt2[:, 128 * hh_:128 * hh_ + 128], invm["S1"][:, hh_, :], ident[:])
                    if last:
                        ins.then_inc(htr, 1)
                mk.emit("tensor", f)
            mk.wait("vector", "datr")
            mk.op("vector", lambda e: e.tensor_copy(
                invm["S1t"][:].rearrange("p h n -> p (h n)"), pt2[:]), "datr_ev")
            if not stage_ge("inv"):
                mk.wait("vector", "datr_ev")
                mk.op("vector", lambda e: e.memset(sqscr[0:1, 0:1], 0.0), "mdone")
                continue
            sq_plan = [("P2", "S1t", "S1"), ("P2t", "S1", "S1t"),
                       ("P4", "P2t", "P2"), ("P4t", "P2", "P2t"),
                       ("P8", "P4t", "P4"), ("P8t", "P4", "P4t"),
                       ("P16", "P8t", "P8"), ("P16t", "P8", "P8t")]
            mk.wait("tensor", "datr_ev")
            for si, (dstn, ln, rn) in enumerate(sq_plan):
                rb = ROT[si % 2]
                if si >= 2:
                    mk.wait("tensor", "dsqe", mk.cnt("dsqe") - 1)
                hsq = mk.inc("dsq")
                for hh_ in range(H):
                    def f(e, hh_=hh_, rb=rb, ln=ln, rn=rn, hsq=hsq, last=(hh_ == H - 1)):
                        ins = e.matmul(rb[:, 128 * hh_:128 * hh_ + 128],
                                       invm[ln][:, hh_, :], invm[rn][:, hh_, :],
                                       start=True, stop=True)
                        if last:
                            ins.then_inc(hsq, 1)
                    mk.emit("tensor", f)
                mk.wait("vector", "dsq")
                mk.op("vector", lambda e, rb=rb, dstn=dstn: e.tensor_copy(
                    invm[dstn][:].rearrange("p h n -> p (h n)"), rb[:]), "dsqe")
            bcur = "S1t"
            for r, (pn, ptn) in enumerate([("P2", "P2t"), ("P4", "P4t"), ("P8", "P8t"), ("P16", "P16t")]):
                rb = ROT[r % 2]
                mk.wait("tensor", "dsqe")
                if r >= 1:
                    mk.wait("tensor", "dbce", mk.cnt("dbce"))
                hbc = mk.inc("dbc")
                for hh_ in range(H):
                    def f(e, hh_=hh_, rb=rb, pn=pn, ptn=ptn, bcur=bcur, hbc=hbc, last=(hh_ == H - 1)):
                        e.matmul(rb[:, 128 * hh_:128 * hh_ + 128], invm[pn][:, hh_, :],
                                 invm[bcur][:, hh_, :], start=True, stop=False)
                        e.matmul(rb[:, 128 * hh_:128 * hh_ + 128], ident[:],
                                 invm[bcur][:, hh_, :], start=False, stop=False)
                        ins = e.matmul(rb[:, 128 * hh_:128 * hh_ + 128], ident[:],
                                       invm[ptn][:, hh_, :], start=False, stop=True)
                        if last:
                            ins.then_inc(hbc, 1)
                    mk.emit("tensor", f)
                nxt = "Brl%d" % (r % 2) if r < 3 else "IB"
                mk.wait("vector", "dbc")
                if r < 3:
                    mk.op("vector", lambda e, rb=rb, nxt=nxt: e.tensor_copy(
                        invm[nxt][:].rearrange("p h n -> p (h n)"), rb[:]), "dbce")
                else:
                    for hh_ in range(H):
                        mk.op("vector", lambda e, rb=rb, hh_=hh_: e.tensor_tensor(
                            invm["IB"][:, hh_, :], rb[:, 128 * hh_:128 * hh_ + 128],
                            ident[:], ALU.add), "dbce" if hh_ == H - 1 else None)
                bcur = nxt
            if not stage_ge("uw"):
                mk.wait("vector", "dbce")
                mk.op("vector", lambda e: e.memset(sqscr[0:1, 0:1], 0.0), "mdone")
                continue
            mk.wait("tensor", "dbce")
            mk.wait("tensor", "dts")
            mk.wait("tensor", "dts2")
            huw = mk.inc("duw")
            for hh_ in range(H):
                def f(e, hh_=hh_):
                    e.matmul(UA[hh_ // 2][:, 256 * (hh_ % 2):256 * (hh_ % 2) + 256],
                             invm["IB"][:, hh_, :], vb_sl[:, DH * hh_:DH * hh_ + DH],
                             start=(hh_ % 2 == 0), stop=False)
                mk.emit("tensor", f)
            for hh_ in range(H):
                def f(e, hh_=hh_, huw=huw, last=(hh_ == H - 1)):
                    ins = e.matmul(ROT[hh_ // 2][:, 256 * (hh_ % 2):256 * (hh_ % 2) + 256],
                                   invm["IB"][:, hh_, :], kbt_sl[:, DH * hh_:DH * hh_ + DH],
                                   start=(hh_ % 2 == 0), stop=(hh_ % 2 == 1))
                    if last:
                        ins.then_inc(huw, 1)
                mk.emit("tensor", f)
            mk.wait("vector", "duw")
            for half in range(2):
                mk.op("vector", lambda e, half=half: e.tensor_scalar(
                    out=w_sc[:, 2 * half:2 * half + 2, :].rearrange("p a b -> p (a b)"),
                    in0=ROT[half][:], scalar1=-1.0, scalar2=None, op0=ALU.mult),
                    "dwsc" if half == 1 else None)
            WSC = mk.cnt("dwsc")
            mk.wait("sync", "dwsc", WSC)
            for hh_ in range(H):
                mk.dmaT(w_sl[:, hh_, :, :], w_sc[:, hh_, :], sem="dwsl")
            DWSL = mk.cnt("dwsl")
            if not stage_ge("gtw"):
                mk.wait("vector", "dwsc", WSC)
                mk.op("vector", lambda e: e.memset(sqscr[0:1, 0:1], 0.0), "mdone")
                continue
            mk.wait("tensor", "dwsc", WSC)
            if m > 0:
                mk.wait("tensor", "dgte", mk.cnt("dgte"))
            hgt = mk.inc("dgt")
            if stage_ge("gtw"):
                for hh_ in range(H):
                    def f(e, hh_=hh_):
                        e.matmul(pb7[:, 128 * hh_:128 * hh_ + 128], invm["tmpA"][:, hh_, :],
                                 invm["IB"][:, hh_, :], start=True, stop=True)
                    mk.emit("tensor", f)
            if stage_ge("gtq"):
                for hh_ in range(H):
                    for ks in range(2):
                        def f(e, hh_=hh_, ks=ks, hgt=hgt, last=(hh_ == H - 1 and ks == 1)):
                            ins = e.matmul(ROT[0][:, 128 * hh_:128 * hh_ + 128],
                                           kn_sl[:, 2 * hh_ + ks, :], qn_sl[:, 2 * hh_ + ks, :],
                                           start=(ks == 0), stop=(ks == 1))
                            if last:
                                ins.then_inc(hgt, 1)
                        mk.emit("tensor", f)
            else:
                def f(e, hgt=hgt):
                    e.matmul(ROT[0][:, 0:128], invm["tmpA"][:, 0, :],
                             invm["IB"][:, 0, :], start=True, stop=True).then_inc(hgt, 1)
                mk.emit("tensor", f)
            mk.wait("vector", "dgt")
            if stage_ge("gtw"):
                mk.op("vector", lambda e: e.tensor_scalar(
                    out=GTw[:].rearrange("p h n -> p (h n)"), in0=pb7[:],
                    scalar1=-1.0, scalar2=None, op0=ALU.mult), None)
            if stage_ge("gtq"):
                for hh_ in range(H):
                    mk.op("vector", lambda e, hh_=hh_: e.tensor_tensor(
                        GTq[:, hh_, :], ROT[0][:, 128 * hh_:128 * hh_ + 128], maskGT[:], ALU.mult),
                        "dgte" if hh_ == H - 1 else None)
            else:
                mk.op("vector", lambda e: e.memset(GTq[:, 0, 0:1], 0.0), "dgte")
            if not stage_ge("chain"):
                mk.wait("vector", "dgte")
                mk.op("vector", lambda e: e.memset(sqscr[0:1, 0:1], 0.0), "mdone")
                continue
            mk.wait("tensor", "dwsl", DWSL)
            mk.wait("tensor", "dgte")
            if m > 0:
                mk.wait("tensor", "dout")
                mk.wait("tensor", "doutv")
            hws = mk.inc("dws")
            for hh_ in range(H):
                for ks in range(2):
                    def f(e, hh_=hh_, ks=ks, hws=hws, last=(hh_ == H - 1 and ks == 1)):
                        ins = e.matmul(UA[hh_ // 2][:, 256 * (hh_ % 2):256 * (hh_ % 2) + 256],
                                       w_sl[:, hh_, ks, :], S_sb[:, hh_, ks, :],
                                       start=False, stop=(ks == 1))
                        if last:
                            ins.then_inc(hws, 1)
                    mk.emit("tensor", f)
            for hh_ in range(H):
                for ks in range(2):
                    def f(e, hh_=hh_, ks=ks):
                        e.matmul(OUT[hh_ // 2][:, 256 * (hh_ % 2):256 * (hh_ % 2) + 256],
                                 qn_sl[:, 2 * hh_ + ks, :], S_sb[:, hh_, ks, :],
                                 start=(ks == 0 and hh_ % 2 == 0), stop=(ks == 1))
                    mk.emit("tensor", f)
            for j in range(4):
                if j > 0:
                    mk.wait("tensor", "duev", mk.cnt("duev"))
                    hcr = mk.inc("dcr")
                    for hh_ in range(H):
                        def f(e, j=j, hh_=hh_, hcr=hcr, last=(hh_ == H - 1)):
                            ins = e.matmul(
                                UA[hh_ // 2][32 * j:32 * j + 32, 256 * (hh_ % 2):256 * (hh_ % 2) + 256],
                                GTw[0:32 * j, hh_, 32 * j:32 * j + 32],
                                uadj[0:32 * j, DH * hh_:DH * hh_ + DH],
                                start=False, stop=True, tile_position=(0, 32 * j))
                            if last:
                                ins.then_inc(hcr, 1)
                        mk.emit("tensor", f)
                    for hh_ in range(H):
                        def f(e, j=j, hh_=hh_):
                            e.matmul(
                                OUT[hh_ // 2][32 * j:32 * j + 32, 256 * (hh_ % 2):256 * (hh_ % 2) + 256],
                                GTq[0:32 * j, hh_, 32 * j:32 * j + 32],
                                uadj[0:32 * j, DH * hh_:DH * hh_ + DH],
                                start=False, stop=True, tile_position=(0, 32 * j))
                        mk.emit("tensor", f)
                    mk.wait("vector", "dcr")
                    mk.wait("scalar", "dcr")
                else:
                    mk.wait("vector", "dws")
                    mk.wait("scalar", "dws")
                mk.op("vector", lambda e, j=j: e.tensor_copy(
                    uadj[32 * j:32 * j + 32, 0:512], UA[0][32 * j:32 * j + 32, :]), "duev")
                mk.op("scalar", lambda e, j=j: e.activation(
                    uadj[32 * j:32 * j + 32, 512:1024], UA[1][32 * j:32 * j + 32, :], AF.Copy), "duev")
                mk.wait("tensor", "duev", mk.cnt("duev"))
                hat = mk.inc("dat")
                for hh_ in range(H):
                    def f(e, j=j, hh_=hh_, hat=hat, last=(hh_ == H - 1)):
                        ins = e.matmul(
                            OUT[hh_ // 2][32 * j:32 * j + 32, 256 * (hh_ % 2):256 * (hh_ % 2) + 256],
                            GTq[32 * j:32 * j + 32, hh_, 32 * j:32 * j + 32],
                            uadj[32 * j:32 * j + 32, DH * hh_:DH * hh_ + DH],
                            start=False, stop=True, tile_position=(32 * j, 32 * j))
                        if last:
                            ins.then_inc(hat, 1)
                    mk.emit("tensor", f)
            hds = mk.inc("dsu")
            for hh_ in range(H):
                for ks in range(2):
                    def f(e, hh_=hh_, ks=ks, m=m, hds=hds, last=(hh_ == H - 1 and ks == 1)):
                        ins = e.matmul(DS_BANKS[hh_][:, 256 * ks:256 * ks + 256],
                                       kn_tok.s(m, slice(DH * hh_ + 128 * ks, DH * hh_ + 128 * ks + 128)),
                                       uadj[:, DH * hh_:DH * hh_ + DH],
                                       start=(ks == 0), stop=(ks == 1))
                        if last:
                            ins.then_inc(hds, 1)
                    mk.emit("tensor", f)
            mk.wait("vector", "dat")
            mk.wait("scalar", "dat")
            mk.op("vector", lambda e, m=m: e.tensor_copy(delta_tok.s(m, slice(0, 512)), OUT[0][:]), "doutv")
            mk.op("scalar", lambda e, m=m: e.activation(delta_tok.s(m, slice(512, 1024)), OUT[1][:], AF.Copy), "dout")
            mk.wait("vector", "dsu")
            for hh_ in range(H):
                mk.op("vector", lambda e, hh_=hh_: e.tensor_tensor(
                    S_sb[:, hh_, :, :].rearrange("p k d -> p (k d)"),
                    S_sb[:, hh_, :, :].rearrange("p k d -> p (k d)"),
                    DS_BANKS[hh_][:], ALU.add), "dsr" if hh_ == H - 1 else None)
            mk.wait("vector", "dout")
            mk.wait("vector", "dsr")
            mk.op("vector", lambda e: e.memset(sqscr[0:1, 0:1], 0.0), "mdone")
        mk.wait("sync", "mdone", delta_macros)

    emit_delta()

    FL, FF = 192, 97

    def dftc():
        l = np.arange(FL)[:, None]
        f = np.arange(FF)[None, :]
        ang = 2.0 * np.pi * l * f / FL
        Fre, Fim = np.cos(ang), -np.sin(ang)
        lo = np.arange(64, FL)[None, :]
        angi = 2.0 * np.pi * f.T * lo / FL
        c = np.full((FF, 1), 2.0); c[0, 0] = 1.0; c[96, 0] = 1.0
        iRe = (c / FL) * np.cos(angi)
        iIm = -(c / FL) * np.sin(angi)
        return Fre, Fim, iRe, iIm

    Fre_np, Fim_np, iRe_np, iIm_np = dftc()
    FB_np = np.stack([Fre_np[64:192], Fim_np[64:192]], 1)
    FA_np = np.zeros((128, 2, FF), np.float64)
    FA_np[64:128, 0, :] = Fre_np[0:64]
    FA_np[64:128, 1, :] = Fim_np[0:64]
    iF_np = np.stack([iRe_np, iIm_np], 1)
    def fwmat(K):
        k = np.arange(K)[:, None]
        f = np.arange(FF)[None, :]
        ang = 2.0 * np.pi * f * (K - 1 - k) / FL
        return np.stack([np.cos(ang), -np.sin(ang)], 1)
    FAc = mk.sb("FAc", [128, 2, FF], BF16)
    FBc = mk.sb("FBc", [128, 2, FF], BF16)
    iFc = mk.sb("iFc", [FF, 2, 128], BF16)
    Fw64 = mk.sb("Fw64", [64, 2, FF], BF16)
    Fw5 = mk.sb("Fw5", [5, 2, FF], BF16)
    mk.dma("sync", FAc[:], mk.const("FAc_c", FA_np.astype(BF)), sem="lde")
    mk.dma("sync", FBc[:], mk.const("FBc_c", FB_np.astype(BF)), sem="lde")
    mk.dma("sync", iFc[:], mk.const("iFc_c", iF_np.astype(BF)), sem="lde")
    mk.dma("sync", Fw64[:], mk.const("Fw64_c", fwmat(64).astype(BF)), sem="lde")
    mk.dma("sync", Fw5[:], mk.const("Fw5_c", fwmat(5).astype(BF)), sem="lde")

    W2T = mk.sb("W2T", [128, ND, H], BF16)
    b1f = mk.sb("b1f", [128, ND], F32)
    b2row = mk.sb("b2row", [1, H], BF16)
    b2rep = mk.sb("b2rep", [128, H], F32)
    onwrow = mk.sb("onwrow", [1, DH], BF16)
    assert mk.addr(sqacc) + 256 == mk.addr(wbT) + 1088
    onwrep = mk.over(wbT, [128, DH], BF16, 0)
    rtemp = mk.over(wbT, [128, 1], F32, 512)
    rss128 = mk.over(wbT, [128, 1], F32, 544)
    rsl128 = mk.over(wbT, [128, 1], F32, 576)
    rmsacc = mk.over(wbT, [128, NT, H], F32, 608)
    scal3 = mk.over(cw, [1, 8], F32, 0)
    onesrow = mk.sb("onesrow", [1, 128], BF16)
    mk.dma("sync", onesrow[:], mk.const("ones_c", np.ones((1, 128), np.float32).astype(BF)), sem="lde")
    ones512_c = mk.const("ones512_c", np.ones((1, 512), np.float32).astype(BF))
    assert mk.addr(maskA) == mk.addr(S_sb) + 4096
    assert mk.addr(maskGT) == mk.addr(maskA) + 256
    mv4 = mk.over(S_sb, [128, NT, H, 4, 2], BF16, 0)
    bn6 = mk.over(S_sb, [128, NT, H, 6], F32, 1024)
    aacc = mk.over(S_sb, [128, NT, H, 4], F32, 2560)
    ones512 = mk.over(S_sb, [1, 512], BF16, 3584)
    stats_tok = mk.place("RC1", [128, NT, H, 16], BF16, off=6144)
    logits_tok = mk.over(diag, [128, NT, H, H], F32, 0)
    fw_tok = mk.over(diag, [128, NT, H, H], F32, 1024)
    smax = mk.over(pre, [128, NT, H, 1], F32, 1696)

    W1xT = mk.place("RE0", [128, ND, D], BF16)
    WoT = mk.place("RE1", [128, ND, D], BF16)
    stg32b = mk.place("RC0", [128, 4, 1024], F32)
    wstgb = mk.place("RC1", [128, ND, D], BF16)
    Xsb = mk.place("RC0", [FF, 2, D], BF16, off=0)
    t_ab = mk.place("RC0", [FF, 2, D], BF16, off=4096)
    Ysb = mk.place("RC0", [FF, 2, D], BF16, off=8192)
    Whats = mk.place("RC1", [FF, 4, D], BF16, off=0)
    firT64 = mk.place("RC1", [64, ND, 128], BF16, off=8192)
    firT5 = mk.place("RC1", [5, ND, 128], BF16, off=10240)
    W1sT = mk.place("RC1", [16, D], BF16, off=12288)
    ll_tok = TokT(mk, "RA0", "RA1")
    ls_tok = TokT(mk, "RF0", "RF1")
    o_tok = ls_tok
    G0t = mk.place("RC0", [128, ND, 512], BF16, off=0)
    hdnt = mk.place("RC0", [128, ND, 512], BF16, off=8192)
    statsKt = mk.place("RC1", [16, H, 4, 128], BF16, off=2048)
    dstage = mk.place("RC0", [128, 2, D], BF16, off=0)
    mixA = mk.place("RC0", [128, D], BF16, off=4096)
    mixB = mk.place("RC0", [128, D], BF16, off=6144)
    zstage = mk.place("RC0", [128, ND, 512], BF16, off=0)
    ztokt = mk.place("RC0", [128, 4, D], BF16, off=8192)
    o_feat = FeatT(mk, "RA0", "RA1")
    x1_feat = o_feat
    hfeat = FeatT(mk, "RC0", "RC1")
    qwstg = mk.place("RB0", [128, 4, 1024], I32)
    qprim = mk.place("RB1", [128, 4, 1024], BF16)
    qbrow = mk.place("RB1", [1, D], BF16, off=8192)
    lngrow = mk.place("RB1", [1, D], BF16, off=10240)
    lnbrow = mk.place("RB1", [1, D], BF16, off=12288)
    qsF = mk.place("RB1", [128, ND, 64], F32, off=14336)
    WdqT = mk.place("RE0", [128, ND, D], BF16)
    ystage = mk.place("RD0", [128, ND, 512], BF16, off=0)
    ytokt = mk.place("RD0", [128, 4, D], BF16, off=8192)
    Zlora = mk.place("RD1", [32, T], BF16, off=0)
    grep = mk.place("RD1", [128, D], BF16, off=4096)
    brep = mk.place("RD1", [128, D], BF16, off=6144)
    laT = mk.place("RD1", [128, ND, 32], BF16, off=8192)
    lbT = mk.place("RD1", [32, ND, 128], BF16, off=8704)
    larow = mk.place("RD1", [32, D], BF16, off=10752)
    lbstg = mk.place("RD1", [128, ND, 32], BF16, off=12800)
    outstage = mk.place("RD0", [128, 4, D], F32, off=0)
    ln2tok = mk.place("RD1", [128, 4, D], BF16, off=8192)
    z_feat = FeatT(mk, "RF0", "RF1")
    w1s_stg = mk.over(pre, [128, ND, 16], BF16, 1120)
    lnmv = mk.over(pre, [128, ND, 2], F32, 1376)
    sder = mk.over(pre, [128, NT, H], F32, 1440)

    ROT3 = [pm, pb6, pb7]

    def eng_gate(name):
        for eng in ("tensor", "vector", "scalar", "sync", "gpsimd"):
            mk.wait(eng, name)

    def stats_pass(X, j, gate_sem):
        mk.wait("vector", gate_sem)
        mk.wait("scalar", gate_sem)
        for s in range(NT):
            for hh_ in range(H):
                sem = "ebn" if (s == NT - 1 and hh_ == H - 1) else None
                mk.op("vector", lambda e, s=s, hh_=hh_, X=X: e.bn_stats(
                    bn6[:, s, hh_, :], X.s(s, slice(DH * hh_, DH * hh_ + DH))), sem)
        mk.emit("vector", lambda e: e.drain())
        for s in range(NT):
            for hh_ in range(H):
                sem = "ebn2" if (s == NT - 1 and hh_ == H - 1) else None
                mk.op("vector", lambda e, s=s, hh_=hh_, j=j: e.bn_aggr(
                    mv4[:, s, hh_, j, :], bn6[:, s, hh_, :]), sem)
        for s in range(NT):
            for hh_ in range(H):
                sem = "eab" if (s == NT - 1 and hh_ == H - 1) else None
                mk.op("scalar", lambda e, s=s, hh_=hh_, j=j, X=X: e.activation(
                    sqscr[:], X.s(s, slice(DH * hh_, DH * hh_ + DH)), AF.Abs,
                    accum_out=aacc[:, s, hh_, j:j + 1]), sem)

    def peg(mmfns, sem):
        h = mk.inc(sem, eng="tensor")
        n = len(mmfns)
        for i, fn in enumerate(mmfns):
            def g(e, fn=fn, h=h, last=(i == n - 1)):
                ins = fn(e)
                if last:
                    ins.then_inc(h, 1)
            mk.emit("tensor", g)

    def emit_epilogue():
        stats_pass(delta_tok, 2, "mdone")
        stats_pass(v_tok, 3, "v_tok_rdy")
        mk.wait("sync", "mdone")
        mk.ldbg("Ldelta", [delta_tok.h[0][:].rearrange("p s n -> p (s n)"),
                           delta_tok.h[1][:].rearrange("p s n -> p (s n)")], [256, 8 * D], BF16)
        for half in range(2):
            mk.dma("sync", dscr[half], delta_tok.h[half][:], sem="espl")
        SPL = mk.cnt("espl")
        mk.wait("sync", "espl", SPL)
        if "Ldsc3" in mk.debug_taps:
            mk.dma("sync", wstgb[:, 0, :], dscr[0, :, 3, :], sem="edbg")
            mk.wait("sync", "edbg", mk.cnt("edbg"))
            mk.ldbg("Ldsc3", wstgb[:, 0, :], [128, D], BF16)
        mk.wait("sync", "ebn")
        mk.wait("sync", "eab")
        mk.dma("sync", ones512[:], ones512_c, sem="elds")

        def load_weightE(win, dstT):
            for hf in range(2):
                mk.wait("sync", "ecast")
                mk.dma("sync", stg32b[:], win.rearrange("(s p) d -> p s d", p=128)[:, 4 * hf:4 * hf + 4, :],
                       sem="elds")
                mk.wait("vector", "elds", mk.cnt("elds"))
                if hf == 0:
                    mk.wait("vector", "etr")
                mk.op("vector", lambda e, hf=hf: e.tensor_copy(
                    wstgb[:, 4 * hf:4 * hf + 4, :], stg32b[:]), "ecast")
            mk.wait("sync", "ecast")
            for os_ in range(ND):
                mk.dmaT(dstT[:, :, 128 * os_:128 * os_ + 128], wstgb[:, os_, :], sem="etr")
            return mk.cnt("etr")

        TR_W1X = load_weightE(fw1_in[:, 0:D], W1xT)
        TR_WO = load_weightE(wo_in, WoT)
        firstg = mk.over(pre, [128, ND, 64], BF16, 0)
        firstg5 = mk.over(pre, [128, ND, 5], BF16, 1024)
        mk.wait("gpsimd", "convg")
        mk.wait("gpsimd", "diag_b")
        mk.wait("gpsimd", "beta_tr")
        mk.dma("gpsimd", w1s_stg[:], fw1_in[:, D:D + 16].rearrange("(s p) j -> p s j", p=128), sem="elsm")
        mk.dma("gpsimd", bst[:], fw2_in, sem="elsm")
        mk.dma("gpsimd", b2row[:], fb2_in[None, :], sem="elsm")
        mk.dma("gpsimd", onwrow[:], onw_in[None, :], sem="elsm")
        mk.dma("gpsimd", scal3[0:1, 0:1], ltemp_in[None, :], sem="elsm")
        mk.dma("gpsimd", scal3[0:1, 1:2], rss_in[None, :], sem="elsm")
        mk.dma("gpsimd", scal3[0:1, 2:3], rsl_in[None, :], sem="elsm")
        mk.dma("gpsimd", firstg[:], firl_in.rearrange("h (s p) k -> p (h s) k", p=128), sem="elsm")
        mk.dma("gpsimd", firstg5[:], firs_in.rearrange("h (s p) k -> p (h s) k", p=128), sem="elsm")
        h_ = mk.inc("elsm", 16, eng="gpsimd", dma=True)
        mk.emit("gpsimd", lambda e, h_=h_: e.dma_start(
            b1f[:], fb1_in.rearrange("(s p) -> p s", p=128),
            allow_slow_non_contiguous=True).then_inc(h_, 16))
        ELSM = mk.cnt("elsm")
        mk.wait("tensor", "elsm", ELSM)
        mk.wait("tensor", "lde")

        mk.wait("vector", "etr")

        def tpair(tin_fn, nin, nout, out_ap):
            mk.wait("tensor", "etrpe")
            mk.op("tensor", lambda e, tin_fn=tin_fn, nin=nin, nout=nout: e.transpose(
                pt_bf[0:nout, 0:nin], tin_fn(), ident[0:nin, 0:nin]), "etrpt")
            mk.wait("vector", "etrpt")
            mk.op("vector", lambda e, out_ap=out_ap, nin=nin, nout=nout: e.tensor_copy(
                out_ap, pt_bf[0:nout, 0:nin]), "etrpe")

        for ds in range(ND):
            tpair(lambda ds=ds: firstg[:, ds, :], 128, 64, firT64[:, ds, :])
        for ds in range(ND):
            tpair(lambda ds=ds: firstg5[:, ds, :], 128, 5, firT5[:, ds, :])
        for ds in range(ND):
            tpair(lambda ds=ds: w1s_stg[:, ds, :], 128, 16, W1sT[:, 128 * ds:128 * ds + 128])
        for ds in range(ND):
            tpair(lambda ds=ds: bst[:, 128 * ds:128 * ds + 128], 4, 128, W2T[:, ds, :])
        mk.dbg("firT64", firT64[:].rearrange("p s n -> p (s n)"), [64, ND * 128], BF16, "etrpe")
        mk.dbg("firstg", firstg[:].rearrange("p s n -> p (s n)"), [128, ND * 64], BF16, "etrpe")

        mk.wait("scalar", "elsm", ELSM)
        mk.op("scalar", lambda e: e.activation(scal3[0:1, 3:4], scal3[0:1, 0:1], AF.Abs), None)
        mk.emit("scalar", lambda e: e.drain())
        mk.op("scalar", lambda e: e.activation(scal3[0:1, 4:5], scal3[0:1, 3:4], AF.Exp, scale=-1.0), None)
        mk.emit("scalar", lambda e: e.drain())
        mk.op("scalar", lambda e: e.activation(scal3[0:1, 5:6], scal3[0:1, 4:5], AF.Ln, bias=1.0), None)
        mk.emit("scalar", lambda e: e.drain())
        mk.op("scalar", lambda e: e.activation(scal3[0:1, 6:7], scal3[0:1, 0:1], AF.Relu), "esp")
        mk.wait("vector", "esp")
        mk.op("vector", lambda e: e.tensor_add(scal3[0:1, 7:8], scal3[0:1, 5:6], scal3[0:1, 6:7]), None)
        mk.emit("vector", lambda e: e.drain())
        mk.op("vector", lambda e: e.tensor_scalar(out=scal3[0:1, 7:8], in0=scal3[0:1, 7:8],
                                                  scalar1=1e-4, scalar2=None, op0=ALU.add), None)
        mk.emit("vector", lambda e: e.drain())
        mk.op("vector", lambda e: e.reciprocal(scal3[0:1, 3:4], scal3[0:1, 7:8]), "esp2")
        mk.wait("tensor", "esp2")
        mk.wait("tensor", "etrpe")
        sc16 = mk.over(cw, [1, 8], BF16, 32)
        mk.wait("vector", "esp2")
        mk.op("vector", lambda e: e.tensor_copy(sc16[:], scal3[0:1, 0:8]), "esc16")
        mk.wait("tensor", "esc16")
        peg([lambda e: e.matmul(ROT3[0][0:128, 0:1], onesrow[0:1, 0:128], sc16[0:1, 3:4], start=True, stop=True),
             lambda e: e.matmul(ROT3[0][0:128, 1:2], onesrow[0:1, 0:128], sc16[0:1, 1:2], start=False, stop=True),
             lambda e: e.matmul(ROT3[0][0:128, 2:3], onesrow[0:1, 0:128], sc16[0:1, 2:3], start=False, stop=True),
             lambda e: e.matmul(ROT3[0][0:128, 4:8], onesrow[0:1, 0:128], b2row[0:1, :], start=False, stop=True),
             lambda e: e.matmul(ROT3[0][0:128, 8:264], onesrow[0:1, 0:128], onwrow[0:1, :], start=False, stop=True)],
            "ebc")
        mk.wait("vector", "ebc")
        mk.op("vector", lambda e: e.tensor_copy(rtemp[:], ROT3[0][:, 0:1]), None)
        mk.op("vector", lambda e: e.tensor_copy(rss128[:], ROT3[0][:, 1:2]), None)
        mk.op("vector", lambda e: e.tensor_copy(rsl128[:], ROT3[0][:, 2:3]), None)
        mk.op("vector", lambda e: e.tensor_copy(b2rep[:], ROT3[0][:, 4:8]), None)
        mk.op("vector", lambda e: e.tensor_copy(onwrep[:], ROT3[0][:, 8:264]), "ebce")

        mk.wait("tensor", "ebce")
        for fi, (Fw, firT, wslot) in enumerate([(Fw64, firT64, 0), (Fw5, firT5, 2)]):
            for ri in range(2):
                for nh in range(2):
                    peg([lambda e, Fw=Fw, firT=firT, ri=ri, nh=nh: e.matmul(
                        ROT3[nh][0:FF, 0:512], Fw[:, ri, :],
                        firT[:].rearrange("k s n -> k (s n)")[:, 512 * nh:512 * nh + 512],
                        start=True, stop=True)], "ewh")
                mk.wait("vector", "ewh")
                mk.op("vector", lambda e, wslot=wslot, ri=ri: e.tensor_copy(
                    Whats[:, wslot + ri, 0:512], ROT3[0][0:FF, 0:512]), None)
                mk.op("vector", lambda e, wslot=wslot, ri=ri: e.tensor_copy(
                    Whats[:, wslot + ri, 512:1024], ROT3[1][0:FF, 0:512]), "ewhe")
                mk.wait("tensor", "ewhe", mk.cnt("ewhe"))

        XB = [pp[0], pp[1], pc[0], pc[1]]
        for b in range(NT):
            mms = []
            for ri in range(2):
                for nh in range(2):
                    def f(e, b=b, ri=ri, nh=nh):
                        return e.matmul(XB[2 * ri + nh][0:FF, :], FBc[:, ri, :],
                                        v_tok.s(b, slice(512 * nh, 512 * nh + 512)),
                                        start=True, stop=(b == 0))
                    mms.append(f)
                    if b > 0:
                        def f2(e, b=b, ri=ri, nh=nh):
                            return e.matmul(XB[2 * ri + nh][0:FF, :], FAc[64:128, ri, :],
                                            v_tok.s(b - 1, slice(512 * nh, 512 * nh + 512))[64:128, :],
                                            start=False, stop=True)
                        mms.append(f2)
            if b > 0:
                mk.wait("tensor", "exe", mk.cnt("exe"))
            peg(mms, "efw")
            mk.wait("scalar", "efw")
            for k4 in range(4):
                sem = "exe" if k4 == 3 else None
                mk.op("scalar", lambda e, k4=k4: e.activation(
                    Xsb[:, k4 // 2, 512 * (k4 % 2):512 * (k4 % 2) + 512],
                    XB[k4][0:FF, :], AF.Copy), sem)
            mk.wait("vector", "exe")
            for fi, wslot in ((0, 0), (1, 2)):
                mk.op("vector", lambda e, wslot=wslot: e.tensor_mul(
                    t_ab[:, 0, :], Xsb[:, 0, :], Whats[:, wslot + 0, :]), None)
                mk.op("vector", lambda e, wslot=wslot: e.tensor_mul(
                    t_ab[:, 1, :], Xsb[:, 1, :], Whats[:, wslot + 1, :]), None)
                mk.emit("vector", lambda e: e.drain())
                mk.op("vector", lambda e: e.tensor_sub(Ysb[:, 0, :], t_ab[:, 0, :], t_ab[:, 1, :]), None)
                mk.op("vector", lambda e, wslot=wslot: e.tensor_mul(
                    t_ab[:, 0, :], Xsb[:, 0, :], Whats[:, wslot + 1, :]), None)
                mk.op("vector", lambda e, wslot=wslot: e.tensor_mul(
                    t_ab[:, 1, :], Xsb[:, 1, :], Whats[:, wslot + 0, :]), None)
                mk.emit("vector", lambda e: e.drain())
                mk.op("vector", lambda e: e.tensor_add(Ysb[:, 1, :], t_ab[:, 0, :], t_ab[:, 1, :]), "epw")
                mk.wait("tensor", "epw")
                if b > 0 or fi > 0:
                    mk.wait("tensor", "eie", mk.cnt("eie"))
                mms = []
                for nh in range(2):
                    for ri in range(2):
                        def f(e, nh=nh, ri=ri):
                            return e.matmul(ROT3[nh][:, :], iFc[:, ri, :],
                                            Ysb[:, ri, 512 * nh:512 * nh + 512],
                                            start=(ri == 0), stop=(ri == 1))
                        mms.append(f)
                peg(mms, "einv")
                mk.wait("vector", "einv")
                mk.wait("scalar", "einv")
                dstT = ll_tok if fi == 0 else ls_tok
                mk.op("vector", lambda e, dstT=dstT, b=b: e.tensor_copy(
                    dstT.s(b, slice(0, 512)), ROT3[0][:, :]), "eie")
                mk.op("scalar", lambda e, dstT=dstT, b=b: e.activation(
                    dstT.s(b, slice(512, 1024)), ROT3[1][:, :], AF.Copy), "eie")
        mk.dbg("Xsb", Xsb[:].rearrange("p a n -> p (a n)"), [FF, 2 * D], BF16, "eie")
        if "Lll" in mk.debug_taps or "Lls" in mk.debug_taps:
            mk.wait("sync", "eie", mk.cnt("eie"))
            mk.ldbg("Lll", [ll_tok.h[0][:].rearrange("p s n -> p (s n)"),
                            ll_tok.h[1][:].rearrange("p s n -> p (s n)")], [256, 8 * D], BF16)
            mk.ldbg("Lls", [ls_tok.h[0][:].rearrange("p s n -> p (s n)"),
                            ls_tok.h[1][:].rearrange("p s n -> p (s n)")], [256, 8 * D], BF16)
        mk.op("vector", lambda e: e.memset(sqscr[0:1, 0:1], 0.0), "efir")

        stats_pass(ll_tok, 1, "efir")
        stats_pass(ls_tok, 0, "efir")
        mk.wait("vector", "ebn2")
        mk.wait("scalar", "eab")
        mk.wait("vector", "eab")
        for j in range(4):
            mk.op("vector", lambda e, j=j: e.tensor_copy(
                stats_tok[:, :, :, 4 * j + 0], mv4[:, :, :, j, 0]), None)
            mk.op("vector", lambda e, j=j: e.tensor_copy(
                stats_tok[:, :, :, 4 * j + 1], mv4[:, :, :, j, 1]), None)
            mk.op("vector", lambda e, j=j: e.tensor_scalar(
                out=stats_tok[:, :, :, 4 * j + 2], in0=aacc[:, :, :, j],
                scalar1=1.0 / DH, scalar2=None, op0=ALU.mult), None)
            mk.op("vector", lambda e, j=j: e.tensor_mul(
                sder[:, :, :], mv4[:, :, :, j, 0], mv4[:, :, :, j, 0]), None)
            mk.emit("vector", lambda e: e.drain())
            mk.op("vector", lambda e, j=j: e.tensor_add(
                sder[:, :, :], sder[:, :, :], mv4[:, :, :, j, 1]), "esd")
            mk.wait("scalar", "esd")
            mk.op("scalar", lambda e, j=j: e.activation(
                stats_tok[:, :, :, 4 * j + 3], sder[:, :, :], AF.Sqrt, scale=float(DH)), "esd2")
        mk.wait("vector", "esd2")
        mk.op("vector", lambda e: e.memset(sqscr[0:1, 0:1], 0.0), "estats")
        if "Lstats" in mk.debug_taps:
            mk.wait("sync", "esd2", mk.cnt("esd2"))
            mk.ldbg("Lstats", stats_tok[:].rearrange("p s h c -> p (s h c)"), [128, NT * H * 16], BF16)

        mk.wait("tensor", "etr", TR_WO)
        mk.wait("tensor", "estats")
        GR = [pp[0], pp[1]]
        HR = [pc[0], pc[1]]
        for tt in range(4):
            for hh_ in range(H):
                for sr in range(4):
                    mk.wait("tensor", "egkt", mk.cnt("egkt"))
                    mk.op("tensor", lambda e, hh_=hh_, sr=sr, tt=tt: e.transpose(
                        pt_bf[0:16, 128 * sr:128 * sr + 128], stats_tok[:, 4 * tt + sr, hh_, :],
                        ident[:]), "egk")
                mk.wait("vector", "egk")
                mk.op("vector", lambda e, hh_=hh_: e.tensor_copy(
                    statsKt[:, hh_, :, :].rearrange("p a n -> p (a n)"), pt_bf[0:16, :]), "egkt")
            for os_ in range(ND):
                gb = GR[os_ % 2]
                if os_ >= 2:
                    mk.wait("tensor", "eg0e", mk.cnt("eg0e") - 1)
                mms = []
                for ks in range(ND):
                    def f(e, gb=gb, ks=ks, os_=os_, tt=tt):
                        return e.matmul(gb[:], W1xT[:, ks, 128 * os_:128 * os_ + 128],
                                        x_feat.ks(ks, slice(512 * tt, 512 * tt + 512)),
                                        start=(ks == 0), stop=(ks == ND - 1))
                    mms.append(f)
                peg(mms, "eg0")
                mk.wait("scalar", "eg0")
                mk.op("scalar", lambda e, gb=gb, os_=os_: e.activation(
                    G0t[:, os_, :], gb[:], AF.Copy), "eg0e")
            for hh_ in range(H):
                mk.wait("tensor", "eg0e", mk.cnt("eg0e"))
                mk.wait("tensor", "egkt", mk.cnt("egkt"))
                for os_ in range(ND):
                    hb = HR[os_ % 2]
                    if os_ >= 2 or hh_ > 0:
                        mk.wait("tensor", "ehde", mk.cnt("ehde") - 1)
                    peg([lambda e, hb=hb, os_=os_, hh_=hh_: e.matmul(
                            hb[:], W1sT[:, 128 * os_:128 * os_ + 128], statsKt[:, hh_, :, :].rearrange("p a n -> p (a n)"),
                            start=True, stop=False),
                         lambda e, hb=hb, os_=os_: e.matmul(
                            hb[:], ident[:], G0t[:, os_, :], start=False, stop=True)], "ehd")
                    mk.wait("scalar", "ehd")
                    mk.op("scalar", lambda e, hb=hb, os_=os_: e.activation(
                        hdnt[:, os_, :], hb[:], AF.Gelu, bias=b1f[:, os_:os_ + 1]), "ehde")
                mk.wait("tensor", "ehde", mk.cnt("ehde"))
                mms = []
                for c4 in range(4):
                    for os_ in range(ND):
                        def f(e, c4=c4, os_=os_):
                            return e.matmul(pm[:, 4 * c4:4 * c4 + 4],
                                            hdnt[:, os_, 128 * c4:128 * c4 + 128], W2T[:, os_, :],
                                            start=(os_ == 0), stop=(os_ == ND - 1))
                        mms.append(f)
                peg(mms, "elg")
                mk.wait("vector", "elg")
                mk.op("vector", lambda e, tt=tt, hh_=hh_: e.tensor_copy(
                    logits_tok[:, 4 * tt:4 * tt + 4, hh_, :], pm[:, 0:16].rearrange("p (a b) -> p a b", a=4)), "elge")
                mk.wait("tensor", "elge", mk.cnt("elge"))
        mk.wait("vector", "elge")
        mk.op("vector", lambda e: e.tensor_add(
            logits_tok[:], logits_tok[:],
            b2rep[:, None, None, :].broadcast_to([128, NT, H, H])), None)
        mk.emit("vector", lambda e: e.drain())
        mk.op("vector", lambda e: e.reduce_max(
            smax[:], logits_tok[:], mybir.AxisListType.X), None)
        mk.emit("vector", lambda e: e.drain())
        mk.op("vector", lambda e: e.tensor_sub(
            logits_tok[:], logits_tok[:], smax[:].broadcast_to([128, NT, H, H])), "esm1")
        mk.wait("scalar", "esm1")
        mk.op("scalar", lambda e: e.activation(
            fw_tok[:], logits_tok[:], AF.Exp, scale=rtemp[:, 0:1]), "esm2")
        mk.wait("vector", "esm2")
        mk.op("vector", lambda e: e.reduce_sum(
            smax[:], fw_tok[:], mybir.AxisListType.X), None)
        mk.emit("vector", lambda e: e.drain())
        mk.op("vector", lambda e: e.reciprocal(smax[:], smax[:]), None)
        mk.emit("vector", lambda e: e.drain())
        mk.op("vector", lambda e: e.tensor_mul(
            fw_tok[:], fw_tok[:], smax[:].broadcast_to([128, NT, H, H])), None)
        mk.emit("vector", lambda e: e.drain())
        mk.op("vector", lambda e: e.tensor_scalar(
            out=fw_tok[:, :, :, 0], in0=fw_tok[:, :, :, 0],
            scalar1=rss128[:, 0:1], scalar2=None, op0=ALU.add), None)
        mk.emit("vector", lambda e: e.drain())
        mk.op("vector", lambda e: e.tensor_scalar(
            out=fw_tok[:, :, :, 1], in0=fw_tok[:, :, :, 1],
            scalar1=rsl128[:, 0:1], scalar2=None, op0=ALU.add), "efwr")

        mk.wait("sync", "efwr")
        mk.ldbg("Lfw", fw_tok[:].rearrange("p s h c -> p (s h c)"), [128, NT * H * H], F32)
        mk.ldbg("Lvm", [v_tok.h[0][:].rearrange("p s n -> p (s n)"),
                        v_tok.h[1][:].rearrange("p s n -> p (s n)")], [256, 8 * D], BF16)
        mk.ldbg("Lllm", [ll_tok.h[0][:].rearrange("p s n -> p (s n)"),
                         ll_tok.h[1][:].rearrange("p s n -> p (s n)")], [256, 8 * D], BF16)
        mk.ldbg("Llsm", [ls_tok.h[0][:].rearrange("p s n -> p (s n)"),
                         ls_tok.h[1][:].rearrange("p s n -> p (s n)")], [256, 8 * D], BF16)
        for s in range(NT):
            if s >= 2:
                mk.wait("sync", "emix", s - 1)
            mk.dma("sync", dstage[:, s % 2, :], dscr[s // 8, :, s % 8, :], sem=f"edst{s % 2}")
            mk.wait("vector", f"edst{s % 2}", mk.cnt(f"edst{s % 2}"))
            mk.wait("vector", "efwr")

            def bc(j, s=s):
                return fw_tok[:, s, :, j:j + 1].broadcast_to([128, H, DH])
            mk.op("vector", lambda e, a=dstage[:, s % 2, :].rearrange("p (h d) -> p h d", h=H), b=bc(2):
                  e.tensor_tensor(mixA[:].rearrange("p (h d) -> p h d", h=H), a, b, ALU.mult), None)
            mk.op("vector", lambda e, a=ll_tok.s(s).rearrange("p (h d) -> p h d", h=H), b=bc(1):
                  e.tensor_tensor(mixB[:].rearrange("p (h d) -> p h d", h=H), a, b, ALU.mult), None)
            mk.emit("vector", lambda e: e.drain())
            mk.op("vector", lambda e: e.tensor_add(mixA[:], mixA[:], mixB[:]), None)
            mk.op("vector", lambda e, a=ls_tok.s(s).rearrange("p (h d) -> p h d", h=H), b=bc(0):
                  e.tensor_tensor(mixB[:].rearrange("p (h d) -> p h d", h=H), a, b, ALU.mult), None)
            mk.emit("vector", lambda e: e.drain())
            mk.op("vector", lambda e: e.tensor_add(mixA[:], mixA[:], mixB[:]), None)
            mk.op("vector", lambda e, a=v_tok.s(s).rearrange("p (h d) -> p h d", h=H), b=bc(3):
                  e.tensor_tensor(mixB[:].rearrange("p (h d) -> p h d", h=H), a, b, ALU.mult), None)
            mk.emit("vector", lambda e: e.drain())
            mk.op("vector", lambda e, a=o_tok.s(s): e.tensor_add(a, mixA[:], mixB[:]), "emix")
            mk.wait("scalar", "emix", mk.cnt("emix"))
            for hh_ in range(H):
                sem = "erms" if hh_ == H - 1 else None
                mk.op("scalar", lambda e, s=s, hh_=hh_: e.activation(
                    sqscr[:], o_tok.s(s, slice(DH * hh_, DH * hh_ + DH)), AF.Square,
                    accum_out=rmsacc[:, s, hh_:hh_ + 1]), sem)
        if "Lo2" in mk.debug_taps:
            mk.wait("sync", "emix", mk.cnt("emix"))
            mk.ldbg("Lo2", [o_tok.h[0][:].rearrange("p s n -> p (s n)"),
                            o_tok.h[1][:].rearrange("p s n -> p (s n)")], [256, 8 * D], BF16)
            mk.wait("vector", "st_out", mk.cnt("st_out"))
        mk.wait("vector", "erms")
        mk.op("vector", lambda e: e.tensor_scalar(
            out=rmsacc[:], in0=rmsacc[:], scalar1=1.0 / DH, scalar2=EPS,
            op0=ALU.mult, op1=ALU.add), "erst")
        mk.wait("scalar", "erst")
        mk.op("scalar", lambda e: e.activation(rmsacc[:], rmsacc[:], AF.Sqrt), "erst2")
        mk.wait("vector", "erst2")
        mk.op("vector", lambda e: e.reciprocal(rmsacc[:], rmsacc[:]), "erst3")
        mk.wait("vector", "erst3")
        for s in range(NT):
            for hh_ in range(H):
                sem = "eon" if (s == NT - 1 and hh_ == H - 1) else None
                mk.op("vector", lambda e, s=s, hh_=hh_: e.scalar_tensor_tensor(
                    out=o_tok.s(s, slice(DH * hh_, DH * hh_ + DH)),
                    in0=o_tok.s(s, slice(DH * hh_, DH * hh_ + DH)),
                    scalar=rmsacc[:, s, hh_:hh_ + 1], in1=onwrep[:],
                    op0=ALU.mult, op1=ALU.mult), sem)
        mk.wait("sync", "eon")
        mk.ldbg("Lo", [o_tok.h[0][:].rearrange("p s n -> p (s n)"),
                       o_tok.h[1][:].rearrange("p s n -> p (s n)")], [256, 8 * D], BF16)
        dmaT_feat_from_tok(mk, o_feat, o_tok)
        TR_O = mk.cnt("tr")

        if "Lof" in mk.debug_taps:
            mk.wait("sync", "tr", TR_O)
            mk.ldbg("Lof", [o_feat.h[0][:].rearrange("p s n -> p (s n)"),
                            o_feat.h[1][:].rearrange("p s n -> p (s n)")], [256, 4 * T], BF16)
        if "Lonw" in mk.debug_taps:
            mk.wait("sync", "ebce", mk.cnt("ebce"))
            mk.ldbg("Lonw", onwrep[:], [128, DH], BF16)
        mk.wait("tensor", "tr", TR_O)
        for tt in range(4):
            for ds2 in range(ND):
                zb = GR[ds2 % 2]
                if ds2 >= 2 or tt > 0:
                    mk.wait("tensor", "eze", mk.cnt("eze") - 1)
                mms = []
                for cs in range(ND):
                    def f(e, zb=zb, cs=cs, ds2=ds2, tt=tt):
                        return e.matmul(zb[:], WoT[:, cs, 128 * ds2:128 * ds2 + 128],
                                        o_feat.ks(cs, slice(512 * tt, 512 * tt + 512)),
                                        start=(cs == 0), stop=False)
                    mms.append(f)
                mms.append(lambda e, zb=zb, ds2=ds2, tt=tt: e.matmul(
                    zb[:], ident[:], x_feat.ks(ds2, slice(512 * tt, 512 * tt + 512)),
                    start=False, stop=True))
                peg(mms, "ez")
                mk.wait("scalar", "ez")
                mk.op("scalar", lambda e, zb=zb, ds2=ds2: e.activation(
                    zstage[:, ds2, :], zb[:], AF.Copy), "eze")
            mk.wait("sync", "eze", mk.cnt("eze"))
            if tt > 0:
                mk.wait("sync", "ex1t", mk.cnt("ex1t"))
            if tt == 0:
                mk.ldbg("Lz0", zstage[:].rearrange("p s n -> p (s n)"), [128, ND * 512], BF16)
            for ds in range(ND):
                mk.dmaT(ztokt[:, :, 128 * ds:128 * ds + 128], zstage[:, ds, :], sem="eztr")
            ZTR = mk.cnt("eztr")
            if tt == 0 and "Lzt0" in mk.debug_taps:
                mk.wait("sync", "eztr", ZTR)
                mk.ldbg("Lzt0", ztokt[:].rearrange("p s n -> p (s n)"), [128, 4 * D], BF16)
            mk.wait("vector", "eztr", ZTR)
            for sr in range(4):
                for hf2 in range(2):
                    sem = "ebnz" if (sr == 3 and hf2 == 1) else None
                    mk.op("vector", lambda e, sr=sr, hf2=hf2: e.bn_stats(
                        bn6[:, 2 * sr + hf2, 0, :], ztokt[:, sr, 512 * hf2:512 * hf2 + 512]), sem)
            mk.emit("vector", lambda e: e.drain())
            for sr in range(4):
                sem = "ebnz2" if sr == 3 else None
                mk.op("vector", lambda e, sr=sr: e.bn_aggr(
                    lnmv[:, sr, :], bn6[:, 2 * sr:2 * sr + 2, 0, :]), sem)
            mk.emit("vector", lambda e: e.drain())
            mk.op("vector", lambda e: e.tensor_scalar(
                out=lnmv[:, 0:4, 1], in0=lnmv[:, 0:4, 1], scalar1=EPS, scalar2=None, op0=ALU.add), "elnv")
            mk.wait("scalar", "elnv", mk.cnt("elnv"))
            mk.op("scalar", lambda e: e.activation(lnmv[:, 0:4, 1], lnmv[:, 0:4, 1], AF.Sqrt), "elns")
            mk.wait("vector", "elns", mk.cnt("elns"))
            mk.op("vector", lambda e: e.reciprocal(lnmv[:, 0:4, 1], lnmv[:, 0:4, 1]), "elnr")
            mk.emit("vector", lambda e: e.drain())
            for sr in range(4):
                sem = "ex1" if sr == 3 else None
                mk.op("vector", lambda e, sr=sr: e.tensor_scalar(
                    out=ztokt[:, sr, :], in0=ztokt[:, sr, :],
                    scalar1=lnmv[:, sr, 0:1], scalar2=lnmv[:, sr, 1:2],
                    op0=ALU.subtract, op1=ALU.mult), sem)
            mk.wait("sync", "ex1", mk.cnt("ex1"))
            if tt == 0:
                mk.ldbg("Lzn0", ztokt[:].rearrange("p s n -> p (s n)"), [128, 4 * D], BF16)
                mk.ldbg("Llnmv", lnmv[:].rearrange("p s n -> p (s n)"), [128, 2 * ND], F32)
            for sr in range(4):
                s_ = 4 * tt + sr
                for half in range(2):
                    mk.dmaT(x1_feat.h[half][:, :, 128 * s_:128 * s_ + 128],
                            ztokt[:, sr, 512 * half:512 * half + 512], sem="ex1t")
        mk.wait("vector", "ex1t", mk.cnt("ex1t"))
        for half in range(2):
            sem = "ehcp" if half == 1 else None
            mk.op("vector", lambda e, half=half: e.tensor_copy(
                hfeat.h[half][:], x1_feat.h[half][:]), sem)

        if "Lx1" in mk.debug_taps:
            mk.wait("sync", "ex1t", mk.cnt("ex1t"))
            mk.ldbg("Lx1", [x1_feat.h[0][:].rearrange("p s n -> p (s n)"),
                            x1_feat.h[1][:].rearrange("p s n -> p (s n)")], [256, 4 * T], BF16)

        YR = [pp[0], pp[1]]

        def qlora(i, IN, OUT_IS_H, want_ln_gelu):
            b = i // 2
            mk.wait("gpsimd", "eqmm")
            mk.dma("gpsimd", larow[0:32, :], la_in[i], sem="eql")
            mk.dma("gpsimd", lbstg[:], lb_in[i].rearrange("(s p) r -> p s r", p=128), sem="eql")
            mk.dma("gpsimd", qbrow[:], qb_in[i][None, :], sem="eql")
            mk.dma("gpsimd", qsF[:], qs_in[i].rearrange("(s p) g -> p s g", p=128), sem="eql")
            if want_ln_gelu:
                mk.dma("gpsimd", lngrow[:], lng_in[b][None, :], sem="eql")
                mk.dma("gpsimd", lnbrow[:], lnb_in[b][None, :], sem="eql")
            QL = mk.cnt("eql")
            mk.wait("tensor", "eql", QL)
            for ds in range(ND):
                tpair(lambda ds=ds: larow[0:32, 128 * ds:128 * ds + 128], 32, 128, laT[:, ds, :])
            for ds in range(ND):
                tpair(lambda ds=ds: lbstg[:, ds, :], 128, 32, lbT[:, ds, :])
            if want_ln_gelu:
                mk.wait("tensor", "egbe", mk.cnt("egbe"))
                peg([lambda e: e.matmul(ROT3[2][:, 0:512], onesrow[0:1, 0:128], lngrow[0:1, 0:512],
                                        start=True, stop=True),
                     lambda e: e.matmul(ROT3[1][:, 0:512],
                                        onesrow[0:1, 0:128], lngrow[0:1, 512:1024], start=True, stop=True)],
                    "egb")
                mk.wait("vector", "egb")
                mk.op("vector", lambda e: e.tensor_copy(grep[:, 0:512], ROT3[2][:, 0:512]), None)
                mk.op("vector", lambda e: e.tensor_copy(grep[:, 512:1024], ROT3[1][:, 0:512]), "egbe")
                mk.wait("tensor", "egbe", mk.cnt("egbe"))
                peg([lambda e: e.matmul(ROT3[2][:, 0:512], onesrow[0:1, 0:128], lnbrow[0:1, 0:512],
                                        start=True, stop=True),
                     lambda e: e.matmul(ROT3[1][:, 0:512], onesrow[0:1, 0:128], lnbrow[0:1, 512:1024],
                                        start=True, stop=True)], "egb")
                mk.wait("vector", "egb")
                mk.op("vector", lambda e: e.tensor_copy(brep[:, 0:512], ROT3[2][:, 0:512]), None)
                mk.op("vector", lambda e: e.tensor_copy(brep[:, 512:1024], ROT3[1][:, 0:512]), "egbe")
            for hf in range(2):
                mk.wait("sync", "eqdq")
                if i > 0 and hf == 0:
                    mk.wait("sync", "eqmm")
                mk.dma("sync", qwstg[:], qw_in[i].rearrange("(s p) d -> p s d", p=128)[:, 4 * hf:4 * hf + 4, :],
                       sem="eqw")
                mk.wait("scalar", "eqw", mk.cnt("eqw"))
                mk.wait("scalar", "eql", QL)
                mk.op("scalar", lambda e: e.activation(qprim[:], qwstg[:], AF.Copy,
                                                       scale=1.0 / 7.5, bias=-1.0), "eqa")
                mk.wait("vector", "eqa", mk.cnt("eqa"))
                for osr in range(4):
                    sem = "eqd" if osr == 3 else None
                    mk.op("vector", lambda e, osr=osr, hf=hf: e.tensor_tensor(
                        qprim[:, osr, :].rearrange("p (g r) -> p g r", g=64),
                        qprim[:, osr, :].rearrange("p (g r) -> p g r", g=64),
                        qsF[:, 4 * hf + osr, :, None].broadcast_to([128, 64, 16]), ALU.mult), sem)
                mk.wait("sync", "eqd", mk.cnt("eqd"))
                for osr in range(4):
                    mk.dmaT(WdqT[:, :, 128 * (4 * hf + osr):128 * (4 * hf + osr) + 128],
                            qprim[:, osr, :], sem="eqdq")
            DQ = mk.cnt("eqdq")
            mk.wait("tensor", "eqdq", DQ)
            mk.wait("tensor", "etrpe", mk.cnt("etrpe"))
            mk.wait("tensor", "ehcp")
            mk.wait("tensor", "ezf", mk.cnt("ezf"))
            mk.wait("tensor", "eye", mk.cnt("eye"))
            for tt in range(4):
                mms = []
                for ks in range(ND):
                    def f(e, ks=ks, tt=tt, IN=IN):
                        return e.matmul(pc[0][0:32, :], laT[:, ks, :],
                                        IN.ks(ks, slice(512 * tt, 512 * tt + 512)),
                                        start=(ks == 0), stop=(ks == ND - 1))
                    mms.append(f)
                peg(mms, "ezl")
                mk.wait("vector", "ezl")
                mk.op("vector", lambda e, tt=tt: e.tensor_copy(
                    Zlora[0:32, 512 * tt:512 * tt + 512], pc[0][0:32, :]), "ezle")
                mk.wait("tensor", "ezle", mk.cnt("ezle"))
            for tt in range(4):
                for os_ in range(ND):
                    yb = YR[os_ % 2]
                    if os_ >= 2 or tt > 0:
                        mk.wait("tensor", "eye", mk.cnt("eye") - 1)
                    mms = []
                    for ks in range(ND):
                        def f(e, yb=yb, ks=ks, os_=os_, tt=tt, IN=IN):
                            return e.matmul(yb[:], WdqT[:, ks, 128 * os_:128 * os_ + 128],
                                            IN.ks(ks, slice(512 * tt, 512 * tt + 512)),
                                            start=(ks == 0), stop=False)
                        mms.append(f)
                    mms.append(lambda e, yb=yb, os_=os_: e.matmul(
                        yb[:], qbrow[0:1, 128 * os_:128 * os_ + 128], ones512[0:1, :],
                        start=False, stop=False))
                    mms.append(lambda e, yb=yb, os_=os_, tt=tt: e.matmul(
                        yb[:], lbT[0:32, os_, :], Zlora[0:32, 512 * tt:512 * tt + 512],
                        start=False, stop=not OUT_IS_H))
                    if OUT_IS_H:
                        mms.append(lambda e, yb=yb, os_=os_, tt=tt: e.matmul(
                            yb[:], ident[:], hfeat.ks(os_, slice(512 * tt, 512 * tt + 512)),
                            start=False, stop=True))
                    peg(mms, "eym")
                    mk.wait("scalar", "eym")
                    if OUT_IS_H:
                        mk.op("scalar", lambda e, yb=yb, os_=os_, tt=tt: e.activation(
                            hfeat.ks(os_, slice(512 * tt, 512 * tt + 512)), yb[:], AF.Copy), "eye")
                    else:
                        mk.op("scalar", lambda e, yb=yb, os_=os_: e.activation(
                            ystage[:, os_, :], yb[:], AF.Copy), "eye")
                if not OUT_IS_H:
                    mk.wait("sync", "eye", mk.cnt("eye"))
                    mk.wait("sync", "ezf", mk.cnt("ezf"))
                    for ds in range(ND):
                        mk.dmaT(ytokt[:, :, 128 * ds:128 * ds + 128], ystage[:, ds, :], sem="eytr")
                    mk.wait("vector", "eytr", mk.cnt("eytr"))
                    for sr in range(4):
                        for hf2 in range(2):
                            sem = "ebny" if (sr == 3 and hf2 == 1) else None
                            mk.op("vector", lambda e, sr=sr, hf2=hf2: e.bn_stats(
                                bn6[:, 2 * sr + hf2, 0, :], ytokt[:, sr, 512 * hf2:512 * hf2 + 512]), sem)
                    mk.emit("vector", lambda e: e.drain())
                    for sr in range(4):
                        sem = "ebny2" if sr == 3 else None
                        mk.op("vector", lambda e, sr=sr: e.bn_aggr(
                            lnmv[:, sr, :], bn6[:, 2 * sr:2 * sr + 2, 0, :]), sem)
                    mk.emit("vector", lambda e: e.drain())
                    mk.op("vector", lambda e: e.tensor_scalar(
                        out=lnmv[:, 0:4, 1], in0=lnmv[:, 0:4, 1], scalar1=EPS, scalar2=None,
                        op0=ALU.add), "elnv2")
                    mk.wait("scalar", "elnv2", mk.cnt("elnv2"))
                    mk.op("scalar", lambda e: e.activation(lnmv[:, 0:4, 1], lnmv[:, 0:4, 1], AF.Sqrt), "elns2")
                    mk.wait("vector", "elns2", mk.cnt("elns2"))
                    mk.op("vector", lambda e: e.reciprocal(lnmv[:, 0:4, 1], lnmv[:, 0:4, 1]), None)
                    mk.emit("vector", lambda e: e.drain())
                    for sr in range(4):
                        mk.op("vector", lambda e, sr=sr: e.tensor_scalar(
                            out=ytokt[:, sr, :], in0=ytokt[:, sr, :],
                            scalar1=lnmv[:, sr, 0:1], scalar2=lnmv[:, sr, 1:2],
                            op0=ALU.subtract, op1=ALU.mult), None)
                    mk.emit("vector", lambda e: e.drain())
                    mk.op("vector", lambda e: e.tensor_tensor(
                        ytokt[:], ytokt[:], grep[:, None, :].broadcast_to([128, 4, D]), ALU.mult), None)
                    mk.emit("vector", lambda e: e.drain())
                    mk.op("vector", lambda e: e.tensor_tensor(
                        ytokt[:], ytokt[:], brep[:, None, :].broadcast_to([128, 4, D]), ALU.add), "elngb")
                    mk.wait("scalar", "elngb", mk.cnt("elngb"))
                    mk.op("scalar", lambda e: e.activation(ytokt[:], ytokt[:], AF.Gelu), "egel")
                    mk.wait("sync", "egel", mk.cnt("egel"))
                    for sr in range(4):
                        s_ = 4 * tt + sr
                        for half in range(2):
                            mk.dmaT(z_feat.h[half][:, :, 128 * s_:128 * s_ + 128],
                                    ytokt[:, sr, 512 * half:512 * half + 512], sem="ezf")
            mk.wait("gpsimd", "eym")
            hqm = mk.inc("eqmm", eng="gpsimd")
            mk.emit("gpsimd", lambda e, hqm=hqm: e.sem_inc(hqm, 1))

        mk.wait("gpsimd", "ehcp")
        for b in range(3):
            IN1 = hfeat
            qlora(2 * b, IN1, False, True)
            qlora(2 * b + 1, z_feat, True, False)
            if f"Lh{b + 1}" in mk.debug_taps:
                mk.wait("sync", "eye", mk.cnt("eye"))
                mk.ldbg(f"Lh{b + 1}", [hfeat.h[0][:].rearrange("p s n -> p (s n)"),
                                       hfeat.h[1][:].rearrange("p s n -> p (s n)")], [256, 4 * T], BF16)

        mk.wait("tensor", "eye", mk.cnt("eye"))
        for tt in range(4):
            if tt > 0:
                mk.wait("scalar", "st_out", mk.cnt("st_out"))
            for ds2 in range(ND):
                zb = YR[ds2 % 2]
                if ds2 >= 2 or tt > 0:
                    mk.wait("tensor", "efze", mk.cnt("efze") - 1)
                peg([lambda e, zb=zb, ds2=ds2, tt=tt: e.matmul(
                        zb[:], ident[:], x1_feat.ks(ds2, slice(512 * tt, 512 * tt + 512)),
                        start=True, stop=False),
                     lambda e, zb=zb, ds2=ds2, tt=tt: e.matmul(
                        zb[:], ident[:], hfeat.ks(ds2, slice(512 * tt, 512 * tt + 512)),
                        start=False, stop=True)], "efz")
                mk.wait("scalar", "efz")
                mk.op("scalar", lambda e, zb=zb, ds2=ds2: e.activation(
                    ystage[:, ds2, :], zb[:], AF.Copy), "efze")
            mk.wait("sync", "efze", mk.cnt("efze"))
            for ds in range(ND):
                mk.dmaT(ln2tok[:, :, 128 * ds:128 * ds + 128], ystage[:, ds, :], sem="el2t")
            mk.wait("vector", "el2t", mk.cnt("el2t"))
            for sr in range(4):
                for hf2 in range(2):
                    sem = "ebnf" if (sr == 3 and hf2 == 1) else None
                    mk.op("vector", lambda e, sr=sr, hf2=hf2: e.bn_stats(
                        bn6[:, 2 * sr + hf2, 0, :], ln2tok[:, sr, 512 * hf2:512 * hf2 + 512]), sem)
            mk.emit("vector", lambda e: e.drain())
            for sr in range(4):
                sem = "ebnf2" if sr == 3 else None
                mk.op("vector", lambda e, sr=sr: e.bn_aggr(
                    lnmv[:, sr, :], bn6[:, 2 * sr:2 * sr + 2, 0, :]), sem)
            mk.emit("vector", lambda e: e.drain())
            mk.op("vector", lambda e: e.tensor_scalar(
                out=lnmv[:, 0:4, 1], in0=lnmv[:, 0:4, 1], scalar1=EPS, scalar2=None,
                op0=ALU.add), "efnv")
            mk.wait("scalar", "efnv", mk.cnt("efnv"))
            mk.op("scalar", lambda e: e.activation(lnmv[:, 0:4, 1], lnmv[:, 0:4, 1], AF.Sqrt), "efns")
            mk.wait("vector", "efns", mk.cnt("efns"))
            mk.op("vector", lambda e: e.reciprocal(lnmv[:, 0:4, 1], lnmv[:, 0:4, 1]), None)
            mk.emit("vector", lambda e: e.drain())
            for sr in range(4):
                sem = "efout" if sr == 3 else None
                mk.op("vector", lambda e, sr=sr: e.tensor_scalar(
                    out=outstage[:, sr, :], in0=ln2tok[:, sr, :],
                    scalar1=lnmv[:, sr, 0:1], scalar2=lnmv[:, sr, 1:2],
                    op0=ALU.subtract, op1=ALU.mult), sem)
            mk.wait("sync", "efout", mk.cnt("efout"))
            for sr in range(4):
                s_ = 4 * tt + sr
                mk.dma("sync", out_dram[128 * s_:128 * s_ + 128, :], outstage[:, sr, :], sem="st_out")

    emit_epilogue()


    def tok_dump(t):
        return [t.h[0][:].rearrange("p s d -> p (s d)"), t.h[1][:].rearrange("p s d -> p (s d)")]
    mk.dbg("qn_tok", tok_dump(qn_tok), [256, 8 * D], BF16, "q_tok_rdy")
    mk.dbg("delta_tok", tok_dump(delta_tok), [256, 8 * D], BF16, "mdone")
    if delta_stage == 'all' and delta_macros == NT:
        mk.dbg("ll_tok", tok_dump(ll_tok), [256, 8 * D], BF16, "eie")
        mk.dbg("ls_tok", tok_dump(ls_tok), [256, 8 * D], BF16, "eie")
        mk.dbg("stats_tok", stats_tok[:].rearrange("p s h k -> p (s h k)"), [128, NT * H * 16], BF16, "estats")
        mk.dbg("fw_tok", fw_tok[:].rearrange("p s h j -> p (s h j)"), [128, NT * H * H], F32, "efwr")
        mk.dbg("o_tok", tok_dump(o_tok), [256, 8 * D], BF16, "eon")
        mk.dbg("Whats", Whats[:].rearrange("p a n -> p (a n)"), [FF, 4 * D], BF16, "ewhe")
        mk.dbg("mv4", mv4[:].rearrange("p s h j a -> p (s h j a)"), [128, NT * H * 8], BF16, "estats")

    for nm_ in ("S1", "S1t", "IB", "tmpA", "P2"):
        mk.dbg("inv_" + nm_, invm[nm_][:].rearrange("p h n -> p (h n)"), [128, H * 128], BF16, "mdone")
    mk.dbg("uadj", uadj[:], [128, D], BF16, "mdone")
    mk.dbg("GTq", GTq[:].rearrange("p h n -> p (h n)"), [128, H * 128], BF16, "mdone")
    mk.dbg("GTw", GTw[:].rearrange("p h n -> p (h n)"), [128, H * 128], BF16, "mdone")
    mk.dbg("S_sb", S_sb[:].rearrange("p h k d -> p (h k d)"), [128, H * 2 * DH], BF16, "mdone")
    mk.dbg("w_sl", w_sl[:].rearrange("p h k n -> p (h k n)"), [128, H * 2 * 128], BF16, "mdone")
    mk.dbg("kn_tok", tok_dump(kn_tok), [256, 8 * D], BF16, "k_tok_rdy")
    mk.dbg("v_tok", tok_dump(v_tok), [256, 8 * D], BF16, "v_tok_rdy")
    mk.dbg("beta_tok", beta_tok[:].rearrange("p s h -> p (s h)"), [128, NT * H], F32, "beta_tok_rdy")
    mk.dbg("x_feat", [x_feat.h[0][:].rearrange("p s t -> p (s t)"),
                      x_feat.h[1][:].rearrange("p s t -> p (s t)")], [256, 4 * T], BF16, "tr")

    mk.finalize(out_dram, None)
    return mk


def shard_inputs(inputs):
    x = np.ascontiguousarray(inputs["L0_attn_hidden_states"])
    shared = {
        "wq": np.ascontiguousarray(inputs["L0_attn_q_proj_w"]),
        "wk": np.ascontiguousarray(inputs["L0_attn_k_proj_w"]),
        "wv": np.ascontiguousarray(inputs["L0_attn_v_proj_w"]),
        "wb": np.ascontiguousarray(inputs["L0_attn_b_proj_w"]),
        "cq": np.ascontiguousarray(inputs["L0_attn_q_conv_w"]),
        "ck": np.ascontiguousarray(inputs["L0_attn_k_conv_w"]),
        "cv": np.ascontiguousarray(inputs["L0_attn_v_conv_w"]),
        "fw1": np.ascontiguousarray(inputs["L0_attn_fuse_w1"]),
        "fb1": np.ascontiguousarray(inputs["L0_attn_fuse_b1"]),
        "fw2": np.ascontiguousarray(inputs["L0_attn_fuse_w2"]),
        "fb2": np.ascontiguousarray(inputs["L0_attn_fuse_b2"]),
        "ltemp": np.ascontiguousarray(inputs["L0_attn_logit_temp"]),
        "rss": np.ascontiguousarray(inputs["L0_attn_res_scale_short"]),
        "rsl": np.ascontiguousarray(inputs["L0_attn_res_scale_long"]),
        "onw": np.ascontiguousarray(inputs["L0_attn_o_norm_w"]),
        "wo": np.ascontiguousarray(inputs["L0_attn_o_proj_w"]),
        "firs": np.ascontiguousarray(inputs["L0_attn_fir_short_w"]),
        "firl": np.ascontiguousarray(inputs["L0_attn_fir_long_w"]),
        "qw": np.ascontiguousarray(inputs["L0_ffn_qweights"]),
        "qs": np.ascontiguousarray(inputs["L0_ffn_scales"]),
        "qb": np.ascontiguousarray(inputs["L0_ffn_biases"]),
        "la": np.ascontiguousarray(inputs["L0_ffn_lora_a"]),
        "lb": np.ascontiguousarray(inputs["L0_ffn_lora_b"]),
        "lng": np.ascontiguousarray(inputs["L0_ffn_ln_gamma"]),
        "lnb": np.ascontiguousarray(inputs["L0_ffn_ln_beta"]),
    }
    maps = []
    for c in range(NCORES):
        m = dict(shared)
        m["x"] = np.ascontiguousarray(x[c])
        maps.append(m)
    return maps


def kernel(**inputs):
    mk = build_program()
    nc = mk.build()
    maps = shard_inputs(inputs)
    res = run_bass_kernel_spmd(nc, maps, core_ids=list(range(NCORES)))
    return np.stack([r["out"] for r in res.results], axis=0)


# version 04
# speedup vs baseline: 1.0059x; 1.0022x over previous
import os
import math
import numpy as np
import ml_dtypes
import concourse.bass as bass
import concourse.mybir as mybir
from concourse.bass_utils import run_bass_kernel_spmd

F32 = mybir.dt.float32
BF16 = mybir.dt.bfloat16
I32 = mybir.dt.int32
AF = mybir.ActivationFunctionType
ALU = mybir.AluOpType
BF = ml_dtypes.bfloat16

T = 2048
D = 1024
H = 4
DH = 256
NCORES = 8
EPS = 1e-5
NT = T // 128
ND = D // 128
ENGINES = ("sync", "tensor", "vector", "scalar", "gpsimd")


class MK:
    def __init__(self, debug_taps=(), stop_after=None):
        self.nc = bass.Bass("TRN2", target_bir_lowering=False)
        self.debug_taps = set(debug_taps)
        self.stop_after = stop_after
        self.q = {e: [] for e in ENGINES}
        self.sems = {}
        self.semcnt = {}
        self.resolved = {}
        self.dbg_list = []
        self.consts = {}
        self.regions = {}
        self.nview = 0

    KEEP = {"edst0", "edst1", "emix", "eqmm", "ld", "cast", "tr", "dsl", "dwsc",
            "dwsl", "espl", "elsm", "eztr", "eql", "eqdq",
            "mm", "evp", "hist", "silu", "convg", "diag_b", "beta_mm",
            "beta_sig", "mdone", "duev", "dbce", "dsqe", "dgte", "dat", "dgt",
            "dws", "dout", "doutv", "dsr", "dsu", "dgram", "dbc", "dsq",
            "duw", "dcr", "datr", "btr", "beta_tr", "st_out", "eie",
            "datr_ev", "btr_ev", "etrpt", "etrpe"}

    def _resolve(self, name, eng=None, dma=False):
        if name is None or name in self.KEEP:
            return name
        g = None
        if eng is not None:
            g = f"__d_{eng}" if dma else f"__e_{eng}"
        if name in self.resolved:
            assert g is None or g == self.resolved[name], \
                f"sem {name} incremented from {g} and {self.resolved[name]}"
            return self.resolved[name]
        if g is None:
            raise KeyError(f"sem {name} waited before any inc")
        self.resolved[name] = g
        return g

    def _sem(self, name):
        if name not in self.sems:
            self.sems[name] = self.nc.alloc_semaphore(name)
            self.semcnt[name] = 0
        return self.sems[name]

    def inc(self, name, by=1, eng=None, dma=False):
        name = self._resolve(name, eng, dma)
        h = self._sem(name)
        self.semcnt[name] += by
        return h

    def cnt(self, name):
        if name not in self.KEEP and name in self.resolved:
            name = self.resolved[name]
        elif name not in self.KEEP and name not in self.sems:
            return 0
        self._sem(name)
        return self.semcnt[name]

    def wait(self, eng, name, val=None):
        nm = name
        if nm not in self.KEEP and nm in self.resolved:
            nm = self.resolved[nm]
        elif nm not in self.KEEP and nm not in self.sems:
            if val is None or val == 0:
                return
            raise KeyError(f"wait on unknown sem {name}")
        h = self._sem(nm)
        if val is None:
            val = self.semcnt[nm]
        if val > 0:
            self.q[eng].append(lambda e, h=h, val=val: e.wait_ge(h, val))

    REGION_BASE = 229344 - 12 * 16384

    def place(self, reg, shape, dtype, off=0):
        if isinstance(reg, str):
            reg = self.RN.index(reg)
        self.nview += 1
        return self.nc.alloc_sbuf_tensor_at(
            f"t{self.nview}", shape, dtype, offset=self.REGION_BASE + reg * 16384 + off)

    def sb(self, name, shape, dtype):
        return self.nc.alloc_sbuf_tensor(name, shape, dtype)

    def addr(self, t):
        return self.nc.lookup_mloc(t).addr

    def over(self, t, shape, dtype, off=0):
        self.nview += 1
        return self.nc.alloc_sbuf_tensor_at(
            f"t{self.nview}", shape, dtype, offset=self.addr(t) + off)

    def ps(self, name, shape, dtype=F32):
        return self.nc.alloc_psum_tensor(name, shape, dtype)

    def const(self, name, arr):
        if name not in self.consts:
            self.consts[name] = self.nc.inline_tensor(np.ascontiguousarray(arr), name).ap()
        return self.consts[name]

    def emit(self, eng, fn):
        self.q[eng].append(fn)

    def op(self, eng, fn, sem=None, by=1):
        h = self.inc(sem, by, eng=eng) if sem else None
        def g(e, fn=fn, h=h, by=by):
            ins = fn(e)
            if h is not None:
                ins.then_inc(h, by)
        self.q[eng].append(g)

    def dma(self, eng, dst, src, sem="ld"):
        h = self.inc(sem, 16, eng=eng, dma=True)
        self.q[eng].append(lambda e, h=h, dst=dst, src=src: e.dma_start(dst, src).then_inc(h, 16))

    def dmaT(self, dst, src, sem="tr"):
        h = self.inc(sem, 16, eng="sync", dma=True)
        self.q["sync"].append(lambda e, h=h, dst=dst, src=src: e.dma_start_transpose(dst, src).then_inc(h, 16))

    def ldbg(self, tap, srcs, shape, dtype):
        if tap not in self.debug_taps:
            return
        d = self.nc.dram_tensor("dbg_" + tap, shape, dtype, kind="ExternalOutput").ap()
        if not isinstance(srcs, (list, tuple)):
            srcs = [srcs]
        nrow = shape[0] // len(srcs)
        for i, src in enumerate(srcs):
            self.dma("sync", d[i * nrow:(i + 1) * nrow], src, sem="st_out")

    def dbg(self, tap, src_aps, shape, dtype, ready_sem):
        if tap not in self.debug_taps:
            return
        d = self.nc.dram_tensor("dbg_" + tap, shape, dtype, kind="ExternalOutput").ap()
        self.dbg_list.append((tap, d, src_aps, ready_sem, self.semcnt.get(ready_sem, 0)))

    def finalize(self, out_dram, out_src, ready_sems=()):
        for tap, d, srcs, rsem, rval in self.dbg_list:
            if rsem is not None and rval > 0:
                h = self.sems[rsem]
                self.q["sync"].append(lambda e, h=h, v=rval: e.wait_ge(h, v))
            if not isinstance(srcs, (list, tuple)):
                srcs = [srcs]
            nrow = d.shape[0] // len(srcs)
            for i, src in enumerate(srcs):
                h2 = self.inc("st_out", 16)
                self.q["sync"].append(lambda e, d=d, i=i, nrow=nrow, src=src, h2=h2:
                                      e.dma_start(d[i * nrow:(i + 1) * nrow], src).then_inc(h2, 16))
        if out_src is not None:
            for s in ready_sems:
                self.wait("sync", s)
            h2 = self.inc("st_out", 16)
            self.q["sync"].append(lambda e, h2=h2, out_dram=out_dram, out_src=out_src:
                                  e.dma_start(out_dram, out_src).then_inc(h2, 16))
        v = self.cnt("st_out")
        if v > 0:
            h = self.sems["st_out"]
            self.q["sync"].append(lambda e, h=h, v=v: e.wait_ge(h, v))

    def build(self):
        nc = self.nc
        assert nc.sbuf_base <= self.REGION_BASE, \
            f"normal sbuf allocs ({nc.sbuf_base}) overlap regions ({self.REGION_BASE})"
        with nc.Block() as blk:
            @blk.sync
            def _(e):
                for f in self.q["sync"]:
                    f(e)
            @blk.tensor
            def _(e):
                for f in self.q["tensor"]:
                    f(e)
            @blk.vector
            def _(e):
                for f in self.q["vector"]:
                    f(e)
            @blk.scalar
            def _(e):
                for f in self.q["scalar"]:
                    f(e)
            @blk.gpsimd
            def _(e):
                for f in self.q["gpsimd"]:
                    f(e)
        return nc


class FeatT:
    def __init__(self, mk, r0, r1):
        self.h = [mk.place(r0, [128, 4, T], BF16), mk.place(r1, [128, 4, T], BF16)]

    def ks(self, ks, sl=slice(None)):
        return self.h[ks // 4][:, ks % 4, sl]


class TokT:
    def __init__(self, mk, r0, r1, dtype=BF16, C=D):
        n = 16384 // (mybir.dt.size(dtype) * C)
        self.n = n
        self.h = [mk.place(r0, [128, n, C], dtype), mk.place(r1, [128, n, C], dtype)]

    def s(self, s, sl=slice(None)):
        return self.h[s // self.n][:, s % self.n, sl]


def dmaT_tok_from_feat(mk, tok, feat):
    for ds in range(ND):
        for half in range(2):
            dst = tok.h[half][:, :, 128 * ds:128 * ds + 128]
            mk.dmaT(dst, feat.ks(ds, slice(1024 * half, 1024 * half + 1024)))


def dmaT_feat_from_tok(mk, feat, tok):
    for s in range(NT):
        for half in range(2):
            dst = feat.h[half][:, :, 128 * s:128 * s + 128]
            mk.dmaT(dst, tok.s(s, slice(512 * half, 512 * half + 512)))


def build_program(debug_taps=(), stop_after=None, delta_macros=NT, delta_stage='all'):
    mk = MK(debug_taps, stop_after)
    nc = mk.nc

    x_in = nc.dram_tensor("x", [T, D], F32, kind="ExternalInput").ap()
    wq_in = nc.dram_tensor("wq", [D, D], F32, kind="ExternalInput").ap()
    wk_in = nc.dram_tensor("wk", [D, D], F32, kind="ExternalInput").ap()
    wv_in = nc.dram_tensor("wv", [D, D], F32, kind="ExternalInput").ap()
    wb_in = nc.dram_tensor("wb", [H, D], F32, kind="ExternalInput").ap()
    cq_in = nc.dram_tensor("cq", [D, 4], F32, kind="ExternalInput").ap()
    ck_in = nc.dram_tensor("ck", [D, 4], F32, kind="ExternalInput").ap()
    cv_in = nc.dram_tensor("cv", [D, 4], F32, kind="ExternalInput").ap()
    out_dram = nc.dram_tensor("out", [T, D], F32, kind="ExternalOutput").ap()
    fw1_in = nc.dram_tensor("fw1", [D, D + 16], F32, kind="ExternalInput").ap()
    fb1_in = nc.dram_tensor("fb1", [D], F32, kind="ExternalInput").ap()
    fw2_in = nc.dram_tensor("fw2", [H, D], F32, kind="ExternalInput").ap()
    fb2_in = nc.dram_tensor("fb2", [H], F32, kind="ExternalInput").ap()
    ltemp_in = nc.dram_tensor("ltemp", [1], F32, kind="ExternalInput").ap()
    rss_in = nc.dram_tensor("rss", [1], F32, kind="ExternalInput").ap()
    rsl_in = nc.dram_tensor("rsl", [1], F32, kind="ExternalInput").ap()
    onw_in = nc.dram_tensor("onw", [DH], F32, kind="ExternalInput").ap()
    wo_in = nc.dram_tensor("wo", [D, D], F32, kind="ExternalInput").ap()
    firs_in = nc.dram_tensor("firs", [H, DH, 5], F32, kind="ExternalInput").ap()
    firl_in = nc.dram_tensor("firl", [H, DH, 64], F32, kind="ExternalInput").ap()
    qw_in = nc.dram_tensor("qw", [6, D, D], I32, kind="ExternalInput").ap()
    qs_in = nc.dram_tensor("qs", [6, D, 64], F32, kind="ExternalInput").ap()
    qb_in = nc.dram_tensor("qb", [6, D], F32, kind="ExternalInput").ap()
    la_in = nc.dram_tensor("la", [6, 32, D], F32, kind="ExternalInput").ap()
    lb_in = nc.dram_tensor("lb", [6, D, 32], F32, kind="ExternalInput").ap()
    lng_in = nc.dram_tensor("lng", [3, D], F32, kind="ExternalInput").ap()
    lnb_in = nc.dram_tensor("lnb", [3, D], F32, kind="ExternalInput").ap()
    dscr = nc.dram_tensor("dscr", [2, 128, 8, D], BF16, kind="Internal").ap()

    mk.RN = ["RA0", "RA1", "RB0", "RB1", "RC0", "RC1", "RD0", "RD1", "RE0", "RE1", "RF0", "RF1"]
    ident = mk.sb("ident", [128, 128], BF16)
    cw = mk.sb("cw", [128, ND, 3, 4], F32)
    bst = mk.sb("bst", [H, D], BF16)
    wbT = mk.sb("wbT", [128, ND, H], BF16)
    beta_tok = mk.sb("beta_tok", [128, NT, H], F32)
    rqrk = mk.sb("rqrk", [128, NT, 2 * H], F32)
    sqacc = mk.sb("sqacc", [128, NT, H], F32)
    sqscr = mk.sb("sqscr", [128, 256], BF16)
    pre = mk.sb("pre", [128, 2, 4 + 512 + 4], BF16)
    diag = mk.sb("diag", [128, 2, 4, 128], BF16)

    idc = mk.const("identc", np.eye(128, dtype=np.float32).astype(BF))

    x_tok = TokT(mk, "RA0", "RA1")
    x_feat = FeatT(mk, "RB0", "RB1")
    featst = FeatT(mk, "RC0", "RC1")
    v_tok = TokT(mk, "RD0", "RD1")
    qn_tok = TokT(mk, "RA0", "RA1")
    kn_tok = TokT(mk, "RF0", "RF1")
    wT = [mk.place("RE0", [128, ND, D], BF16), mk.place("RE1", [128, ND, D], BF16)]
    stg32 = mk.place("RD0", [128, 4, 1024], F32)
    bst32 = mk.place("RD1", [H, 1024], F32)
    wstg = mk.place("RF0", [128, ND, D], BF16)
    beta_f = mk.place("RC0", [H, T], F32)
    brow16 = mk.place("RC0", [H, T], BF16, off=8192)

    pp = [mk.ps(f"pp{i}", [128, 512], F32) for i in range(2)]
    pc = [mk.ps(f"pc{i}", [128, 512], F32) for i in range(2)]
    pm = mk.ps("pm", [128, 512], F32)
    pt_bf = mk.ps("pt_bf", [128, 512], BF16)

    mk.dma("sync", cw[:, :, 0, :], cq_in.rearrange("(s p) j -> p s j", p=128))
    mk.dma("sync", cw[:, :, 1, :], ck_in.rearrange("(s p) j -> p s j", p=128))
    mk.dma("sync", cw[:, :, 2, :], cv_in.rearrange("(s p) j -> p s j", p=128))
    mk.dma("sync", ident[:], idc)
    LD_ALL = mk.cnt("ld")

    for qt in range(4):
        if qt > 0:
            mk.wait("sync", "cast", qt)
        mk.dma("sync", stg32[:], x_in[512 * qt:512 * qt + 512].rearrange(
            "(s p) d -> p s d", p=128), sem="lds")
        mk.wait("vector", "lds", mk.cnt("lds"))
        mk.op("vector", lambda e, qt=qt: e.tensor_copy(
            x_tok.h[qt // 2][:, 4 * (qt % 2):4 * (qt % 2) + 4, :], stg32[:]), "cast")

    def load_weight(win, slot, gate=None):
        if gate is not None:
            mk.wait("sync", *gate)
        for hf in range(2):
            mk.wait("sync", "cast")
            mk.dma("sync", stg32[:], win.rearrange("(s p) d -> p s d", p=128)[:, 4 * hf:4 * hf + 4, :],
                   sem="lds")
            mk.wait("vector", "lds", mk.cnt("lds"))
            if hf == 0:
                mk.wait("vector", "tr")
            mk.op("vector", lambda e, hf=hf: e.tensor_copy(
                wstg[:, 4 * hf:4 * hf + 4, :], stg32[:]), "cast")
        mk.wait("sync", "cast")
        for os_ in range(ND):
            mk.dmaT(wT[slot][:, :, 128 * os_:128 * os_ + 128], wstg[:, os_, :])
        return mk.cnt("tr")

    mk.wait("sync", "cast", 4)
    dmaT_feat_from_tok(mk, x_feat, x_tok)
    TR_X = mk.cnt("tr")

    TR_WQ = load_weight(wq_in, 0)
    TR_WK = load_weight(wk_in, 1)
    mk.wait("sync", "cast")
    mk.dma("sync", bst32[:], wb_in, sem="lds")
    mk.wait("vector", "lds", mk.cnt("lds"))
    mk.op("vector", lambda e: e.tensor_copy(bst[:], bst32[:]), "cast")
    CAST_BST = mk.cnt("cast")

    mk.wait("tensor", "ld", LD_ALL)
    mk.wait("tensor", "cast", CAST_BST)
    mk.wait("tensor", "tr", TR_X)
    for ds in range(ND):
        sem = "btr" if ds == ND - 1 else None
        mk.op("tensor", lambda e, ds=ds: e.transpose(
            pt_bf[:, 4 * ds:4 * ds + 4], bst[:, 128 * ds:128 * ds + 128], ident[0:H, 0:H]), sem)
    mk.wait("vector", "btr")
    mk.op("vector", lambda e: e.tensor_copy(wbT[:].rearrange("p s h -> p (s h)"), pt_bf[:, 0:32]), "btr_ev")
    mk.wait("tensor", "btr_ev")
    for tt in range(4):
        if tt > 0:
            mk.wait("tensor", "beta_sig", tt)
        hb = mk.inc("beta_mm")
        for ks in range(ND):
            def f(e, tt=tt, ks=ks, hb=hb, last=(ks == ND - 1)):
                ins = e.matmul(pm[0:4, :], wbT[:, ks, :], x_feat.ks(ks, slice(512 * tt, 512 * tt + 512)),
                               start=(ks == 0), stop=(ks == ND - 1))
                if last:
                    ins.then_inc(hb, 1)
            mk.emit("tensor", f)
        mk.wait("scalar", "beta_mm", tt + 1)
        mk.op("scalar", lambda e, tt=tt: e.activation(
            beta_f[:, 512 * tt:512 * tt + 512], pm[0:4, :], AF.Sigmoid), "beta_sig")
    mk.wait("vector", "beta_sig", 4)
    mk.op("vector", lambda e: e.tensor_copy(brow16[:], beta_f[:]), "beta_cast")
    mk.wait("tensor", "beta_cast")
    for ts_ in range(NT):
        sem = "beta_tr" if ts_ == NT - 1 else None
        mk.op("tensor", lambda e, ts_=ts_: e.transpose(
            pt_bf[:, 4 * ts_:4 * ts_ + 4], brow16[:, 128 * ts_:128 * ts_ + 128], ident[0:H, 0:H]), sem)
    mk.wait("vector", "beta_tr")
    mk.op("vector", lambda e: e.tensor_copy(
        beta_tok[:].rearrange("p s h -> p (s h)"), pt_bf[:, 0:64]), "beta_tok_rdy")

    mk.wait("vector", "ld", LD_ALL)

    G = [0]
    DG = [0]

    def emit_proj(slot, cwi, tr_needed, featst_free=None):
        mk.wait("tensor", "tr", tr_needed)
        if featst_free is not None:
            mk.wait("scalar", "tr", featst_free)
        for os_ in range(ND):
            d = DG[0]
            if d >= 2:
                mk.wait("vector", "convg", 4 * (d - 1))
            for j in range(4):
                sem = "diag_b" if j == 3 else None
                def f(e, par=d % 2, j=j, os_=os_, cwi=cwi):
                    return e.tensor_scalar(out=diag[:, par, j, :], in0=ident[:],
                                           scalar1=cw[:, os_, cwi, 3 - j:4 - j], scalar2=None,
                                           op0=ALU.mult)
                mk.op("vector", f, sem)
            dtarget = mk.cnt("diag_b")
            DG[0] += 1
            for tt in range(4):
                g = G[0]
                bank = pp[g % 2]
                cbank = pc[g % 2]
                par = g % 2
                if g >= 2:
                    mk.wait("tensor", "evp", g - 1)
                hmm_ = mk.inc("mm")
                for ks in range(ND):
                    def f(e, bank=bank, ks=ks, os_=os_, tt=tt, slot=slot, hmm_=hmm_, last=(ks == ND - 1)):
                        ins = e.matmul(bank[:], wT[slot][:, ks, 128 * os_:128 * os_ + 128],
                                       x_feat.ks(ks, slice(512 * tt, 512 * tt + 512)),
                                       start=(ks == 0), stop=(ks == ND - 1))
                        if last:
                            ins.then_inc(hmm_, 1)
                    mk.emit("tensor", f)
                mk.wait("scalar", "mm", g + 1)
                if g >= 1:
                    mk.wait("scalar", "hist", g)
                mk.op("scalar", lambda e, bank=bank, par=par: e.activation(
                    pre[:, par, 4:516], bank[:], AF.Copy), "evp")
                if tt == 0:
                    if g >= 2:
                        mk.wait("vector", "convg", g - 1)
                    mk.op("vector", lambda e, par=par: e.memset(pre[:, par, 0:4], 0.0), "hist")
                else:
                    mk.wait("vector", "evp", g)
                    mk.op("vector", lambda e, par=par: e.tensor_copy(
                        pre[:, par, 0:4], pre[:, 1 - par, 512:516]), "hist")
                mk.wait("tensor", "evp", g + 1)
                mk.wait("tensor", "hist", g + 1)
                if g >= 2:
                    mk.wait("tensor", "silu", g - 1)
                mk.wait("tensor", "diag_b", dtarget)
                hcv = mk.inc("convg")
                for j in range(4):
                    def f(e, cbank=cbank, par2=d % 2, j=j, par=par, hcv=hcv, last=(j == 3)):
                        ins = e.matmul(cbank[:], diag[:, par2, j, :],
                                       pre[:, par, 4 - j:516 - j],
                                       start=(j == 0), stop=(j == 3))
                        if last:
                            ins.then_inc(hcv, 1)
                    mk.emit("tensor", f)
                mk.wait("scalar", "convg", g + 1)
                mk.op("scalar", lambda e, cbank=cbank, os_=os_, tt=tt: e.activation(
                    featst.ks(os_, slice(512 * tt, 512 * tt + 512)), cbank[:], AF.Silu), "silu")
                G[0] += 1

    def emit_tok(nm, tok_dst, norm_col):
        mk.wait("sync", "silu")
        dmaT_tok_from_feat(mk, tok_dst, featst)
        trv = mk.cnt("tr")
        if norm_col is None:
            mk.wait("vector", "tr", trv)
            mk.op("vector", lambda e: e.memset(sqscr[0:1, 0:1], 0.0), nm + "_tok_rdy")
            return
        mk.wait("scalar", "tr", trv)
        for s in range(NT):
            for hh_ in range(H):
                sem = nm + "_sq" if (s == NT - 1 and hh_ == H - 1) else None
                def f(e, s=s, hh_=hh_, tok_dst=tok_dst):
                    return e.activation(sqscr[:], tok_dst.s(s, slice(DH * hh_, DH * hh_ + DH)), AF.Square,
                                        accum_out=sqacc[:, s, hh_:hh_ + 1])
                mk.op("scalar", f, sem)
        mk.wait("scalar", nm + "_sq")
        mk.emit("scalar", lambda e: e.drain())
        mk.op("scalar", lambda e: e.activation(sqacc[:], sqacc[:], AF.Sqrt), nm + "_sqrt")
        mk.wait("vector", nm + "_sqrt")
        mk.op("vector", lambda e, norm_col=norm_col: e.reciprocal(
            rqrk[:, :, norm_col * H:(norm_col + 1) * H], sqacc[:]), nm + "_rs")
        mk.wait("vector", nm + "_rs")
        for s in range(NT):
            for hh_ in range(H):
                sem = nm + "_tok_rdy" if (s == NT - 1 and hh_ == H - 1) else None
                def f(e, s=s, hh_=hh_, tok_dst=tok_dst, norm_col=norm_col):
                    return e.tensor_scalar(out=tok_dst.s(s, slice(DH * hh_, DH * hh_ + DH)),
                                           in0=tok_dst.s(s, slice(DH * hh_, DH * hh_ + DH)),
                                           scalar1=rqrk[:, s, norm_col * H + hh_:norm_col * H + hh_ + 1],
                                           scalar2=None, op0=ALU.mult)
                mk.op("vector", f, sem)

    emit_proj(0, 0, TR_WQ)
    emit_tok("q", qn_tok, 0)
    TR_QTOK = mk.cnt("tr")
    emit_proj(1, 1, TR_WK, featst_free=TR_QTOK)
    TR_WV = load_weight(wv_in, 0, gate=("silu", 32))
    emit_tok("k", kn_tok, 1)
    TR_KTOK = mk.cnt("tr")
    emit_proj(0, 2, TR_WV, featst_free=TR_KTOK)
    emit_tok("v", v_tok, None)

    delta_tok = TokT(mk, "RC0", "RC1")
    pb6 = mk.ps("pb6", [128, 512], F32)
    pb7 = mk.ps("pb7", [128, 512], F32)
    pt2 = pt_bf
    S_sb = mk.sb("S_sb", [128, H, 2, DH], BF16)

    qn_sl = mk.place("RE0", [128, 8, 128], BF16, off=0)
    kn_sl = mk.place("RE0", [128, 8, 128], BF16, off=2048)
    vb_sl = mk.place("RE0", [128, D], BF16, off=4096)
    kbt_sl = mk.place("RE0", [128, D], BF16, off=6144)
    w_sc = mk.place("RE0", [128, H, DH], BF16, off=8192)
    w_sl = mk.place("RE0", [128, H, 2, 128], BF16, off=10240)
    uadj = mk.place("RE0", [128, D], BF16, off=12288)
    GTw = mk.place("RE0", [128, H, 128], BF16, off=14336)
    GTq = mk.place("RE0", [128, H, 128], BF16, off=15360)
    invm = {}
    for i, nm_ in enumerate(["S1", "S1t", "P2", "P2t", "P4", "P4t", "P8", "P8t",
                             "P16", "P16t", "Brl0", "Brl1", "IB", "tmpA"]):
        invm[nm_] = mk.place("RE1", [128, H, 128], BF16, off=1024 * i)

    _blk = np.zeros((128, 128), np.float32)
    for j in range(4):
        _blk[32 * j:32 * j + 32, 32 * j:32 * j + 32] = np.tril(np.ones((32, 32)), -1)
    maskA_c = mk.const("maskA_c", (-_blk).astype(BF))
    _gt = np.zeros((128, 128), np.float32)
    for l in range(4):
        for j in range(4):
            if l < j:
                _gt[32 * l:32 * l + 32, 32 * j:32 * j + 32] = 1.0
            elif l == j:
                _gt[32 * l:32 * l + 32, 32 * j:32 * j + 32] = np.triu(np.ones((32, 32)))
    maskGT_c = mk.const("maskGT_c", _gt.astype(BF))
    maskA = mk.sb("maskA", [128, 128], BF16)
    maskGT = mk.sb("maskGT", [128, 128], BF16)
    mk.dma("gpsimd", maskA[:], maskA_c, sem="ldm")
    mk.dma("gpsimd", maskGT[:], maskGT_c, sem="ldm")

    UA = [pp[0], pp[1]]
    OUT = [pc[0], pc[1]]
    ROT = [pm, pb6]
    DS_BANKS = [pm, pb6, pb7, pp[0]]

    STAGES = ["nil", "dsl", "ts", "mm", "stt", "gram", "inv", "uw", "gtw", "gtq", "gt", "chain", "all"]
    def stage_ge(a):
        return STAGES.index(delta_stage) >= STAGES.index(a)

    def emit_delta():
        mk.op("vector", lambda e: e.memset(S_sb[:].rearrange("p h k d -> p (h k d)"), 0.0), "s_init")
        mk.wait("vector", "ldm")
        for eng in ("tensor", "sync", "vector", "scalar"):
            mk.wait(eng, "q_tok_rdy")
            mk.wait(eng, "k_tok_rdy")
            mk.wait(eng, "v_tok_rdy")
            mk.wait(eng, "beta_tok_rdy")
        mk.wait("tensor", "s_init")
        mk.wait("tensor", "ldm")

        for m in range(delta_macros):
            if m > 0:
                mk.wait("sync", "mdone", m)
            if not stage_ge("dsl"):
                mk.op("vector", lambda e: e.memset(sqscr[0:1, 0:1], 0.0), "mdone")
                continue
            mk.dmaT(qn_sl[:], qn_tok.s(m), sem="dsl")
            mk.dmaT(kn_sl[:], kn_tok.s(m), sem="dsl")
            DSL = mk.cnt("dsl")
            if not stage_ge("ts"):
                mk.wait("vector", "dsl", DSL)
                mk.op("vector", lambda e: e.memset(sqscr[0:1, 0:1], 0.0), "mdone")
                continue
            if m > 0:
                mk.wait("vector", "mdone", m)
            for hh_ in range(H):
                mk.op("vector", lambda e, hh_=hh_, m=m: e.tensor_scalar(
                    out=vb_sl[:, DH * hh_:DH * hh_ + DH], in0=v_tok.s(m, slice(DH * hh_, DH * hh_ + DH)),
                    scalar1=beta_tok[:, m, hh_:hh_ + 1], scalar2=None, op0=ALU.mult),
                    "dts" if hh_ == H - 1 else None)
            for hh_ in range(H):
                mk.op("vector", lambda e, hh_=hh_, m=m: e.tensor_scalar(
                    out=kbt_sl[:, DH * hh_:DH * hh_ + DH], in0=kn_tok.s(m, slice(DH * hh_, DH * hh_ + DH)),
                    scalar1=beta_tok[:, m, hh_:hh_ + 1], scalar2=None, op0=ALU.mult),
                    "dts2" if hh_ == H - 1 else None)
            if not stage_ge("mm"):
                mk.wait("vector", "dts")
                mk.wait("vector", "dts2")
                mk.wait("vector", "dsl", DSL)
                mk.op("vector", lambda e: e.memset(sqscr[0:1, 0:1], 0.0), "mdone")
                continue
            mk.wait("tensor", "dsl", DSL)
            hgr = mk.inc("dgram")
            for hh_ in range(H):
                for ks in range(2):
                    def f(e, hh_=hh_, ks=ks, hgr=hgr, last=(hh_ == H - 1 and ks == 1)):
                        ins = e.matmul(ROT[0][:, 128 * hh_:128 * hh_ + 128],
                                       kn_sl[:, 2 * hh_ + ks, :], kn_sl[:, 2 * hh_ + ks, :],
                                       start=(ks == 0), stop=(ks == 1))
                        if last:
                            ins.then_inc(hgr, 1)
                    mk.emit("tensor", f)
            if not stage_ge("stt"):
                mk.wait("vector", "dgram")
                mk.op("vector", lambda e: e.memset(sqscr[0:1, 0:1], 0.0), "mdone")
                continue
            mk.wait("vector", "dgram")
            for hh_ in range(H):
                mk.op("vector", lambda e, hh_=hh_, m=m: e.scalar_tensor_tensor(
                    out=invm["S1"][:, hh_, :], in0=ROT[0][:, 128 * hh_:128 * hh_ + 128],
                    scalar=beta_tok[:, m, hh_:hh_ + 1], in1=maskA[:],
                    op0=ALU.mult, op1=ALU.mult), None)
            for hh_ in range(H):
                mk.op("vector", lambda e, hh_=hh_, m=m: e.tensor_scalar(
                    out=invm["tmpA"][:, hh_, :], in0=ROT[0][:, 128 * hh_:128 * hh_ + 128],
                    scalar1=beta_tok[:, m, hh_:hh_ + 1], scalar2=None, op0=ALU.mult),
                    "da0")
            mk.wait("tensor", "da0")
            htr = mk.inc("datr")
            for hh_ in range(H):
                def f(e, hh_=hh_, htr=htr, last=(hh_ == H - 1)):
                    ins = e.transpose(pt2[:, 128 * hh_:128 * hh_ + 128], invm["S1"][:, hh_, :], ident[:])
                    if last:
                        ins.then_inc(htr, 1)
                mk.emit("tensor", f)
            mk.wait("vector", "datr")
            mk.op("vector", lambda e: e.tensor_copy(
                invm["S1t"][:].rearrange("p h n -> p (h n)"), pt2[:]), "datr_ev")
            if not stage_ge("inv"):
                mk.wait("vector", "datr_ev")
                mk.op("vector", lambda e: e.memset(sqscr[0:1, 0:1], 0.0), "mdone")
                continue
            sq_plan = [("P2", "S1t", "S1"), ("P2t", "S1", "S1t"),
                       ("P4", "P2t", "P2"), ("P4t", "P2", "P2t"),
                       ("P8", "P4t", "P4"), ("P8t", "P4", "P4t"),
                       ("P16", "P8t", "P8"), ("P16t", "P8", "P8t")]
            mk.wait("tensor", "datr_ev")
            for si, (dstn, ln, rn) in enumerate(sq_plan):
                rb = ROT[si % 2]
                if si >= 2:
                    mk.wait("tensor", "dsqe", mk.cnt("dsqe") - 1)
                hsq = mk.inc("dsq")
                for hh_ in range(H):
                    def f(e, hh_=hh_, rb=rb, ln=ln, rn=rn, hsq=hsq, last=(hh_ == H - 1)):
                        ins = e.matmul(rb[:, 128 * hh_:128 * hh_ + 128],
                                       invm[ln][:, hh_, :], invm[rn][:, hh_, :],
                                       start=True, stop=True)
                        if last:
                            ins.then_inc(hsq, 1)
                    mk.emit("tensor", f)
                mk.wait("vector", "dsq")
                mk.op("vector", lambda e, rb=rb, dstn=dstn: e.tensor_copy(
                    invm[dstn][:].rearrange("p h n -> p (h n)"), rb[:]), "dsqe")
            bcur = "S1t"
            for r, (pn, ptn) in enumerate([("P2", "P2t"), ("P4", "P4t"), ("P8", "P8t"), ("P16", "P16t")]):
                rb = ROT[r % 2]
                mk.wait("tensor", "dsqe")
                if r >= 1:
                    mk.wait("tensor", "dbce", mk.cnt("dbce"))
                hbc = mk.inc("dbc")
                for hh_ in range(H):
                    def f(e, hh_=hh_, rb=rb, pn=pn, ptn=ptn, bcur=bcur, hbc=hbc, last=(hh_ == H - 1)):
                        e.matmul(rb[:, 128 * hh_:128 * hh_ + 128], invm[pn][:, hh_, :],
                                 invm[bcur][:, hh_, :], start=True, stop=False)
                        e.matmul(rb[:, 128 * hh_:128 * hh_ + 128], ident[:],
                                 invm[bcur][:, hh_, :], start=False, stop=False)
                        ins = e.matmul(rb[:, 128 * hh_:128 * hh_ + 128], ident[:],
                                       invm[ptn][:, hh_, :], start=False, stop=True)
                        if last:
                            ins.then_inc(hbc, 1)
                    mk.emit("tensor", f)
                nxt = "Brl%d" % (r % 2) if r < 3 else "IB"
                mk.wait("vector", "dbc")
                if r < 3:
                    mk.op("vector", lambda e, rb=rb, nxt=nxt: e.tensor_copy(
                        invm[nxt][:].rearrange("p h n -> p (h n)"), rb[:]), "dbce")
                else:
                    for hh_ in range(H):
                        mk.op("vector", lambda e, rb=rb, hh_=hh_: e.tensor_tensor(
                            invm["IB"][:, hh_, :], rb[:, 128 * hh_:128 * hh_ + 128],
                            ident[:], ALU.add), "dbce" if hh_ == H - 1 else None)
                bcur = nxt
            if not stage_ge("uw"):
                mk.wait("vector", "dbce")
                mk.op("vector", lambda e: e.memset(sqscr[0:1, 0:1], 0.0), "mdone")
                continue
            mk.wait("tensor", "dbce")
            mk.wait("tensor", "dts")
            mk.wait("tensor", "dts2")
            huw = mk.inc("duw")
            for hh_ in range(H):
                def f(e, hh_=hh_):
                    e.matmul(UA[hh_ // 2][:, 256 * (hh_ % 2):256 * (hh_ % 2) + 256],
                             invm["IB"][:, hh_, :], vb_sl[:, DH * hh_:DH * hh_ + DH],
                             start=(hh_ % 2 == 0), stop=False)
                mk.emit("tensor", f)
            for hh_ in range(H):
                def f(e, hh_=hh_, huw=huw, last=(hh_ == H - 1)):
                    ins = e.matmul(ROT[hh_ // 2][:, 256 * (hh_ % 2):256 * (hh_ % 2) + 256],
                                   invm["IB"][:, hh_, :], kbt_sl[:, DH * hh_:DH * hh_ + DH],
                                   start=(hh_ % 2 == 0), stop=(hh_ % 2 == 1))
                    if last:
                        ins.then_inc(huw, 1)
                mk.emit("tensor", f)
            mk.wait("vector", "duw")
            for half in range(2):
                mk.op("vector", lambda e, half=half: e.tensor_scalar(
                    out=w_sc[:, 2 * half:2 * half + 2, :].rearrange("p a b -> p (a b)"),
                    in0=ROT[half][:], scalar1=-1.0, scalar2=None, op0=ALU.mult),
                    "dwsc" if half == 1 else None)
            WSC = mk.cnt("dwsc")
            mk.wait("sync", "dwsc", WSC)
            for hh_ in range(H):
                mk.dmaT(w_sl[:, hh_, :, :], w_sc[:, hh_, :], sem="dwsl")
            DWSL = mk.cnt("dwsl")
            if not stage_ge("gtw"):
                mk.wait("vector", "dwsc", WSC)
                mk.op("vector", lambda e: e.memset(sqscr[0:1, 0:1], 0.0), "mdone")
                continue
            mk.wait("tensor", "dwsc", WSC)
            if m > 0:
                mk.wait("tensor", "dgte", mk.cnt("dgte"))
            hgt = mk.inc("dgt")
            if stage_ge("gtw"):
                for hh_ in range(H):
                    def f(e, hh_=hh_):
                        e.matmul(pb7[:, 128 * hh_:128 * hh_ + 128], invm["tmpA"][:, hh_, :],
                                 invm["IB"][:, hh_, :], start=True, stop=True)
                    mk.emit("tensor", f)
            if stage_ge("gtq"):
                for hh_ in range(H):
                    for ks in range(2):
                        def f(e, hh_=hh_, ks=ks, hgt=hgt, last=(hh_ == H - 1 and ks == 1)):
                            ins = e.matmul(ROT[0][:, 128 * hh_:128 * hh_ + 128],
                                           kn_sl[:, 2 * hh_ + ks, :], qn_sl[:, 2 * hh_ + ks, :],
                                           start=(ks == 0), stop=(ks == 1))
                            if last:
                                ins.then_inc(hgt, 1)
                        mk.emit("tensor", f)
            else:
                def f(e, hgt=hgt):
                    e.matmul(ROT[0][:, 0:128], invm["tmpA"][:, 0, :],
                             invm["IB"][:, 0, :], start=True, stop=True).then_inc(hgt, 1)
                mk.emit("tensor", f)
            mk.wait("vector", "dgt")
            if stage_ge("gtw"):
                mk.op("vector", lambda e: e.tensor_scalar(
                    out=GTw[:].rearrange("p h n -> p (h n)"), in0=pb7[:],
                    scalar1=-1.0, scalar2=None, op0=ALU.mult), None)
            if stage_ge("gtq"):
                for hh_ in range(H):
                    mk.op("vector", lambda e, hh_=hh_: e.tensor_tensor(
                        GTq[:, hh_, :], ROT[0][:, 128 * hh_:128 * hh_ + 128], maskGT[:], ALU.mult),
                        "dgte" if hh_ == H - 1 else None)
            else:
                mk.op("vector", lambda e: e.memset(GTq[:, 0, 0:1], 0.0), "dgte")
            if not stage_ge("chain"):
                mk.wait("vector", "dgte")
                mk.op("vector", lambda e: e.memset(sqscr[0:1, 0:1], 0.0), "mdone")
                continue
            mk.wait("tensor", "dwsl", DWSL)
            mk.wait("tensor", "dgte")
            if m > 0:
                mk.wait("tensor", "dout")
                mk.wait("tensor", "doutv")
            hws = mk.inc("dws")
            for hh_ in range(H):
                for ks in range(2):
                    def f(e, hh_=hh_, ks=ks, hws=hws, last=(hh_ == H - 1 and ks == 1)):
                        ins = e.matmul(UA[hh_ // 2][:, 256 * (hh_ % 2):256 * (hh_ % 2) + 256],
                                       w_sl[:, hh_, ks, :], S_sb[:, hh_, ks, :],
                                       start=False, stop=(ks == 1))
                        if last:
                            ins.then_inc(hws, 1)
                    mk.emit("tensor", f)
            for hh_ in range(H):
                for ks in range(2):
                    def f(e, hh_=hh_, ks=ks):
                        e.matmul(OUT[hh_ // 2][:, 256 * (hh_ % 2):256 * (hh_ % 2) + 256],
                                 qn_sl[:, 2 * hh_ + ks, :], S_sb[:, hh_, ks, :],
                                 start=(ks == 0 and hh_ % 2 == 0), stop=(ks == 1))
                    mk.emit("tensor", f)
            for j in range(4):
                if j > 0:
                    mk.wait("tensor", "duev", mk.cnt("duev"))
                    hcr = mk.inc("dcr")
                    for hh_ in range(H):
                        def f(e, j=j, hh_=hh_, hcr=hcr, last=(hh_ == H - 1)):
                            ins = e.matmul(
                                UA[hh_ // 2][32 * j:32 * j + 32, 256 * (hh_ % 2):256 * (hh_ % 2) + 256],
                                GTw[0:32 * j, hh_, 32 * j:32 * j + 32],
                                uadj[0:32 * j, DH * hh_:DH * hh_ + DH],
                                start=False, stop=True, tile_position=(0, 32 * j))
                            if last:
                                ins.then_inc(hcr, 1)
                        mk.emit("tensor", f)
                    for hh_ in range(H):
                        def f(e, j=j, hh_=hh_):
                            e.matmul(
                                OUT[hh_ // 2][32 * j:32 * j + 32, 256 * (hh_ % 2):256 * (hh_ % 2) + 256],
                                GTq[0:32 * j, hh_, 32 * j:32 * j + 32],
                                uadj[0:32 * j, DH * hh_:DH * hh_ + DH],
                                start=False, stop=True, tile_position=(0, 32 * j))
                        mk.emit("tensor", f)
                    mk.wait("vector", "dcr")
                    mk.wait("scalar", "dcr")
                else:
                    mk.wait("vector", "dws")
                    mk.wait("scalar", "dws")
                mk.op("vector", lambda e, j=j: e.tensor_copy(
                    uadj[32 * j:32 * j + 32, 0:512], UA[0][32 * j:32 * j + 32, :]), "duev")
                mk.op("scalar", lambda e, j=j: e.activation(
                    uadj[32 * j:32 * j + 32, 512:1024], UA[1][32 * j:32 * j + 32, :], AF.Copy), "duev")
                mk.wait("tensor", "duev", mk.cnt("duev"))
                hat = mk.inc("dat")
                for hh_ in range(H):
                    def f(e, j=j, hh_=hh_, hat=hat, last=(hh_ == H - 1)):
                        ins = e.matmul(
                            OUT[hh_ // 2][32 * j:32 * j + 32, 256 * (hh_ % 2):256 * (hh_ % 2) + 256],
                            GTq[32 * j:32 * j + 32, hh_, 32 * j:32 * j + 32],
                            uadj[32 * j:32 * j + 32, DH * hh_:DH * hh_ + DH],
                            start=False, stop=True, tile_position=(32 * j, 32 * j))
                        if last:
                            ins.then_inc(hat, 1)
                    mk.emit("tensor", f)
            hds = mk.inc("dsu")
            for hh_ in range(H):
                for ks in range(2):
                    def f(e, hh_=hh_, ks=ks, m=m, hds=hds, last=(hh_ == H - 1 and ks == 1)):
                        ins = e.matmul(DS_BANKS[hh_][:, 256 * ks:256 * ks + 256],
                                       kn_tok.s(m, slice(DH * hh_ + 128 * ks, DH * hh_ + 128 * ks + 128)),
                                       uadj[:, DH * hh_:DH * hh_ + DH],
                                       start=(ks == 0), stop=(ks == 1))
                        if last:
                            ins.then_inc(hds, 1)
                    mk.emit("tensor", f)
            mk.wait("vector", "dat")
            mk.wait("scalar", "dat")
            mk.op("vector", lambda e, m=m: e.tensor_copy(delta_tok.s(m, slice(0, 512)), OUT[0][:]), "doutv")
            mk.op("scalar", lambda e, m=m: e.activation(delta_tok.s(m, slice(512, 1024)), OUT[1][:], AF.Copy), "dout")
            mk.wait("vector", "dsu")
            for hh_ in range(H):
                mk.op("vector", lambda e, hh_=hh_: e.tensor_tensor(
                    S_sb[:, hh_, :, :].rearrange("p k d -> p (k d)"),
                    S_sb[:, hh_, :, :].rearrange("p k d -> p (k d)"),
                    DS_BANKS[hh_][:], ALU.add), "dsr" if hh_ == H - 1 else None)
            mk.wait("vector", "dout")
            mk.wait("vector", "dsr")
            mk.op("vector", lambda e: e.memset(sqscr[0:1, 0:1], 0.0), "mdone")
        mk.wait("sync", "mdone", delta_macros)

    emit_delta()

    FL, FF = 192, 97

    def dftc():
        l = np.arange(FL)[:, None]
        f = np.arange(FF)[None, :]
        ang = 2.0 * np.pi * l * f / FL
        Fre, Fim = np.cos(ang), -np.sin(ang)
        lo = np.arange(64, FL)[None, :]
        angi = 2.0 * np.pi * f.T * lo / FL
        c = np.full((FF, 1), 2.0); c[0, 0] = 1.0; c[96, 0] = 1.0
        iRe = (c / FL) * np.cos(angi)
        iIm = -(c / FL) * np.sin(angi)
        return Fre, Fim, iRe, iIm

    Fre_np, Fim_np, iRe_np, iIm_np = dftc()
    FB_np = np.stack([Fre_np[64:192], Fim_np[64:192]], 1)
    FA_np = np.zeros((128, 2, FF), np.float64)
    FA_np[64:128, 0, :] = Fre_np[0:64]
    FA_np[64:128, 1, :] = Fim_np[0:64]
    iF_np = np.stack([iRe_np, iIm_np], 1)
    def fwmat(K):
        k = np.arange(K)[:, None]
        f = np.arange(FF)[None, :]
        ang = 2.0 * np.pi * f * (K - 1 - k) / FL
        return np.stack([np.cos(ang), -np.sin(ang)], 1)
    FAc = mk.sb("FAc", [128, 2, FF], BF16)
    FBc = mk.sb("FBc", [128, 2, FF], BF16)
    iFc = mk.sb("iFc", [FF, 2, 128], BF16)
    Fw64 = mk.sb("Fw64", [64, 2, FF], BF16)
    Fw5 = mk.sb("Fw5", [5, 2, FF], BF16)
    mk.dma("sync", FAc[:], mk.const("FAc_c", FA_np.astype(BF)), sem="lde")
    mk.dma("sync", FBc[:], mk.const("FBc_c", FB_np.astype(BF)), sem="lde")
    mk.dma("sync", iFc[:], mk.const("iFc_c", iF_np.astype(BF)), sem="lde")
    mk.dma("sync", Fw64[:], mk.const("Fw64_c", fwmat(64).astype(BF)), sem="lde")
    mk.dma("sync", Fw5[:], mk.const("Fw5_c", fwmat(5).astype(BF)), sem="lde")

    W2T = mk.sb("W2T", [128, ND, H], BF16)
    b1f = mk.sb("b1f", [128, ND], F32)
    b2row = mk.sb("b2row", [1, H], BF16)
    b2rep = mk.sb("b2rep", [128, H], F32)
    onwrow = mk.sb("onwrow", [1, DH], BF16)
    assert mk.addr(sqacc) + 256 == mk.addr(wbT) + 1088
    onwrep = mk.over(wbT, [128, DH], BF16, 0)
    rtemp = mk.over(wbT, [128, 1], F32, 512)
    rss128 = mk.over(wbT, [128, 1], F32, 544)
    rsl128 = mk.over(wbT, [128, 1], F32, 576)
    rmsacc = mk.over(wbT, [128, NT, H], F32, 608)
    scal3 = mk.over(cw, [1, 8], F32, 0)
    onesrow = mk.sb("onesrow", [1, 128], BF16)
    mk.dma("sync", onesrow[:], mk.const("ones_c", np.ones((1, 128), np.float32).astype(BF)), sem="lde")
    ones512_c = mk.const("ones512_c", np.ones((1, 512), np.float32).astype(BF))
    assert mk.addr(maskA) == mk.addr(S_sb) + 4096
    assert mk.addr(maskGT) == mk.addr(maskA) + 256
    mv4 = mk.over(S_sb, [128, NT, H, 4, 2], BF16, 0)
    bn6 = mk.over(S_sb, [128, NT, H, 6], F32, 1024)
    aacc = mk.over(S_sb, [128, NT, H, 4], F32, 2560)
    ones512 = mk.over(S_sb, [1, 512], BF16, 3584)
    stats_tok = mk.place("RC1", [128, NT, H, 16], BF16, off=6144)
    logits_tok = mk.over(diag, [128, NT, H, H], F32, 0)
    fw_tok = mk.over(diag, [128, NT, H, H], F32, 1024)
    smax = mk.over(pre, [128, NT, H, 1], F32, 1696)

    W1xT = mk.place("RE0", [128, ND, D], BF16)
    WoT = mk.place("RE1", [128, ND, D], BF16)
    stg32b = mk.place("RC0", [128, 4, 1024], F32)
    wstgb = mk.place("RC1", [128, ND, D], BF16)
    Xsb = mk.place("RC0", [FF, 2, D], BF16, off=0)
    t_ab = mk.place("RC0", [FF, 2, D], BF16, off=4096)
    Ysb = mk.place("RC0", [FF, 2, D], BF16, off=8192)
    Whats = mk.place("RC1", [FF, 4, D], BF16, off=0)
    firT64 = mk.place("RC1", [64, ND, 128], BF16, off=8192)
    firT5 = mk.place("RC1", [5, ND, 128], BF16, off=10240)
    W1sT = mk.place("RC1", [16, D], BF16, off=12288)
    ll_tok = TokT(mk, "RA0", "RA1")
    ls_tok = TokT(mk, "RF0", "RF1")
    o_tok = ls_tok
    G0t = mk.place("RC0", [128, ND, 512], BF16, off=0)
    hdnt = mk.place("RC0", [128, ND, 512], BF16, off=8192)
    statsKt = mk.place("RC1", [16, H, 4, 128], BF16, off=2048)
    dstage = mk.place("RC0", [128, 2, D], BF16, off=0)
    mixA = mk.place("RC0", [128, D], BF16, off=4096)
    mixB = mk.place("RC0", [128, D], BF16, off=6144)
    zstage = mk.place("RC0", [128, ND, 512], BF16, off=0)
    ztokt = mk.place("RC0", [128, 4, D], BF16, off=8192)
    o_feat = FeatT(mk, "RA0", "RA1")
    x1_feat = o_feat
    hfeat = FeatT(mk, "RC0", "RC1")
    qwstg = mk.place("RB0", [128, 4, 1024], I32)
    qprim = mk.place("RB1", [128, 4, 1024], BF16)
    qbrow = mk.place("RB1", [1, D], BF16, off=8192)
    lngrow = mk.place("RB1", [1, D], BF16, off=10240)
    lnbrow = mk.place("RB1", [1, D], BF16, off=12288)
    qsF = mk.place("RB1", [128, ND, 64], F32, off=14336)
    WdqT = mk.place("RE0", [128, ND, D], BF16)
    WdqT2 = mk.place("RE1", [128, ND, D], BF16)
    WdqTs = [WdqT, WdqT2]
    qbrow2 = mk.place("RD1", [1, D], BF16, off=13312)
    qbrows = None
    qbrows = [qbrow, qbrow2]
    ystage = mk.place("RD0", [128, ND, 512], BF16, off=0)
    ytokt = mk.place("RD0", [128, 4, D], BF16, off=8192)
    Zlora = mk.place("RD1", [32, T], BF16, off=0)
    grep = mk.place("RD1", [128, D], BF16, off=4096)
    brep = mk.place("RD1", [128, D], BF16, off=6144)
    laT = mk.place("RD1", [128, ND, 32], BF16, off=8192)
    lbT = mk.place("RD1", [32, ND, 128], BF16, off=8704)
    larow = mk.place("RD1", [32, D], BF16, off=10752)
    lbstg = mk.place("RD1", [128, ND, 32], BF16, off=12800)
    outstage = mk.place("RD0", [128, 4, D], F32, off=0)
    ln2tok = mk.place("RD1", [128, 4, D], BF16, off=8192)
    z_feat = FeatT(mk, "RF0", "RF1")
    w1s_stg = mk.over(pre, [128, ND, 16], BF16, 1120)
    lnmv = mk.over(pre, [128, ND, 2], F32, 1376)
    sder = mk.over(pre, [128, NT, H], F32, 1440)

    ROT3 = [pm, pb6, pb7]

    def eng_gate(name):
        for eng in ("tensor", "vector", "scalar", "sync", "gpsimd"):
            mk.wait(eng, name)

    def stats_pass(X, j, gate_sem):
        mk.wait("vector", gate_sem)
        mk.wait("scalar", gate_sem)
        for s in range(NT):
            for hh_ in range(H):
                sem = "ebn" if (s == NT - 1 and hh_ == H - 1) else None
                mk.op("vector", lambda e, s=s, hh_=hh_, X=X: e.bn_stats(
                    bn6[:, s, hh_, :], X.s(s, slice(DH * hh_, DH * hh_ + DH))), sem)
        mk.emit("vector", lambda e: e.drain())
        for s in range(NT):
            for hh_ in range(H):
                sem = "ebn2" if (s == NT - 1 and hh_ == H - 1) else None
                mk.op("vector", lambda e, s=s, hh_=hh_, j=j: e.bn_aggr(
                    mv4[:, s, hh_, j, :], bn6[:, s, hh_, :]), sem)
        for s in range(NT):
            for hh_ in range(H):
                sem = "eab" if (s == NT - 1 and hh_ == H - 1) else None
                mk.op("scalar", lambda e, s=s, hh_=hh_, j=j, X=X: e.activation(
                    sqscr[:], X.s(s, slice(DH * hh_, DH * hh_ + DH)), AF.Abs,
                    accum_out=aacc[:, s, hh_, j:j + 1]), sem)

    def peg(mmfns, sem):
        h = mk.inc(sem, eng="tensor")
        n = len(mmfns)
        for i, fn in enumerate(mmfns):
            def g(e, fn=fn, h=h, last=(i == n - 1)):
                ins = fn(e)
                if last:
                    ins.then_inc(h, 1)
            mk.emit("tensor", g)

    def emit_epilogue():
        stats_pass(delta_tok, 2, "mdone")
        stats_pass(v_tok, 3, "v_tok_rdy")
        mk.wait("sync", "mdone")
        mk.ldbg("Ldelta", [delta_tok.h[0][:].rearrange("p s n -> p (s n)"),
                           delta_tok.h[1][:].rearrange("p s n -> p (s n)")], [256, 8 * D], BF16)
        for half in range(2):
            mk.dma("sync", dscr[half], delta_tok.h[half][:], sem="espl")
        SPL = mk.cnt("espl")
        mk.wait("sync", "espl", SPL)
        if "Ldsc3" in mk.debug_taps:
            mk.dma("sync", wstgb[:, 0, :], dscr[0, :, 3, :], sem="edbg")
            mk.wait("sync", "edbg", mk.cnt("edbg"))
            mk.ldbg("Ldsc3", wstgb[:, 0, :], [128, D], BF16)
        mk.wait("sync", "ebn")
        mk.wait("sync", "eab")
        mk.dma("sync", ones512[:], ones512_c, sem="elds")

        def load_weightE(win, dstT):
            for hf in range(2):
                mk.wait("sync", "ecast")
                mk.dma("sync", stg32b[:], win.rearrange("(s p) d -> p s d", p=128)[:, 4 * hf:4 * hf + 4, :],
                       sem="elds")
                mk.wait("vector", "elds", mk.cnt("elds"))
                if hf == 0:
                    mk.wait("vector", "etr")
                mk.op("vector", lambda e, hf=hf: e.tensor_copy(
                    wstgb[:, 4 * hf:4 * hf + 4, :], stg32b[:]), "ecast")
            mk.wait("sync", "ecast")
            for os_ in range(ND):
                mk.dmaT(dstT[:, :, 128 * os_:128 * os_ + 128], wstgb[:, os_, :], sem="etr")
            return mk.cnt("etr")

        TR_W1X = load_weightE(fw1_in[:, 0:D], W1xT)
        TR_WO = load_weightE(wo_in, WoT)
        firstg = mk.over(pre, [128, ND, 64], BF16, 0)
        firstg5 = mk.over(pre, [128, ND, 5], BF16, 1024)
        mk.wait("gpsimd", "convg")
        mk.wait("gpsimd", "diag_b")
        mk.wait("gpsimd", "beta_tr")
        mk.dma("gpsimd", w1s_stg[:], fw1_in[:, D:D + 16].rearrange("(s p) j -> p s j", p=128), sem="elsm")
        mk.dma("gpsimd", bst[:], fw2_in, sem="elsm")
        mk.dma("gpsimd", b2row[:], fb2_in[None, :], sem="elsm")
        mk.dma("gpsimd", onwrow[:], onw_in[None, :], sem="elsm")
        mk.dma("gpsimd", scal3[0:1, 0:1], ltemp_in[None, :], sem="elsm")
        mk.dma("gpsimd", scal3[0:1, 1:2], rss_in[None, :], sem="elsm")
        mk.dma("gpsimd", scal3[0:1, 2:3], rsl_in[None, :], sem="elsm")
        mk.dma("gpsimd", firstg[:], firl_in.rearrange("h (s p) k -> p (h s) k", p=128), sem="elsm")
        mk.dma("gpsimd", firstg5[:], firs_in.rearrange("h (s p) k -> p (h s) k", p=128), sem="elsm")
        h_ = mk.inc("elsm", 16, eng="gpsimd", dma=True)
        mk.emit("gpsimd", lambda e, h_=h_: e.dma_start(
            b1f[:], fb1_in.rearrange("(s p) -> p s", p=128),
            allow_slow_non_contiguous=True).then_inc(h_, 16))
        ELSM = mk.cnt("elsm")
        mk.wait("tensor", "elsm", ELSM)
        mk.wait("tensor", "lde")

        mk.wait("vector", "etr")

        def tpair(tin_fn, nin, nout, out_ap):
            mk.wait("tensor", "etrpe")
            mk.op("tensor", lambda e, tin_fn=tin_fn, nin=nin, nout=nout: e.transpose(
                pt_bf[0:nout, 0:nin], tin_fn(), ident[0:nin, 0:nin]), "etrpt")
            mk.wait("vector", "etrpt")
            mk.op("vector", lambda e, out_ap=out_ap, nin=nin, nout=nout: e.tensor_copy(
                out_ap, pt_bf[0:nout, 0:nin]), "etrpe")

        for ds in range(ND):
            tpair(lambda ds=ds: firstg[:, ds, :], 128, 64, firT64[:, ds, :])
        for ds in range(ND):
            tpair(lambda ds=ds: firstg5[:, ds, :], 128, 5, firT5[:, ds, :])
        for ds in range(ND):
            tpair(lambda ds=ds: w1s_stg[:, ds, :], 128, 16, W1sT[:, 128 * ds:128 * ds + 128])
        for ds in range(ND):
            tpair(lambda ds=ds: bst[:, 128 * ds:128 * ds + 128], 4, 128, W2T[:, ds, :])
        mk.dbg("firT64", firT64[:].rearrange("p s n -> p (s n)"), [64, ND * 128], BF16, "etrpe")
        mk.dbg("firstg", firstg[:].rearrange("p s n -> p (s n)"), [128, ND * 64], BF16, "etrpe")

        mk.wait("scalar", "elsm", ELSM)
        mk.op("scalar", lambda e: e.activation(scal3[0:1, 3:4], scal3[0:1, 0:1], AF.Abs), None)
        mk.emit("scalar", lambda e: e.drain())
        mk.op("scalar", lambda e: e.activation(scal3[0:1, 4:5], scal3[0:1, 3:4], AF.Exp, scale=-1.0), None)
        mk.emit("scalar", lambda e: e.drain())
        mk.op("scalar", lambda e: e.activation(scal3[0:1, 5:6], scal3[0:1, 4:5], AF.Ln, bias=1.0), None)
        mk.emit("scalar", lambda e: e.drain())
        mk.op("scalar", lambda e: e.activation(scal3[0:1, 6:7], scal3[0:1, 0:1], AF.Relu), "esp")
        mk.wait("vector", "esp")
        mk.op("vector", lambda e: e.tensor_add(scal3[0:1, 7:8], scal3[0:1, 5:6], scal3[0:1, 6:7]), None)
        mk.emit("vector", lambda e: e.drain())
        mk.op("vector", lambda e: e.tensor_scalar(out=scal3[0:1, 7:8], in0=scal3[0:1, 7:8],
                                                  scalar1=1e-4, scalar2=None, op0=ALU.add), None)
        mk.emit("vector", lambda e: e.drain())
        mk.op("vector", lambda e: e.reciprocal(scal3[0:1, 3:4], scal3[0:1, 7:8]), "esp2")
        mk.wait("tensor", "esp2")
        mk.wait("tensor", "etrpe")
        sc16 = mk.over(cw, [1, 8], BF16, 32)
        mk.wait("vector", "esp2")
        mk.op("vector", lambda e: e.tensor_copy(sc16[:], scal3[0:1, 0:8]), "esc16")
        mk.wait("tensor", "esc16")
        peg([lambda e: e.matmul(ROT3[0][0:128, 0:1], onesrow[0:1, 0:128], sc16[0:1, 3:4], start=True, stop=True),
             lambda e: e.matmul(ROT3[0][0:128, 1:2], onesrow[0:1, 0:128], sc16[0:1, 1:2], start=False, stop=True),
             lambda e: e.matmul(ROT3[0][0:128, 2:3], onesrow[0:1, 0:128], sc16[0:1, 2:3], start=False, stop=True),
             lambda e: e.matmul(ROT3[0][0:128, 4:8], onesrow[0:1, 0:128], b2row[0:1, :], start=False, stop=True),
             lambda e: e.matmul(ROT3[0][0:128, 8:264], onesrow[0:1, 0:128], onwrow[0:1, :], start=False, stop=True)],
            "ebc")
        mk.wait("vector", "ebc")
        mk.op("vector", lambda e: e.tensor_copy(rtemp[:], ROT3[0][:, 0:1]), None)
        mk.op("vector", lambda e: e.tensor_copy(rss128[:], ROT3[0][:, 1:2]), None)
        mk.op("vector", lambda e: e.tensor_copy(rsl128[:], ROT3[0][:, 2:3]), None)
        mk.op("vector", lambda e: e.tensor_copy(b2rep[:], ROT3[0][:, 4:8]), None)
        mk.op("vector", lambda e: e.tensor_copy(onwrep[:], ROT3[0][:, 8:264]), "ebce")

        mk.wait("tensor", "ebce")
        for fi, (Fw, firT, wslot) in enumerate([(Fw64, firT64, 0), (Fw5, firT5, 2)]):
            for ri in range(2):
                for nh in range(2):
                    peg([lambda e, Fw=Fw, firT=firT, ri=ri, nh=nh: e.matmul(
                        ROT3[nh][0:FF, 0:512], Fw[:, ri, :],
                        firT[:].rearrange("k s n -> k (s n)")[:, 512 * nh:512 * nh + 512],
                        start=True, stop=True)], "ewh")
                mk.wait("vector", "ewh")
                mk.op("vector", lambda e, wslot=wslot, ri=ri: e.tensor_copy(
                    Whats[:, wslot + ri, 0:512], ROT3[0][0:FF, 0:512]), None)
                mk.op("vector", lambda e, wslot=wslot, ri=ri: e.tensor_copy(
                    Whats[:, wslot + ri, 512:1024], ROT3[1][0:FF, 0:512]), "ewhe")
                mk.wait("tensor", "ewhe", mk.cnt("ewhe"))

        XB = [pp[0], pp[1], pc[0], pc[1]]
        for b in range(NT):
            mms = []
            for ri in range(2):
                for nh in range(2):
                    def f(e, b=b, ri=ri, nh=nh):
                        return e.matmul(XB[2 * ri + nh][0:FF, :], FBc[:, ri, :],
                                        v_tok.s(b, slice(512 * nh, 512 * nh + 512)),
                                        start=True, stop=(b == 0))
                    mms.append(f)
                    if b > 0:
                        def f2(e, b=b, ri=ri, nh=nh):
                            return e.matmul(XB[2 * ri + nh][0:FF, :], FAc[64:128, ri, :],
                                            v_tok.s(b - 1, slice(512 * nh, 512 * nh + 512))[64:128, :],
                                            start=False, stop=True)
                        mms.append(f2)
            if b > 0:
                mk.wait("tensor", "exe", mk.cnt("exe"))
            peg(mms, "efw")
            mk.wait("scalar", "efw")
            for k4 in range(4):
                sem = "exe" if k4 == 3 else None
                mk.op("scalar", lambda e, k4=k4: e.activation(
                    Xsb[:, k4 // 2, 512 * (k4 % 2):512 * (k4 % 2) + 512],
                    XB[k4][0:FF, :], AF.Copy), sem)
            mk.wait("vector", "exe")
            for fi, wslot in ((0, 0), (1, 2)):
                mk.op("vector", lambda e, wslot=wslot: e.tensor_mul(
                    t_ab[:, 0, :], Xsb[:, 0, :], Whats[:, wslot + 0, :]), None)
                mk.op("vector", lambda e, wslot=wslot: e.tensor_mul(
                    t_ab[:, 1, :], Xsb[:, 1, :], Whats[:, wslot + 1, :]), None)
                mk.emit("vector", lambda e: e.drain())
                mk.op("vector", lambda e: e.tensor_sub(Ysb[:, 0, :], t_ab[:, 0, :], t_ab[:, 1, :]), None)
                mk.op("vector", lambda e, wslot=wslot: e.tensor_mul(
                    t_ab[:, 0, :], Xsb[:, 0, :], Whats[:, wslot + 1, :]), None)
                mk.op("vector", lambda e, wslot=wslot: e.tensor_mul(
                    t_ab[:, 1, :], Xsb[:, 1, :], Whats[:, wslot + 0, :]), None)
                mk.emit("vector", lambda e: e.drain())
                mk.op("vector", lambda e: e.tensor_add(Ysb[:, 1, :], t_ab[:, 0, :], t_ab[:, 1, :]), "epw")
                mk.wait("tensor", "epw")
                if b > 0 or fi > 0:
                    mk.wait("tensor", "eie", mk.cnt("eie"))
                mms = []
                for nh in range(2):
                    for ri in range(2):
                        def f(e, nh=nh, ri=ri):
                            return e.matmul(ROT3[nh][:, :], iFc[:, ri, :],
                                            Ysb[:, ri, 512 * nh:512 * nh + 512],
                                            start=(ri == 0), stop=(ri == 1))
                        mms.append(f)
                peg(mms, "einv")
                mk.wait("vector", "einv")
                mk.wait("scalar", "einv")
                dstT = ll_tok if fi == 0 else ls_tok
                mk.op("vector", lambda e, dstT=dstT, b=b: e.tensor_copy(
                    dstT.s(b, slice(0, 512)), ROT3[0][:, :]), "eie")
                mk.op("scalar", lambda e, dstT=dstT, b=b: e.activation(
                    dstT.s(b, slice(512, 1024)), ROT3[1][:, :], AF.Copy), "eie")
        mk.dbg("Xsb", Xsb[:].rearrange("p a n -> p (a n)"), [FF, 2 * D], BF16, "eie")
        if "Lll" in mk.debug_taps or "Lls" in mk.debug_taps:
            mk.wait("sync", "eie", mk.cnt("eie"))
            mk.ldbg("Lll", [ll_tok.h[0][:].rearrange("p s n -> p (s n)"),
                            ll_tok.h[1][:].rearrange("p s n -> p (s n)")], [256, 8 * D], BF16)
            mk.ldbg("Lls", [ls_tok.h[0][:].rearrange("p s n -> p (s n)"),
                            ls_tok.h[1][:].rearrange("p s n -> p (s n)")], [256, 8 * D], BF16)
        mk.op("vector", lambda e: e.memset(sqscr[0:1, 0:1], 0.0), "efir")

        stats_pass(ll_tok, 1, "efir")
        stats_pass(ls_tok, 0, "efir")
        mk.wait("vector", "ebn2")
        mk.wait("scalar", "eab")
        mk.wait("vector", "eab")
        for j in range(4):
            mk.op("vector", lambda e, j=j: e.tensor_copy(
                stats_tok[:, :, :, 4 * j + 0], mv4[:, :, :, j, 0]), None)
            mk.op("vector", lambda e, j=j: e.tensor_copy(
                stats_tok[:, :, :, 4 * j + 1], mv4[:, :, :, j, 1]), None)
            mk.op("vector", lambda e, j=j: e.tensor_scalar(
                out=stats_tok[:, :, :, 4 * j + 2], in0=aacc[:, :, :, j],
                scalar1=1.0 / DH, scalar2=None, op0=ALU.mult), None)
            mk.op("vector", lambda e, j=j: e.tensor_mul(
                sder[:, :, :], mv4[:, :, :, j, 0], mv4[:, :, :, j, 0]), None)
            mk.emit("vector", lambda e: e.drain())
            mk.op("vector", lambda e, j=j: e.tensor_add(
                sder[:, :, :], sder[:, :, :], mv4[:, :, :, j, 1]), "esd")
            mk.wait("scalar", "esd")
            mk.op("scalar", lambda e, j=j: e.activation(
                stats_tok[:, :, :, 4 * j + 3], sder[:, :, :], AF.Sqrt, scale=float(DH)), "esd2")
        mk.wait("vector", "esd2")
        mk.op("vector", lambda e: e.memset(sqscr[0:1, 0:1], 0.0), "estats")
        if "Lstats" in mk.debug_taps:
            mk.wait("sync", "esd2", mk.cnt("esd2"))
            mk.ldbg("Lstats", stats_tok[:].rearrange("p s h c -> p (s h c)"), [128, NT * H * 16], BF16)

        mk.wait("tensor", "etr", TR_WO)
        mk.wait("tensor", "estats")
        GR = [pp[0], pp[1]]
        HR = [pc[0], pc[1]]
        for tt in range(4):
            for hh_ in range(H):
                for sr in range(4):
                    mk.wait("tensor", "egkt", mk.cnt("egkt"))
                    mk.op("tensor", lambda e, hh_=hh_, sr=sr, tt=tt: e.transpose(
                        pt_bf[0:16, 128 * sr:128 * sr + 128], stats_tok[:, 4 * tt + sr, hh_, :],
                        ident[:]), "egk")
                mk.wait("vector", "egk")
                mk.op("vector", lambda e, hh_=hh_: e.tensor_copy(
                    statsKt[:, hh_, :, :].rearrange("p a n -> p (a n)"), pt_bf[0:16, :]), "egkt")
            for os_ in range(ND):
                gb = GR[os_ % 2]
                if os_ >= 2:
                    mk.wait("tensor", "eg0e", mk.cnt("eg0e") - 1)
                mms = []
                for ks in range(ND):
                    def f(e, gb=gb, ks=ks, os_=os_, tt=tt):
                        return e.matmul(gb[:], W1xT[:, ks, 128 * os_:128 * os_ + 128],
                                        x_feat.ks(ks, slice(512 * tt, 512 * tt + 512)),
                                        start=(ks == 0), stop=(ks == ND - 1))
                    mms.append(f)
                peg(mms, "eg0")
                mk.wait("scalar", "eg0")
                mk.op("scalar", lambda e, gb=gb, os_=os_: e.activation(
                    G0t[:, os_, :], gb[:], AF.Copy), "eg0e")
            for hh_ in range(H):
                mk.wait("tensor", "eg0e", mk.cnt("eg0e"))
                mk.wait("tensor", "egkt", mk.cnt("egkt"))
                for os_ in range(ND):
                    hb = HR[os_ % 2]
                    if os_ >= 2 or hh_ > 0:
                        mk.wait("tensor", "ehde", mk.cnt("ehde") - 1)
                    peg([lambda e, hb=hb, os_=os_, hh_=hh_: e.matmul(
                            hb[:], W1sT[:, 128 * os_:128 * os_ + 128], statsKt[:, hh_, :, :].rearrange("p a n -> p (a n)"),
                            start=True, stop=False),
                         lambda e, hb=hb, os_=os_: e.matmul(
                            hb[:], ident[:], G0t[:, os_, :], start=False, stop=True)], "ehd")
                    mk.wait("scalar", "ehd")
                    mk.op("scalar", lambda e, hb=hb, os_=os_: e.activation(
                        hdnt[:, os_, :], hb[:], AF.Gelu, bias=b1f[:, os_:os_ + 1]), "ehde")
                mk.wait("tensor", "ehde", mk.cnt("ehde"))
                mms = []
                for c4 in range(4):
                    for os_ in range(ND):
                        def f(e, c4=c4, os_=os_):
                            return e.matmul(pm[:, 4 * c4:4 * c4 + 4],
                                            hdnt[:, os_, 128 * c4:128 * c4 + 128], W2T[:, os_, :],
                                            start=(os_ == 0), stop=(os_ == ND - 1))
                        mms.append(f)
                peg(mms, "elg")
                mk.wait("vector", "elg")
                mk.op("vector", lambda e, tt=tt, hh_=hh_: e.tensor_copy(
                    logits_tok[:, 4 * tt:4 * tt + 4, hh_, :], pm[:, 0:16].rearrange("p (a b) -> p a b", a=4)), "elge")
                mk.wait("tensor", "elge", mk.cnt("elge"))
        mk.wait("vector", "elge")
        mk.op("vector", lambda e: e.tensor_add(
            logits_tok[:], logits_tok[:],
            b2rep[:, None, None, :].broadcast_to([128, NT, H, H])), None)
        mk.emit("vector", lambda e: e.drain())
        mk.op("vector", lambda e: e.reduce_max(
            smax[:], logits_tok[:], mybir.AxisListType.X), None)
        mk.emit("vector", lambda e: e.drain())
        mk.op("vector", lambda e: e.tensor_sub(
            logits_tok[:], logits_tok[:], smax[:].broadcast_to([128, NT, H, H])), "esm1")
        mk.wait("scalar", "esm1")
        mk.op("scalar", lambda e: e.activation(
            fw_tok[:], logits_tok[:], AF.Exp, scale=rtemp[:, 0:1]), "esm2")
        mk.wait("vector", "esm2")
        mk.op("vector", lambda e: e.reduce_sum(
            smax[:], fw_tok[:], mybir.AxisListType.X), None)
        mk.emit("vector", lambda e: e.drain())
        mk.op("vector", lambda e: e.reciprocal(smax[:], smax[:]), None)
        mk.emit("vector", lambda e: e.drain())
        mk.op("vector", lambda e: e.tensor_mul(
            fw_tok[:], fw_tok[:], smax[:].broadcast_to([128, NT, H, H])), None)
        mk.emit("vector", lambda e: e.drain())
        mk.op("vector", lambda e: e.tensor_scalar(
            out=fw_tok[:, :, :, 0], in0=fw_tok[:, :, :, 0],
            scalar1=rss128[:, 0:1], scalar2=None, op0=ALU.add), None)
        mk.emit("vector", lambda e: e.drain())
        mk.op("vector", lambda e: e.tensor_scalar(
            out=fw_tok[:, :, :, 1], in0=fw_tok[:, :, :, 1],
            scalar1=rsl128[:, 0:1], scalar2=None, op0=ALU.add), "efwr")

        mk.wait("sync", "efwr")
        mk.ldbg("Lfw", fw_tok[:].rearrange("p s h c -> p (s h c)"), [128, NT * H * H], F32)
        mk.ldbg("Lvm", [v_tok.h[0][:].rearrange("p s n -> p (s n)"),
                        v_tok.h[1][:].rearrange("p s n -> p (s n)")], [256, 8 * D], BF16)
        mk.ldbg("Lllm", [ll_tok.h[0][:].rearrange("p s n -> p (s n)"),
                         ll_tok.h[1][:].rearrange("p s n -> p (s n)")], [256, 8 * D], BF16)
        mk.ldbg("Llsm", [ls_tok.h[0][:].rearrange("p s n -> p (s n)"),
                         ls_tok.h[1][:].rearrange("p s n -> p (s n)")], [256, 8 * D], BF16)
        for s in range(NT):
            if s >= 2:
                mk.wait("sync", "emix", s - 1)
            mk.dma("sync", dstage[:, s % 2, :], dscr[s // 8, :, s % 8, :], sem=f"edst{s % 2}")
            mk.wait("vector", f"edst{s % 2}", mk.cnt(f"edst{s % 2}"))
            mk.wait("vector", "efwr")

            def bc(j, s=s):
                return fw_tok[:, s, :, j:j + 1].broadcast_to([128, H, DH])
            mk.op("vector", lambda e, a=dstage[:, s % 2, :].rearrange("p (h d) -> p h d", h=H), b=bc(2):
                  e.tensor_tensor(mixA[:].rearrange("p (h d) -> p h d", h=H), a, b, ALU.mult), None)
            mk.op("vector", lambda e, a=ll_tok.s(s).rearrange("p (h d) -> p h d", h=H), b=bc(1):
                  e.tensor_tensor(mixB[:].rearrange("p (h d) -> p h d", h=H), a, b, ALU.mult), None)
            mk.emit("vector", lambda e: e.drain())
            mk.op("vector", lambda e: e.tensor_add(mixA[:], mixA[:], mixB[:]), None)
            mk.op("vector", lambda e, a=ls_tok.s(s).rearrange("p (h d) -> p h d", h=H), b=bc(0):
                  e.tensor_tensor(mixB[:].rearrange("p (h d) -> p h d", h=H), a, b, ALU.mult), None)
            mk.emit("vector", lambda e: e.drain())
            mk.op("vector", lambda e: e.tensor_add(mixA[:], mixA[:], mixB[:]), None)
            mk.op("vector", lambda e, a=v_tok.s(s).rearrange("p (h d) -> p h d", h=H), b=bc(3):
                  e.tensor_tensor(mixB[:].rearrange("p (h d) -> p h d", h=H), a, b, ALU.mult), None)
            mk.emit("vector", lambda e: e.drain())
            mk.op("vector", lambda e, a=o_tok.s(s): e.tensor_add(a, mixA[:], mixB[:]), "emix")
            mk.wait("scalar", "emix", mk.cnt("emix"))
            for hh_ in range(H):
                sem = "erms" if hh_ == H - 1 else None
                mk.op("scalar", lambda e, s=s, hh_=hh_: e.activation(
                    sqscr[:], o_tok.s(s, slice(DH * hh_, DH * hh_ + DH)), AF.Square,
                    accum_out=rmsacc[:, s, hh_:hh_ + 1]), sem)
        if "Lo2" in mk.debug_taps:
            mk.wait("sync", "emix", mk.cnt("emix"))
            mk.ldbg("Lo2", [o_tok.h[0][:].rearrange("p s n -> p (s n)"),
                            o_tok.h[1][:].rearrange("p s n -> p (s n)")], [256, 8 * D], BF16)
            mk.wait("vector", "st_out", mk.cnt("st_out"))
        mk.wait("vector", "erms")
        mk.op("vector", lambda e: e.tensor_scalar(
            out=rmsacc[:], in0=rmsacc[:], scalar1=1.0 / DH, scalar2=EPS,
            op0=ALU.mult, op1=ALU.add), "erst")
        mk.wait("scalar", "erst")
        mk.op("scalar", lambda e: e.activation(rmsacc[:], rmsacc[:], AF.Sqrt), "erst2")
        mk.wait("vector", "erst2")
        mk.op("vector", lambda e: e.reciprocal(rmsacc[:], rmsacc[:]), "erst3")
        mk.wait("vector", "erst3")
        for s in range(NT):
            for hh_ in range(H):
                sem = "eon" if (s == NT - 1 and hh_ == H - 1) else None
                mk.op("vector", lambda e, s=s, hh_=hh_: e.scalar_tensor_tensor(
                    out=o_tok.s(s, slice(DH * hh_, DH * hh_ + DH)),
                    in0=o_tok.s(s, slice(DH * hh_, DH * hh_ + DH)),
                    scalar=rmsacc[:, s, hh_:hh_ + 1], in1=onwrep[:],
                    op0=ALU.mult, op1=ALU.mult), sem)
        mk.wait("sync", "eon")
        mk.ldbg("Lo", [o_tok.h[0][:].rearrange("p s n -> p (s n)"),
                       o_tok.h[1][:].rearrange("p s n -> p (s n)")], [256, 8 * D], BF16)
        dmaT_feat_from_tok(mk, o_feat, o_tok)
        TR_O = mk.cnt("tr")

        if "Lof" in mk.debug_taps:
            mk.wait("sync", "tr", TR_O)
            mk.ldbg("Lof", [o_feat.h[0][:].rearrange("p s n -> p (s n)"),
                            o_feat.h[1][:].rearrange("p s n -> p (s n)")], [256, 4 * T], BF16)
        if "Lonw" in mk.debug_taps:
            mk.wait("sync", "ebce", mk.cnt("ebce"))
            mk.ldbg("Lonw", onwrep[:], [128, DH], BF16)
        mk.wait("tensor", "tr", TR_O)
        for tt in range(4):
            for ds2 in range(ND):
                zb = GR[ds2 % 2]
                if ds2 >= 2 or tt > 0:
                    mk.wait("tensor", "eze", mk.cnt("eze") - 1)
                mms = []
                for cs in range(ND):
                    def f(e, zb=zb, cs=cs, ds2=ds2, tt=tt):
                        return e.matmul(zb[:], WoT[:, cs, 128 * ds2:128 * ds2 + 128],
                                        o_feat.ks(cs, slice(512 * tt, 512 * tt + 512)),
                                        start=(cs == 0), stop=False)
                    mms.append(f)
                mms.append(lambda e, zb=zb, ds2=ds2, tt=tt: e.matmul(
                    zb[:], ident[:], x_feat.ks(ds2, slice(512 * tt, 512 * tt + 512)),
                    start=False, stop=True))
                peg(mms, "ez")
                mk.wait("scalar", "ez")
                mk.op("scalar", lambda e, zb=zb, ds2=ds2: e.activation(
                    zstage[:, ds2, :], zb[:], AF.Copy), "eze")
            mk.wait("sync", "eze", mk.cnt("eze"))
            if tt > 0:
                mk.wait("sync", "ex1t", mk.cnt("ex1t"))
            if tt == 0:
                mk.ldbg("Lz0", zstage[:].rearrange("p s n -> p (s n)"), [128, ND * 512], BF16)
            for ds in range(ND):
                mk.dmaT(ztokt[:, :, 128 * ds:128 * ds + 128], zstage[:, ds, :], sem="eztr")
            ZTR = mk.cnt("eztr")
            if tt == 0 and "Lzt0" in mk.debug_taps:
                mk.wait("sync", "eztr", ZTR)
                mk.ldbg("Lzt0", ztokt[:].rearrange("p s n -> p (s n)"), [128, 4 * D], BF16)
            mk.wait("vector", "eztr", ZTR)
            for sr in range(4):
                for hf2 in range(2):
                    sem = "ebnz" if (sr == 3 and hf2 == 1) else None
                    mk.op("vector", lambda e, sr=sr, hf2=hf2: e.bn_stats(
                        bn6[:, 2 * sr + hf2, 0, :], ztokt[:, sr, 512 * hf2:512 * hf2 + 512]), sem)
            mk.emit("vector", lambda e: e.drain())
            for sr in range(4):
                sem = "ebnz2" if sr == 3 else None
                mk.op("vector", lambda e, sr=sr: e.bn_aggr(
                    lnmv[:, sr, :], bn6[:, 2 * sr:2 * sr + 2, 0, :]), sem)
            mk.emit("vector", lambda e: e.drain())
            mk.op("vector", lambda e: e.tensor_scalar(
                out=lnmv[:, 0:4, 1], in0=lnmv[:, 0:4, 1], scalar1=EPS, scalar2=None, op0=ALU.add), "elnv")
            mk.wait("scalar", "elnv", mk.cnt("elnv"))
            mk.op("scalar", lambda e: e.activation(lnmv[:, 0:4, 1], lnmv[:, 0:4, 1], AF.Sqrt), "elns")
            mk.wait("vector", "elns", mk.cnt("elns"))
            mk.op("vector", lambda e: e.reciprocal(lnmv[:, 0:4, 1], lnmv[:, 0:4, 1]), "elnr")
            mk.emit("vector", lambda e: e.drain())
            for sr in range(4):
                sem = "ex1" if sr == 3 else None
                mk.op("vector", lambda e, sr=sr: e.tensor_scalar(
                    out=ztokt[:, sr, :], in0=ztokt[:, sr, :],
                    scalar1=lnmv[:, sr, 0:1], scalar2=lnmv[:, sr, 1:2],
                    op0=ALU.subtract, op1=ALU.mult), sem)
            mk.wait("sync", "ex1", mk.cnt("ex1"))
            if tt == 0:
                mk.ldbg("Lzn0", ztokt[:].rearrange("p s n -> p (s n)"), [128, 4 * D], BF16)
                mk.ldbg("Llnmv", lnmv[:].rearrange("p s n -> p (s n)"), [128, 2 * ND], F32)
            for sr in range(4):
                s_ = 4 * tt + sr
                for half in range(2):
                    mk.dmaT(x1_feat.h[half][:, :, 128 * s_:128 * s_ + 128],
                            ztokt[:, sr, 512 * half:512 * half + 512], sem="ex1t")
        mk.wait("vector", "ex1t", mk.cnt("ex1t"))
        for half in range(2):
            sem = "ehcp" if half == 1 else None
            mk.op("vector", lambda e, half=half: e.tensor_copy(
                hfeat.h[half][:], x1_feat.h[half][:]), sem)

        if "Lx1" in mk.debug_taps:
            mk.wait("sync", "ex1t", mk.cnt("ex1t"))
            mk.ldbg("Lx1", [x1_feat.h[0][:].rearrange("p s n -> p (s n)"),
                            x1_feat.h[1][:].rearrange("p s n -> p (s n)")], [256, 4 * T], BF16)

        YR = [pp[0], pp[1]]

        def qlora(i, IN, OUT_IS_H, want_ln_gelu):
            b = i // 2
            WdqTc = WdqTs[i % 2]
            qbrowc = qbrows[i % 2]
            mk.wait("gpsimd", "etrpe")
            mk.dma("gpsimd", larow[0:32, :], la_in[i], sem="eql")
            mk.dma("gpsimd", lbstg[:], lb_in[i].rearrange("(s p) r -> p s r", p=128), sem="eql")
            mk.dma("gpsimd", qbrowc[:], qb_in[i][None, :], sem="eql")
            mk.dma("gpsimd", qsF[:], qs_in[i].rearrange("(s p) g -> p s g", p=128), sem="eql")
            if want_ln_gelu:
                mk.dma("gpsimd", lngrow[:], lng_in[b][None, :], sem="eql")
                mk.dma("gpsimd", lnbrow[:], lnb_in[b][None, :], sem="eql")
            QL = mk.cnt("eql")
            mk.wait("tensor", "eql", QL)
            for ds in range(ND):
                tpair(lambda ds=ds: larow[0:32, 128 * ds:128 * ds + 128], 32, 128, laT[:, ds, :])
            for ds in range(ND):
                tpair(lambda ds=ds: lbstg[:, ds, :], 128, 32, lbT[:, ds, :])
            if want_ln_gelu:
                mk.wait("tensor", "egbe", mk.cnt("egbe"))
                peg([lambda e: e.matmul(ROT3[2][:, 0:512], onesrow[0:1, 0:128], lngrow[0:1, 0:512],
                                        start=True, stop=True),
                     lambda e: e.matmul(ROT3[1][:, 0:512],
                                        onesrow[0:1, 0:128], lngrow[0:1, 512:1024], start=True, stop=True)],
                    "egb")
                mk.wait("vector", "egb")
                mk.op("vector", lambda e: e.tensor_copy(grep[:, 0:512], ROT3[2][:, 0:512]), None)
                mk.op("vector", lambda e: e.tensor_copy(grep[:, 512:1024], ROT3[1][:, 0:512]), "egbe")
                mk.wait("tensor", "egbe", mk.cnt("egbe"))
                peg([lambda e: e.matmul(ROT3[2][:, 0:512], onesrow[0:1, 0:128], lnbrow[0:1, 0:512],
                                        start=True, stop=True),
                     lambda e: e.matmul(ROT3[1][:, 0:512], onesrow[0:1, 0:128], lnbrow[0:1, 512:1024],
                                        start=True, stop=True)], "egb")
                mk.wait("vector", "egb")
                mk.op("vector", lambda e: e.tensor_copy(brep[:, 0:512], ROT3[2][:, 0:512]), None)
                mk.op("vector", lambda e: e.tensor_copy(brep[:, 512:1024], ROT3[1][:, 0:512]), "egbe")
            for hf in range(2):
                mk.wait("sync", "eqdq")
                if i >= 2 and hf == 0:
                    mk.wait("sync", "eqmm", i - 1)
                mk.dma("sync", qwstg[:], qw_in[i].rearrange("(s p) d -> p s d", p=128)[:, 4 * hf:4 * hf + 4, :],
                       sem="eqw")
                mk.wait("scalar", "eqw", mk.cnt("eqw"))
                mk.wait("scalar", "eql", QL)
                mk.op("scalar", lambda e: e.activation(qprim[:], qwstg[:], AF.Copy,
                                                       scale=1.0 / 7.5, bias=-1.0), "eqa")
                mk.wait("vector", "eqa", mk.cnt("eqa"))
                for osr in range(4):
                    sem = "eqd" if osr == 3 else None
                    mk.op("vector", lambda e, osr=osr, hf=hf: e.tensor_tensor(
                        qprim[:, osr, :].rearrange("p (g r) -> p g r", g=64),
                        qprim[:, osr, :].rearrange("p (g r) -> p g r", g=64),
                        qsF[:, 4 * hf + osr, :, None].broadcast_to([128, 64, 16]), ALU.mult), sem)
                mk.wait("sync", "eqd", mk.cnt("eqd"))
                for osr in range(4):
                    mk.dmaT(WdqTc[:, :, 128 * (4 * hf + osr):128 * (4 * hf + osr) + 128],
                            qprim[:, osr, :], sem="eqdq")
            DQ = mk.cnt("eqdq")
            mk.wait("tensor", "eqdq", DQ)
            mk.wait("tensor", "etrpe", mk.cnt("etrpe"))
            mk.wait("tensor", "ehcp")
            mk.wait("tensor", "ezf", mk.cnt("ezf"))
            mk.wait("tensor", "eye", mk.cnt("eye"))
            for tt in range(4):
                mms = []
                for ks in range(ND):
                    def f(e, ks=ks, tt=tt, IN=IN):
                        return e.matmul(pc[0][0:32, :], laT[:, ks, :],
                                        IN.ks(ks, slice(512 * tt, 512 * tt + 512)),
                                        start=(ks == 0), stop=(ks == ND - 1))
                    mms.append(f)
                peg(mms, "ezl")
                mk.wait("vector", "ezl")
                mk.op("vector", lambda e, tt=tt: e.tensor_copy(
                    Zlora[0:32, 512 * tt:512 * tt + 512], pc[0][0:32, :]), "ezle")
                mk.wait("tensor", "ezle", mk.cnt("ezle"))
            for tt in range(4):
                for os_ in range(ND):
                    yb = YR[os_ % 2]
                    if os_ >= 2 or tt > 0:
                        mk.wait("tensor", "eye", mk.cnt("eye") - 1)
                    mms = []
                    for ks in range(ND):
                        def f(e, yb=yb, ks=ks, os_=os_, tt=tt, IN=IN, WdqTc=WdqTc):
                            return e.matmul(yb[:], WdqTc[:, ks, 128 * os_:128 * os_ + 128],
                                            IN.ks(ks, slice(512 * tt, 512 * tt + 512)),
                                            start=(ks == 0), stop=False)
                        mms.append(f)
                    mms.append(lambda e, yb=yb, os_=os_, qbrowc=qbrowc: e.matmul(
                        yb[:], qbrowc[0:1, 128 * os_:128 * os_ + 128], ones512[0:1, :],
                        start=False, stop=False))
                    mms.append(lambda e, yb=yb, os_=os_, tt=tt: e.matmul(
                        yb[:], lbT[0:32, os_, :], Zlora[0:32, 512 * tt:512 * tt + 512],
                        start=False, stop=not OUT_IS_H))
                    if OUT_IS_H:
                        mms.append(lambda e, yb=yb, os_=os_, tt=tt: e.matmul(
                            yb[:], ident[:], hfeat.ks(os_, slice(512 * tt, 512 * tt + 512)),
                            start=False, stop=True))
                    peg(mms, "eym")
                    mk.wait("scalar", "eym")
                    if OUT_IS_H:
                        mk.op("scalar", lambda e, yb=yb, os_=os_, tt=tt: e.activation(
                            hfeat.ks(os_, slice(512 * tt, 512 * tt + 512)), yb[:], AF.Copy), "eye")
                    else:
                        mk.op("scalar", lambda e, yb=yb, os_=os_: e.activation(
                            ystage[:, os_, :], yb[:], AF.Copy), "eye")
                if not OUT_IS_H:
                    mk.wait("sync", "eye", mk.cnt("eye"))
                    mk.wait("sync", "ezf", mk.cnt("ezf"))
                    for ds in range(ND):
                        mk.dmaT(ytokt[:, :, 128 * ds:128 * ds + 128], ystage[:, ds, :], sem="eytr")
                    mk.wait("vector", "eytr", mk.cnt("eytr"))
                    for sr in range(4):
                        for hf2 in range(2):
                            sem = "ebny" if (sr == 3 and hf2 == 1) else None
                            mk.op("vector", lambda e, sr=sr, hf2=hf2: e.bn_stats(
                                bn6[:, 2 * sr + hf2, 0, :], ytokt[:, sr, 512 * hf2:512 * hf2 + 512]), sem)
                    mk.emit("vector", lambda e: e.drain())
                    for sr in range(4):
                        sem = "ebny2" if sr == 3 else None
                        mk.op("vector", lambda e, sr=sr: e.bn_aggr(
                            lnmv[:, sr, :], bn6[:, 2 * sr:2 * sr + 2, 0, :]), sem)
                    mk.emit("vector", lambda e: e.drain())
                    mk.op("vector", lambda e: e.tensor_scalar(
                        out=lnmv[:, 0:4, 1], in0=lnmv[:, 0:4, 1], scalar1=EPS, scalar2=None,
                        op0=ALU.add), "elnv2")
                    mk.wait("scalar", "elnv2", mk.cnt("elnv2"))
                    mk.op("scalar", lambda e: e.activation(lnmv[:, 0:4, 1], lnmv[:, 0:4, 1], AF.Sqrt), "elns2")
                    mk.wait("vector", "elns2", mk.cnt("elns2"))
                    mk.op("vector", lambda e: e.reciprocal(lnmv[:, 0:4, 1], lnmv[:, 0:4, 1]), None)
                    mk.emit("vector", lambda e: e.drain())
                    for sr in range(4):
                        mk.op("vector", lambda e, sr=sr: e.tensor_scalar(
                            out=ytokt[:, sr, :], in0=ytokt[:, sr, :],
                            scalar1=lnmv[:, sr, 0:1], scalar2=lnmv[:, sr, 1:2],
                            op0=ALU.subtract, op1=ALU.mult), None)
                    mk.emit("vector", lambda e: e.drain())
                    mk.op("vector", lambda e: e.tensor_tensor(
                        ytokt[:], ytokt[:], grep[:, None, :].broadcast_to([128, 4, D]), ALU.mult), None)
                    mk.emit("vector", lambda e: e.drain())
                    mk.op("vector", lambda e: e.tensor_tensor(
                        ytokt[:], ytokt[:], brep[:, None, :].broadcast_to([128, 4, D]), ALU.add), "elngb")
                    mk.wait("scalar", "elngb", mk.cnt("elngb"))
                    mk.op("scalar", lambda e: e.activation(ytokt[:], ytokt[:], AF.Gelu), "egel")
                    mk.wait("sync", "egel", mk.cnt("egel"))
                    for sr in range(4):
                        s_ = 4 * tt + sr
                        for half in range(2):
                            mk.dmaT(z_feat.h[half][:, :, 128 * s_:128 * s_ + 128],
                                    ytokt[:, sr, 512 * half:512 * half + 512], sem="ezf")
            mk.wait("gpsimd", "eym")
            hqm = mk.inc("eqmm", eng="gpsimd")
            mk.emit("gpsimd", lambda e, hqm=hqm: e.sem_inc(hqm, 1))

        mk.wait("gpsimd", "ehcp")
        for b in range(3):
            IN1 = hfeat
            qlora(2 * b, IN1, False, True)
            qlora(2 * b + 1, z_feat, True, False)
            if f"Lh{b + 1}" in mk.debug_taps:
                mk.wait("sync", "eye", mk.cnt("eye"))
                mk.ldbg(f"Lh{b + 1}", [hfeat.h[0][:].rearrange("p s n -> p (s n)"),
                                       hfeat.h[1][:].rearrange("p s n -> p (s n)")], [256, 4 * T], BF16)

        mk.wait("tensor", "eye", mk.cnt("eye"))
        for tt in range(4):
            if tt > 0:
                mk.wait("scalar", "st_out", mk.cnt("st_out"))
            for ds2 in range(ND):
                zb = YR[ds2 % 2]
                if ds2 >= 2 or tt > 0:
                    mk.wait("tensor", "efze", mk.cnt("efze") - 1)
                peg([lambda e, zb=zb, ds2=ds2, tt=tt: e.matmul(
                        zb[:], ident[:], x1_feat.ks(ds2, slice(512 * tt, 512 * tt + 512)),
                        start=True, stop=False),
                     lambda e, zb=zb, ds2=ds2, tt=tt: e.matmul(
                        zb[:], ident[:], hfeat.ks(ds2, slice(512 * tt, 512 * tt + 512)),
                        start=False, stop=True)], "efz")
                mk.wait("scalar", "efz")
                mk.op("scalar", lambda e, zb=zb, ds2=ds2: e.activation(
                    ystage[:, ds2, :], zb[:], AF.Copy), "efze")
            mk.wait("sync", "efze", mk.cnt("efze"))
            for ds in range(ND):
                mk.dmaT(ln2tok[:, :, 128 * ds:128 * ds + 128], ystage[:, ds, :], sem="el2t")
            mk.wait("vector", "el2t", mk.cnt("el2t"))
            for sr in range(4):
                for hf2 in range(2):
                    sem = "ebnf" if (sr == 3 and hf2 == 1) else None
                    mk.op("vector", lambda e, sr=sr, hf2=hf2: e.bn_stats(
                        bn6[:, 2 * sr + hf2, 0, :], ln2tok[:, sr, 512 * hf2:512 * hf2 + 512]), sem)
            mk.emit("vector", lambda e: e.drain())
            for sr in range(4):
                sem = "ebnf2" if sr == 3 else None
                mk.op("vector", lambda e, sr=sr: e.bn_aggr(
                    lnmv[:, sr, :], bn6[:, 2 * sr:2 * sr + 2, 0, :]), sem)
            mk.emit("vector", lambda e: e.drain())
            mk.op("vector", lambda e: e.tensor_scalar(
                out=lnmv[:, 0:4, 1], in0=lnmv[:, 0:4, 1], scalar1=EPS, scalar2=None,
                op0=ALU.add), "efnv")
            mk.wait("scalar", "efnv", mk.cnt("efnv"))
            mk.op("scalar", lambda e: e.activation(lnmv[:, 0:4, 1], lnmv[:, 0:4, 1], AF.Sqrt), "efns")
            mk.wait("vector", "efns", mk.cnt("efns"))
            mk.op("vector", lambda e: e.reciprocal(lnmv[:, 0:4, 1], lnmv[:, 0:4, 1]), None)
            mk.emit("vector", lambda e: e.drain())
            for sr in range(4):
                sem = "efout" if sr == 3 else None
                mk.op("vector", lambda e, sr=sr: e.tensor_scalar(
                    out=outstage[:, sr, :], in0=ln2tok[:, sr, :],
                    scalar1=lnmv[:, sr, 0:1], scalar2=lnmv[:, sr, 1:2],
                    op0=ALU.subtract, op1=ALU.mult), sem)
            mk.wait("sync", "efout", mk.cnt("efout"))
            for sr in range(4):
                s_ = 4 * tt + sr
                mk.dma("sync", out_dram[128 * s_:128 * s_ + 128, :], outstage[:, sr, :], sem="st_out")

    emit_epilogue()


    def tok_dump(t):
        return [t.h[0][:].rearrange("p s d -> p (s d)"), t.h[1][:].rearrange("p s d -> p (s d)")]
    mk.dbg("qn_tok", tok_dump(qn_tok), [256, 8 * D], BF16, "q_tok_rdy")
    mk.dbg("delta_tok", tok_dump(delta_tok), [256, 8 * D], BF16, "mdone")
    if delta_stage == 'all' and delta_macros == NT:
        mk.dbg("ll_tok", tok_dump(ll_tok), [256, 8 * D], BF16, "eie")
        mk.dbg("ls_tok", tok_dump(ls_tok), [256, 8 * D], BF16, "eie")
        mk.dbg("stats_tok", stats_tok[:].rearrange("p s h k -> p (s h k)"), [128, NT * H * 16], BF16, "estats")
        mk.dbg("fw_tok", fw_tok[:].rearrange("p s h j -> p (s h j)"), [128, NT * H * H], F32, "efwr")
        mk.dbg("o_tok", tok_dump(o_tok), [256, 8 * D], BF16, "eon")
        mk.dbg("Whats", Whats[:].rearrange("p a n -> p (a n)"), [FF, 4 * D], BF16, "ewhe")
        mk.dbg("mv4", mv4[:].rearrange("p s h j a -> p (s h j a)"), [128, NT * H * 8], BF16, "estats")

    for nm_ in ("S1", "S1t", "IB", "tmpA", "P2"):
        mk.dbg("inv_" + nm_, invm[nm_][:].rearrange("p h n -> p (h n)"), [128, H * 128], BF16, "mdone")
    mk.dbg("uadj", uadj[:], [128, D], BF16, "mdone")
    mk.dbg("GTq", GTq[:].rearrange("p h n -> p (h n)"), [128, H * 128], BF16, "mdone")
    mk.dbg("GTw", GTw[:].rearrange("p h n -> p (h n)"), [128, H * 128], BF16, "mdone")
    mk.dbg("S_sb", S_sb[:].rearrange("p h k d -> p (h k d)"), [128, H * 2 * DH], BF16, "mdone")
    mk.dbg("w_sl", w_sl[:].rearrange("p h k n -> p (h k n)"), [128, H * 2 * 128], BF16, "mdone")
    mk.dbg("kn_tok", tok_dump(kn_tok), [256, 8 * D], BF16, "k_tok_rdy")
    mk.dbg("v_tok", tok_dump(v_tok), [256, 8 * D], BF16, "v_tok_rdy")
    mk.dbg("beta_tok", beta_tok[:].rearrange("p s h -> p (s h)"), [128, NT * H], F32, "beta_tok_rdy")
    mk.dbg("x_feat", [x_feat.h[0][:].rearrange("p s t -> p (s t)"),
                      x_feat.h[1][:].rearrange("p s t -> p (s t)")], [256, 4 * T], BF16, "tr")

    mk.finalize(out_dram, None)
    return mk


def shard_inputs(inputs):
    x = np.ascontiguousarray(inputs["L0_attn_hidden_states"])
    shared = {
        "wq": np.ascontiguousarray(inputs["L0_attn_q_proj_w"]),
        "wk": np.ascontiguousarray(inputs["L0_attn_k_proj_w"]),
        "wv": np.ascontiguousarray(inputs["L0_attn_v_proj_w"]),
        "wb": np.ascontiguousarray(inputs["L0_attn_b_proj_w"]),
        "cq": np.ascontiguousarray(inputs["L0_attn_q_conv_w"]),
        "ck": np.ascontiguousarray(inputs["L0_attn_k_conv_w"]),
        "cv": np.ascontiguousarray(inputs["L0_attn_v_conv_w"]),
        "fw1": np.ascontiguousarray(inputs["L0_attn_fuse_w1"]),
        "fb1": np.ascontiguousarray(inputs["L0_attn_fuse_b1"]),
        "fw2": np.ascontiguousarray(inputs["L0_attn_fuse_w2"]),
        "fb2": np.ascontiguousarray(inputs["L0_attn_fuse_b2"]),
        "ltemp": np.ascontiguousarray(inputs["L0_attn_logit_temp"]),
        "rss": np.ascontiguousarray(inputs["L0_attn_res_scale_short"]),
        "rsl": np.ascontiguousarray(inputs["L0_attn_res_scale_long"]),
        "onw": np.ascontiguousarray(inputs["L0_attn_o_norm_w"]),
        "wo": np.ascontiguousarray(inputs["L0_attn_o_proj_w"]),
        "firs": np.ascontiguousarray(inputs["L0_attn_fir_short_w"]),
        "firl": np.ascontiguousarray(inputs["L0_attn_fir_long_w"]),
        "qw": np.ascontiguousarray(inputs["L0_ffn_qweights"]),
        "qs": np.ascontiguousarray(inputs["L0_ffn_scales"]),
        "qb": np.ascontiguousarray(inputs["L0_ffn_biases"]),
        "la": np.ascontiguousarray(inputs["L0_ffn_lora_a"]),
        "lb": np.ascontiguousarray(inputs["L0_ffn_lora_b"]),
        "lng": np.ascontiguousarray(inputs["L0_ffn_ln_gamma"]),
        "lnb": np.ascontiguousarray(inputs["L0_ffn_ln_beta"]),
    }
    maps = []
    for c in range(NCORES):
        m = dict(shared)
        m["x"] = np.ascontiguousarray(x[c])
        maps.append(m)
    return maps


def kernel(**inputs):
    mk = build_program()
    nc = mk.build()
    maps = shard_inputs(inputs)
    res = run_bass_kernel_spmd(nc, maps, core_ids=list(range(NCORES)))
    return np.stack([r["out"] for r in res.results], axis=0)


# version 05
# speedup vs baseline: 1.0077x; 1.0017x over previous
import os
import math
import numpy as np
import ml_dtypes
import concourse.bass as bass
import concourse.mybir as mybir
from concourse.bass_utils import run_bass_kernel_spmd

F32 = mybir.dt.float32
BF16 = mybir.dt.bfloat16
I32 = mybir.dt.int32
AF = mybir.ActivationFunctionType
ALU = mybir.AluOpType
BF = ml_dtypes.bfloat16

T = 2048
D = 1024
H = 4
DH = 256
NCORES = 8
EPS = 1e-5
NT = T // 128
ND = D // 128
ENGINES = ("sync", "tensor", "vector", "scalar", "gpsimd")


class MK:
    def __init__(self, debug_taps=(), stop_after=None):
        self.nc = bass.Bass("TRN2", target_bir_lowering=False)
        self.debug_taps = set(debug_taps)
        self.stop_after = stop_after
        self.q = {e: [] for e in ENGINES}
        self.sems = {}
        self.semcnt = {}
        self.resolved = {}
        self.dbg_list = []
        self.consts = {}
        self.regions = {}
        self.nview = 0

    KEEP = {"edst0", "edst1", "emix", "eqmm", "ld", "cast", "tr", "dsl", "dwsc",
            "dwsl", "espl", "elsm", "eztr", "eql", "eqdq",
            "mm", "evp", "hist", "silu", "convg", "diag_b", "beta_mm",
            "beta_sig", "mdone", "duev", "dbce", "dsqe", "dgte", "dat", "dgt",
            "dws", "dout", "doutv", "dsr", "dsu", "dgram", "dbc", "dsq",
            "duw", "dcr", "datr", "btr", "beta_tr", "st_out", "eie",
            "datr_ev", "btr_ev", "etrpt", "etrpe"}

    def _resolve(self, name, eng=None, dma=False):
        if name is None or name in self.KEEP:
            return name
        g = None
        if eng is not None:
            g = f"__d_{eng}" if dma else f"__e_{eng}"
        if name in self.resolved:
            assert g is None or g == self.resolved[name], \
                f"sem {name} incremented from {g} and {self.resolved[name]}"
            return self.resolved[name]
        if g is None:
            raise KeyError(f"sem {name} waited before any inc")
        self.resolved[name] = g
        return g

    def _sem(self, name):
        if name not in self.sems:
            self.sems[name] = self.nc.alloc_semaphore(name)
            self.semcnt[name] = 0
        return self.sems[name]

    def inc(self, name, by=1, eng=None, dma=False):
        name = self._resolve(name, eng, dma)
        h = self._sem(name)
        self.semcnt[name] += by
        return h

    def cnt(self, name):
        if name not in self.KEEP and name in self.resolved:
            name = self.resolved[name]
        elif name not in self.KEEP and name not in self.sems:
            return 0
        self._sem(name)
        return self.semcnt[name]

    def wait(self, eng, name, val=None):
        nm = name
        if nm not in self.KEEP and nm in self.resolved:
            nm = self.resolved[nm]
        elif nm not in self.KEEP and nm not in self.sems:
            if val is None or val == 0:
                return
            raise KeyError(f"wait on unknown sem {name}")
        h = self._sem(nm)
        if val is None:
            val = self.semcnt[nm]
        if val > 0:
            self.q[eng].append(lambda e, h=h, val=val: e.wait_ge(h, val))

    REGION_BASE = 229344 - 12 * 16384

    def place(self, reg, shape, dtype, off=0):
        if isinstance(reg, str):
            reg = self.RN.index(reg)
        self.nview += 1
        return self.nc.alloc_sbuf_tensor_at(
            f"t{self.nview}", shape, dtype, offset=self.REGION_BASE + reg * 16384 + off)

    def sb(self, name, shape, dtype):
        return self.nc.alloc_sbuf_tensor(name, shape, dtype)

    def addr(self, t):
        return self.nc.lookup_mloc(t).addr

    def over(self, t, shape, dtype, off=0):
        self.nview += 1
        return self.nc.alloc_sbuf_tensor_at(
            f"t{self.nview}", shape, dtype, offset=self.addr(t) + off)

    def ps(self, name, shape, dtype=F32):
        return self.nc.alloc_psum_tensor(name, shape, dtype)

    def const(self, name, arr):
        if name not in self.consts:
            self.consts[name] = self.nc.inline_tensor(np.ascontiguousarray(arr), name).ap()
        return self.consts[name]

    def emit(self, eng, fn):
        self.q[eng].append(fn)

    def op(self, eng, fn, sem=None, by=1):
        h = self.inc(sem, by, eng=eng) if sem else None
        def g(e, fn=fn, h=h, by=by):
            ins = fn(e)
            if h is not None:
                ins.then_inc(h, by)
        self.q[eng].append(g)

    def dma(self, eng, dst, src, sem="ld"):
        h = self.inc(sem, 16, eng=eng, dma=True)
        self.q[eng].append(lambda e, h=h, dst=dst, src=src: e.dma_start(dst, src).then_inc(h, 16))

    def dmaT(self, dst, src, sem="tr"):
        h = self.inc(sem, 16, eng="sync", dma=True)
        self.q["sync"].append(lambda e, h=h, dst=dst, src=src: e.dma_start_transpose(dst, src).then_inc(h, 16))

    def ldbg(self, tap, srcs, shape, dtype):
        if tap not in self.debug_taps:
            return
        d = self.nc.dram_tensor("dbg_" + tap, shape, dtype, kind="ExternalOutput").ap()
        if not isinstance(srcs, (list, tuple)):
            srcs = [srcs]
        nrow = shape[0] // len(srcs)
        for i, src in enumerate(srcs):
            self.dma("sync", d[i * nrow:(i + 1) * nrow], src, sem="st_out")

    def dbg(self, tap, src_aps, shape, dtype, ready_sem):
        if tap not in self.debug_taps:
            return
        d = self.nc.dram_tensor("dbg_" + tap, shape, dtype, kind="ExternalOutput").ap()
        self.dbg_list.append((tap, d, src_aps, ready_sem, self.semcnt.get(ready_sem, 0)))

    def finalize(self, out_dram, out_src, ready_sems=()):
        for tap, d, srcs, rsem, rval in self.dbg_list:
            if rsem is not None and rval > 0:
                h = self.sems[rsem]
                self.q["sync"].append(lambda e, h=h, v=rval: e.wait_ge(h, v))
            if not isinstance(srcs, (list, tuple)):
                srcs = [srcs]
            nrow = d.shape[0] // len(srcs)
            for i, src in enumerate(srcs):
                h2 = self.inc("st_out", 16)
                self.q["sync"].append(lambda e, d=d, i=i, nrow=nrow, src=src, h2=h2:
                                      e.dma_start(d[i * nrow:(i + 1) * nrow], src).then_inc(h2, 16))
        if out_src is not None:
            for s in ready_sems:
                self.wait("sync", s)
            h2 = self.inc("st_out", 16)
            self.q["sync"].append(lambda e, h2=h2, out_dram=out_dram, out_src=out_src:
                                  e.dma_start(out_dram, out_src).then_inc(h2, 16))
        v = self.cnt("st_out")
        if v > 0:
            h = self.sems["st_out"]
            self.q["sync"].append(lambda e, h=h, v=v: e.wait_ge(h, v))

    def build(self):
        nc = self.nc
        assert nc.sbuf_base <= self.REGION_BASE, \
            f"normal sbuf allocs ({nc.sbuf_base}) overlap regions ({self.REGION_BASE})"
        with nc.Block() as blk:
            @blk.sync
            def _(e):
                for f in self.q["sync"]:
                    f(e)
            @blk.tensor
            def _(e):
                for f in self.q["tensor"]:
                    f(e)
            @blk.vector
            def _(e):
                for f in self.q["vector"]:
                    f(e)
            @blk.scalar
            def _(e):
                for f in self.q["scalar"]:
                    f(e)
            @blk.gpsimd
            def _(e):
                for f in self.q["gpsimd"]:
                    f(e)
        return nc


class FeatT:
    def __init__(self, mk, r0, r1):
        self.h = [mk.place(r0, [128, 4, T], BF16), mk.place(r1, [128, 4, T], BF16)]

    def ks(self, ks, sl=slice(None)):
        return self.h[ks // 4][:, ks % 4, sl]


class TokT:
    def __init__(self, mk, r0, r1, dtype=BF16, C=D):
        n = 16384 // (mybir.dt.size(dtype) * C)
        self.n = n
        self.h = [mk.place(r0, [128, n, C], dtype), mk.place(r1, [128, n, C], dtype)]

    def s(self, s, sl=slice(None)):
        return self.h[s // self.n][:, s % self.n, sl]


def dmaT_tok_from_feat(mk, tok, feat):
    for ds in range(ND):
        for half in range(2):
            dst = tok.h[half][:, :, 128 * ds:128 * ds + 128]
            mk.dmaT(dst, feat.ks(ds, slice(1024 * half, 1024 * half + 1024)))


def dmaT_feat_from_tok(mk, feat, tok):
    for s in range(NT):
        for half in range(2):
            dst = feat.h[half][:, :, 128 * s:128 * s + 128]
            mk.dmaT(dst, tok.s(s, slice(512 * half, 512 * half + 512)))


def build_program(debug_taps=(), stop_after=None, delta_macros=NT, delta_stage='all'):
    mk = MK(debug_taps, stop_after)
    nc = mk.nc

    x_in = nc.dram_tensor("x", [T, D], F32, kind="ExternalInput").ap()
    wq_in = nc.dram_tensor("wq", [D, D], F32, kind="ExternalInput").ap()
    wk_in = nc.dram_tensor("wk", [D, D], F32, kind="ExternalInput").ap()
    wv_in = nc.dram_tensor("wv", [D, D], F32, kind="ExternalInput").ap()
    wb_in = nc.dram_tensor("wb", [H, D], F32, kind="ExternalInput").ap()
    cq_in = nc.dram_tensor("cq", [D, 4], F32, kind="ExternalInput").ap()
    ck_in = nc.dram_tensor("ck", [D, 4], F32, kind="ExternalInput").ap()
    cv_in = nc.dram_tensor("cv", [D, 4], F32, kind="ExternalInput").ap()
    out_dram = nc.dram_tensor("out", [T, D], F32, kind="ExternalOutput").ap()
    fw1_in = nc.dram_tensor("fw1", [D, D + 16], F32, kind="ExternalInput").ap()
    fb1_in = nc.dram_tensor("fb1", [D], F32, kind="ExternalInput").ap()
    fw2_in = nc.dram_tensor("fw2", [H, D], F32, kind="ExternalInput").ap()
    fb2_in = nc.dram_tensor("fb2", [H], F32, kind="ExternalInput").ap()
    ltemp_in = nc.dram_tensor("ltemp", [1], F32, kind="ExternalInput").ap()
    rss_in = nc.dram_tensor("rss", [1], F32, kind="ExternalInput").ap()
    rsl_in = nc.dram_tensor("rsl", [1], F32, kind="ExternalInput").ap()
    onw_in = nc.dram_tensor("onw", [DH], F32, kind="ExternalInput").ap()
    wo_in = nc.dram_tensor("wo", [D, D], F32, kind="ExternalInput").ap()
    firs_in = nc.dram_tensor("firs", [H, DH, 5], F32, kind="ExternalInput").ap()
    firl_in = nc.dram_tensor("firl", [H, DH, 64], F32, kind="ExternalInput").ap()
    qw_in = nc.dram_tensor("qw", [6, D, D], I32, kind="ExternalInput").ap()
    qs_in = nc.dram_tensor("qs", [6, D, 64], F32, kind="ExternalInput").ap()
    qb_in = nc.dram_tensor("qb", [6, D], F32, kind="ExternalInput").ap()
    la_in = nc.dram_tensor("la", [6, 32, D], F32, kind="ExternalInput").ap()
    lb_in = nc.dram_tensor("lb", [6, D, 32], F32, kind="ExternalInput").ap()
    lng_in = nc.dram_tensor("lng", [3, D], F32, kind="ExternalInput").ap()
    lnb_in = nc.dram_tensor("lnb", [3, D], F32, kind="ExternalInput").ap()
    dscr = nc.dram_tensor("dscr", [2, 128, 8, D], BF16, kind="Internal").ap()

    mk.RN = ["RA0", "RA1", "RB0", "RB1", "RC0", "RC1", "RD0", "RD1", "RE0", "RE1", "RF0", "RF1"]
    ident = mk.sb("ident", [128, 128], BF16)
    cw = mk.sb("cw", [128, ND, 3, 4], F32)
    bst = mk.sb("bst", [H, D], BF16)
    wbT = mk.sb("wbT", [128, ND, H], BF16)
    beta_tok = mk.sb("beta_tok", [128, NT, H], F32)
    rqrk = mk.sb("rqrk", [128, NT, 2 * H], F32)
    sqacc = mk.sb("sqacc", [128, NT, H], F32)
    sqscr = mk.sb("sqscr", [128, 256], BF16)
    pre = mk.sb("pre", [128, 2, 4 + 512 + 4], BF16)
    diag = mk.sb("diag", [128, 2, 4, 128], BF16)

    idc = mk.const("identc", np.eye(128, dtype=np.float32).astype(BF))

    x_tok = TokT(mk, "RA0", "RA1")
    x_feat = FeatT(mk, "RB0", "RB1")
    featst = FeatT(mk, "RC0", "RC1")
    v_tok = TokT(mk, "RD0", "RD1")
    qn_tok = TokT(mk, "RA0", "RA1")
    kn_tok = TokT(mk, "RF0", "RF1")
    wT = [mk.place("RE0", [128, ND, D], BF16), mk.place("RE1", [128, ND, D], BF16)]
    stg32 = mk.place("RD0", [128, 4, 1024], F32)
    bst32 = mk.place("RD1", [H, 1024], F32)
    wstg = mk.place("RF0", [128, ND, D], BF16)
    beta_f = mk.place("RC0", [H, T], F32)
    brow16 = mk.place("RC0", [H, T], BF16, off=8192)

    pp = [mk.ps(f"pp{i}", [128, 512], F32) for i in range(2)]
    pc = [mk.ps(f"pc{i}", [128, 512], F32) for i in range(2)]
    pm = mk.ps("pm", [128, 512], F32)
    pt_bf = mk.ps("pt_bf", [128, 512], BF16)

    mk.dma("sync", cw[:, :, 0, :], cq_in.rearrange("(s p) j -> p s j", p=128))
    mk.dma("sync", cw[:, :, 1, :], ck_in.rearrange("(s p) j -> p s j", p=128))
    mk.dma("sync", cw[:, :, 2, :], cv_in.rearrange("(s p) j -> p s j", p=128))
    mk.dma("sync", ident[:], idc)
    LD_ALL = mk.cnt("ld")

    for qt in range(4):
        if qt > 0:
            mk.wait("sync", "cast", qt)
        mk.dma("sync", stg32[:], x_in[512 * qt:512 * qt + 512].rearrange(
            "(s p) d -> p s d", p=128), sem="lds")
        mk.wait("vector", "lds", mk.cnt("lds"))
        mk.op("vector", lambda e, qt=qt: e.tensor_copy(
            x_tok.h[qt // 2][:, 4 * (qt % 2):4 * (qt % 2) + 4, :], stg32[:]), "cast")

    def load_weight(win, slot, gate=None):
        if gate is not None:
            mk.wait("sync", *gate)
        for hf in range(2):
            mk.wait("sync", "cast")
            mk.dma("sync", stg32[:], win.rearrange("(s p) d -> p s d", p=128)[:, 4 * hf:4 * hf + 4, :],
                   sem="lds")
            mk.wait("vector", "lds", mk.cnt("lds"))
            if hf == 0:
                mk.wait("vector", "tr")
            mk.op("vector", lambda e, hf=hf: e.tensor_copy(
                wstg[:, 4 * hf:4 * hf + 4, :], stg32[:]), "cast")
        mk.wait("sync", "cast")
        for os_ in range(ND):
            mk.dmaT(wT[slot][:, :, 128 * os_:128 * os_ + 128], wstg[:, os_, :])
        return mk.cnt("tr")

    mk.wait("sync", "cast", 4)
    dmaT_feat_from_tok(mk, x_feat, x_tok)
    TR_X = mk.cnt("tr")

    TR_WQ = load_weight(wq_in, 0)
    TR_WK = load_weight(wk_in, 1)
    mk.wait("sync", "cast")
    mk.dma("sync", bst32[:], wb_in, sem="lds")
    mk.wait("vector", "lds", mk.cnt("lds"))
    mk.op("vector", lambda e: e.tensor_copy(bst[:], bst32[:]), "cast")
    CAST_BST = mk.cnt("cast")

    mk.wait("tensor", "ld", LD_ALL)
    mk.wait("tensor", "cast", CAST_BST)
    mk.wait("tensor", "tr", TR_X)
    for ds in range(ND):
        sem = "btr" if ds == ND - 1 else None
        mk.op("tensor", lambda e, ds=ds: e.transpose(
            pt_bf[:, 4 * ds:4 * ds + 4], bst[:, 128 * ds:128 * ds + 128], ident[0:H, 0:H]), sem)
    mk.wait("vector", "btr")
    mk.op("vector", lambda e: e.tensor_copy(wbT[:].rearrange("p s h -> p (s h)"), pt_bf[:, 0:32]), "btr_ev")
    mk.wait("tensor", "btr_ev")
    for tt in range(4):
        if tt > 0:
            mk.wait("tensor", "beta_sig", tt)
        hb = mk.inc("beta_mm")
        for ks in range(ND):
            def f(e, tt=tt, ks=ks, hb=hb, last=(ks == ND - 1)):
                ins = e.matmul(pm[0:4, :], wbT[:, ks, :], x_feat.ks(ks, slice(512 * tt, 512 * tt + 512)),
                               start=(ks == 0), stop=(ks == ND - 1))
                if last:
                    ins.then_inc(hb, 1)
            mk.emit("tensor", f)
        mk.wait("scalar", "beta_mm", tt + 1)
        mk.op("scalar", lambda e, tt=tt: e.activation(
            beta_f[:, 512 * tt:512 * tt + 512], pm[0:4, :], AF.Sigmoid), "beta_sig")
    mk.wait("vector", "beta_sig", 4)
    mk.op("vector", lambda e: e.tensor_copy(brow16[:], beta_f[:]), "beta_cast")
    mk.wait("tensor", "beta_cast")
    for ts_ in range(NT):
        sem = "beta_tr" if ts_ == NT - 1 else None
        mk.op("tensor", lambda e, ts_=ts_: e.transpose(
            pt_bf[:, 4 * ts_:4 * ts_ + 4], brow16[:, 128 * ts_:128 * ts_ + 128], ident[0:H, 0:H]), sem)
    mk.wait("vector", "beta_tr")
    mk.op("vector", lambda e: e.tensor_copy(
        beta_tok[:].rearrange("p s h -> p (s h)"), pt_bf[:, 0:64]), "beta_tok_rdy")

    mk.wait("vector", "ld", LD_ALL)

    G = [0]
    DG = [0]

    def emit_proj(slot, cwi, tr_needed, featst_free=None):
        mk.wait("tensor", "tr", tr_needed)
        if featst_free is not None:
            mk.wait("scalar", "tr", featst_free)
        for os_ in range(ND):
            d = DG[0]
            if d >= 2:
                mk.wait("vector", "convg", 4 * (d - 1))
            for j in range(4):
                sem = "diag_b" if j == 3 else None
                def f(e, par=d % 2, j=j, os_=os_, cwi=cwi):
                    return e.tensor_scalar(out=diag[:, par, j, :], in0=ident[:],
                                           scalar1=cw[:, os_, cwi, 3 - j:4 - j], scalar2=None,
                                           op0=ALU.mult)
                mk.op("vector", f, sem)
            dtarget = mk.cnt("diag_b")
            DG[0] += 1
            for tt in range(4):
                g = G[0]
                bank = pp[g % 2]
                cbank = pc[g % 2]
                par = g % 2
                if g >= 2:
                    mk.wait("tensor", "evp", g - 1)
                hmm_ = mk.inc("mm")
                for ks in range(ND):
                    def f(e, bank=bank, ks=ks, os_=os_, tt=tt, slot=slot, hmm_=hmm_, last=(ks == ND - 1)):
                        ins = e.matmul(bank[:], wT[slot][:, ks, 128 * os_:128 * os_ + 128],
                                       x_feat.ks(ks, slice(512 * tt, 512 * tt + 512)),
                                       start=(ks == 0), stop=(ks == ND - 1))
                        if last:
                            ins.then_inc(hmm_, 1)
                    mk.emit("tensor", f)
                mk.wait("scalar", "mm", g + 1)
                if g >= 1:
                    mk.wait("scalar", "hist", g)
                mk.op("scalar", lambda e, bank=bank, par=par: e.activation(
                    pre[:, par, 4:516], bank[:], AF.Copy), "evp")
                if tt == 0:
                    if g >= 2:
                        mk.wait("vector", "convg", g - 1)
                    mk.op("vector", lambda e, par=par: e.memset(pre[:, par, 0:4], 0.0), "hist")
                else:
                    mk.wait("vector", "evp", g)
                    mk.op("vector", lambda e, par=par: e.tensor_copy(
                        pre[:, par, 0:4], pre[:, 1 - par, 512:516]), "hist")
                mk.wait("tensor", "evp", g + 1)
                mk.wait("tensor", "hist", g + 1)
                if g >= 2:
                    mk.wait("tensor", "silu", g - 1)
                mk.wait("tensor", "diag_b", dtarget)
                hcv = mk.inc("convg")
                for j in range(4):
                    def f(e, cbank=cbank, par2=d % 2, j=j, par=par, hcv=hcv, last=(j == 3)):
                        ins = e.matmul(cbank[:], diag[:, par2, j, :],
                                       pre[:, par, 4 - j:516 - j],
                                       start=(j == 0), stop=(j == 3))
                        if last:
                            ins.then_inc(hcv, 1)
                    mk.emit("tensor", f)
                mk.wait("scalar", "convg", g + 1)
                mk.op("scalar", lambda e, cbank=cbank, os_=os_, tt=tt: e.activation(
                    featst.ks(os_, slice(512 * tt, 512 * tt + 512)), cbank[:], AF.Silu), "silu")
                G[0] += 1

    def emit_tok(nm, tok_dst, norm_col):
        mk.wait("sync", "silu")
        dmaT_tok_from_feat(mk, tok_dst, featst)
        trv = mk.cnt("tr")
        if norm_col is None:
            mk.wait("vector", "tr", trv)
            mk.op("vector", lambda e: e.memset(sqscr[0:1, 0:1], 0.0), nm + "_tok_rdy")
            return
        mk.wait("scalar", "tr", trv)
        for s in range(NT):
            for hh_ in range(H):
                sem = nm + "_sq" if (s == NT - 1 and hh_ == H - 1) else None
                def f(e, s=s, hh_=hh_, tok_dst=tok_dst):
                    return e.activation(sqscr[:], tok_dst.s(s, slice(DH * hh_, DH * hh_ + DH)), AF.Square,
                                        accum_out=sqacc[:, s, hh_:hh_ + 1])
                mk.op("scalar", f, sem)
        mk.wait("scalar", nm + "_sq")
        mk.emit("scalar", lambda e: e.drain())
        mk.op("scalar", lambda e: e.activation(sqacc[:], sqacc[:], AF.Sqrt), nm + "_sqrt")
        mk.wait("vector", nm + "_sqrt")
        mk.op("vector", lambda e, norm_col=norm_col: e.reciprocal(
            rqrk[:, :, norm_col * H:(norm_col + 1) * H], sqacc[:]), nm + "_rs")
        mk.wait("vector", nm + "_rs")
        for s in range(NT):
            for hh_ in range(H):
                sem = nm + "_tok_rdy" if (s == NT - 1 and hh_ == H - 1) else None
                def f(e, s=s, hh_=hh_, tok_dst=tok_dst, norm_col=norm_col):
                    return e.tensor_scalar(out=tok_dst.s(s, slice(DH * hh_, DH * hh_ + DH)),
                                           in0=tok_dst.s(s, slice(DH * hh_, DH * hh_ + DH)),
                                           scalar1=rqrk[:, s, norm_col * H + hh_:norm_col * H + hh_ + 1],
                                           scalar2=None, op0=ALU.mult)
                mk.op("vector", f, sem)

    emit_proj(0, 0, TR_WQ)
    emit_tok("q", qn_tok, 0)
    TR_QTOK = mk.cnt("tr")
    emit_proj(1, 1, TR_WK, featst_free=TR_QTOK)
    TR_WV = load_weight(wv_in, 0, gate=("silu", 32))
    emit_tok("k", kn_tok, 1)
    TR_KTOK = mk.cnt("tr")
    emit_proj(0, 2, TR_WV, featst_free=TR_KTOK)
    emit_tok("v", v_tok, None)

    delta_tok = TokT(mk, "RC0", "RC1")
    pb6 = mk.ps("pb6", [128, 512], F32)
    pb7 = mk.ps("pb7", [128, 512], F32)
    pt2 = pt_bf
    S_sb = mk.sb("S_sb", [128, H, 2, DH], BF16)

    qn_sl = mk.place("RE0", [128, 8, 128], BF16, off=0)
    kn_sl = mk.place("RE0", [128, 8, 128], BF16, off=2048)
    vb_sl = mk.place("RE0", [128, D], BF16, off=4096)
    kbt_sl = mk.place("RE0", [128, D], BF16, off=6144)
    w_sc = mk.place("RE0", [128, H, DH], BF16, off=8192)
    w_sl = mk.place("RE0", [128, H, 2, 128], BF16, off=10240)
    uadj = mk.place("RE0", [128, D], BF16, off=12288)
    GTw = mk.place("RE0", [128, H, 128], BF16, off=14336)
    GTq = mk.place("RE0", [128, H, 128], BF16, off=15360)
    invm = {}
    for i, nm_ in enumerate(["S1", "S1t", "P2", "P2t", "P4", "P4t", "P8", "P8t",
                             "P16", "P16t", "Brl0", "Brl1", "IB", "tmpA"]):
        invm[nm_] = mk.place("RE1", [128, H, 128], BF16, off=1024 * i)

    _blk = np.zeros((128, 128), np.float32)
    for j in range(4):
        _blk[32 * j:32 * j + 32, 32 * j:32 * j + 32] = np.tril(np.ones((32, 32)), -1)
    maskA_c = mk.const("maskA_c", (-_blk).astype(BF))
    _gt = np.zeros((128, 128), np.float32)
    for l in range(4):
        for j in range(4):
            if l < j:
                _gt[32 * l:32 * l + 32, 32 * j:32 * j + 32] = 1.0
            elif l == j:
                _gt[32 * l:32 * l + 32, 32 * j:32 * j + 32] = np.triu(np.ones((32, 32)))
    maskGT_c = mk.const("maskGT_c", _gt.astype(BF))
    maskA = mk.sb("maskA", [128, 128], BF16)
    maskGT = mk.sb("maskGT", [128, 128], BF16)
    mk.dma("gpsimd", maskA[:], maskA_c, sem="ldm")
    mk.dma("gpsimd", maskGT[:], maskGT_c, sem="ldm")

    UA = [pp[0], pp[1]]
    OUT = [pc[0], pc[1]]
    ROT = [pm, pb6]
    DS_BANKS = [pm, pb6, pb7, pp[0]]

    STAGES = ["nil", "dsl", "ts", "mm", "stt", "gram", "inv", "uw", "gtw", "gtq", "gt", "chain", "all"]
    def stage_ge(a):
        return STAGES.index(delta_stage) >= STAGES.index(a)

    def emit_delta():
        mk.op("vector", lambda e: e.memset(S_sb[:].rearrange("p h k d -> p (h k d)"), 0.0), "s_init")
        mk.wait("vector", "ldm")
        for eng in ("tensor", "sync", "vector", "scalar"):
            mk.wait(eng, "q_tok_rdy")
            mk.wait(eng, "k_tok_rdy")
            mk.wait(eng, "v_tok_rdy")
            mk.wait(eng, "beta_tok_rdy")
        mk.wait("tensor", "s_init")
        mk.wait("tensor", "ldm")

        for m in range(delta_macros):
            if m > 0:
                mk.wait("sync", "mdone", m)
            if not stage_ge("dsl"):
                mk.op("vector", lambda e: e.memset(sqscr[0:1, 0:1], 0.0), "mdone")
                continue
            mk.dmaT(qn_sl[:], qn_tok.s(m), sem="dsl")
            mk.dmaT(kn_sl[:], kn_tok.s(m), sem="dsl")
            DSL = mk.cnt("dsl")
            if not stage_ge("ts"):
                mk.wait("vector", "dsl", DSL)
                mk.op("vector", lambda e: e.memset(sqscr[0:1, 0:1], 0.0), "mdone")
                continue
            if m > 0:
                mk.wait("vector", "mdone", m)
            for hh_ in range(H):
                mk.op("vector", lambda e, hh_=hh_, m=m: e.tensor_scalar(
                    out=vb_sl[:, DH * hh_:DH * hh_ + DH], in0=v_tok.s(m, slice(DH * hh_, DH * hh_ + DH)),
                    scalar1=beta_tok[:, m, hh_:hh_ + 1], scalar2=None, op0=ALU.mult),
                    "dts" if hh_ == H - 1 else None)
            for hh_ in range(H):
                mk.op("vector", lambda e, hh_=hh_, m=m: e.tensor_scalar(
                    out=kbt_sl[:, DH * hh_:DH * hh_ + DH], in0=kn_tok.s(m, slice(DH * hh_, DH * hh_ + DH)),
                    scalar1=beta_tok[:, m, hh_:hh_ + 1], scalar2=None, op0=ALU.mult),
                    "dts2" if hh_ == H - 1 else None)
            if not stage_ge("mm"):
                mk.wait("vector", "dts")
                mk.wait("vector", "dts2")
                mk.wait("vector", "dsl", DSL)
                mk.op("vector", lambda e: e.memset(sqscr[0:1, 0:1], 0.0), "mdone")
                continue
            mk.wait("tensor", "dsl", DSL)
            hgr = mk.inc("dgram")
            for hh_ in range(H):
                for ks in range(2):
                    def f(e, hh_=hh_, ks=ks, hgr=hgr, last=(hh_ == H - 1 and ks == 1)):
                        ins = e.matmul(ROT[0][:, 128 * hh_:128 * hh_ + 128],
                                       kn_sl[:, 2 * hh_ + ks, :], kn_sl[:, 2 * hh_ + ks, :],
                                       start=(ks == 0), stop=(ks == 1))
                        if last:
                            ins.then_inc(hgr, 1)
                    mk.emit("tensor", f)
            if not stage_ge("stt"):
                mk.wait("vector", "dgram")
                mk.op("vector", lambda e: e.memset(sqscr[0:1, 0:1], 0.0), "mdone")
                continue
            mk.wait("vector", "dgram")
            for hh_ in range(H):
                mk.op("vector", lambda e, hh_=hh_, m=m: e.scalar_tensor_tensor(
                    out=invm["S1"][:, hh_, :], in0=ROT[0][:, 128 * hh_:128 * hh_ + 128],
                    scalar=beta_tok[:, m, hh_:hh_ + 1], in1=maskA[:],
                    op0=ALU.mult, op1=ALU.mult), None)
            for hh_ in range(H):
                mk.op("vector", lambda e, hh_=hh_, m=m: e.tensor_scalar(
                    out=invm["tmpA"][:, hh_, :], in0=ROT[0][:, 128 * hh_:128 * hh_ + 128],
                    scalar1=beta_tok[:, m, hh_:hh_ + 1], scalar2=None, op0=ALU.mult),
                    "da0")
            mk.wait("tensor", "da0")
            htr = mk.inc("datr")
            for hh_ in range(H):
                def f(e, hh_=hh_, htr=htr, last=(hh_ == H - 1)):
                    ins = e.transpose(pt2[:, 128 * hh_:128 * hh_ + 128], invm["S1"][:, hh_, :], ident[:])
                    if last:
                        ins.then_inc(htr, 1)
                mk.emit("tensor", f)
            mk.wait("vector", "datr")
            mk.op("vector", lambda e: e.tensor_copy(
                invm["S1t"][:].rearrange("p h n -> p (h n)"), pt2[:]), "datr_ev")
            if not stage_ge("inv"):
                mk.wait("vector", "datr_ev")
                mk.op("vector", lambda e: e.memset(sqscr[0:1, 0:1], 0.0), "mdone")
                continue
            sq_plan = [("P2", "S1t", "S1"), ("P2t", "S1", "S1t"),
                       ("P4", "P2t", "P2"), ("P4t", "P2", "P2t"),
                       ("P8", "P4t", "P4"), ("P8t", "P4", "P4t"),
                       ("P16", "P8t", "P8"), ("P16t", "P8", "P8t")]
            mk.wait("tensor", "datr_ev")
            for si, (dstn, ln, rn) in enumerate(sq_plan):
                rb = ROT[si % 2]
                if si >= 2:
                    mk.wait("tensor", "dsqe", mk.cnt("dsqe") - 1)
                hsq = mk.inc("dsq")
                for hh_ in range(H):
                    def f(e, hh_=hh_, rb=rb, ln=ln, rn=rn, hsq=hsq, last=(hh_ == H - 1)):
                        ins = e.matmul(rb[:, 128 * hh_:128 * hh_ + 128],
                                       invm[ln][:, hh_, :], invm[rn][:, hh_, :],
                                       start=True, stop=True)
                        if last:
                            ins.then_inc(hsq, 1)
                    mk.emit("tensor", f)
                mk.wait("vector", "dsq")
                mk.op("vector", lambda e, rb=rb, dstn=dstn: e.tensor_copy(
                    invm[dstn][:].rearrange("p h n -> p (h n)"), rb[:]), "dsqe")
            bcur = "S1t"
            for r, (pn, ptn) in enumerate([("P2", "P2t"), ("P4", "P4t"), ("P8", "P8t"), ("P16", "P16t")]):
                rb = ROT[r % 2]
                mk.wait("tensor", "dsqe")
                if r >= 1:
                    mk.wait("tensor", "dbce", mk.cnt("dbce"))
                hbc = mk.inc("dbc")
                for hh_ in range(H):
                    def f(e, hh_=hh_, rb=rb, pn=pn, ptn=ptn, bcur=bcur, hbc=hbc, last=(hh_ == H - 1)):
                        e.matmul(rb[:, 128 * hh_:128 * hh_ + 128], invm[pn][:, hh_, :],
                                 invm[bcur][:, hh_, :], start=True, stop=False)
                        e.matmul(rb[:, 128 * hh_:128 * hh_ + 128], ident[:],
                                 invm[bcur][:, hh_, :], start=False, stop=False)
                        ins = e.matmul(rb[:, 128 * hh_:128 * hh_ + 128], ident[:],
                                       invm[ptn][:, hh_, :], start=False, stop=True)
                        if last:
                            ins.then_inc(hbc, 1)
                    mk.emit("tensor", f)
                nxt = "Brl%d" % (r % 2) if r < 3 else "IB"
                mk.wait("vector", "dbc")
                if r < 3:
                    mk.op("vector", lambda e, rb=rb, nxt=nxt: e.tensor_copy(
                        invm[nxt][:].rearrange("p h n -> p (h n)"), rb[:]), "dbce")
                else:
                    for hh_ in range(H):
                        mk.op("vector", lambda e, rb=rb, hh_=hh_: e.tensor_tensor(
                            invm["IB"][:, hh_, :], rb[:, 128 * hh_:128 * hh_ + 128],
                            ident[:], ALU.add), "dbce" if hh_ == H - 1 else None)
                bcur = nxt
            if not stage_ge("uw"):
                mk.wait("vector", "dbce")
                mk.op("vector", lambda e: e.memset(sqscr[0:1, 0:1], 0.0), "mdone")
                continue
            mk.wait("tensor", "dbce")
            mk.wait("tensor", "dts")
            mk.wait("tensor", "dts2")
            huw = mk.inc("duw")
            for hh_ in range(H):
                def f(e, hh_=hh_):
                    e.matmul(UA[hh_ // 2][:, 256 * (hh_ % 2):256 * (hh_ % 2) + 256],
                             invm["IB"][:, hh_, :], vb_sl[:, DH * hh_:DH * hh_ + DH],
                             start=(hh_ % 2 == 0), stop=False)
                mk.emit("tensor", f)
            for hh_ in range(H):
                def f(e, hh_=hh_, huw=huw, last=(hh_ == H - 1)):
                    ins = e.matmul(ROT[hh_ // 2][:, 256 * (hh_ % 2):256 * (hh_ % 2) + 256],
                                   invm["IB"][:, hh_, :], kbt_sl[:, DH * hh_:DH * hh_ + DH],
                                   start=(hh_ % 2 == 0), stop=(hh_ % 2 == 1))
                    if last:
                        ins.then_inc(huw, 1)
                mk.emit("tensor", f)
            mk.wait("vector", "duw")
            for half in range(2):
                mk.op("vector", lambda e, half=half: e.tensor_scalar(
                    out=w_sc[:, 2 * half:2 * half + 2, :].rearrange("p a b -> p (a b)"),
                    in0=ROT[half][:], scalar1=-1.0, scalar2=None, op0=ALU.mult),
                    "dwsc" if half == 1 else None)
            WSC = mk.cnt("dwsc")
            mk.wait("sync", "dwsc", WSC)
            for hh_ in range(H):
                mk.dmaT(w_sl[:, hh_, :, :], w_sc[:, hh_, :], sem="dwsl")
            DWSL = mk.cnt("dwsl")
            if not stage_ge("gtw"):
                mk.wait("vector", "dwsc", WSC)
                mk.op("vector", lambda e: e.memset(sqscr[0:1, 0:1], 0.0), "mdone")
                continue
            mk.wait("tensor", "dwsc", WSC)
            if m > 0:
                mk.wait("tensor", "dgte", mk.cnt("dgte"))
            hgt = mk.inc("dgt")
            if stage_ge("gtw"):
                for hh_ in range(H):
                    def f(e, hh_=hh_):
                        e.matmul(pb7[:, 128 * hh_:128 * hh_ + 128], invm["tmpA"][:, hh_, :],
                                 invm["IB"][:, hh_, :], start=True, stop=True)
                    mk.emit("tensor", f)
            if stage_ge("gtq"):
                for hh_ in range(H):
                    for ks in range(2):
                        def f(e, hh_=hh_, ks=ks, hgt=hgt, last=(hh_ == H - 1 and ks == 1)):
                            ins = e.matmul(ROT[0][:, 128 * hh_:128 * hh_ + 128],
                                           kn_sl[:, 2 * hh_ + ks, :], qn_sl[:, 2 * hh_ + ks, :],
                                           start=(ks == 0), stop=(ks == 1))
                            if last:
                                ins.then_inc(hgt, 1)
                        mk.emit("tensor", f)
            else:
                def f(e, hgt=hgt):
                    e.matmul(ROT[0][:, 0:128], invm["tmpA"][:, 0, :],
                             invm["IB"][:, 0, :], start=True, stop=True).then_inc(hgt, 1)
                mk.emit("tensor", f)
            mk.wait("vector", "dgt")
            if stage_ge("gtw"):
                mk.op("vector", lambda e: e.tensor_scalar(
                    out=GTw[:].rearrange("p h n -> p (h n)"), in0=pb7[:],
                    scalar1=-1.0, scalar2=None, op0=ALU.mult), None)
            if stage_ge("gtq"):
                for hh_ in range(H):
                    mk.op("vector", lambda e, hh_=hh_: e.tensor_tensor(
                        GTq[:, hh_, :], ROT[0][:, 128 * hh_:128 * hh_ + 128], maskGT[:], ALU.mult),
                        "dgte" if hh_ == H - 1 else None)
            else:
                mk.op("vector", lambda e: e.memset(GTq[:, 0, 0:1], 0.0), "dgte")
            if not stage_ge("chain"):
                mk.wait("vector", "dgte")
                mk.op("vector", lambda e: e.memset(sqscr[0:1, 0:1], 0.0), "mdone")
                continue
            mk.wait("tensor", "dwsl", DWSL)
            mk.wait("tensor", "dgte")
            if m > 0:
                mk.wait("tensor", "dout")
                mk.wait("tensor", "doutv")
            hws = mk.inc("dws")
            for hh_ in range(H):
                for ks in range(2):
                    def f(e, hh_=hh_, ks=ks, hws=hws, last=(hh_ == H - 1 and ks == 1)):
                        ins = e.matmul(UA[hh_ // 2][:, 256 * (hh_ % 2):256 * (hh_ % 2) + 256],
                                       w_sl[:, hh_, ks, :], S_sb[:, hh_, ks, :],
                                       start=False, stop=(ks == 1))
                        if last:
                            ins.then_inc(hws, 1)
                    mk.emit("tensor", f)
            for hh_ in range(H):
                for ks in range(2):
                    def f(e, hh_=hh_, ks=ks):
                        e.matmul(OUT[hh_ // 2][:, 256 * (hh_ % 2):256 * (hh_ % 2) + 256],
                                 qn_sl[:, 2 * hh_ + ks, :], S_sb[:, hh_, ks, :],
                                 start=(ks == 0 and hh_ % 2 == 0), stop=(ks == 1))
                    mk.emit("tensor", f)
            for j in range(4):
                if j > 0:
                    mk.wait("tensor", "duev", mk.cnt("duev"))
                    hcr = mk.inc("dcr")
                    for hh_ in range(H):
                        def f(e, j=j, hh_=hh_, hcr=hcr, last=(hh_ == H - 1)):
                            ins = e.matmul(
                                UA[hh_ // 2][32 * j:32 * j + 32, 256 * (hh_ % 2):256 * (hh_ % 2) + 256],
                                GTw[0:32 * j, hh_, 32 * j:32 * j + 32],
                                uadj[0:32 * j, DH * hh_:DH * hh_ + DH],
                                start=False, stop=True, tile_position=(0, 32 * j))
                            if last:
                                ins.then_inc(hcr, 1)
                        mk.emit("tensor", f)
                    for hh_ in range(H):
                        def f(e, j=j, hh_=hh_):
                            e.matmul(
                                OUT[hh_ // 2][32 * j:32 * j + 32, 256 * (hh_ % 2):256 * (hh_ % 2) + 256],
                                GTq[0:32 * j, hh_, 32 * j:32 * j + 32],
                                uadj[0:32 * j, DH * hh_:DH * hh_ + DH],
                                start=False, stop=True, tile_position=(0, 32 * j))
                        mk.emit("tensor", f)
                    mk.wait("vector", "dcr")
                    mk.wait("scalar", "dcr")
                else:
                    mk.wait("vector", "dws")
                    mk.wait("scalar", "dws")
                mk.op("vector", lambda e, j=j: e.tensor_copy(
                    uadj[32 * j:32 * j + 32, 0:512], UA[0][32 * j:32 * j + 32, :]), "duev")
                mk.op("scalar", lambda e, j=j: e.activation(
                    uadj[32 * j:32 * j + 32, 512:1024], UA[1][32 * j:32 * j + 32, :], AF.Copy), "duev")
                mk.wait("tensor", "duev", mk.cnt("duev"))
                hat = mk.inc("dat")
                for hh_ in range(H):
                    def f(e, j=j, hh_=hh_, hat=hat, last=(hh_ == H - 1)):
                        ins = e.matmul(
                            OUT[hh_ // 2][32 * j:32 * j + 32, 256 * (hh_ % 2):256 * (hh_ % 2) + 256],
                            GTq[32 * j:32 * j + 32, hh_, 32 * j:32 * j + 32],
                            uadj[32 * j:32 * j + 32, DH * hh_:DH * hh_ + DH],
                            start=False, stop=True, tile_position=(32 * j, 32 * j))
                        if last:
                            ins.then_inc(hat, 1)
                    mk.emit("tensor", f)
            hds = mk.inc("dsu")
            for hh_ in range(H):
                for ks in range(2):
                    def f(e, hh_=hh_, ks=ks, m=m, hds=hds, last=(hh_ == H - 1 and ks == 1)):
                        ins = e.matmul(DS_BANKS[hh_][:, 256 * ks:256 * ks + 256],
                                       kn_tok.s(m, slice(DH * hh_ + 128 * ks, DH * hh_ + 128 * ks + 128)),
                                       uadj[:, DH * hh_:DH * hh_ + DH],
                                       start=(ks == 0), stop=(ks == 1))
                        if last:
                            ins.then_inc(hds, 1)
                    mk.emit("tensor", f)
            mk.wait("vector", "dat")
            mk.wait("scalar", "dat")
            mk.op("vector", lambda e, m=m: e.tensor_copy(delta_tok.s(m, slice(0, 512)), OUT[0][:]), "doutv")
            mk.op("scalar", lambda e, m=m: e.activation(delta_tok.s(m, slice(512, 1024)), OUT[1][:], AF.Copy), "dout")
            mk.wait("vector", "dsu")
            for hh_ in range(H):
                mk.op("vector", lambda e, hh_=hh_: e.tensor_tensor(
                    S_sb[:, hh_, :, :].rearrange("p k d -> p (k d)"),
                    S_sb[:, hh_, :, :].rearrange("p k d -> p (k d)"),
                    DS_BANKS[hh_][:], ALU.add), "dsr" if hh_ == H - 1 else None)
            mk.wait("vector", "dout")
            mk.wait("vector", "dsr")
            mk.op("vector", lambda e: e.memset(sqscr[0:1, 0:1], 0.0), "mdone")
        mk.wait("sync", "mdone", delta_macros)

    emit_delta()

    FL, FF = 192, 97

    def dftc():
        l = np.arange(FL)[:, None]
        f = np.arange(FF)[None, :]
        ang = 2.0 * np.pi * l * f / FL
        Fre, Fim = np.cos(ang), -np.sin(ang)
        lo = np.arange(64, FL)[None, :]
        angi = 2.0 * np.pi * f.T * lo / FL
        c = np.full((FF, 1), 2.0); c[0, 0] = 1.0; c[96, 0] = 1.0
        iRe = (c / FL) * np.cos(angi)
        iIm = -(c / FL) * np.sin(angi)
        return Fre, Fim, iRe, iIm

    Fre_np, Fim_np, iRe_np, iIm_np = dftc()
    FB_np = np.stack([Fre_np[64:192], Fim_np[64:192]], 1)
    FA_np = np.zeros((128, 2, FF), np.float64)
    FA_np[64:128, 0, :] = Fre_np[0:64]
    FA_np[64:128, 1, :] = Fim_np[0:64]
    iF_np = np.stack([iRe_np, iIm_np], 1)
    def fwmat(K):
        k = np.arange(K)[:, None]
        f = np.arange(FF)[None, :]
        ang = 2.0 * np.pi * f * (K - 1 - k) / FL
        return np.stack([np.cos(ang), -np.sin(ang)], 1)
    FAc = mk.sb("FAc", [128, 2, FF], BF16)
    FBc = mk.sb("FBc", [128, 2, FF], BF16)
    iFc = mk.sb("iFc", [FF, 2, 128], BF16)
    Fw64 = mk.sb("Fw64", [64, 2, FF], BF16)
    Fw5 = mk.sb("Fw5", [5, 2, FF], BF16)
    mk.dma("sync", FAc[:], mk.const("FAc_c", FA_np.astype(BF)), sem="lde")
    mk.dma("sync", FBc[:], mk.const("FBc_c", FB_np.astype(BF)), sem="lde")
    mk.dma("sync", iFc[:], mk.const("iFc_c", iF_np.astype(BF)), sem="lde")
    mk.dma("sync", Fw64[:], mk.const("Fw64_c", fwmat(64).astype(BF)), sem="lde")
    mk.dma("sync", Fw5[:], mk.const("Fw5_c", fwmat(5).astype(BF)), sem="lde")

    W2T = mk.sb("W2T", [128, ND, H], BF16)
    b1f = mk.sb("b1f", [128, ND], F32)
    b2row = mk.sb("b2row", [1, H], BF16)
    b2rep = mk.sb("b2rep", [128, H], F32)
    onwrow = mk.sb("onwrow", [1, DH], BF16)
    assert mk.addr(sqacc) + 256 == mk.addr(wbT) + 1088
    onwrep = mk.over(wbT, [128, DH], BF16, 0)
    rtemp = mk.over(wbT, [128, 1], F32, 512)
    rss128 = mk.over(wbT, [128, 1], F32, 544)
    rsl128 = mk.over(wbT, [128, 1], F32, 576)
    rmsacc = mk.over(wbT, [128, NT, H], F32, 608)
    scal3 = mk.over(cw, [1, 8], F32, 0)
    onesrow = mk.sb("onesrow", [1, 128], BF16)
    mk.dma("sync", onesrow[:], mk.const("ones_c", np.ones((1, 128), np.float32).astype(BF)), sem="lde")
    ones512_c = mk.const("ones512_c", np.ones((1, 512), np.float32).astype(BF))
    assert mk.addr(maskA) == mk.addr(S_sb) + 4096
    assert mk.addr(maskGT) == mk.addr(maskA) + 256
    mv4 = mk.over(S_sb, [128, NT, H, 4, 2], BF16, 0)
    bn6 = mk.over(S_sb, [128, NT, H, 6], F32, 1024)
    aacc = mk.over(S_sb, [128, NT, H, 4], F32, 2560)
    ones512 = mk.over(S_sb, [1, 512], BF16, 3584)
    stats_tok = mk.place("RC1", [128, NT, H, 16], BF16, off=6144)
    logits_tok = mk.over(diag, [128, NT, H, H], F32, 0)
    fw_tok = mk.over(diag, [128, NT, H, H], F32, 1024)
    smax = mk.over(pre, [128, NT, H, 1], F32, 1696)

    W1xT = mk.place("RE0", [128, ND, D], BF16)
    WoT = mk.place("RE1", [128, ND, D], BF16)
    stg32b = mk.place("RC0", [128, 4, 1024], F32)
    wstgb = mk.place("RC1", [128, ND, D], BF16)
    Xsb = mk.place("RC0", [FF, 2, D], BF16, off=0)
    t_ab = mk.place("RC0", [FF, 2, D], BF16, off=4096)
    Ysb = mk.place("RC0", [FF, 2, D], BF16, off=8192)
    Whats = mk.place("RC1", [FF, 4, D], BF16, off=0)
    firT64 = mk.place("RC1", [64, ND, 128], BF16, off=8192)
    firT5 = mk.place("RC1", [5, ND, 128], BF16, off=10240)
    W1sT = mk.place("RC1", [16, D], BF16, off=12288)
    ll_tok = TokT(mk, "RA0", "RA1")
    ls_tok = TokT(mk, "RF0", "RF1")
    o_tok = ls_tok
    G0t = mk.place("RC0", [128, ND, 512], BF16, off=0)
    hdnt = mk.place("RC0", [128, ND, 512], BF16, off=8192)
    statsKt = mk.place("RC1", [16, H, 4, 128], BF16, off=2048)
    dstage = mk.place("RC0", [128, 2, D], BF16, off=0)
    mixA = mk.place("RC0", [128, D], BF16, off=4096)
    mixB = mk.place("RC0", [128, D], BF16, off=6144)
    zstage = mk.place("RC0", [128, ND, 512], BF16, off=0)
    ztokt = mk.place("RC0", [128, 4, D], BF16, off=8192)
    o_feat = FeatT(mk, "RA0", "RA1")
    x1_feat = o_feat
    hfeat = FeatT(mk, "RC0", "RC1")
    qwstg = mk.place("RB0", [128, 4, 1024], I32)
    qprim = mk.place("RB1", [128, 4, 1024], BF16)
    qbrow = mk.place("RB1", [1, D], BF16, off=8192)
    lngrow = mk.place("RB1", [1, D], BF16, off=10240)
    lnbrow = mk.place("RB1", [1, D], BF16, off=12288)
    qsF = mk.place("RB1", [128, ND, 64], F32, off=14336)
    WdqT = mk.place("RE0", [128, ND, D], BF16)
    WdqT2 = mk.place("RE1", [128, ND, D], BF16)
    WdqTs = [WdqT, WdqT2]
    qbrow2 = mk.place("RD1", [1, D], BF16, off=13312)
    qbrows = None
    qbrows = [qbrow, qbrow2]
    ystage = mk.place("RD0", [128, ND, 512], BF16, off=0)
    ytokt = mk.place("RD0", [128, 4, D], BF16, off=8192)
    Zlora = mk.place("RD1", [32, T], BF16, off=0)
    grep = mk.place("RD1", [128, D], BF16, off=4096)
    brep = mk.place("RD1", [128, D], BF16, off=6144)
    laT = mk.place("RD1", [128, ND, 32], BF16, off=8192)
    lbT = mk.place("RD1", [32, ND, 128], BF16, off=8704)
    larow = mk.place("RD1", [32, D], BF16, off=10752)
    lbstg = mk.place("RD1", [128, ND, 32], BF16, off=12800)
    outstage = mk.place("RD0", [128, 4, D], F32, off=0)
    ln2tok = mk.place("RD1", [128, 4, D], BF16, off=8192)
    z_feat = FeatT(mk, "RF0", "RF1")
    w1s_stg = mk.over(pre, [128, ND, 16], BF16, 1120)
    lnmv = mk.over(pre, [128, ND, 2], F32, 1376)
    sder = mk.over(pre, [128, NT, H], F32, 1440)

    ROT3 = [pm, pb6, pb7]

    def eng_gate(name):
        for eng in ("tensor", "vector", "scalar", "sync", "gpsimd"):
            mk.wait(eng, name)

    def stats_pass(X, j, gate_sem):
        mk.wait("vector", gate_sem)
        mk.wait("scalar", gate_sem)
        for s in range(NT):
            for hh_ in range(H):
                sem = "ebn" if (s == NT - 1 and hh_ == H - 1) else None
                mk.op("vector", lambda e, s=s, hh_=hh_, X=X: e.bn_stats(
                    bn6[:, s, hh_, :], X.s(s, slice(DH * hh_, DH * hh_ + DH))), sem)
        mk.emit("vector", lambda e: e.drain())
        for s in range(NT):
            for hh_ in range(H):
                sem = "ebn2" if (s == NT - 1 and hh_ == H - 1) else None
                mk.op("vector", lambda e, s=s, hh_=hh_, j=j: e.bn_aggr(
                    mv4[:, s, hh_, j, :], bn6[:, s, hh_, :]), sem)
        for s in range(NT):
            for hh_ in range(H):
                sem = "eab" if (s == NT - 1 and hh_ == H - 1) else None
                mk.op("scalar", lambda e, s=s, hh_=hh_, j=j, X=X: e.activation(
                    sqscr[:], X.s(s, slice(DH * hh_, DH * hh_ + DH)), AF.Abs,
                    accum_out=aacc[:, s, hh_, j:j + 1]), sem)

    def peg(mmfns, sem):
        h = mk.inc(sem, eng="tensor")
        n = len(mmfns)
        for i, fn in enumerate(mmfns):
            def g(e, fn=fn, h=h, last=(i == n - 1)):
                ins = fn(e)
                if last:
                    ins.then_inc(h, 1)
            mk.emit("tensor", g)

    def emit_epilogue():
        stats_pass(delta_tok, 2, "mdone")
        stats_pass(v_tok, 3, "v_tok_rdy")
        mk.wait("sync", "mdone")
        mk.ldbg("Ldelta", [delta_tok.h[0][:].rearrange("p s n -> p (s n)"),
                           delta_tok.h[1][:].rearrange("p s n -> p (s n)")], [256, 8 * D], BF16)
        for half in range(2):
            mk.dma("sync", dscr[half], delta_tok.h[half][:], sem="espl")
        SPL = mk.cnt("espl")
        mk.wait("sync", "espl", SPL)
        if "Ldsc3" in mk.debug_taps:
            mk.dma("sync", wstgb[:, 0, :], dscr[0, :, 3, :], sem="edbg")
            mk.wait("sync", "edbg", mk.cnt("edbg"))
            mk.ldbg("Ldsc3", wstgb[:, 0, :], [128, D], BF16)
        mk.wait("sync", "ebn")
        mk.wait("sync", "eab")
        mk.dma("sync", ones512[:], ones512_c, sem="elds")

        def load_weightE(win, dstT):
            for hf in range(2):
                mk.wait("sync", "ecast")
                mk.dma("sync", stg32b[:], win.rearrange("(s p) d -> p s d", p=128)[:, 4 * hf:4 * hf + 4, :],
                       sem="elds")
                mk.wait("vector", "elds", mk.cnt("elds"))
                if hf == 0:
                    mk.wait("vector", "etr")
                mk.op("vector", lambda e, hf=hf: e.tensor_copy(
                    wstgb[:, 4 * hf:4 * hf + 4, :], stg32b[:]), "ecast")
            mk.wait("sync", "ecast")
            for os_ in range(ND):
                mk.dmaT(dstT[:, :, 128 * os_:128 * os_ + 128], wstgb[:, os_, :], sem="etr")
            return mk.cnt("etr")

        TR_W1X = load_weightE(fw1_in[:, 0:D], W1xT)
        TR_WO = load_weightE(wo_in, WoT)
        firstg = mk.over(pre, [128, ND, 64], BF16, 0)
        firstg5 = mk.over(pre, [128, ND, 5], BF16, 1024)
        mk.wait("gpsimd", "convg")
        mk.wait("gpsimd", "diag_b")
        mk.wait("gpsimd", "beta_tr")
        mk.dma("gpsimd", w1s_stg[:], fw1_in[:, D:D + 16].rearrange("(s p) j -> p s j", p=128), sem="elsm")
        mk.dma("gpsimd", bst[:], fw2_in, sem="elsm")
        mk.dma("gpsimd", b2row[:], fb2_in[None, :], sem="elsm")
        mk.dma("gpsimd", onwrow[:], onw_in[None, :], sem="elsm")
        mk.dma("gpsimd", scal3[0:1, 0:1], ltemp_in[None, :], sem="elsm")
        mk.dma("gpsimd", scal3[0:1, 1:2], rss_in[None, :], sem="elsm")
        mk.dma("gpsimd", scal3[0:1, 2:3], rsl_in[None, :], sem="elsm")
        mk.dma("gpsimd", firstg[:], firl_in.rearrange("h (s p) k -> p (h s) k", p=128), sem="elsm")
        mk.dma("gpsimd", firstg5[:], firs_in.rearrange("h (s p) k -> p (h s) k", p=128), sem="elsm")
        h_ = mk.inc("elsm", 16, eng="gpsimd", dma=True)
        mk.emit("gpsimd", lambda e, h_=h_: e.dma_start(
            b1f[:], fb1_in.rearrange("(s p) -> p s", p=128),
            allow_slow_non_contiguous=True).then_inc(h_, 16))
        ELSM = mk.cnt("elsm")
        mk.wait("tensor", "elsm", ELSM)
        mk.wait("tensor", "lde")

        mk.wait("vector", "etr")

        def tpair(tin_fn, nin, nout, out_ap):
            mk.wait("tensor", "etrpe")
            mk.op("tensor", lambda e, tin_fn=tin_fn, nin=nin, nout=nout: e.transpose(
                pt_bf[0:nout, 0:nin], tin_fn(), ident[0:nin, 0:nin]), "etrpt")
            mk.wait("vector", "etrpt")
            mk.op("vector", lambda e, out_ap=out_ap, nin=nin, nout=nout: e.tensor_copy(
                out_ap, pt_bf[0:nout, 0:nin]), "etrpe")

        for ds in range(ND):
            tpair(lambda ds=ds: firstg[:, ds, :], 128, 64, firT64[:, ds, :])
        for ds in range(ND):
            tpair(lambda ds=ds: firstg5[:, ds, :], 128, 5, firT5[:, ds, :])
        for ds in range(ND):
            tpair(lambda ds=ds: w1s_stg[:, ds, :], 128, 16, W1sT[:, 128 * ds:128 * ds + 128])
        for ds in range(ND):
            tpair(lambda ds=ds: bst[:, 128 * ds:128 * ds + 128], 4, 128, W2T[:, ds, :])
        mk.dbg("firT64", firT64[:].rearrange("p s n -> p (s n)"), [64, ND * 128], BF16, "etrpe")
        mk.dbg("firstg", firstg[:].rearrange("p s n -> p (s n)"), [128, ND * 64], BF16, "etrpe")

        mk.wait("scalar", "elsm", ELSM)
        mk.op("scalar", lambda e: e.activation(scal3[0:1, 3:4], scal3[0:1, 0:1], AF.Abs), None)
        mk.emit("scalar", lambda e: e.drain())
        mk.op("scalar", lambda e: e.activation(scal3[0:1, 4:5], scal3[0:1, 3:4], AF.Exp, scale=-1.0), None)
        mk.emit("scalar", lambda e: e.drain())
        mk.op("scalar", lambda e: e.activation(scal3[0:1, 5:6], scal3[0:1, 4:5], AF.Ln, bias=1.0), None)
        mk.emit("scalar", lambda e: e.drain())
        mk.op("scalar", lambda e: e.activation(scal3[0:1, 6:7], scal3[0:1, 0:1], AF.Relu), "esp")
        mk.wait("vector", "esp")
        mk.op("vector", lambda e: e.tensor_add(scal3[0:1, 7:8], scal3[0:1, 5:6], scal3[0:1, 6:7]), None)
        mk.emit("vector", lambda e: e.drain())
        mk.op("vector", lambda e: e.tensor_scalar(out=scal3[0:1, 7:8], in0=scal3[0:1, 7:8],
                                                  scalar1=1e-4, scalar2=None, op0=ALU.add), None)
        mk.emit("vector", lambda e: e.drain())
        mk.op("vector", lambda e: e.reciprocal(scal3[0:1, 3:4], scal3[0:1, 7:8]), "esp2")
        mk.wait("tensor", "esp2")
        mk.wait("tensor", "etrpe")
        sc16 = mk.over(cw, [1, 8], BF16, 32)
        mk.wait("vector", "esp2")
        mk.op("vector", lambda e: e.tensor_copy(sc16[:], scal3[0:1, 0:8]), "esc16")
        mk.wait("tensor", "esc16")
        peg([lambda e: e.matmul(ROT3[0][0:128, 0:1], onesrow[0:1, 0:128], sc16[0:1, 3:4], start=True, stop=True),
             lambda e: e.matmul(ROT3[0][0:128, 1:2], onesrow[0:1, 0:128], sc16[0:1, 1:2], start=False, stop=True),
             lambda e: e.matmul(ROT3[0][0:128, 2:3], onesrow[0:1, 0:128], sc16[0:1, 2:3], start=False, stop=True),
             lambda e: e.matmul(ROT3[0][0:128, 4:8], onesrow[0:1, 0:128], b2row[0:1, :], start=False, stop=True),
             lambda e: e.matmul(ROT3[0][0:128, 8:264], onesrow[0:1, 0:128], onwrow[0:1, :], start=False, stop=True)],
            "ebc")
        mk.wait("vector", "ebc")
        mk.op("vector", lambda e: e.tensor_copy(rtemp[:], ROT3[0][:, 0:1]), None)
        mk.op("vector", lambda e: e.tensor_copy(rss128[:], ROT3[0][:, 1:2]), None)
        mk.op("vector", lambda e: e.tensor_copy(rsl128[:], ROT3[0][:, 2:3]), None)
        mk.op("vector", lambda e: e.tensor_copy(b2rep[:], ROT3[0][:, 4:8]), None)
        mk.op("vector", lambda e: e.tensor_copy(onwrep[:], ROT3[0][:, 8:264]), "ebce")

        mk.wait("tensor", "ebce")
        for fi, (Fw, firT, wslot) in enumerate([(Fw64, firT64, 0), (Fw5, firT5, 2)]):
            for ri in range(2):
                for nh in range(2):
                    peg([lambda e, Fw=Fw, firT=firT, ri=ri, nh=nh: e.matmul(
                        ROT3[nh][0:FF, 0:512], Fw[:, ri, :],
                        firT[:].rearrange("k s n -> k (s n)")[:, 512 * nh:512 * nh + 512],
                        start=True, stop=True)], "ewh")
                mk.wait("vector", "ewh")
                mk.op("vector", lambda e, wslot=wslot, ri=ri: e.tensor_copy(
                    Whats[:, wslot + ri, 0:512], ROT3[0][0:FF, 0:512]), None)
                mk.op("vector", lambda e, wslot=wslot, ri=ri: e.tensor_copy(
                    Whats[:, wslot + ri, 512:1024], ROT3[1][0:FF, 0:512]), "ewhe")
                mk.wait("tensor", "ewhe", mk.cnt("ewhe"))

        XB = [pp[0], pp[1], pc[0], pc[1]]
        for b in range(NT):
            mms = []
            for ri in range(2):
                for nh in range(2):
                    def f(e, b=b, ri=ri, nh=nh):
                        return e.matmul(XB[2 * ri + nh][0:FF, :], FBc[:, ri, :],
                                        v_tok.s(b, slice(512 * nh, 512 * nh + 512)),
                                        start=True, stop=(b == 0))
                    mms.append(f)
                    if b > 0:
                        def f2(e, b=b, ri=ri, nh=nh):
                            return e.matmul(XB[2 * ri + nh][0:FF, :], FAc[64:128, ri, :],
                                            v_tok.s(b - 1, slice(512 * nh, 512 * nh + 512))[64:128, :],
                                            start=False, stop=True)
                        mms.append(f2)
            if b > 0:
                mk.wait("tensor", "exe", mk.cnt("exe"))
            peg(mms, "efw")
            mk.wait("scalar", "efw")
            for k4 in range(4):
                sem = "exe" if k4 == 3 else None
                mk.op("scalar", lambda e, k4=k4: e.activation(
                    Xsb[:, k4 // 2, 512 * (k4 % 2):512 * (k4 % 2) + 512],
                    XB[k4][0:FF, :], AF.Copy), sem)
            mk.wait("vector", "exe")
            for fi, wslot in ((0, 0), (1, 2)):
                mk.op("vector", lambda e, wslot=wslot: e.tensor_mul(
                    t_ab[:, 0, :], Xsb[:, 0, :], Whats[:, wslot + 0, :]), None)
                mk.op("vector", lambda e, wslot=wslot: e.tensor_mul(
                    t_ab[:, 1, :], Xsb[:, 1, :], Whats[:, wslot + 1, :]), None)
                mk.emit("vector", lambda e: e.drain())
                mk.op("vector", lambda e: e.tensor_sub(Ysb[:, 0, :], t_ab[:, 0, :], t_ab[:, 1, :]), None)
                mk.op("vector", lambda e, wslot=wslot: e.tensor_mul(
                    t_ab[:, 0, :], Xsb[:, 0, :], Whats[:, wslot + 1, :]), None)
                mk.op("vector", lambda e, wslot=wslot: e.tensor_mul(
                    t_ab[:, 1, :], Xsb[:, 1, :], Whats[:, wslot + 0, :]), None)
                mk.emit("vector", lambda e: e.drain())
                mk.op("vector", lambda e: e.tensor_add(Ysb[:, 1, :], t_ab[:, 0, :], t_ab[:, 1, :]), "epw")
                mk.wait("tensor", "epw")
                if b > 0 or fi > 0:
                    mk.wait("tensor", "eie", mk.cnt("eie"))
                mms = []
                for nh in range(2):
                    for ri in range(2):
                        def f(e, nh=nh, ri=ri):
                            return e.matmul(ROT3[nh][:, :], iFc[:, ri, :],
                                            Ysb[:, ri, 512 * nh:512 * nh + 512],
                                            start=(ri == 0), stop=(ri == 1))
                        mms.append(f)
                peg(mms, "einv")
                mk.wait("vector", "einv")
                mk.wait("scalar", "einv")
                dstT = ll_tok if fi == 0 else ls_tok
                mk.op("vector", lambda e, dstT=dstT, b=b: e.tensor_copy(
                    dstT.s(b, slice(0, 512)), ROT3[0][:, :]), "eie")
                mk.op("scalar", lambda e, dstT=dstT, b=b: e.activation(
                    dstT.s(b, slice(512, 1024)), ROT3[1][:, :], AF.Copy), "eie")
        mk.dbg("Xsb", Xsb[:].rearrange("p a n -> p (a n)"), [FF, 2 * D], BF16, "eie")
        if "Lll" in mk.debug_taps or "Lls" in mk.debug_taps:
            mk.wait("sync", "eie", mk.cnt("eie"))
            mk.ldbg("Lll", [ll_tok.h[0][:].rearrange("p s n -> p (s n)"),
                            ll_tok.h[1][:].rearrange("p s n -> p (s n)")], [256, 8 * D], BF16)
            mk.ldbg("Lls", [ls_tok.h[0][:].rearrange("p s n -> p (s n)"),
                            ls_tok.h[1][:].rearrange("p s n -> p (s n)")], [256, 8 * D], BF16)
        mk.op("vector", lambda e: e.memset(sqscr[0:1, 0:1], 0.0), "efir")

        stats_pass(ll_tok, 1, "efir")
        stats_pass(ls_tok, 0, "efir")
        mk.wait("vector", "ebn2")
        mk.wait("scalar", "eab")
        mk.wait("vector", "eab")
        for j in range(4):
            mk.op("vector", lambda e, j=j: e.tensor_copy(
                stats_tok[:, :, :, 4 * j + 0], mv4[:, :, :, j, 0]), None)
            mk.op("vector", lambda e, j=j: e.tensor_copy(
                stats_tok[:, :, :, 4 * j + 1], mv4[:, :, :, j, 1]), None)
            mk.op("vector", lambda e, j=j: e.tensor_scalar(
                out=stats_tok[:, :, :, 4 * j + 2], in0=aacc[:, :, :, j],
                scalar1=1.0 / DH, scalar2=None, op0=ALU.mult), None)
            mk.op("vector", lambda e, j=j: e.tensor_mul(
                sder[:, :, :], mv4[:, :, :, j, 0], mv4[:, :, :, j, 0]), None)
            mk.emit("vector", lambda e: e.drain())
            mk.op("vector", lambda e, j=j: e.tensor_add(
                sder[:, :, :], sder[:, :, :], mv4[:, :, :, j, 1]), "esd")
            mk.wait("scalar", "esd")
            mk.op("scalar", lambda e, j=j: e.activation(
                stats_tok[:, :, :, 4 * j + 3], sder[:, :, :], AF.Sqrt, scale=float(DH)), "esd2")
        mk.wait("vector", "esd2")
        mk.op("vector", lambda e: e.memset(sqscr[0:1, 0:1], 0.0), "estats")
        if "Lstats" in mk.debug_taps:
            mk.wait("sync", "esd2", mk.cnt("esd2"))
            mk.ldbg("Lstats", stats_tok[:].rearrange("p s h c -> p (s h c)"), [128, NT * H * 16], BF16)

        mk.wait("tensor", "etr", TR_WO)
        mk.wait("tensor", "estats")
        GR = [pp[0], pp[1]]
        HR = [pc[0], pc[1]]
        for tt in range(4):
            for hh_ in range(H):
                for sr in range(4):
                    mk.wait("tensor", "egkt", mk.cnt("egkt"))
                    mk.op("tensor", lambda e, hh_=hh_, sr=sr, tt=tt: e.transpose(
                        pt_bf[0:16, 128 * sr:128 * sr + 128], stats_tok[:, 4 * tt + sr, hh_, :],
                        ident[:]), "egk")
                mk.wait("vector", "egk")
                mk.op("vector", lambda e, hh_=hh_: e.tensor_copy(
                    statsKt[:, hh_, :, :].rearrange("p a n -> p (a n)"), pt_bf[0:16, :]), "egkt")
            for os_ in range(ND):
                gb = GR[os_ % 2]
                if os_ >= 2:
                    mk.wait("tensor", "eg0e", mk.cnt("eg0e") - 1)
                mms = []
                for ks in range(ND):
                    def f(e, gb=gb, ks=ks, os_=os_, tt=tt):
                        return e.matmul(gb[:], W1xT[:, ks, 128 * os_:128 * os_ + 128],
                                        x_feat.ks(ks, slice(512 * tt, 512 * tt + 512)),
                                        start=(ks == 0), stop=(ks == ND - 1))
                    mms.append(f)
                peg(mms, "eg0")
                mk.wait("scalar", "eg0")
                mk.op("scalar", lambda e, gb=gb, os_=os_: e.activation(
                    G0t[:, os_, :], gb[:], AF.Copy), "eg0e")
            for hh_ in range(H):
                mk.wait("tensor", "eg0e", mk.cnt("eg0e"))
                mk.wait("tensor", "egkt", mk.cnt("egkt"))
                for os_ in range(ND):
                    hb = HR[os_ % 2]
                    if os_ >= 2 or hh_ > 0:
                        mk.wait("tensor", "ehde", mk.cnt("ehde") - 1)
                    peg([lambda e, hb=hb, os_=os_, hh_=hh_: e.matmul(
                            hb[:], W1sT[:, 128 * os_:128 * os_ + 128], statsKt[:, hh_, :, :].rearrange("p a n -> p (a n)"),
                            start=True, stop=False),
                         lambda e, hb=hb, os_=os_: e.matmul(
                            hb[:], ident[:], G0t[:, os_, :], start=False, stop=True)], "ehd")
                    mk.wait("scalar", "ehd")
                    mk.op("scalar", lambda e, hb=hb, os_=os_: e.activation(
                        hdnt[:, os_, :], hb[:], AF.Gelu, bias=b1f[:, os_:os_ + 1]), "ehde")
                mk.wait("tensor", "ehde", mk.cnt("ehde"))
                mms = []
                for c4 in range(4):
                    for os_ in range(ND):
                        def f(e, c4=c4, os_=os_):
                            return e.matmul(pm[:, 4 * c4:4 * c4 + 4],
                                            hdnt[:, os_, 128 * c4:128 * c4 + 128], W2T[:, os_, :],
                                            start=(os_ == 0), stop=(os_ == ND - 1))
                        mms.append(f)
                peg(mms, "elg")
                mk.wait("vector", "elg")
                mk.op("vector", lambda e, tt=tt, hh_=hh_: e.tensor_copy(
                    logits_tok[:, 4 * tt:4 * tt + 4, hh_, :], pm[:, 0:16].rearrange("p (a b) -> p a b", a=4)), "elge")
                mk.wait("tensor", "elge", mk.cnt("elge"))
        mk.wait("vector", "elge")
        mk.op("vector", lambda e: e.tensor_add(
            logits_tok[:], logits_tok[:],
            b2rep[:, None, None, :].broadcast_to([128, NT, H, H])), None)
        mk.emit("vector", lambda e: e.drain())
        mk.op("vector", lambda e: e.reduce_max(
            smax[:], logits_tok[:], mybir.AxisListType.X), None)
        mk.emit("vector", lambda e: e.drain())
        mk.op("vector", lambda e: e.tensor_sub(
            logits_tok[:], logits_tok[:], smax[:].broadcast_to([128, NT, H, H])), "esm1")
        mk.wait("scalar", "esm1")
        mk.op("scalar", lambda e: e.activation(
            fw_tok[:], logits_tok[:], AF.Exp, scale=rtemp[:, 0:1]), "esm2")
        mk.wait("vector", "esm2")
        mk.op("vector", lambda e: e.reduce_sum(
            smax[:], fw_tok[:], mybir.AxisListType.X), None)
        mk.emit("vector", lambda e: e.drain())
        mk.op("vector", lambda e: e.reciprocal(smax[:], smax[:]), None)
        mk.emit("vector", lambda e: e.drain())
        mk.op("vector", lambda e: e.tensor_mul(
            fw_tok[:], fw_tok[:], smax[:].broadcast_to([128, NT, H, H])), None)
        mk.emit("vector", lambda e: e.drain())
        mk.op("vector", lambda e: e.tensor_scalar(
            out=fw_tok[:, :, :, 0], in0=fw_tok[:, :, :, 0],
            scalar1=rss128[:, 0:1], scalar2=None, op0=ALU.add), None)
        mk.emit("vector", lambda e: e.drain())
        mk.op("vector", lambda e: e.tensor_scalar(
            out=fw_tok[:, :, :, 1], in0=fw_tok[:, :, :, 1],
            scalar1=rsl128[:, 0:1], scalar2=None, op0=ALU.add), "efwr")

        mk.wait("sync", "efwr")
        mk.ldbg("Lfw", fw_tok[:].rearrange("p s h c -> p (s h c)"), [128, NT * H * H], F32)
        mk.ldbg("Lvm", [v_tok.h[0][:].rearrange("p s n -> p (s n)"),
                        v_tok.h[1][:].rearrange("p s n -> p (s n)")], [256, 8 * D], BF16)
        mk.ldbg("Lllm", [ll_tok.h[0][:].rearrange("p s n -> p (s n)"),
                         ll_tok.h[1][:].rearrange("p s n -> p (s n)")], [256, 8 * D], BF16)
        mk.ldbg("Llsm", [ls_tok.h[0][:].rearrange("p s n -> p (s n)"),
                         ls_tok.h[1][:].rearrange("p s n -> p (s n)")], [256, 8 * D], BF16)
        for s in range(NT):
            if s >= 2:
                mk.wait("sync", "emix", s - 1)
            mk.dma("sync", dstage[:, s % 2, :], dscr[s // 8, :, s % 8, :], sem=f"edst{s % 2}")
            mk.wait("vector", f"edst{s % 2}", mk.cnt(f"edst{s % 2}"))
            mk.wait("vector", "efwr")

            def bc(j, s=s):
                return fw_tok[:, s, :, j:j + 1].broadcast_to([128, H, DH])
            mk.op("vector", lambda e, a=dstage[:, s % 2, :].rearrange("p (h d) -> p h d", h=H), b=bc(2):
                  e.tensor_tensor(mixA[:].rearrange("p (h d) -> p h d", h=H), a, b, ALU.mult), None)
            mk.op("vector", lambda e, a=ll_tok.s(s).rearrange("p (h d) -> p h d", h=H), b=bc(1):
                  e.tensor_tensor(mixB[:].rearrange("p (h d) -> p h d", h=H), a, b, ALU.mult), None)
            mk.emit("vector", lambda e: e.drain())
            mk.op("vector", lambda e: e.tensor_add(mixA[:], mixA[:], mixB[:]), None)
            mk.op("vector", lambda e, a=ls_tok.s(s).rearrange("p (h d) -> p h d", h=H), b=bc(0):
                  e.tensor_tensor(mixB[:].rearrange("p (h d) -> p h d", h=H), a, b, ALU.mult), None)
            mk.emit("vector", lambda e: e.drain())
            mk.op("vector", lambda e: e.tensor_add(mixA[:], mixA[:], mixB[:]), None)
            mk.op("vector", lambda e, a=v_tok.s(s).rearrange("p (h d) -> p h d", h=H), b=bc(3):
                  e.tensor_tensor(mixB[:].rearrange("p (h d) -> p h d", h=H), a, b, ALU.mult), None)
            mk.emit("vector", lambda e: e.drain())
            mk.op("vector", lambda e, a=o_tok.s(s): e.tensor_add(a, mixA[:], mixB[:]), "emix")
            mk.wait("scalar", "emix", mk.cnt("emix"))
            for hh_ in range(H):
                sem = "erms" if hh_ == H - 1 else None
                mk.op("scalar", lambda e, s=s, hh_=hh_: e.activation(
                    sqscr[:], o_tok.s(s, slice(DH * hh_, DH * hh_ + DH)), AF.Square,
                    accum_out=rmsacc[:, s, hh_:hh_ + 1]), sem)
        if "Lo2" in mk.debug_taps:
            mk.wait("sync", "emix", mk.cnt("emix"))
            mk.ldbg("Lo2", [o_tok.h[0][:].rearrange("p s n -> p (s n)"),
                            o_tok.h[1][:].rearrange("p s n -> p (s n)")], [256, 8 * D], BF16)
            mk.wait("vector", "st_out", mk.cnt("st_out"))
        mk.wait("vector", "erms")
        mk.op("vector", lambda e: e.tensor_scalar(
            out=rmsacc[:], in0=rmsacc[:], scalar1=1.0 / DH, scalar2=EPS,
            op0=ALU.mult, op1=ALU.add), "erst")
        mk.wait("scalar", "erst")
        mk.op("scalar", lambda e: e.activation(rmsacc[:], rmsacc[:], AF.Sqrt), "erst2")
        mk.wait("vector", "erst2")
        mk.op("vector", lambda e: e.reciprocal(rmsacc[:], rmsacc[:]), "erst3")
        mk.wait("vector", "erst3")
        for s in range(NT):
            for hh_ in range(H):
                sem = "eon" if (s == NT - 1 and hh_ == H - 1) else None
                mk.op("vector", lambda e, s=s, hh_=hh_: e.scalar_tensor_tensor(
                    out=o_tok.s(s, slice(DH * hh_, DH * hh_ + DH)),
                    in0=o_tok.s(s, slice(DH * hh_, DH * hh_ + DH)),
                    scalar=rmsacc[:, s, hh_:hh_ + 1], in1=onwrep[:],
                    op0=ALU.mult, op1=ALU.mult), sem)
        mk.wait("sync", "eon")
        mk.ldbg("Lo", [o_tok.h[0][:].rearrange("p s n -> p (s n)"),
                       o_tok.h[1][:].rearrange("p s n -> p (s n)")], [256, 8 * D], BF16)
        dmaT_feat_from_tok(mk, o_feat, o_tok)
        TR_O = mk.cnt("tr")

        if "Lof" in mk.debug_taps:
            mk.wait("sync", "tr", TR_O)
            mk.ldbg("Lof", [o_feat.h[0][:].rearrange("p s n -> p (s n)"),
                            o_feat.h[1][:].rearrange("p s n -> p (s n)")], [256, 4 * T], BF16)
        if "Lonw" in mk.debug_taps:
            mk.wait("sync", "ebce", mk.cnt("ebce"))
            mk.ldbg("Lonw", onwrep[:], [128, DH], BF16)
        mk.wait("tensor", "tr", TR_O)
        for tt in range(4):
            for ds2 in range(ND):
                zb = GR[ds2 % 2]
                if ds2 >= 2 or tt > 0:
                    mk.wait("tensor", "eze", mk.cnt("eze") - 1)
                mms = []
                for cs in range(ND):
                    def f(e, zb=zb, cs=cs, ds2=ds2, tt=tt):
                        return e.matmul(zb[:], WoT[:, cs, 128 * ds2:128 * ds2 + 128],
                                        o_feat.ks(cs, slice(512 * tt, 512 * tt + 512)),
                                        start=(cs == 0), stop=False)
                    mms.append(f)
                mms.append(lambda e, zb=zb, ds2=ds2, tt=tt: e.matmul(
                    zb[:], ident[:], x_feat.ks(ds2, slice(512 * tt, 512 * tt + 512)),
                    start=False, stop=True))
                peg(mms, "ez")
                mk.wait("scalar", "ez")
                mk.op("scalar", lambda e, zb=zb, ds2=ds2: e.activation(
                    zstage[:, ds2, :], zb[:], AF.Copy), "eze")
            mk.wait("sync", "eze", mk.cnt("eze"))
            if tt > 0:
                mk.wait("sync", "ex1t", mk.cnt("ex1t"))
            if tt == 0:
                mk.ldbg("Lz0", zstage[:].rearrange("p s n -> p (s n)"), [128, ND * 512], BF16)
            for ds in range(ND):
                mk.dmaT(ztokt[:, :, 128 * ds:128 * ds + 128], zstage[:, ds, :], sem="eztr")
            ZTR = mk.cnt("eztr")
            if tt == 0 and "Lzt0" in mk.debug_taps:
                mk.wait("sync", "eztr", ZTR)
                mk.ldbg("Lzt0", ztokt[:].rearrange("p s n -> p (s n)"), [128, 4 * D], BF16)
            mk.wait("vector", "eztr", ZTR)
            for sr in range(4):
                for hf2 in range(2):
                    sem = "ebnz" if (sr == 3 and hf2 == 1) else None
                    mk.op("vector", lambda e, sr=sr, hf2=hf2: e.bn_stats(
                        bn6[:, 2 * sr + hf2, 0, :], ztokt[:, sr, 512 * hf2:512 * hf2 + 512]), sem)
            mk.emit("vector", lambda e: e.drain())
            for sr in range(4):
                sem = "ebnz2" if sr == 3 else None
                mk.op("vector", lambda e, sr=sr: e.bn_aggr(
                    lnmv[:, sr, :], bn6[:, 2 * sr:2 * sr + 2, 0, :]), sem)
            mk.emit("vector", lambda e: e.drain())
            mk.op("vector", lambda e: e.tensor_scalar(
                out=lnmv[:, 0:4, 1], in0=lnmv[:, 0:4, 1], scalar1=EPS, scalar2=None, op0=ALU.add), "elnv")
            mk.wait("scalar", "elnv", mk.cnt("elnv"))
            mk.op("scalar", lambda e: e.activation(lnmv[:, 0:4, 1], lnmv[:, 0:4, 1], AF.Sqrt), "elns")
            mk.wait("vector", "elns", mk.cnt("elns"))
            mk.op("vector", lambda e: e.reciprocal(lnmv[:, 0:4, 1], lnmv[:, 0:4, 1]), "elnr")
            mk.emit("vector", lambda e: e.drain())
            for sr in range(4):
                sem = "ex1" if sr == 3 else None
                mk.op("vector", lambda e, sr=sr: e.tensor_scalar(
                    out=ztokt[:, sr, :], in0=ztokt[:, sr, :],
                    scalar1=lnmv[:, sr, 0:1], scalar2=lnmv[:, sr, 1:2],
                    op0=ALU.subtract, op1=ALU.mult), sem)
            mk.wait("sync", "ex1", mk.cnt("ex1"))
            if tt == 0:
                mk.ldbg("Lzn0", ztokt[:].rearrange("p s n -> p (s n)"), [128, 4 * D], BF16)
                mk.ldbg("Llnmv", lnmv[:].rearrange("p s n -> p (s n)"), [128, 2 * ND], F32)
            for sr in range(4):
                s_ = 4 * tt + sr
                for half in range(2):
                    mk.dmaT(x1_feat.h[half][:, :, 128 * s_:128 * s_ + 128],
                            ztokt[:, sr, 512 * half:512 * half + 512], sem="ex1t")
        mk.wait("vector", "ex1t", mk.cnt("ex1t"))
        for half in range(2):
            sem = "ehcp" if half == 1 else None
            mk.op("vector", lambda e, half=half: e.tensor_copy(
                hfeat.h[half][:], x1_feat.h[half][:]), sem)

        if "Lx1" in mk.debug_taps:
            mk.wait("sync", "ex1t", mk.cnt("ex1t"))
            mk.ldbg("Lx1", [x1_feat.h[0][:].rearrange("p s n -> p (s n)"),
                            x1_feat.h[1][:].rearrange("p s n -> p (s n)")], [256, 4 * T], BF16)

        YR = [pp[0], pp[1]]

        def qlora(i, IN, OUT_IS_H, want_ln_gelu):
            b = i // 2
            WdqTc = WdqTs[i % 2]
            qbrowc = qbrows[i % 2]
            mk.wait("gpsimd", "etrpe")
            mk.dma("gpsimd", larow[0:32, :], la_in[i], sem="eql")
            mk.dma("gpsimd", lbstg[:], lb_in[i].rearrange("(s p) r -> p s r", p=128), sem="eql")
            mk.dma("gpsimd", qbrowc[:], qb_in[i][None, :], sem="eql")
            mk.dma("gpsimd", qsF[:], qs_in[i].rearrange("(s p) g -> p s g", p=128), sem="eql")
            if want_ln_gelu:
                mk.dma("gpsimd", lngrow[:], lng_in[b][None, :], sem="eql")
                mk.dma("gpsimd", lnbrow[:], lnb_in[b][None, :], sem="eql")
            QL = mk.cnt("eql")
            mk.wait("tensor", "eql", QL)
            for k4 in range(2):
                mk.wait("tensor", "etrpe")
                for j4 in range(4):
                    sem = "etrpt" if j4 == 3 else None
                    mk.op("tensor", lambda e, ds=4 * k4 + j4, j4=j4: e.transpose(
                        pt_bf[0:128, 32 * j4:32 * j4 + 32],
                        larow[0:32, 128 * ds:128 * ds + 128], ident[0:32, 0:32]), sem)
                mk.wait("vector", "etrpt")
                mk.op("vector", lambda e, k4=k4: e.tensor_copy(
                    laT[:, 4 * k4:4 * k4 + 4, :].rearrange("p a n -> p (a n)"),
                    pt_bf[0:128, 0:128]), "etrpe")
            for k4 in range(2):
                mk.wait("tensor", "etrpe")
                for j4 in range(4):
                    sem = "etrpt" if j4 == 3 else None
                    mk.op("tensor", lambda e, ds=4 * k4 + j4, j4=j4: e.transpose(
                        pt_bf[0:32, 128 * j4:128 * j4 + 128],
                        lbstg[:, ds, :], ident[:]), sem)
                mk.wait("vector", "etrpt")
                mk.op("vector", lambda e, k4=k4: e.tensor_copy(
                    lbT[:, 4 * k4:4 * k4 + 4, :].rearrange("p a n -> p (a n)"),
                    pt_bf[0:32, 0:512]), "etrpe")
            if want_ln_gelu:
                mk.wait("tensor", "egbe", mk.cnt("egbe"))
                peg([lambda e: e.matmul(ROT3[2][:, 0:512], onesrow[0:1, 0:128], lngrow[0:1, 0:512],
                                        start=True, stop=True),
                     lambda e: e.matmul(ROT3[1][:, 0:512],
                                        onesrow[0:1, 0:128], lngrow[0:1, 512:1024], start=True, stop=True)],
                    "egb")
                mk.wait("vector", "egb")
                mk.op("vector", lambda e: e.tensor_copy(grep[:, 0:512], ROT3[2][:, 0:512]), None)
                mk.op("vector", lambda e: e.tensor_copy(grep[:, 512:1024], ROT3[1][:, 0:512]), "egbe")
                mk.wait("tensor", "egbe", mk.cnt("egbe"))
                peg([lambda e: e.matmul(ROT3[2][:, 0:512], onesrow[0:1, 0:128], lnbrow[0:1, 0:512],
                                        start=True, stop=True),
                     lambda e: e.matmul(ROT3[1][:, 0:512], onesrow[0:1, 0:128], lnbrow[0:1, 512:1024],
                                        start=True, stop=True)], "egb")
                mk.wait("vector", "egb")
                mk.op("vector", lambda e: e.tensor_copy(brep[:, 0:512], ROT3[2][:, 0:512]), None)
                mk.op("vector", lambda e: e.tensor_copy(brep[:, 512:1024], ROT3[1][:, 0:512]), "egbe")
            for hf in range(2):
                mk.wait("sync", "eqdq")
                if i >= 2 and hf == 0:
                    mk.wait("sync", "eqmm", i - 1)
                mk.dma("sync", qwstg[:], qw_in[i].rearrange("(s p) d -> p s d", p=128)[:, 4 * hf:4 * hf + 4, :],
                       sem="eqw")
                mk.wait("scalar", "eqw", mk.cnt("eqw"))
                mk.wait("scalar", "eql", QL)
                mk.op("scalar", lambda e: e.activation(qprim[:], qwstg[:], AF.Copy,
                                                       scale=1.0 / 7.5, bias=-1.0), "eqa")
                mk.wait("vector", "eqa", mk.cnt("eqa"))
                for osr in range(4):
                    sem = "eqd" if osr == 3 else None
                    mk.op("vector", lambda e, osr=osr, hf=hf: e.tensor_tensor(
                        qprim[:, osr, :].rearrange("p (g r) -> p g r", g=64),
                        qprim[:, osr, :].rearrange("p (g r) -> p g r", g=64),
                        qsF[:, 4 * hf + osr, :, None].broadcast_to([128, 64, 16]), ALU.mult), sem)
                mk.wait("sync", "eqd", mk.cnt("eqd"))
                for osr in range(4):
                    mk.dmaT(WdqTc[:, :, 128 * (4 * hf + osr):128 * (4 * hf + osr) + 128],
                            qprim[:, osr, :], sem="eqdq")
            DQ = mk.cnt("eqdq")
            mk.wait("tensor", "eqdq", DQ)
            mk.wait("tensor", "etrpe", mk.cnt("etrpe"))
            mk.wait("tensor", "ehcp")
            mk.wait("tensor", "ezf", mk.cnt("ezf"))
            mk.wait("tensor", "eye", mk.cnt("eye"))
            for tt in range(4):
                mms = []
                for ks in range(ND):
                    def f(e, ks=ks, tt=tt, IN=IN):
                        return e.matmul(pc[0][0:32, :], laT[:, ks, :],
                                        IN.ks(ks, slice(512 * tt, 512 * tt + 512)),
                                        start=(ks == 0), stop=(ks == ND - 1))
                    mms.append(f)
                peg(mms, "ezl")
                mk.wait("vector", "ezl")
                mk.op("vector", lambda e, tt=tt: e.tensor_copy(
                    Zlora[0:32, 512 * tt:512 * tt + 512], pc[0][0:32, :]), "ezle")
                mk.wait("tensor", "ezle", mk.cnt("ezle"))
            for tt in range(4):
                for os_ in range(ND):
                    yb = YR[os_ % 2]
                    if os_ >= 2 or tt > 0:
                        mk.wait("tensor", "eye", mk.cnt("eye") - 1)
                    mms = []
                    for ks in range(ND):
                        def f(e, yb=yb, ks=ks, os_=os_, tt=tt, IN=IN, WdqTc=WdqTc):
                            return e.matmul(yb[:], WdqTc[:, ks, 128 * os_:128 * os_ + 128],
                                            IN.ks(ks, slice(512 * tt, 512 * tt + 512)),
                                            start=(ks == 0), stop=False)
                        mms.append(f)
                    mms.append(lambda e, yb=yb, os_=os_, qbrowc=qbrowc: e.matmul(
                        yb[:], qbrowc[0:1, 128 * os_:128 * os_ + 128], ones512[0:1, :],
                        start=False, stop=False))
                    mms.append(lambda e, yb=yb, os_=os_, tt=tt: e.matmul(
                        yb[:], lbT[0:32, os_, :], Zlora[0:32, 512 * tt:512 * tt + 512],
                        start=False, stop=not OUT_IS_H))
                    if OUT_IS_H:
                        mms.append(lambda e, yb=yb, os_=os_, tt=tt: e.matmul(
                            yb[:], ident[:], hfeat.ks(os_, slice(512 * tt, 512 * tt + 512)),
                            start=False, stop=True))
                    peg(mms, "eym")
                    mk.wait("scalar", "eym")
                    if OUT_IS_H:
                        mk.op("scalar", lambda e, yb=yb, os_=os_, tt=tt: e.activation(
                            hfeat.ks(os_, slice(512 * tt, 512 * tt + 512)), yb[:], AF.Copy), "eye")
                    else:
                        mk.op("scalar", lambda e, yb=yb, os_=os_: e.activation(
                            ystage[:, os_, :], yb[:], AF.Copy), "eye")
                if not OUT_IS_H:
                    mk.wait("sync", "eye", mk.cnt("eye"))
                    mk.wait("sync", "ezf", mk.cnt("ezf"))
                    for ds in range(ND):
                        mk.dmaT(ytokt[:, :, 128 * ds:128 * ds + 128], ystage[:, ds, :], sem="eytr")
                    mk.wait("vector", "eytr", mk.cnt("eytr"))
                    for sr in range(4):
                        for hf2 in range(2):
                            sem = "ebny" if (sr == 3 and hf2 == 1) else None
                            mk.op("vector", lambda e, sr=sr, hf2=hf2: e.bn_stats(
                                bn6[:, 2 * sr + hf2, 0, :], ytokt[:, sr, 512 * hf2:512 * hf2 + 512]), sem)
                    mk.emit("vector", lambda e: e.drain())
                    for sr in range(4):
                        sem = "ebny2" if sr == 3 else None
                        mk.op("vector", lambda e, sr=sr: e.bn_aggr(
                            lnmv[:, sr, :], bn6[:, 2 * sr:2 * sr + 2, 0, :]), sem)
                    mk.emit("vector", lambda e: e.drain())
                    mk.op("vector", lambda e: e.tensor_scalar(
                        out=lnmv[:, 0:4, 1], in0=lnmv[:, 0:4, 1], scalar1=EPS, scalar2=None,
                        op0=ALU.add), "elnv2")
                    mk.wait("scalar", "elnv2", mk.cnt("elnv2"))
                    mk.op("scalar", lambda e: e.activation(lnmv[:, 0:4, 1], lnmv[:, 0:4, 1], AF.Sqrt), "elns2")
                    mk.wait("vector", "elns2", mk.cnt("elns2"))
                    mk.op("vector", lambda e: e.reciprocal(lnmv[:, 0:4, 1], lnmv[:, 0:4, 1]), None)
                    mk.emit("vector", lambda e: e.drain())
                    for sr in range(4):
                        mk.op("vector", lambda e, sr=sr: e.tensor_scalar(
                            out=ytokt[:, sr, :], in0=ytokt[:, sr, :],
                            scalar1=lnmv[:, sr, 0:1], scalar2=lnmv[:, sr, 1:2],
                            op0=ALU.subtract, op1=ALU.mult), None)
                    mk.emit("vector", lambda e: e.drain())
                    mk.op("vector", lambda e: e.tensor_tensor(
                        ytokt[:], ytokt[:], grep[:, None, :].broadcast_to([128, 4, D]), ALU.mult), None)
                    mk.emit("vector", lambda e: e.drain())
                    mk.op("vector", lambda e: e.tensor_tensor(
                        ytokt[:], ytokt[:], brep[:, None, :].broadcast_to([128, 4, D]), ALU.add), "elngb")
                    mk.wait("scalar", "elngb", mk.cnt("elngb"))
                    mk.op("scalar", lambda e: e.activation(ytokt[:], ytokt[:], AF.Gelu), "egel")
                    mk.wait("sync", "egel", mk.cnt("egel"))
                    for sr in range(4):
                        s_ = 4 * tt + sr
                        for half in range(2):
                            mk.dmaT(z_feat.h[half][:, :, 128 * s_:128 * s_ + 128],
                                    ytokt[:, sr, 512 * half:512 * half + 512], sem="ezf")
            mk.wait("gpsimd", "eym")
            hqm = mk.inc("eqmm", eng="gpsimd")
            mk.emit("gpsimd", lambda e, hqm=hqm: e.sem_inc(hqm, 1))

        mk.wait("gpsimd", "ehcp")
        for b in range(3):
            IN1 = hfeat
            qlora(2 * b, IN1, False, True)
            qlora(2 * b + 1, z_feat, True, False)
            if f"Lh{b + 1}" in mk.debug_taps:
                mk.wait("sync", "eye", mk.cnt("eye"))
                mk.ldbg(f"Lh{b + 1}", [hfeat.h[0][:].rearrange("p s n -> p (s n)"),
                                       hfeat.h[1][:].rearrange("p s n -> p (s n)")], [256, 4 * T], BF16)

        mk.wait("tensor", "eye", mk.cnt("eye"))
        for tt in range(4):
            if tt > 0:
                mk.wait("scalar", "st_out", mk.cnt("st_out"))
            for ds2 in range(ND):
                zb = YR[ds2 % 2]
                if ds2 >= 2 or tt > 0:
                    mk.wait("tensor", "efze", mk.cnt("efze") - 1)
                peg([lambda e, zb=zb, ds2=ds2, tt=tt: e.matmul(
                        zb[:], ident[:], x1_feat.ks(ds2, slice(512 * tt, 512 * tt + 512)),
                        start=True, stop=False),
                     lambda e, zb=zb, ds2=ds2, tt=tt: e.matmul(
                        zb[:], ident[:], hfeat.ks(ds2, slice(512 * tt, 512 * tt + 512)),
                        start=False, stop=True)], "efz")
                mk.wait("scalar", "efz")
                mk.op("scalar", lambda e, zb=zb, ds2=ds2: e.activation(
                    ystage[:, ds2, :], zb[:], AF.Copy), "efze")
            mk.wait("sync", "efze", mk.cnt("efze"))
            for ds in range(ND):
                mk.dmaT(ln2tok[:, :, 128 * ds:128 * ds + 128], ystage[:, ds, :], sem="el2t")
            mk.wait("vector", "el2t", mk.cnt("el2t"))
            for sr in range(4):
                for hf2 in range(2):
                    sem = "ebnf" if (sr == 3 and hf2 == 1) else None
                    mk.op("vector", lambda e, sr=sr, hf2=hf2: e.bn_stats(
                        bn6[:, 2 * sr + hf2, 0, :], ln2tok[:, sr, 512 * hf2:512 * hf2 + 512]), sem)
            mk.emit("vector", lambda e: e.drain())
            for sr in range(4):
                sem = "ebnf2" if sr == 3 else None
                mk.op("vector", lambda e, sr=sr: e.bn_aggr(
                    lnmv[:, sr, :], bn6[:, 2 * sr:2 * sr + 2, 0, :]), sem)
            mk.emit("vector", lambda e: e.drain())
            mk.op("vector", lambda e: e.tensor_scalar(
                out=lnmv[:, 0:4, 1], in0=lnmv[:, 0:4, 1], scalar1=EPS, scalar2=None,
                op0=ALU.add), "efnv")
            mk.wait("scalar", "efnv", mk.cnt("efnv"))
            mk.op("scalar", lambda e: e.activation(lnmv[:, 0:4, 1], lnmv[:, 0:4, 1], AF.Sqrt), "efns")
            mk.wait("vector", "efns", mk.cnt("efns"))
            mk.op("vector", lambda e: e.reciprocal(lnmv[:, 0:4, 1], lnmv[:, 0:4, 1]), None)
            mk.emit("vector", lambda e: e.drain())
            for sr in range(4):
                sem = "efout" if sr == 3 else None
                mk.op("vector", lambda e, sr=sr: e.tensor_scalar(
                    out=outstage[:, sr, :], in0=ln2tok[:, sr, :],
                    scalar1=lnmv[:, sr, 0:1], scalar2=lnmv[:, sr, 1:2],
                    op0=ALU.subtract, op1=ALU.mult), sem)
            mk.wait("sync", "efout", mk.cnt("efout"))
            for sr in range(4):
                s_ = 4 * tt + sr
                mk.dma("sync", out_dram[128 * s_:128 * s_ + 128, :], outstage[:, sr, :], sem="st_out")

    emit_epilogue()


    def tok_dump(t):
        return [t.h[0][:].rearrange("p s d -> p (s d)"), t.h[1][:].rearrange("p s d -> p (s d)")]
    mk.dbg("qn_tok", tok_dump(qn_tok), [256, 8 * D], BF16, "q_tok_rdy")
    mk.dbg("delta_tok", tok_dump(delta_tok), [256, 8 * D], BF16, "mdone")
    if delta_stage == 'all' and delta_macros == NT:
        mk.dbg("ll_tok", tok_dump(ll_tok), [256, 8 * D], BF16, "eie")
        mk.dbg("ls_tok", tok_dump(ls_tok), [256, 8 * D], BF16, "eie")
        mk.dbg("stats_tok", stats_tok[:].rearrange("p s h k -> p (s h k)"), [128, NT * H * 16], BF16, "estats")
        mk.dbg("fw_tok", fw_tok[:].rearrange("p s h j -> p (s h j)"), [128, NT * H * H], F32, "efwr")
        mk.dbg("o_tok", tok_dump(o_tok), [256, 8 * D], BF16, "eon")
        mk.dbg("Whats", Whats[:].rearrange("p a n -> p (a n)"), [FF, 4 * D], BF16, "ewhe")
        mk.dbg("mv4", mv4[:].rearrange("p s h j a -> p (s h j a)"), [128, NT * H * 8], BF16, "estats")

    for nm_ in ("S1", "S1t", "IB", "tmpA", "P2"):
        mk.dbg("inv_" + nm_, invm[nm_][:].rearrange("p h n -> p (h n)"), [128, H * 128], BF16, "mdone")
    mk.dbg("uadj", uadj[:], [128, D], BF16, "mdone")
    mk.dbg("GTq", GTq[:].rearrange("p h n -> p (h n)"), [128, H * 128], BF16, "mdone")
    mk.dbg("GTw", GTw[:].rearrange("p h n -> p (h n)"), [128, H * 128], BF16, "mdone")
    mk.dbg("S_sb", S_sb[:].rearrange("p h k d -> p (h k d)"), [128, H * 2 * DH], BF16, "mdone")
    mk.dbg("w_sl", w_sl[:].rearrange("p h k n -> p (h k n)"), [128, H * 2 * 128], BF16, "mdone")
    mk.dbg("kn_tok", tok_dump(kn_tok), [256, 8 * D], BF16, "k_tok_rdy")
    mk.dbg("v_tok", tok_dump(v_tok), [256, 8 * D], BF16, "v_tok_rdy")
    mk.dbg("beta_tok", beta_tok[:].rearrange("p s h -> p (s h)"), [128, NT * H], F32, "beta_tok_rdy")
    mk.dbg("x_feat", [x_feat.h[0][:].rearrange("p s t -> p (s t)"),
                      x_feat.h[1][:].rearrange("p s t -> p (s t)")], [256, 4 * T], BF16, "tr")

    mk.finalize(out_dram, None)
    return mk


def shard_inputs(inputs):
    x = np.ascontiguousarray(inputs["L0_attn_hidden_states"])
    shared = {
        "wq": np.ascontiguousarray(inputs["L0_attn_q_proj_w"]),
        "wk": np.ascontiguousarray(inputs["L0_attn_k_proj_w"]),
        "wv": np.ascontiguousarray(inputs["L0_attn_v_proj_w"]),
        "wb": np.ascontiguousarray(inputs["L0_attn_b_proj_w"]),
        "cq": np.ascontiguousarray(inputs["L0_attn_q_conv_w"]),
        "ck": np.ascontiguousarray(inputs["L0_attn_k_conv_w"]),
        "cv": np.ascontiguousarray(inputs["L0_attn_v_conv_w"]),
        "fw1": np.ascontiguousarray(inputs["L0_attn_fuse_w1"]),
        "fb1": np.ascontiguousarray(inputs["L0_attn_fuse_b1"]),
        "fw2": np.ascontiguousarray(inputs["L0_attn_fuse_w2"]),
        "fb2": np.ascontiguousarray(inputs["L0_attn_fuse_b2"]),
        "ltemp": np.ascontiguousarray(inputs["L0_attn_logit_temp"]),
        "rss": np.ascontiguousarray(inputs["L0_attn_res_scale_short"]),
        "rsl": np.ascontiguousarray(inputs["L0_attn_res_scale_long"]),
        "onw": np.ascontiguousarray(inputs["L0_attn_o_norm_w"]),
        "wo": np.ascontiguousarray(inputs["L0_attn_o_proj_w"]),
        "firs": np.ascontiguousarray(inputs["L0_attn_fir_short_w"]),
        "firl": np.ascontiguousarray(inputs["L0_attn_fir_long_w"]),
        "qw": np.ascontiguousarray(inputs["L0_ffn_qweights"]),
        "qs": np.ascontiguousarray(inputs["L0_ffn_scales"]),
        "qb": np.ascontiguousarray(inputs["L0_ffn_biases"]),
        "la": np.ascontiguousarray(inputs["L0_ffn_lora_a"]),
        "lb": np.ascontiguousarray(inputs["L0_ffn_lora_b"]),
        "lng": np.ascontiguousarray(inputs["L0_ffn_ln_gamma"]),
        "lnb": np.ascontiguousarray(inputs["L0_ffn_ln_beta"]),
    }
    maps = []
    for c in range(NCORES):
        m = dict(shared)
        m["x"] = np.ascontiguousarray(x[c])
        maps.append(m)
    return maps


def kernel(**inputs):
    mk = build_program()
    nc = mk.build()
    maps = shard_inputs(inputs)
    res = run_bass_kernel_spmd(nc, maps, core_ids=list(range(NCORES)))
    return np.stack([r["out"] for r in res.results], axis=0)
